# version 04
# baseline (speedup 1.0000x reference)
import numpy as np
import ml_dtypes
import concourse.bass as bass
import concourse.mybir as mybir
from concourse.bass_utils import run_bass_kernel_spmd

F32 = mybir.dt.float32
BF16 = mybir.dt.bfloat16
AF = mybir.ActivationFunctionType
ALU = mybir.AluOpType
AX = mybir.AxisListType
BF = ml_dtypes.bfloat16

B, T, D = 4, 4096, 2048
NCORES = 8


class _Stream:
    def __init__(self, kb, name, sems):
        self.kb = kb
        self.name = name
        self.sems = sems
        self.ops = []
        self.known = {}
        self.snapshots = []

class KB:
    ENGINES = ("pe", "dve", "act", "gp", "sync")

    DMA_ROT = 6

    def __init__(self, nc):
        self.nc = nc
        self.streams = {}
        for e in self.ENGINES:
            sems = {e + "_c": 0}
            if e in ("gp", "sync", "act", "dve"):
                for r in range(self.DMA_ROT):
                    sems[f"{e}_d{r}"] = 0
            self.streams[e] = _Stream(self, e, sems)
        self._dma_idx = {e: 0 for e in self.ENGINES}
        self.last_w = {}
        self.readers = {}
        self.nops = 0

    def op(self, eng, fn, reads=(), writes=(), dma=False, label=""):
        st = self.streams[eng]
        if dma:
            r = self._dma_idx[eng] % self.DMA_ROT
            self._dma_idx[eng] += 1
            sem_name = f"{eng}_d{r}"
            inc = 16
        else:
            sem_name = eng + "_c"
            inc = 1
        waits = {}

        def add_wait(sem, val):
            if val is None:
                return
            if st.known.get(sem, 0) >= val:
                return
            if waits.get(sem, 0) < val:
                waits[sem] = val

        if dma and st.sems[sem_name] > 0:
            add_wait(sem_name, st.sems[sem_name])
        for r in reads:
            lw = self.last_w.get(r)
            if lw:
                add_wait(*lw)
        for w in writes:
            lw = self.last_w.get(w)
            if lw:
                add_wait(*lw)
            for rd in self.readers.get(w, ()):
                add_wait(*rd)
        for sem, val in waits.items():
            st.known[sem] = max(st.known.get(sem, 0), val)
            psname = sem.rsplit("_", 1)[0]
            ps = self.streams[psname]
            idx = self._idx_for(ps, sem, val)
            if idx is not None and idx < len(ps.snapshots):
                for s2, v2 in ps.snapshots[idx].items():
                    st.known[s2] = max(st.known.get(s2, 0), v2)

        st.sems[sem_name] += inc
        val_after = st.sems[sem_name]
        st.ops.append((dict(waits), fn, sem_name, inc, label))
        snap = dict(st.known)
        snap[sem_name] = val_after
        st.snapshots.append(snap)
        mark = (sem_name, val_after)
        for w in writes:
            self.last_w[w] = mark
            self.readers[w] = []
        for r in reads:
            self.readers.setdefault(r, []).append(mark)
        self.nops += 1
        return mark

    def phase_fence(self):
        targets = {}
        for e2 in self.ENGINES:
            for sn, v in self.streams[e2].sems.items():
                if v > 0:
                    targets[sn] = v
        for e in self.ENGINES:
            st = self.streams[e]
            waits = {sn: v for sn, v in targets.items() if st.known.get(sn, 0) < v}
            sem_name = e + "_c"
            st.sems[sem_name] += 1
            st.ops.append((waits, lambda eng: None, sem_name, 1, "fence"))
            for sn, v in targets.items():
                st.known[sn] = max(st.known.get(sn, 0), v)
            snap = dict(st.known)
            snap[sem_name] = st.sems[sem_name]
            st.snapshots.append(snap)
            targets[sem_name] = st.sems[sem_name]
        for e in self.ENGINES:
            st = self.streams[e]
            waits = {sn: v for sn, v in targets.items() if st.known.get(sn, 0) < v}
            sem_name = e + "_c"
            st.sems[sem_name] += 1
            st.ops.append((waits, lambda eng: None, sem_name, 1, "fence2"))
            for sn, v in targets.items():
                st.known[sn] = max(st.known.get(sn, 0), v)
            snap = dict(st.known)
            snap[sem_name] = st.sems[sem_name]
            st.snapshots.append(snap)
        self.last_w.clear()
        self.readers.clear()

    def set_spool(self, handle, cols):
        self._spool_h = handle
        self._spool_cols = cols
        self._spool_off = 0

    def spool_reset(self):
        self._spool_off = 0

    def spool(self, parts, cols, dtype, base=0):
        bcols = cols * (2 if dtype == F32 else 1)
        bcols = (bcols + 1) & ~1
        o = self._spool_off
        assert o + bcols <= self._spool_cols, f"spool overflow {o}+{bcols}"
        self._spool_off = o + bcols
        if dtype == F32:
            v = self._spool_h[:].bitcast(F32)
            return v[base:base + parts, o // 2:o // 2 + cols]
        return self._spool_h[base:base + parts, o:o + cols]

    def spool_group(self, specs):
        maxb = max(c * (2 if dt == F32 else 1) for _, c, dt in specs)
        maxb = (maxb + 1) & ~1
        o = self._spool_off
        assert o + maxb <= self._spool_cols
        self._spool_off = o + maxb
        out = []
        for i, (p, c, dt) in enumerate(specs):
            base = (0, 32, 64, 96)[i]
            if dt == F32:
                v = self._spool_h[:].bitcast(F32)
                out.append(v[base:base + p, o // 2:o // 2 + c])
            else:
                out.append(self._spool_h[base:base + p, o:o + c])
        return out

    def constcol(self, nc, value, parts=128):
        cache = self.__dict__.setdefault("_constcols", {})
        key = (value, parts)
        if key not in cache:
            t = nc.alloc_sbuf_tensor(f"cc_{len(cache)}", [parts, 1], F32)
            kk = f"__cc{len(cache)}"
            self.op("dve", lambda e, o=t[:], v=value: e.memset(o, v), writes=(kk,))
            cache[key] = (t, kk)
        return cache[key]

    def _idx_for(self, stream, sem_name, val):
        mc = stream.__dict__.setdefault("_mark_map", {})
        cts = stream.__dict__.setdefault("_mark_counts", {})
        n_done = stream.__dict__.setdefault("_mark_n", 0)
        if n_done < len(stream.ops):
            for i in range(n_done, len(stream.ops)):
                sn, inc = stream.ops[i][2], stream.ops[i][3]
                cts[sn] = cts.get(sn, 0) + inc
                mc[(sn, cts[sn])] = i
            stream._mark_n = len(stream.ops)
        return mc.get((sem_name, val))

    def build_and_emit(self, block):
        semh = {}
        for e in self.ENGINES:
            for sn in self.streams[e].sems:
                semh[sn] = self.nc.alloc_semaphore(name=sn)
        self.semh = semh
        eng_attr = {"pe": "tensor", "dve": "vector", "act": "scalar",
                    "gp": "gpsimd", "sync": "sync"}
        for e in self.ENGINES:
            st = self.streams[e]
            if not st.ops:
                continue
            def make_body(st=st):
                def body(eng):
                    for waits, fn, sem_name, inc, label in st.ops:
                        for sem, val in waits.items():
                            eng.wait_ge(semh[sem], val)
                        ins = fn(eng)
                        if ins is not None:
                            ins.then_inc(semh[sem_name], inc)
                        else:
                            eng.nop().then_inc(semh[sem_name], inc)
                return body
            getattr(block, eng_attr[e])(make_body())

    def finish(self, eng="sync"):
        st = self.streams[eng]
        targets = []
        for e in self.ENGINES:
            s2 = self.streams[e]
            for sn, v in s2.sems.items():
                if "_d" in sn and v > 0:
                    targets.append((sn, v))
        def fn(engobj):
            return None
        waits = {sn: v for sn, v in targets}
        st.sems[eng + "_c"] += 1
        st.ops.append((waits, lambda e: None, eng + "_c", 1, "finish"))
        st.snapshots.append(dict(st.known))


def build_program(emit_fn, name="prog"):
    nc = bass.Bass("TRN2", target_bir_lowering=False, debug=False)
    kb = KB(nc)
    emit_fn(nc, kb)
    kb.finish()
    with nc.Block() as block:
        kb.build_and_emit(block)
    return nc



def hadamard(n, dtype=np.float32):
    H = np.array([[1.0]], dtype=np.float64)
    while H.shape[0] < n:
        H = np.block([[H, H], [H, -H]])
    return H.astype(dtype)

_CONST_CACHE = {}

def consts():
    if _CONST_CACHE:
        return _CONST_CACHE
    c = _CONST_CACHE
    c["H2048_bf"] = (hadamard(2048) * (2.0 ** -5)).astype(BF)
    c["H128_bf"] = (hadamard(128) * (2.0 ** -6)).astype(BF)
    c["H64_bf"] = hadamard(64).astype(BF)
    c["I128_bf"] = np.eye(128, dtype=np.float32).astype(BF)
    c["LONES_f32"] = np.tril(np.ones((128, 128), np.float32))
    c["ONES2_bf"] = None
    return c



def emit_ln(nc, kb, NT, with_residual):
    x = nc.dram_tensor("x", [NT, D], F32, kind="ExternalInput").ap()
    if with_residual:
        a_in = nc.dram_tensor("a", [NT, D], BF16, kind="ExternalInput").ap()
        f_in = nc.dram_tensor("f", [NT, D], BF16, kind="ExternalInput").ap()
        x1_out = nc.dram_tensor("x1", [NT, D], F32, kind="ExternalOutput").ap()
    xn_out = nc.dram_tensor("xn", [NT, D], BF16, kind="ExternalOutput").ap()

    nt = NT // 128
    xt = nc.alloc_sbuf_tensor("xt", [128, 2 * D], F32)
    if with_residual:
        at = nc.alloc_sbuf_tensor("at", [128, 2 * D], BF16)
        ft = nc.alloc_sbuf_tensor("ft", [128, 2 * D], BF16)
    sq = nc.alloc_sbuf_tensor("sq", [128, D], BF16)
    ssum = nc.alloc_sbuf_tensor("ssum", [128, 2], F32)
    ssq = nc.alloc_sbuf_tensor("ssq", [128, 2], F32)
    mu = nc.alloc_sbuf_tensor("mu", [128, 1], F32)
    var = nc.alloc_sbuf_tensor("var", [128, 1], F32)
    rstd = nc.alloc_sbuf_tensor("rstd", [128, 1], F32)
    xnb = nc.alloc_sbuf_tensor("xnb", [128, 2 * D], BF16)

    for i in range(nt):
        pp = i % 2
        xs = xt[:, pp * D:(pp + 1) * D]
        xn_s = xnb[:, pp * D:(pp + 1) * D]
        kx = f"x{pp}"
        kb.op("sync", lambda e, o=xs, s=x[i * 128:(i + 1) * 128, :]: e.dma_start(out=o, in_=s),
              reads=(), writes=(kx,), dma=True)
        if with_residual:
            ka, kf = f"a{pp}", f"f{pp}"
            as_ = at[:, 0:D]
            fs = ft[:, 0:D]
            kb.op("sync", lambda e, o=as_, s=a_in[i * 128:(i + 1) * 128, :]: e.dma_start(out=o, in_=s),
                  writes=(ka,), dma=True)
            kb.op("sync", lambda e, o=fs, s=f_in[i * 128:(i + 1) * 128, :]: e.dma_start(out=o, in_=s),
                  writes=(kf,), dma=True)
            kb.op("dve", lambda e, o=xs, i0=xs, i1=as_: e.tensor_tensor(out=o, in0=i0, in1=i1, op=ALU.add),
                  reads=(kx, ka), writes=(kx,))
            kb.op("dve", lambda e, o=xs, i0=xs, i1=fs: e.tensor_tensor(out=o, in0=i0, in1=i1, op=ALU.add),
                  reads=(kx, kf), writes=(kx,))
            kb.op("sync", lambda e, o=x1_out[i * 128:(i + 1) * 128, :], s=xs: e.dma_start(out=o, in_=s),
                  reads=(kx,), dma=True)
        kb.op("dve", lambda e, o=ssum[:, 0:1], s=xs: e.tensor_reduce(out=o, in_=s, axis=AX.X, op=ALU.add),
              reads=(kx,), writes=("ssum",))
        kb.op("act", lambda e, o=sq[:], s=xs, a=ssq[:, 0:1]: e.activation(out=o, in_=s, func=AF.Square, accum_out=a),
              reads=(kx,), writes=("ssq", "sq"))
        kb.op("dve", lambda e, o=mu[:]: e.tensor_scalar(out=o, in0=ssum[:, 0:1], scalar1=1.0 / D, scalar2=None, op0=ALU.mult),
              reads=("ssum",), writes=("mu",))
        kb.op("dve", lambda e, o=var[:]: e.scalar_tensor_tensor(out=o, in0=mu[:], scalar=-1.0, in1=mu[:], op0=ALU.mult, op1=ALU.mult),
              reads=("mu",), writes=("var",))
        kb.op("dve", lambda e, o=var[:]: e.scalar_tensor_tensor(out=o, in0=ssq[:, 0:1], scalar=1.0 / D, in1=var[:], op0=ALU.mult, op1=ALU.add),
              reads=("ssq", "var"), writes=("var",))
        epst, epsk = kb.constcol(nc, 1e-5)
        kb.op("act", lambda e, o=rstd[:], b=epst: e.activation(out=o, in_=var[:], func=AF.Ln, bias=b[:, 0:1], scale=1.0),
              reads=("var", epsk), writes=("rstd",))
        kb.op("act", lambda e, o=rstd[:]: e.activation(out=o, in_=rstd[:], func=AF.Exp, scale=-0.5),
              reads=("rstd",), writes=("rstd",))
        kb.op("dve", lambda e, o=xn_s, s=xs: e.tensor_scalar(out=o, in0=s, scalar1=mu[:, 0:1], scalar2=rstd[:, 0:1], op0=ALU.subtract, op1=ALU.mult),
              reads=(kx, "mu", "rstd"), writes=(f"xn{pp}",))
        kb.op("sync", lambda e, o=xn_out[i * 128:(i + 1) * 128, :], s=xn_s: e.dma_start(out=o, in_=s),
              reads=(f"xn{pp}",), dma=True)


def build_ln(NT, with_residual):
    return build_program(lambda nc, kb: emit_ln(nc, kb, NT, with_residual))



def mm_group(kb, psum_ap, pairs, reads, writes, fp32=False):
    def fn(e, pairs=tuple(pairs), o=psum_ap):
        ins = None
        n = len(pairs)
        for idx, (l, r) in enumerate(pairs):
            ins = e.matmul(o, l, r, start=(idx == 0), stop=(idx == n - 1))
        return ins
    return kb.op("pe", fn, reads=tuple(reads), writes=tuple(writes))



def emit_fold(nc, kb):
    OS = D // NCORES
    h64 = nc.dram_tensor("H64", [64, 64], BF16, kind="ExternalInput").ap()
    h64_sb = nc.alloc_sbuf_tensor("h64sb", [64, 64], BF16)
    kb.op("sync", lambda e: e.dma_start(out=h64_sb[:], in_=h64[:, :]), writes=("h64",), dma=True)
    slab = nc.alloc_sbuf_tensor("slab", [64, 4 * 512], BF16)
    oslab = nc.alloc_sbuf_tensor("oslab", [64, 4 * 512], BF16)
    ps = nc.alloc_psum_tensor("psf", [64, 512], F32)
    for li in range(2):
        wd = nc.dram_tensor(f"Wdn{li}", [OS, 8192], F32, kind="ExternalInput").ap()
        d2 = nc.dram_tensor(f"D2_{li}", [OS, 8192], BF16, kind="ExternalOutput").ap()
        for og in range(OS // 4):
            pp = og % 4
            s_in = slab[:, pp * 512:(pp + 1) * 512]
            s_out = oslab[:, pp * 512:(pp + 1) * 512]
            src = wd[og * 4:(og + 1) * 4, :].rearrange("o (c i) -> c o i", i=128)
            dst = d2[og * 4:(og + 1) * 4, :].rearrange("o (c i) -> c o i", i=128)
            kin, kout = f"sl{pp}", f"so{pp}"
            kb.op("gp", lambda e, o=s_in, s=src: e.dma_start(out=o.rearrange("p (a b) -> p a b", a=4), in_=s),
                  writes=(kin,), dma=True)
            kb.op("pe", lambda e, o=ps[:], l=h64_sb[:], r=s_in: e.matmul(o, l, r, start=True, stop=True),
                  reads=(kin, "h64"), writes=("psf",))
            kb.op("act", lambda e, o=s_out, s=ps[:]: e.activation(out=o, in_=s, func=AF.Copy),
                  reads=("psf",), writes=(kout,))
            kb.op("sync", lambda e, o=dst, s=s_out: e.dma_start(out=o, in_=s.rearrange("p (a b) -> p a b", a=4)),
                  reads=(kout,), dma=True)


def build_fold():
    return build_program(emit_fold)



def emit_l0proj(nc, kb):
    TT = T
    xnf = nc.dram_tensor("xnf", [D, TT], BF16, kind="ExternalInput").ap()
    wqkvT = nc.dram_tensor("wqkvT", [D, 1024], F32, kind="ExternalInput").ap()
    wbT = nc.dram_tensor("wbT", [D, 8], F32, kind="ExternalInput").ap()
    waT = nc.dram_tensor("waT", [D, 8], F32, kind="ExternalInput").ap()
    convw = nc.dram_tensor("convw", [1024, 4], F32, kind="ExternalInput").ap()
    dtb = nc.dram_tensor("dtb", [8, 1], F32, kind="ExternalInput").ap()
    alog = nc.dram_tensor("alog", [8, 1], F32, kind="ExternalInput").ap()
    ones2 = nc.dram_tensor("ones2", [128, 2], BF16, kind="ExternalInput").ap()

    qkvf = nc.dram_tensor("qkvf", [1024, TT], BF16, kind="ExternalOutput").ap()
    grow = nc.dram_tensor("grow", [8, TT], F32, kind="ExternalOutput").ap()
    betarow = nc.dram_tensor("betarow", [8, TT], F32, kind="ExternalOutput").ap()
    lnbrow = nc.dram_tensor("lnbrow", [8, TT], F32, kind="ExternalOutput").ap()
    lnrkrow = nc.dram_tensor("lnrkrow", [4, TT], F32, kind="ExternalOutput").ap()
    lnrqrow = nc.dram_tensor("lnrqrow", [4, TT], F32, kind="ExternalOutput").ap()

    HT = TT // 2
    xn_sb = nc.alloc_sbuf_tensor("xn_sb", [128, 16 * HT], BF16)
    wq_sb = nc.alloc_sbuf_tensor("wq_sb", [128, 16 * 1024], BF16)
    wba_sb = nc.alloc_sbuf_tensor("wba_sb", [128, 16 * 16], BF16)
    qp_sb = nc.alloc_sbuf_tensor("qp_sb", [128, 8 * (HT + 3)], BF16)
    conv_sb = nc.alloc_sbuf_tensor("conv_sb", [128, HT], F32)
    qkv_sb = nc.alloc_sbuf_tensor("qkv_sb", [128, HT], BF16)
    sq_sb = nc.alloc_sbuf_tensor("sq_sb", [128, HT], BF16)
    wc_sb = nc.alloc_sbuf_tensor("wc_sb", [128, 8 * 4], F32)
    o2_sb = nc.alloc_sbuf_tensor("o2_sb", [128, 2], BF16)
    zrow_sb = nc.alloc_sbuf_tensor("zrow_sb", [8, 2 * HT], F32)
    t1_sb = nc.alloc_sbuf_tensor("t1_sb", [8, HT], F32)
    t3_sb = nc.alloc_sbuf_tensor("t3_sb", [8, HT], F32)
    t4_sb = nc.alloc_sbuf_tensor("t4_sb", [8, HT], F32)
    gout_sb = nc.alloc_sbuf_tensor("gout_sb", [8, HT], F32)
    ln_sb = nc.alloc_sbuf_tensor("ln_sb", [2, HT], F32)
    alogc = nc.alloc_sbuf_tensor("alogc", [8, 1], F32)
    dtbc = nc.alloc_sbuf_tensor("dtbc", [8, 1], F32)
    nexp_sb = nc.alloc_sbuf_tensor("nexp_sb", [8, 1], F32)

    psA = nc.alloc_psum_tensor("psA", [128, 512], F32)
    psB = nc.alloc_psum_tensor("psB", [128, 512], F32)
    psS = nc.alloc_psum_tensor("psS", [2, 2048], F32)
    psZ = nc.alloc_psum_tensor("psZ", [8, 512], F32)

    one_c, one_k = kb.constcol(nc, 1.0, parts=8)
    eps_c, eps_k = kb.constcol(nc, 1e-6, parts=2)

    kb.op("gp", lambda e: e.dma_start(out=wq_sb[:].rearrange("p (k m) -> p k m", k=16),
                                      in_=wqkvT.rearrange("(k p) m -> p k m", p=128)),
          writes=("wq",), dma=True)
    kb.op("gp", lambda e: e.dma_start(out=wba_sb[:].rearrange("p (k m) -> p k m", k=16)[:, :, 0:8],
                                      in_=wbT.rearrange("(k p) m -> p k m", p=128)),
          writes=("wba_b",), dma=True)
    kb.op("gp", lambda e: e.dma_start(out=wba_sb[:].rearrange("p (k m) -> p k m", k=16)[:, :, 8:16],
                                      in_=waT.rearrange("(k p) m -> p k m", p=128)),
          writes=("wba_a",), dma=True)
    kb.op("sync", lambda e: e.dma_start(out=wc_sb[:].rearrange("p (c f) -> p c f", c=8),
                                        in_=convw.rearrange("(c p) f -> p c f", p=128)),
          writes=("wc",), dma=True)
    kb.op("sync", lambda e: e.dma_start(out=o2_sb[:], in_=ones2[:, :]), writes=("o2",), dma=True)
    kb.op("sync", lambda e: e.dma_start(out=alogc[:], in_=alog[:, :]), writes=("alog",), dma=True)
    kb.op("sync", lambda e: e.dma_start(out=dtbc[:], in_=dtb[:, :]), writes=("dtb",), dma=True)
    kb.op("act", lambda e: e.activation(out=nexp_sb[:], in_=alogc[:], func=AF.Exp),
          reads=("alog",), writes=("nexp",))
    kb.op("dve", lambda e: e.tensor_scalar(out=nexp_sb[:], in0=nexp_sb[:], scalar1=-1.0, scalar2=None, op0=ALU.mult),
          reads=("nexp",), writes=("nexp",))
    kb.op("dve", lambda e: e.memset(qp_sb[:].rearrange("p (c f) -> p c f", c=8)[:, :, 0:3], 0.0),
          writes=("halo",))

    for half in range(2):
        toff = half * HT
        kb.op("sync", lambda e, t=toff: e.dma_start(
            out=xn_sb[:].rearrange("p (k t) -> p k t", k=16),
            in_=xnf.rearrange("(k p) t -> p k t", p=128)[:, :, t:t + HT]),
              writes=("xn",), dma=True)
        for m in range(8):
            qp_chunk = qp_sb[:, m * (HT + 3):(m + 1) * (HT + 3)]
            for ts_ in range(HT // 512):
                ps = psA if (ts_ % 2 == 0) else psB
                psk = "psA" if (ts_ % 2 == 0) else "psB"
                pairs = [(wq_sb[:, k * 1024 + m * 128:(k * 1024 + (m + 1) * 128)],
                          xn_sb[:, k * HT + ts_ * 512: k * HT + (ts_ + 1) * 512])
                         for k in range(16)]
                mm_group(kb, ps[:], pairs, reads=("wq", "xn"), writes=(psk,))
                kb.op("act", lambda e, o=qp_chunk[:, 3 + ts_ * 512: 3 + (ts_ + 1) * 512], s=ps[:]:
                      e.activation(out=o, in_=s, func=AF.Copy),
                      reads=(psk,), writes=(f"qp{m}",))
            kb.op("dve", lambda e, m=m, qp=qp_chunk: e.tensor_scalar(
                out=conv_sb[:], in0=qp[:, 0:HT], scalar1=wc_sb[:, m * 4:m * 4 + 1], scalar2=None, op0=ALU.mult),
                  reads=(f"qp{m}", "wc"), writes=("conv",))
            for tau in range(1, 4):
                kb.op("dve", lambda e, m=m, tau=tau, qp=qp_chunk: e.scalar_tensor_tensor(
                    out=conv_sb[:], in0=qp[:, tau:tau + HT], scalar=wc_sb[:, m * 4 + tau:m * 4 + tau + 1],
                    in1=conv_sb[:], op0=ALU.mult, op1=ALU.add),
                      reads=(f"qp{m}", "wc", "conv"), writes=("conv",))
            kb.op("act", lambda e: e.activation(out=qkv_sb[:], in_=conv_sb[:], func=AF.Silu),
                  reads=("conv",), writes=("qkv",))
            kb.op("sync", lambda e, m=m, t=toff: e.dma_start(out=qkvf[m * 128:(m + 1) * 128, t:t + HT], in_=qkv_sb[:]),
                  reads=("qkv",), dma=True)
            if half == 0:
                kb.op("dve", lambda e, qp=qp_chunk: e.tensor_copy(qp[:, 0:3], qp[:, HT:HT + 3]),
                      reads=(f"qp{m}",), writes=(f"qp{m}",))
            if m < 4:
                kb.op("act", lambda e: e.activation(out=sq_sb[:], in_=qkv_sb[:], func=AF.Square),
                      reads=("qkv",), writes=("sq",))
                for ts_ in range(HT // 512):
                    kb.op("pe", lambda e, o=psS[:, ts_ * 512:(ts_ + 1) * 512], l=o2_sb[:], r=sq_sb[:, ts_ * 512:(ts_ + 1) * 512]:
                          e.matmul(o, l, r, start=True, stop=True),
                          reads=("sq", "o2"), writes=("psS",))
                kb.op("act", lambda e: e.activation(out=ln_sb[:], in_=psS[0:2, 0:HT], func=AF.Ln, bias=eps_c[:, 0:1]),
                      reads=("psS", eps_k), writes=("lnv",))
                if m < 2:
                    kb.op("dve", lambda e: e.tensor_scalar(out=ln_sb[:], in0=ln_sb[:], scalar1=-0.5,
                                                           scalar2=float(-0.5 * np.log(64.0)), op0=ALU.mult, op1=ALU.add),
                          reads=("lnv",), writes=("lnv",))
                    dst = lnrqrow
                else:
                    kb.op("dve", lambda e: e.tensor_scalar(out=ln_sb[:], in0=ln_sb[:], scalar1=-0.5, scalar2=None, op0=ALU.mult),
                          reads=("lnv",), writes=("lnv",))
                    dst = lnrkrow
                r0 = (m % 2) * 2
                kb.op("sync", lambda e, d=dst, r0=r0, t=toff: e.dma_start(out=d[r0:r0 + 2, t:t + HT], in_=ln_sb[:]),
                      reads=("lnv",), dma=True)
        for which in range(2):
            zrow = zrow_sb[:, which * HT:(which + 1) * HT]
            for ts_ in range(HT // 512):
                pairs = [(wba_sb[:, k * 16 + which * 8: k * 16 + which * 8 + 8],
                          xn_sb[:, k * HT + ts_ * 512: k * HT + (ts_ + 1) * 512])
                         for k in range(16)]
                mm_group(kb, psZ[:], pairs, reads=("wba_b", "wba_a", "xn"), writes=("psZ",))
                kb.op("dve", lambda e, o=zrow[:, ts_ * 512:(ts_ + 1) * 512], s=psZ[:]: e.tensor_copy(o, s),
                      reads=("psZ",), writes=(f"z{which}",))
        zb = zrow_sb[:, 0:HT]
        za = zrow_sb[:, HT:2 * HT]
        kb.op("act", lambda e: e.activation(out=gout_sb[:], in_=zb, func=AF.Sigmoid),
              reads=("z0",), writes=("beta",))
        kb.op("sync", lambda e, t=toff: e.dma_start(out=betarow[:, t:t + HT], in_=gout_sb[:]),
              reads=("beta",), dma=True)
        kb.op("act", lambda e: e.activation(out=t1_sb[:], in_=zb, func=AF.Abs),
              reads=("z0",), writes=("t1",))
        kb.op("act", lambda e: e.activation(out=t1_sb[:], in_=t1_sb[:], func=AF.Exp, scale=-1.0),
              reads=("t1",), writes=("t1",))
        kb.op("act", lambda e: e.activation(out=t3_sb[:], in_=t1_sb[:], func=AF.Ln, bias=one_c[:, 0:1]),
              reads=("t1", one_k), writes=("t3",))
        kb.op("act", lambda e: e.activation(out=t4_sb[:], in_=zb, func=AF.Relu, scale=-1.0),
              reads=("z0",), writes=("t4",))
        kb.op("dve", lambda e: e.scalar_tensor_tensor(out=gout_sb[:], in0=t4_sb[:], scalar=-1.0, in1=t3_sb[:],
                                                      op0=ALU.mult, op1=ALU.subtract),
              reads=("t3", "t4"), writes=("lnb",))
        kb.op("sync", lambda e, t=toff: e.dma_start(out=lnbrow[:, t:t + HT], in_=gout_sb[:]),
              reads=("lnb",), dma=True)
        kb.op("act", lambda e: e.activation(out=t1_sb[:], in_=za, func=AF.Abs, bias=dtbc[:, 0:1]),
              reads=("z1", "dtb"), writes=("t1",))
        kb.op("act", lambda e: e.activation(out=t1_sb[:], in_=t1_sb[:], func=AF.Exp, scale=-1.0),
              reads=("t1",), writes=("t1",))
        kb.op("act", lambda e: e.activation(out=t3_sb[:], in_=t1_sb[:], func=AF.Ln, bias=one_c[:, 0:1]),
              reads=("t1", one_k), writes=("t3",))
        kb.op("act", lambda e: e.activation(out=t4_sb[:], in_=za, func=AF.Relu, bias=dtbc[:, 0:1]),
              reads=("z1", "dtb"), writes=("t4",))
        kb.op("dve", lambda e: e.tensor_tensor(out=t4_sb[:], in0=t4_sb[:], in1=t3_sb[:], op=ALU.add),
              reads=("t3", "t4"), writes=("t4",))
        kb.op("dve", lambda e: e.tensor_scalar(out=gout_sb[:], in0=t4_sb[:], scalar1=nexp_sb[:, 0:1], scalar2=None, op0=ALU.mult),
              reads=("t4", "nexp"), writes=("gv",))
        kb.op("sync", lambda e, t=toff: e.dma_start(out=grow[:, t:t + HT], in_=gout_sb[:]),
              reads=("gv",), dma=True)


def build_l0proj():
    return build_program(emit_l0proj)



def emit_scan(nc, kb, nchunks=T // 128):
    C = 128
    qf = nc.dram_tensor("qf", [2, 128, T], BF16, kind="ExternalInput").ap()
    kf = nc.dram_tensor("kf", [2, 128, T], BF16, kind="ExternalInput").ap()
    ktok = nc.dram_tensor("ktok", [T, 256], BF16, kind="ExternalInput").ap()
    vtok = nc.dram_tensor("vtok", [T, 512], BF16, kind="ExternalInput").ap()
    gcol = nc.dram_tensor("gcol", [T, 8], F32, kind="ExternalInput").ap()
    betacol = nc.dram_tensor("betacol", [T, 8], F32, kind="ExternalInput").ap()
    lnbcol = nc.dram_tensor("lnbcol", [T, 8], F32, kind="ExternalInput").ap()
    lnrkcol = nc.dram_tensor("lnrkcol", [T, 4], F32, kind="ExternalInput").ap()
    lnrqcol = nc.dram_tensor("lnrqcol", [T, 4], F32, kind="ExternalInput").ap()
    uinc = nc.dram_tensor("UINC", [128, 128], F32, kind="ExternalInput").ap()
    i128f = nc.dram_tensor("I128F", [128, 128], F32, kind="ExternalInput").ap()
    i128b = nc.dram_tensor("I128B", [128, 128], BF16, kind="ExternalInput").ap()
    ones1 = nc.dram_tensor("ONES1", [1, 128], F32, kind="ExternalInput").ap()
    oout = nc.dram_tensor("o", [T, 512], BF16, kind="ExternalOutput").ap()

    uinc_sb = nc.alloc_sbuf_tensor("uinc_sb", [128, 128], F32)
    i128f_sb = nc.alloc_sbuf_tensor("i128f_sb", [128, 128], F32)
    i128b_sb = nc.alloc_sbuf_tensor("i128b_sb", [128, 128], BF16)
    ones1_sb = nc.alloc_sbuf_tensor("ones1_sb", [1, 128], F32)
    for nm, dst, src in (("uinc", uinc_sb, uinc), ("i128f", i128f_sb, i128f),
                         ("i128b", i128b_sb, i128b), ("ones1", ones1_sb, ones1)):
        kb.op("sync", lambda e, d=dst, s=src: e.dma_start(out=d[:], in_=s[:, :]), writes=(nm,), dma=True)
    CONSTS = ("uinc", "i128f", "i128b", "ones1")

    S_f = nc.alloc_sbuf_tensor("S_f", [64, 8 * 64], F32)
    S_b = nc.alloc_sbuf_tensor("S_b", [128, 8 * 64], BF16)
    kb.op("dve", lambda e: e.memset(S_f[:], 0.0), writes=("S_f",))
    kb.op("dve", lambda e: e.memset(S_b[:], 0.0), writes=("S_b",))

    NB = 2
    kf_sb = nc.alloc_sbuf_tensor("kf_sb", [128, NB * 2 * 128], BF16)
    qf_sb = nc.alloc_sbuf_tensor("qf_sb", [128, NB * 2 * 128], BF16)
    kt_sb = nc.alloc_sbuf_tensor("kt_sb", [128, NB * 256], BF16)
    vt_sb = nc.alloc_sbuf_tensor("vt_sb", [128, NB * 512], BF16)
    cols_sb = nc.alloc_sbuf_tensor("cols_sb", [128, NB * 32], F32)
    cum_sb = nc.alloc_sbuf_tensor("cum_sb", [128, 8], F32)
    biasA_sb = nc.alloc_sbuf_tensor("biasA_sb", [128, 8], F32)
    rowAc_sb = nc.alloc_sbuf_tensor("rowAc_sb", [128, 8], F32)
    biasP_sb = nc.alloc_sbuf_tensor("biasP_sb", [128, 8], F32)
    rows_sb = nc.alloc_sbuf_tensor("rows_sb", [1, 4 * 128], F32)
    ccol_sb = nc.alloc_sbuf_tensor("ccol_sb", [128, 4], F32)
    arg_sb = nc.alloc_sbuf_tensor("arg_sb", [128, 128], F32)
    dmat_sb = nc.alloc_sbuf_tensor("dmat_sb", [128, 128], F32)
    araw_sb = nc.alloc_sbuf_tensor("araw_sb", [128, 128], F32)
    kq_sb = nc.alloc_sbuf_tensor("kq_sb", [128, 128], F32)
    B_sb = nc.alloc_sbuf_tensor("B_sb", [128, 128], BF16)
    Useq = [nc.alloc_sbuf_tensor(f"U{i}", [128, 128], BF16) for i in range(2)]
    Lseq = [nc.alloc_sbuf_tensor(f"L{i}", [128, 128], BF16) for i in range(2)]
    Pp_sb = nc.alloc_sbuf_tensor("Pp_sb", [128, 128], BF16)
    V_sb = [nc.alloc_sbuf_tensor(f"V{i}", [128, 64], BF16) for i in range(2)]
    R_sb = nc.alloc_sbuf_tensor("R_sb", [128, 64], BF16)
    tmp_sb = nc.alloc_sbuf_tensor("tmp_sb", [128, 64], F32)
    tmp2_sb = nc.alloc_sbuf_tensor("tmp2_sb", [128, 64], BF16)
    kw_sb = nc.alloc_sbuf_tensor("kw_sb", [128, 64], BF16)
    lamc_sb = nc.alloc_sbuf_tensor("lamc_sb", [64, 1], F32)
    o_sb = nc.alloc_sbuf_tensor("o_sb", [128, 512], BF16)

    P = [nc.alloc_psum_tensor(f"P{i}", [128, 512], F32) for i in range(8)]

    for c in range(nchunks):
        t0 = c * C
        bb = c % NB
        kfs = kf_sb[:, bb * 256:(bb + 1) * 256]
        qfs = qf_sb[:, bb * 256:(bb + 1) * 256]
        kts = kt_sb[:, bb * 256:(bb + 1) * 256]
        vts = vt_sb[:, bb * 512:(bb + 1) * 512]
        cls = cols_sb[:, bb * 32:(bb + 1) * 32]
        kL = {"kf": f"kf{bb}", "qf": f"qf{bb}", "kt": f"kt{bb}", "vt": f"vt{bb}", "cl": f"cl{bb}"}
        for jj_ in range(2):
            kb.op("sync", lambda e, o=kfs[:, jj_ * 128:(jj_ + 1) * 128], s=kf[jj_, :, t0:t0 + C]: e.dma_start(out=o, in_=s),
                  writes=(kL["kf"],), dma=True)
            kb.op("sync", lambda e, o=qfs[:, jj_ * 128:(jj_ + 1) * 128], s=qf[jj_, :, t0:t0 + C]: e.dma_start(out=o, in_=s),
                  writes=(kL["qf"],), dma=True)
        kb.op("sync", lambda e, o=kts, s=ktok[t0:t0 + C, :]: e.dma_start(out=o, in_=s),
              writes=(kL["kt"],), dma=True)
        kb.op("sync", lambda e, o=vts, s=vtok[t0:t0 + C, :]: e.dma_start(out=o, in_=s),
              writes=(kL["vt"],), dma=True)
        kb.op("sync", lambda e, o=cls[:, 0:8], s=gcol[t0:t0 + C, :]: e.dma_start(out=o, in_=s),
              writes=(kL["cl"],), dma=True)
        kb.op("sync", lambda e, o=cls[:, 8:16], s=betacol[t0:t0 + C, :]: e.dma_start(out=o, in_=s),
              writes=(kL["cl"] + "b",), dma=True)
        kb.op("sync", lambda e, o=cls[:, 16:24], s=lnbcol[t0:t0 + C, :]: e.dma_start(out=o, in_=s),
              writes=(kL["cl"] + "c",), dma=True)
        kb.op("sync", lambda e, o=cls[:, 24:28], s=lnrkcol[t0:t0 + C, :]: e.dma_start(out=o, in_=s),
              writes=(kL["cl"] + "d",), dma=True)
        kb.op("sync", lambda e, o=cls[:, 28:32], s=lnrqcol[t0:t0 + C, :]: e.dma_start(out=o, in_=s),
              writes=(kL["cl"] + "e",), dma=True)
        CLK = (kL["cl"], kL["cl"] + "b", kL["cl"] + "c", kL["cl"] + "d", kL["cl"] + "e")

        kb.op("pe", lambda e, r=cls[:, 0:8]: e.matmul(P[2][0:128, 0:8], uinc_sb[:], r, start=True, stop=True),
              reads=("uinc",) + CLK, writes=("P2",))
        kb.op("dve", lambda e: e.tensor_copy(cum_sb[:], P[2][0:128, 0:8]), reads=("P2",), writes=("cum",))
        kb.op("dve", lambda e, cl=cls: e.tensor_tensor(out=biasA_sb[:], in0=cum_sb[:], in1=cl[:, 16:24], op=ALU.add),
              reads=("cum",) + CLK, writes=("biasA",))
        kb.op("dve", lambda e, cl=cls: e.tensor_tensor(
            out=biasA_sb[:].rearrange("p (a b) -> p a b", b=2), in0=biasA_sb[:].rearrange("p (a b) -> p a b", b=2),
            in1=cl[:, 24:28].unsqueeze(2).broadcast_to((128, 4, 2)), op=ALU.add),
              reads=("biasA",) + CLK, writes=("biasA",))
        kb.op("dve", lambda e, cl=cls: e.scalar_tensor_tensor(
            out=rowAc_sb[:].rearrange("p (a b) -> p a b", b=2), in0=cum_sb[:].rearrange("p (a b) -> p a b", b=2),
            scalar=-1.0, in1=cl[:, 24:28].unsqueeze(2).broadcast_to((128, 4, 2)), op0=ALU.mult, op1=ALU.add),
              reads=("cum",) + CLK, writes=("rowAc",))
        kb.op("dve", lambda e, cl=cls: e.scalar_tensor_tensor(
            out=biasP_sb[:].rearrange("p (a b) -> p a b", b=2), in0=cum_sb[:].rearrange("p (a b) -> p a b", b=2),
            scalar=1.0, in1=cl[:, 28:32].unsqueeze(2).broadcast_to((128, 4, 2)), op0=ALU.mult, op1=ALU.add),
              reads=("cum",) + CLK, writes=("biasP",))

        for h in range(8):
            khl = h // 2
            jj = khl // 2
            pb = 64 * (khl % 2)
            kft = kfs[pb:pb + 64, jj * 128:(jj + 1) * 128]
            qft = qfs[pb:pb + 64, jj * 128:(jj + 1) * 128]
            HK = (f"h{h}",)
            if h % 2 == 0:
                kb.op("pe", lambda e, l=kft: e.matmul(P[0][:, 0:128], l, l, start=True, stop=True),
                      reads=(kL["kf"],), writes=("P0",))
                kb.op("pe", lambda e, l=kft, r=qft: e.matmul(P[1][:, 0:128], l, r, start=True, stop=True),
                      reads=(kL["kf"], kL["qf"]), writes=("P1",))
                kb.op("dve", lambda e: e.tensor_copy(araw_sb[:], P[0][:, 0:128]), reads=("P0",), writes=("araw",))
                kb.op("dve", lambda e: e.tensor_copy(kq_sb[:], P[1][:, 0:128]), reads=("P1",), writes=("kq",))
            for ri, (colt, colk) in enumerate(((rowAc_sb, "rowAc"), (biasA_sb, "biasA"),
                                               (biasP_sb, "biasP"), (cum_sb, "cum"))):
                kb.op("pe", lambda e, ct=colt, h=h, ri=ri: e.matmul(
                    P[2][0:1, 128 + ri * 128: 128 + (ri + 1) * 128] if False else P[2][0:1, 0:128],
                    ct[:, h:h + 1], i128f_sb[:], start=True, stop=True),
                      reads=(colk, "i128f"), writes=("P2",))
                kb.op("dve", lambda e, ri=ri: e.tensor_copy(rows_sb[:, ri * 128:(ri + 1) * 128], P[2][0:1, 0:128]),
                      reads=("P2",), writes=(f"rows{ri}",))
            ROWK = ("rows0", "rows1", "rows2", "rows3")
            kb.op("pe", lambda e: e.matmul(P[2][0:128, 0:1], ones1_sb[:], rows_sb[:, 3 * 128 + 127: 3 * 128 + 128],
                                           start=True, stop=True),
                  reads=("rows3", "ones1"), writes=("P2",))
            kb.op("dve", lambda e: e.tensor_copy(ccol_sb[:, 0:1], P[2][0:128, 0:1]), reads=("P2",), writes=("cC",))
            kb.op("act", lambda e, h=h: e.activation(out=ccol_sb[:, 1:2], in_=biasA_sb[:, h:h + 1], func=AF.Exp),
                  reads=("biasA",), writes=("rho",))
            kb.op("act", lambda e, h=h: e.activation(out=ccol_sb[:, 2:3], in_=biasP_sb[:, h:h + 1], func=AF.Exp),
                  reads=("biasP",), writes=("rhoP",))
            kb.op("dve", lambda e, h=h: e.tensor_tensor(out=ccol_sb[:, 3:4], in0=rowAc_sb[:, h:h + 1],
                                                        in1=ccol_sb[:, 0:1], op=ALU.add),
                  reads=("rowAc", "cC"), writes=("wc",))
            kb.op("act", lambda e: e.activation(out=ccol_sb[:, 3:4], in_=ccol_sb[:, 3:4], func=AF.Exp),
                  reads=("wc",), writes=("wc",))
            specs = (
                (0, biasA_sb, "biasA", 0, "B"),
                (1, rowAc_sb, "rowAc", 1, "U"),
                (2, rowAc_sb, "rowAc", 2, "Pp"),
            )
            for (rsl, bcol, colk, mk, what) in specs:
                kb.op("pe", lambda e, rsl=rsl: e.matmul(P[2][0:128, 0:128], ones1_sb[:],
                                                        rows_sb[:, rsl * 128:(rsl + 1) * 128], start=True, stop=True),
                      reads=(ROWK[rsl], "ones1"), writes=("P2",))
                kb.op("dve", lambda e, bc=bcol, h=h: e.tensor_scalar(
                    out=arg_sb[:], in0=P[2][0:128, 0:128], scalar1=bc[:, h:h + 1], scalar2=30.0,
                    op0=ALU.add, op1=ALU.min),
                      reads=("P2", colk), writes=("arg",))
                if mk == 0:
                    pat, cm, op = [[-1, 128]], 1, ALU.is_gt
                elif mk == 1:
                    pat, cm, op = [[1, 128]], -1, ALU.is_gt
                else:
                    pat, cm, op = [[1, 128]], -1, ALU.is_ge
                def _mask(e, pat=pat, cm=cm, op=op):
                    if not hasattr(e, "_fillreg"):
                        e._fillreg = e.to_reg(-200.0)
                    return e.affine_select(out=dmat_sb[:], in_=arg_sb[:], pattern=pat, compare_op=op,
                                           fill=e._fillreg, base=0, channel_multiplier=cm)
                kb.op("gp", _mask, reads=("arg",), writes=("dmat",))
                kb.op("act", lambda e: e.activation(out=dmat_sb[:], in_=dmat_sb[:], func=AF.Exp),
                      reads=("dmat",), writes=("dmat",))
                if what == "B":
                    kb.op("dve", lambda e: e.scalar_tensor_tensor(out=B_sb[:], in0=dmat_sb[:], scalar=-1.0,
                                                                  in1=araw_sb[:], op0=ALU.mult, op1=ALU.mult),
                          reads=("dmat", "araw"), writes=("Bm",))
                elif what == "U":
                    kb.op("dve", lambda e: e.scalar_tensor_tensor(out=Useq[0][:], in0=dmat_sb[:], scalar=-1.0,
                                                                  in1=araw_sb[:], op0=ALU.mult, op1=ALU.mult),
                          reads=("dmat", "araw"), writes=("U0",))
                else:
                    kb.op("dve", lambda e: e.scalar_tensor_tensor(out=Pp_sb[:], in0=dmat_sb[:], scalar=1.0,
                                                                  in1=kq_sb[:], op0=ALU.mult, op1=ALU.mult),
                          reads=("dmat", "kq"), writes=("Pp",))
            kb.op("pe", lambda e, l=kft, pb=pb, h=h: e.matmul(P[3][:, 0:64], l, S_b[pb:pb + 64, h * 64:(h + 1) * 64], start=True, stop=True),
                  reads=(kL["kf"], "S_b"), writes=("P3",))
            kb.op("dve", lambda e: e.tensor_scalar(out=tmp_sb[:], in0=P[3][:, 0:64], scalar1=ccol_sb[:, 1:2],
                                                   scalar2=None, op0=ALU.mult),
                  reads=("P3", "rho"), writes=("tmp",))
            kb.op("dve", lambda e, h=h, v=vts, cl=cls: e.scalar_tensor_tensor(
                out=R_sb[:], in0=v[:, h * 64:(h + 1) * 64], scalar=cl[:, 8 + h:9 + h], in1=tmp_sb[:],
                op0=ALU.mult, op1=ALU.subtract),
                  reads=(kL["vt"],) + CLK + ("tmp",), writes=("V0v",))
            Vcur, Vk = R_sb, "V0v"
            Ucur, Uk = Useq[0], "U0"
            Lcur, Lk = B_sb, "Bm"
            for s in range(7):
                Vnext = V_sb[s % 2]
                mm_group(kb, P[3][:, 0:64], [(Ucur[:], Vcur[:]), (i128b_sb[:], Vcur[:])],
                         reads=(Uk, Vk, "i128b"), writes=("P3",))
                kb.op("act", lambda e, o=Vnext: e.activation(out=o[:], in_=P[3][:, 0:64], func=AF.Copy),
                      reads=("P3",), writes=(f"V{s + 1}v",))
                if s < 6:
                    Unext = Useq[(s + 1) % 2]
                    kb.op("pe", lambda e, l=Lcur, r=Ucur: e.matmul(P[4][:, 0:128], l[:], r[:], start=True, stop=True),
                          reads=(Lk, Uk), writes=("P4",))
                    if s < 5:
                        Lnext = Lseq[(s + 1) % 2]
                        kb.op("pe", lambda e, l=Ucur, r=Lcur: e.matmul(P[5][:, 0:128], l[:], r[:], start=True, stop=True),
                              reads=(Uk, Lk), writes=("P5",))
                        kb.op("dve", lambda e, o=Lnext: e.tensor_copy(o[:], P[5][:, 0:128]),
                              reads=("P5",), writes=(f"L{s + 1}m",))
                        Lcur, Lk = Lnext, f"L{s + 1}m"
                    kb.op("act", lambda e, o=Unext: e.activation(out=o[:], in_=P[4][:, 0:128], func=AF.Copy),
                          reads=("P4",), writes=(f"U{s + 1}m",))
                    Ucur, Uk = Unext, f"U{s + 1}m"
                Vcur, Vk = Vnext, f"V{s + 1}v"
            kb.op("pe", lambda e, l=qft, pb=pb, h=h: e.matmul(P[7][:, 0:64], l, S_b[pb:pb + 64, h * 64:(h + 1) * 64], start=True, stop=True),
                  reads=(kL["qf"], "S_b"), writes=("P7",))
            kb.op("dve", lambda e: e.tensor_scalar(out=tmp2_sb[:], in0=P[7][:, 0:64], scalar1=ccol_sb[:, 2:3],
                                                   scalar2=None, op0=ALU.mult),
                  reads=("P7", "rhoP"), writes=("tmp2",))
            mm_group(kb, P[6][:, 0:64], [(Pp_sb[:], Vcur[:]), (i128b_sb[:], tmp2_sb[:])],
                     reads=("Pp", Vk, "tmp2", "i128b"), writes=("P6",))
            kb.op("act", lambda e, h=h: e.activation(out=o_sb[:, h * 64:(h + 1) * 64], in_=P[6][:, 0:64], func=AF.Copy),
                  reads=("P6",), writes=(f"osl{h}",))
            kb.op("dve", lambda e, h=h, kt=kts, khl=khl: e.tensor_scalar(out=kw_sb[:], in0=kt[:, khl * 64:(khl + 1) * 64],
                                                                scalar1=ccol_sb[:, 3:4], scalar2=None, op0=ALU.mult),
                  reads=(kL["kt"], "wc"), writes=("kw",))
            kb.op("pe", lambda e: e.matmul(P[7][0:64, 64:128], kw_sb[:], Vcur[:], start=True, stop=True),
                  reads=("kw", Vk), writes=("P7",))
            kb.op("act", lambda e: e.activation(out=lamc_sb[:], in_=ccol_sb[0:64, 0:1], func=AF.Exp),
                  reads=("cC",), writes=("lamC",))
            kb.op("dve", lambda e, h=h: e.scalar_tensor_tensor(
                out=S_f[:, h * 64:(h + 1) * 64], in0=S_f[:, h * 64:(h + 1) * 64], scalar=lamc_sb[:, 0:1],
                in1=P[7][0:64, 64:128], op0=ALU.mult, op1=ALU.add),
                  reads=("S_f", "lamC", "P7"), writes=("S_f",))
            kb.op("act", lambda e, h=h: e.activation(out=S_b[0:64, h * 64:(h + 1) * 64], in_=S_f[:, h * 64:(h + 1) * 64],
                                                     func=AF.Copy),
                  reads=("S_f",), writes=("S_b",))
            kb.op("act", lambda e, h=h: e.activation(out=S_b[64:128, h * 64:(h + 1) * 64], in_=S_f[:, h * 64:(h + 1) * 64],
                                                     func=AF.Copy),
                  reads=("S_f",), writes=("S_b",))
        kb.op("sync", lambda e, t0=t0: e.dma_start(out=oout[t0:t0 + C, :], in_=o_sb[:]),
              reads=tuple(f"osl{h}" for h in range(8)), dma=True)


def build_scan():
    return build_program(emit_scan)



def emit_epi(nc, kb, NT=2048):
    o_in = nc.dram_tensor("o_in", [NT, 1024], BF16, kind="ExternalInput").ap()
    xnf = nc.dram_tensor("xnf", [D, NT], BF16, kind="ExternalInput").ap()
    wzT = nc.dram_tensor("wzT", [D, 1024], F32, kind="ExternalInput").ap()
    woutT = nc.dram_tensor("woutT", [1024, D], F32, kind="ExternalInput").ap()
    nw128 = nc.dram_tensor("nw128", [128, 1], F32, kind="ExternalInput").ap()
    attnf = nc.dram_tensor("attnf", [D, NT], BF16, kind="ExternalOutput").ap()
    yn_scr = nc.dram_tensor("yn_scr", [NT, 1024], BF16, kind="Internal").ap()

    xn_sb = nc.alloc_sbuf_tensor("xn_sb", [128, 16 * NT], BF16)
    wz_sb = nc.alloc_sbuf_tensor("wz_sb", [128, 16 * 1024], BF16)
    wo_sb = nc.alloc_sbuf_tensor("wo_sb", [128, 8 * 2048], BF16)
    ynf_sb = nc.alloc_sbuf_tensor("ynf_sb", [128, 8 * 2048], BF16)
    nw_sb = nc.alloc_sbuf_tensor("nw_sb", [128, 1], F32)
    zs_sb = nc.alloc_sbuf_tensor("zs_sb", [128, 2 * 1024], BF16)
    ot_sb = nc.alloc_sbuf_tensor("ot_sb", [128, 2 * 1024], BF16)
    ssq_sb = nc.alloc_sbuf_tensor("ssq_sb", [128, 16], F32)
    rstd_sb = nc.alloc_sbuf_tensor("rstd_sb", [128, 16], F32)
    ynt_sb = nc.alloc_sbuf_tensor("ynt_sb", [128, 2 * 1024], BF16)
    osq_sb = nc.alloc_sbuf_tensor("osq_sb", [128, 1024], BF16)
    out_sb = nc.alloc_sbuf_tensor("out_sb", [128, 512], BF16)
    psA = nc.alloc_psum_tensor("psA", [128, 512], F32)
    psB = nc.alloc_psum_tensor("psB", [128, 512], F32)
    eps_c, eps_k = kb.constcol(nc, 1e-6)

    kb.op("sync", lambda e: e.dma_start(out=xn_sb[:].rearrange("p (k t) -> p k t", k=16),
                                        in_=xnf.rearrange("(k p) t -> p k t", p=128)),
          writes=("xn",), dma=True)
    kb.op("gp", lambda e: e.dma_start(out=wz_sb[:].rearrange("p (k m) -> p k m", k=16),
                                      in_=wzT.rearrange("(k p) m -> p k m", p=128)),
          writes=("wz",), dma=True)
    kb.op("gp", lambda e: e.dma_start(out=wo_sb[:].rearrange("p (k m) -> p k m", k=8),
                                      in_=woutT.rearrange("(k p) m -> p k m", p=128)),
          writes=("wo",), dma=True)
    kb.op("sync", lambda e: e.dma_start(out=nw_sb[:], in_=nw128[:, :]), writes=("nw",), dma=True)
    for ks in range(8):
        kb.op("dve", lambda e, ks=ks: e.tensor_scalar(out=wo_sb[:, ks * 2048:(ks + 1) * 2048],
                                                      in0=wo_sb[:, ks * 2048:(ks + 1) * 2048],
                                                      scalar1=nw_sb[:, 0:1], scalar2=None, op0=ALU.mult),
              reads=("wo", "nw"), writes=("wo",))

    nt = NT // 128
    for i in range(nt):
        pp = i % 2
        zt = zs_sb[:, pp * 1024:(pp + 1) * 1024]
        ot = ot_sb[:, pp * 1024:(pp + 1) * 1024]
        ynt = ynt_sb[:, pp * 1024:(pp + 1) * 1024]
        ko, kz, ky = f"ot{pp}", f"zs{pp}", f"ynt{pp}"
        kb.op("sync", lambda e, i=i, o=ot: e.dma_start(out=o, in_=o_in[i * 128:(i + 1) * 128, :]),
              writes=(ko,), dma=True)
        for zs_ in range(2):
            ps = psA if zs_ == 0 else psB
            psk = "psA" if zs_ == 0 else "psB"
            pairs = [(xn_sb[:, k * NT + i * 128: k * NT + (i + 1) * 128],
                      wz_sb[:, k * 1024 + zs_ * 512: k * 1024 + (zs_ + 1) * 512]) for k in range(16)]
            mm_group(kb, ps[:], pairs, reads=("xn", "wz"), writes=(psk,))
            kb.op("act", lambda e, o=zt[:, zs_ * 512:(zs_ + 1) * 512], s=ps[:]: e.activation(out=o, in_=s, func=AF.Silu),
                  reads=(psk,), writes=(kz,))
        kb.op("dve", lambda e, o=ot: e.scalar_tensor_tensor(out=osq_sb[:], in0=o, scalar=1.0, in1=o,
                                                            op0=ALU.mult, op1=ALU.mult),
              reads=(ko,), writes=("osq",))
        kb.op("dve", lambda e: e.tensor_reduce(out=ssq_sb[:], in_=osq_sb[:].rearrange("p (h d) -> p h d", d=64),
                                               axis=AX.X, op=ALU.add),
              reads=("osq",), writes=("ssq",))
        kb.op("act", lambda e: e.activation(out=rstd_sb[:], in_=ssq_sb[:], func=AF.Ln, bias=eps_c[:, 0:1], scale=1.0 / 64),
              reads=("ssq", eps_k), writes=("rstd",))
        kb.op("act", lambda e: e.activation(out=rstd_sb[:], in_=rstd_sb[:], func=AF.Exp, scale=-0.5),
              reads=("rstd",), writes=("rstd",))
        kb.op("dve", lambda e, o=ot, y=ynt: e.tensor_tensor(
            out=y.rearrange("p (h d) -> p h d", d=64), in0=o.rearrange("p (h d) -> p h d", d=64),
            in1=rstd_sb[:].unsqueeze(2).broadcast_to((128, 16, 64)), op=ALU.mult),
              reads=(ko, "rstd"), writes=(ky,))
        kb.op("dve", lambda e, y=ynt, z=zt: e.tensor_tensor(out=y, in0=y, in1=z, op=ALU.mult),
              reads=(ky, kz), writes=(ky,))
        kb.op("sync", lambda e, i=i, y=ynt: e.dma_start(out=yn_scr[i * 128:(i + 1) * 128, :], in_=y),
              reads=(ky,), writes=(f"ysc{i}",), dma=True)
    for c in range(8):
        kb.op("sync", lambda e, c=c: e.dma_start_transpose(out=ynf_sb[:, c * 2048:(c + 1) * 2048],
                                                           in_=yn_scr[:, c * 128:(c + 1) * 128]),
              reads=tuple(f"ysc{i}" for i in range(16)), writes=(f"ynf{c}",), dma=True)
    for m in range(16):
        for ts_ in range(NT // 512):
            ps = psA if ts_ % 2 == 0 else psB
            psk = "psA" if ts_ % 2 == 0 else "psB"
            pairs = [(wo_sb[:, k * 2048 + m * 128:(k * 2048 + (m + 1) * 128)],
                      ynf_sb[:, k * 2048 + ts_ * 512: k * 2048 + (ts_ + 1) * 512]) for k in range(8)]
            mm_group(kb, ps[:], pairs, reads=("wo",) + tuple(f"ynf{c}" for c in range(8)), writes=(psk,))
            kb.op("act", lambda e, s=ps[:]: e.activation(out=out_sb[:], in_=s, func=AF.Copy),
                  reads=(psk,), writes=("outp",))
            kb.op("sync", lambda e, m=m, ts_=ts_: e.dma_start(out=attnf[m * 128:(m + 1) * 128, ts_ * 512:(ts_ + 1) * 512],
                                                              in_=out_sb[:]),
                  reads=("outp",), dma=True)


def build_epi():
    return build_program(emit_epi)



def emit_ffn(nc, kb, NT=2048):
    xnf = nc.dram_tensor("xnf", [D, NT], BF16, kind="ExternalInput").ap()
    h2048 = nc.dram_tensor("H2048", [D, D], BF16, kind="ExternalInput").ap()
    h128 = nc.dram_tensor("H128", [128, 128], BF16, kind="ExternalInput").ap()
    suug = nc.dram_tensor("suug", [128, 16], F32, kind="ExternalInput").ap()
    sudp = nc.dram_tensor("sudp", [128, 64], F32, kind="ExternalInput").ap()
    wsug = nc.dram_tensor("wsug", [128, 128], F32, kind="ExternalInput").ap()
    wsdp = nc.dram_tensor("wsdp", [128, 16], F32, kind="ExternalInput").ap()
    wupT = nc.dram_tensor("wupT", [D, 8192], F32, kind="ExternalInput").ap()
    wgT = nc.dram_tensor("wgT", [D, 8192], F32, kind="ExternalInput").ap()
    d2T = nc.dram_tensor("d2T", [64, 128, D], BF16, kind="ExternalInput").ap()
    ffnf = nc.dram_tensor("ffnf", [D, NT], BF16, kind="ExternalOutput").ap()
    y_scr = nc.dram_tensor("y_scr", [8192, NT], BF16, kind="Internal").ap()

    hu_sb = nc.alloc_sbuf_tensor("hu_sb", [128, 16 * NT], BF16)
    hh_sb = nc.alloc_sbuf_tensor("hh_sb", [128, 16 * 2048], BF16)
    xns_sb = nc.alloc_sbuf_tensor("xns_sb", [128, 16 * 512], BF16)
    w_sb = nc.alloc_sbuf_tensor("w_sb", [128, 2 * 8192], BF16)
    h128_sb = nc.alloc_sbuf_tensor("h128_sb", [128, 128], BF16)
    sc_sb = nc.alloc_sbuf_tensor("sc_sb", [128, 16 + 64 + 64 + 64 + 16 + 16], F32)
    sg_sb = nc.alloc_sbuf_tensor("sg_sb", [128, 512], F32)
    yt_sb = nc.alloc_sbuf_tensor("yt_sb", [128, 2 * 512], BF16)
    out_sb = nc.alloc_sbuf_tensor("out_sb", [128, 2 * 512], BF16)
    psA = nc.alloc_psum_tensor("psA", [128, 512], F32)
    psB = nc.alloc_psum_tensor("psB", [128, 512], F32)
    psC = nc.alloc_psum_tensor("psC", [128, 512], F32)
    psD = nc.alloc_psum_tensor("psD", [128, 512], F32)

    SQH = float(2.0 ** -0.5)
    kb.op("sync", lambda e: e.dma_start(out=hh_sb[:].rearrange("p (k m) -> p k m", k=16),
                                        in_=h2048.rearrange("(k p) m -> p k m", p=128)),
          writes=tuple(f"H{k}" for k in range(16)), dma=True)
    kb.op("sync", lambda e: e.dma_start(out=h128_sb[:], in_=h128[:, :]), writes=("h128",), dma=True)
    kb.op("sync", lambda e: e.dma_start(out=sc_sb[:, 0:16], in_=suug[:, :]), writes=("suug",), dma=True)
    kb.op("sync", lambda e: e.dma_start(out=sc_sb[:, 16:80], in_=sudp[:, :]), writes=("sudp",), dma=True)
    kb.op("sync", lambda e: e.dma_start(out=sc_sb[:, 80:144], in_=wsug[:, 0:64]), writes=("wsu",), dma=True)
    kb.op("sync", lambda e: e.dma_start(out=sc_sb[:, 144:208], in_=wsug[:, 64:128]), writes=("wsg",), dma=True)
    kb.op("sync", lambda e: e.dma_start(out=sc_sb[:, 208:224], in_=wsdp[:, :]), writes=("wsd",), dma=True)
    kb.op("dve", lambda e: e.tensor_tensor(out=sc_sb[:, 80:144], in0=sc_sb[:, 80:144], in1=sc_sb[:, 16:80], op=ALU.mult),
          reads=("wsu", "sudp"), writes=("wsu",))
    kb.op("dve", lambda e: e.tensor_scalar(out=sc_sb[:, 80:144], in0=sc_sb[:, 80:144], scalar1=SQH, scalar2=None, op0=ALU.mult),
          reads=("wsu",), writes=("wsu",))
    kb.op("dve", lambda e: e.tensor_scalar(out=sc_sb[:, 144:208], in0=sc_sb[:, 144:208], scalar1=SQH, scalar2=None, op0=ALU.mult),
          reads=("wsg",), writes=("wsg",))
    kb.op("dve", lambda e: e.tensor_scalar(out=sc_sb[:, 208:224], in0=sc_sb[:, 208:224], scalar1=SQH, scalar2=None, op0=ALU.mult),
          reads=("wsd",), writes=("wsd",))

    for ts_ in range(NT // 512):
        for k in range(16):
            kxn = f"xns{k}"
            kb.op("sync", lambda e, k=k, ts_=ts_: e.dma_start(
                out=xns_sb[:, k * 512:(k + 1) * 512],
                in_=xnf[k * 128:(k + 1) * 128, ts_ * 512:(ts_ + 1) * 512]),
                  writes=(kxn,), dma=True)
            kb.op("dve", lambda e, k=k: e.tensor_scalar(out=xns_sb[:, k * 512:(k + 1) * 512],
                                                        in0=xns_sb[:, k * 512:(k + 1) * 512],
                                                        scalar1=sc_sb[:, k:k + 1], scalar2=None, op0=ALU.mult),
                  reads=(kxn, "suug"), writes=(kxn,))
        for m in range(16):
            ps = psA if m % 2 == 0 else psB
            psk = "psA" if m % 2 == 0 else "psB"
            pairs = [(hh_sb[:, k * 2048 + m * 128:(k * 2048 + (m + 1) * 128)],
                      xns_sb[:, k * 512:(k + 1) * 512]) for k in range(16)]
            mm_group(kb, ps[:], pairs, reads=tuple(f"xns{k}" for k in range(16)) + tuple(f"H{k}" for k in range(16)),
                     writes=(psk,))
            kb.op("act", lambda e, m=m, ts_=ts_, s=ps[:]: e.activation(
                out=hu_sb[:, m * NT + ts_ * 512: m * NT + (ts_ + 1) * 512], in_=s, func=AF.Copy),
                  reads=(psk,), writes=(f"h{m}",))
    HK = tuple(f"h{m}" for m in range(16))
    for io in range(64):
        wpp = io % 2
        wu = w_sb[:, wpp * 2 * 2048: wpp * 2 * 2048 + 2048]
        wg = w_sb[:, wpp * 2 * 2048 + 2048: (wpp + 1) * 2 * 2048]
        kwu, kwg = f"wu{wpp}", f"wg{wpp}"
        kb.op("gp", lambda e, io=io, o=wu: e.dma_start(
            out=o.rearrange("p (k m) -> p k m", k=16),
            in_=wupT.rearrange("(k p) m -> p k m", p=128)[:, :, io * 128:(io + 1) * 128]),
              writes=(kwu,), dma=True)
        kb.op("gp", lambda e, io=io, o=wg: e.dma_start(
            out=o.rearrange("p (k m) -> p k m", k=16),
            in_=wgT.rearrange("(k p) m -> p k m", p=128)[:, :, io * 128:(io + 1) * 128]),
              writes=(kwg,), dma=True)
        for ts_ in range(NT // 512):
            ypp = ts_ % 2
            yt = yt_sb[:, ypp * 512:(ypp + 1) * 512]
            kyt = f"yt{ypp}"
            pairs_u = [(wu[:, k * 128:(k + 1) * 128], hu_sb[:, k * NT + ts_ * 512: k * NT + (ts_ + 1) * 512])
                       for k in range(16)]
            pairs_g = [(wg[:, k * 128:(k + 1) * 128], hu_sb[:, k * NT + ts_ * 512: k * NT + (ts_ + 1) * 512])
                       for k in range(16)]
            mm_group(kb, psC[:], pairs_u, reads=(kwu,) + HK, writes=("psC",))
            mm_group(kb, psD[:], pairs_g, reads=(kwg,) + HK, writes=("psD",))
            kb.op("act", lambda e, io=io: e.activation(out=sg_sb[:], in_=psD[:], func=AF.Silu,
                                                       scale=sc_sb[:, 144 + io:145 + io]),
                  reads=("psD", "wsg"), writes=("sg",))
            kb.op("dve", lambda e, io=io, y=yt: e.scalar_tensor_tensor(
                out=y, in0=sg_sb[:], scalar=sc_sb[:, 80 + io:81 + io],
                in1=psC[:], op0=ALU.mult, op1=ALU.mult),
                  reads=("sg", "wsu", "psC"), writes=(kyt,))
            kb.op("sync", lambda e, io=io, ts_=ts_, y=yt: e.dma_start(
                out=y_scr[io * 128:(io + 1) * 128, ts_ * 512:(ts_ + 1) * 512], in_=y),
                  reads=(kyt,), writes=(f"ysc{io}_{ts_}",), dma=True)
    for ts_ in range(NT // 512):
        for c in range(64):
            ku = f"u{c}"
            khs = f"h{c // 4}"
            kb.op("sync", lambda e, c=c, ts_=ts_: e.dma_start(
                out=xns_sb[:, (c % 16) * 512:(c % 16 + 1) * 512],
                in_=y_scr[c * 128:(c + 1) * 128, ts_ * 512:(ts_ + 1) * 512]),
                  reads=(f"ysc{c}_{ts_}",), writes=(f"yl{c % 16}", f"xns{c % 16}"), dma=True)
            ps = psA if c % 2 == 0 else psB
            psk = "psA" if c % 2 == 0 else "psB"
            kb.op("pe", lambda e, c=c, s=ps: e.matmul(s[:], h128_sb[:], xns_sb[:, (c % 16) * 512:(c % 16 + 1) * 512],
                                                      start=True, stop=True),
                  reads=(f"yl{c % 16}", "h128"), writes=(psk,))
            eng = "act" if c % 2 == 0 else "dve"
            if eng == "act":
                kb.op("act", lambda e, c=c, s=ps: e.activation(out=hu_sb[:, c * 512:(c + 1) * 512], in_=s[:], func=AF.Copy),
                      reads=(psk,), writes=(ku, khs))
            else:
                kb.op("dve", lambda e, c=c, s=ps: e.tensor_copy(hu_sb[:, c * 512:(c + 1) * 512], s[:]),
                      reads=(psk,), writes=(ku, khs))
        UK = tuple(f"u{c}" for c in range(64))
        for o in range(16):
            wpp = o % 2
            dsl = w_sb[:, wpp * 8192:(wpp + 1) * 8192]
            kds = f"ds{wpp}"
            extraw = ("wu0", "wg0", "wu1", "wg1") if wpp == 0 else ()
            kb.op("gp", lambda e, o=o, dd=dsl: e.dma_start(
                out=dd.rearrange("p (c m) -> p c m", c=64),
                in_=d2T[:, :, o * 128:(o + 1) * 128].rearrange("c p m -> p c m")),
                  writes=(kds,) + extraw, dma=True)
            ps = psC if o % 2 == 0 else psD
            psk = "psC" if o % 2 == 0 else "psD"
            pairs = [(dsl[:, c * 128:(c + 1) * 128], hu_sb[:, c * 512:(c + 1) * 512]) for c in range(64)]
            mm_group(kb, ps[:], pairs, reads=(kds,) + UK, writes=(psk,))
            op2 = out_sb[:, (o % 2) * 512:(o % 2 + 1) * 512]
            kb.op("dve", lambda e, o=o, s=ps, oo=op2: e.tensor_scalar(
                out=oo, in0=s[:], scalar1=sc_sb[:, 208 + o:209 + o], scalar2=None, op0=ALU.mult),
                  reads=(psk, "wsd"), writes=(f"ob{o % 2}",))
            kb.op("sync", lambda e, o=o, ts_=ts_, oo=op2: e.dma_start(
                out=ffnf[o * 128:(o + 1) * 128, ts_ * 512:(ts_ + 1) * 512], in_=oo),
                  reads=(f"ob{o % 2}",), dma=True)


def build_ffn():
    return build_program(emit_ffn)



def emit_proj(nc, kb, projs, NT=2048):
    xnf = nc.dram_tensor("xnf", [D, NT], BF16, kind="ExternalInput").ap()
    xn_sb = nc.alloc_sbuf_tensor("xn_sb", [128, 16 * NT], BF16)
    w_sb = nc.alloc_sbuf_tensor("w_sb", [128, 16 * 2048], BF16)
    b_sb = nc.alloc_sbuf_tensor("b_sb", [128, 16], F32)
    out_sb = nc.alloc_sbuf_tensor("out_sb", [128, 2 * 512], BF16)
    psA = nc.alloc_psum_tensor("psA", [128, 512], F32)
    psB = nc.alloc_psum_tensor("psB", [128, 512], F32)
    kb.op("sync", lambda e: e.dma_start(out=xn_sb[:].rearrange("p (k t) -> p k t", k=16),
                                        in_=xnf.rearrange("(k p) t -> p k t", p=128)),
          writes=("xn",), dma=True)
    zero_c, zero_k = kb.constcol(nc, 0.0)
    for (wname, bname, scale, outname) in projs:
        wT = nc.dram_tensor(wname, [D, D], F32, kind="ExternalInput").ap()
        ov = nc.dram_tensor(outname, [D, NT], BF16, kind="ExternalOutput").ap()
        if bname:
            bv = nc.dram_tensor(bname, [128, 16], F32, kind="ExternalInput").ap()
            kb.op("sync", lambda e, bv=bv: e.dma_start(out=b_sb[:], in_=bv[:, :]), writes=("b",), dma=True)
            if scale != 1.0:
                kb.op("dve", lambda e, sc=scale: e.tensor_scalar(out=b_sb[:], in0=b_sb[:], scalar1=sc,
                                                                 scalar2=None, op0=ALU.mult),
                      reads=("b",), writes=("b",))
        kb.op("gp", lambda e, wT=wT: e.dma_start(out=w_sb[:].rearrange("p (k m) -> p k m", k=16),
                                                 in_=wT.rearrange("(k p) m -> p k m", p=128)),
              writes=("w",), dma=True)
        for m in range(16):
            for ts_ in range(NT // 512):
                ps = psA if ts_ % 2 == 0 else psB
                psk = "psA" if ts_ % 2 == 0 else "psB"
                pairs = [(w_sb[:, k * 2048 + m * 128:(k * 2048 + (m + 1) * 128)],
                          xn_sb[:, k * NT + ts_ * 512: k * NT + (ts_ + 1) * 512]) for k in range(16)]
                mm_group(kb, ps[:], pairs, reads=("w", "xn"), writes=(psk,))
                ob = out_sb[:, (ts_ % 2) * 512:(ts_ % 2 + 1) * 512]
                bias_ap = b_sb[:, m:m + 1] if bname else zero_c[:, 0:1]
                kb.op("dve", lambda e, s=ps[:], ob=ob, ba=bias_ap, sc=scale: e.tensor_scalar(
                    out=ob, in0=s, scalar1=sc, scalar2=ba, op0=ALU.mult, op1=ALU.add),
                      reads=(psk, "b" if bname else zero_k), writes=(f"ob{ts_ % 2}",))
                kb.op("sync", lambda e, m=m, ts_=ts_, ob=ob, ov=ov: e.dma_start(
                    out=ov[m * 128:(m + 1) * 128, ts_ * 512:(ts_ + 1) * 512], in_=ob),
                      reads=(f"ob{ts_ % 2}",), dma=True)


def build_qkv1():
    return build_program(lambda nc, kb: emit_proj(nc, kb, [
        ("wqT", "bq", float(128 ** -0.5), "Qf"), ("wkT", "bk", 1.0, "Kf"), ("wvT", "bv", 1.0, "Vf")]))


def build_wo():
    return build_program(lambda nc, kb: emit_proj(nc, kb, [("woT", "bo", 1.0, "attnf")]))



def emit_attn(nc, kb):
    Qf = nc.dram_tensor("Qf", [1024, T], BF16, kind="ExternalInput").ap()
    Kf = nc.dram_tensor("Kf", [1024, T], BF16, kind="ExternalInput").ap()
    Vtok = nc.dram_tensor("Vtok", [T, 1024], BF16, kind="ExternalInput").ap()
    ones1b = nc.dram_tensor("ONES1B", [1, 128], BF16, kind="ExternalInput").ap()
    onescb = nc.dram_tensor("ONESCB", [128, 1], BF16, kind="ExternalInput").ap()
    ctxf = nc.dram_tensor("ctxf", [1024, T], BF16, kind="ExternalOutput").ap()

    qh_sb = nc.alloc_sbuf_tensor("qh_sb", [128, T], BF16)
    kh_sb = nc.alloc_sbuf_tensor("kh_sb", [128, T], BF16)
    q2_sb = nc.alloc_sbuf_tensor("q2_sb", [65, T], BF16)
    k2_sb = nc.alloc_sbuf_tensor("k2_sb", [65, T], BF16)
    vh_sb = nc.alloc_sbuf_tensor("vh_sb", [128, 32 * 128], BF16)
    sqf_sb = nc.alloc_sbuf_tensor("sqf_sb", [128, T], BF16)
    row_sb = nc.alloc_sbuf_tensor("row_sb", [1, T], F32)
    mrow_sb = nc.alloc_sbuf_tensor("mrow_sb", [1, T], F32)
    kr_sb = nc.alloc_sbuf_tensor("kr_sb", [1, T], F32)
    sc1_sb = nc.alloc_sbuf_tensor("sc1_sb", [1, 4], F32)
    pt_sb = nc.alloc_sbuf_tensor("pt_sb", [128, 2 * 1024], BF16)
    invl_sb = nc.alloc_sbuf_tensor("invl_sb", [1, 1024], F32)
    invb_sb = nc.alloc_sbuf_tensor("invb_sb", [128, 1024], BF16)
    ctx_sb = nc.alloc_sbuf_tensor("ctx_sb", [128, 1024], BF16)
    o1b_sb = nc.alloc_sbuf_tensor("o1b_sb", [1, 128], BF16)
    ocb_sb = nc.alloc_sbuf_tensor("ocb_sb", [128, 1], BF16)
    psS = nc.alloc_psum_tensor("psS", [128, 1024], F32)
    psCx = nc.alloc_psum_tensor("psCx", [128, 1024], F32)
    psL = nc.alloc_psum_tensor("psL", [1, 1024], F32)
    psR = nc.alloc_psum_tensor("psR", [1, 512], F32)
    psBc = psS

    kb.op("sync", lambda e: e.dma_start(out=o1b_sb[:], in_=ones1b[:, :]), writes=("o1b",), dma=True)
    o1f_sb = nc.alloc_sbuf_tensor("o1f_sb", [1, 128], F32)
    kb.op("dve", lambda e: e.memset(o1f_sb[:], 1.0), writes=("o1f",))
    kb.op("sync", lambda e: e.dma_start(out=ocb_sb[:], in_=onescb[:, :]), writes=("ocb",), dma=True)

    for h in range(8):
        r0 = h * 128
        kb.op("sync", lambda e, r0=r0: e.dma_start(out=qh_sb[:], in_=Qf[r0:r0 + 128, :]), writes=("qh",), dma=True)
        kb.op("sync", lambda e, r0=r0: e.dma_start(out=kh_sb[:], in_=Kf[r0:r0 + 128, :]), writes=("kh",), dma=True)
        kb.op("sync", lambda e, r0=r0: e.dma_start(
            out=vh_sb[:].rearrange("p (n d) -> p n d", n=32),
            in_=Vtok.rearrange("(n p) c -> p n c", p=128)[:, :, r0:r0 + 128]),
              writes=("vh",), dma=True)
        kb.op("dve", lambda e: e.tensor_copy(k2_sb[0:64, :], kh_sb[64:128, :]), reads=("kh",), writes=("k2",))
        kb.op("dve", lambda e: e.memset(k2_sb[64:65, :], 1.0), writes=("k2o",))
        kb.op("dve", lambda e: e.tensor_copy(q2_sb[0:64, :], qh_sb[64:128, :]), reads=("qh",), writes=("q2",))
        kb.op("act", lambda e: e.activation(out=sqf_sb[:], in_=qh_sb[:], func=AF.Square), reads=("qh",), writes=("sqf",))
        for ts_ in range(T // 512):
            kb.op("pe", lambda e, ts_=ts_: e.matmul(psR[0:1, 0:512], ocb_sb[:], sqf_sb[:, ts_ * 512:(ts_ + 1) * 512],
                                                    start=True, stop=True),
                  reads=("sqf", "ocb"), writes=("psR",))
            kb.op("dve", lambda e, ts_=ts_: e.tensor_copy(row_sb[:, ts_ * 512:(ts_ + 1) * 512], psR[0:1, 0:512]),
                  reads=("psR",), writes=("qsqr",))
        kb.op("act", lambda e: e.activation(out=sqf_sb[:], in_=kh_sb[:], func=AF.Square), reads=("kh",), writes=("sqf",))
        for ts_ in range(T // 512):
            kb.op("pe", lambda e, ts_=ts_: e.matmul(psR[0:1, 0:512], ocb_sb[:], sqf_sb[:, ts_ * 512:(ts_ + 1) * 512],
                                                    start=True, stop=True),
                  reads=("sqf", "ocb"), writes=("psR",))
            kb.op("dve", lambda e, ts_=ts_: e.tensor_copy(kr_sb[:, ts_ * 512:(ts_ + 1) * 512], psR[0:1, 0:512]),
                  reads=("psR",), writes=("ksqr",))
        kb.op("dve", lambda e: e.tensor_reduce(out=sc1_sb[:, 0:1], in_=kr_sb[:], axis=AX.X, op=ALU.max),
              reads=("ksqr",), writes=("mk2",))
        kb.op("act", lambda e: e.activation(out=sc1_sb[:, 1:2], in_=sc1_sb[:, 0:1], func=AF.Ln),
              reads=("mk2",), writes=("lnmk",))
        kb.op("dve", lambda e: e.tensor_scalar(out=sc1_sb[:, 1:2], in0=sc1_sb[:, 1:2], scalar1=0.5, scalar2=0.1,
                                               op0=ALU.mult, op1=ALU.add),
              reads=("lnmk",), writes=("lnmk",))
        kb.op("act", lambda e: e.activation(out=mrow_sb[:], in_=row_sb[:], func=AF.Ln), reads=("qsqr",), writes=("mrow",))
        kb.op("act", lambda e: e.activation(out=mrow_sb[:], in_=mrow_sb[:], func=AF.Exp, scale=0.5, bias=sc1_sb[:, 1:2]),
              reads=("mrow", "lnmk"), writes=("mrow",))
        kb.op("dve", lambda e: e.tensor_scalar(out=q2_sb[64:65, :], in0=mrow_sb[:], scalar1=-1.0, scalar2=None, op0=ALU.mult),
              reads=("mrow",), writes=("q2o",))
        for sp in range(4):
            q0 = sp * 1024
            nk = (sp + 1) * 8
            for kt in range(nk):
                ppp = kt % 2
                pt = pt_sb[:, ppp * 1024:(ppp + 1) * 1024]
                kpt = f"pt{ppp}"
                for sub in range(2):
                    qs = q0 + sub * 512
                    pairs = [(kh_sb[0:64, kt * 128:(kt + 1) * 128], qh_sb[0:64, qs:qs + 512]),
                             (k2_sb[0:65, kt * 128:(kt + 1) * 128], q2_sb[0:65, qs:qs + 512])]
                    mm_group(kb, psS[:, sub * 512:(sub + 1) * 512], pairs,
                             reads=("kh", "qh", "k2", "k2o", "q2", "q2o"), writes=(f"psS{sub}",))
                kb.op("act", lambda e, pt=pt: e.activation(out=pt, in_=psS[:], func=AF.Exp),
                      reads=("psS0", "psS1"), writes=(kpt,))
                if kt * 128 >= q0:
                    def _mask2(e, pt=pt, base=q0 - kt * 128):
                        if not hasattr(e, "_fillreg0"):
                            e._fillreg0 = e.to_reg(0.0)
                        return e.affine_select(out=pt, in_=pt, pattern=[[1, 1024]], compare_op=ALU.is_ge,
                                               fill=e._fillreg0, base=base, channel_multiplier=-1)
                    kb.op("gp", _mask2, reads=(kpt,), writes=(kpt,))
                for sub in range(2):
                    kb.op("pe", lambda e, kt=kt, sub=sub, pt=pt, nk=nk: e.matmul(
                        psCx[:, sub * 512:(sub + 1) * 512], vh_sb[:, kt * 128:(kt + 1) * 128],
                        pt[:, sub * 512:(sub + 1) * 512], start=(kt == 0), stop=(kt == nk - 1)),
                          reads=("vh", kpt), writes=(f"psCx{sub}",))
                    kb.op("pe", lambda e, kt=kt, sub=sub, pt=pt, nk=nk: e.matmul(
                        psL[0:1, sub * 512:(sub + 1) * 512], ocb_sb[:],
                        pt[:, sub * 512:(sub + 1) * 512], start=(kt == 0), stop=(kt == nk - 1)),
                          reads=("ocb", kpt), writes=(f"psL{sub}",))
            kb.op("act", lambda e: e.activation(out=invl_sb[:], in_=psL[0:1, 0:1024], func=AF.Ln),
                  reads=("psL0", "psL1"), writes=("invl",))
            kb.op("act", lambda e: e.activation(out=invl_sb[:], in_=invl_sb[:], func=AF.Exp, scale=-1.0),
                  reads=("invl",), writes=("invl",))
            for sub in range(2):
                kb.op("pe", lambda e, sub=sub: e.matmul(psBc[:, sub * 512:(sub + 1) * 512], o1f_sb[:],
                                                        invl_sb[0:1, sub * 512:(sub + 1) * 512], start=True, stop=True),
                      reads=("invl", "o1f"), writes=(f"psS{sub}",))
            kb.op("dve", lambda e: e.tensor_copy(invb_sb[:], psBc[:]), reads=("psS0", "psS1"), writes=("invb",))
            kb.op("dve", lambda e: e.scalar_tensor_tensor(out=ctx_sb[:], in0=invb_sb[:], scalar=1.0, in1=psCx[:],
                                                          op0=ALU.mult, op1=ALU.mult),
                  reads=("invb", "psCx0", "psCx1"), writes=("ctx",))
            kb.op("sync", lambda e, r0=r0, q0=q0: e.dma_start(out=ctxf[r0:r0 + 128, q0:q0 + 1024], in_=ctx_sb[:]),
                  reads=("ctx",), dma=True)


def build_attn():
    return build_program(emit_attn)



def _split_tokens(arr, n=NCORES):
    flat = np.ascontiguousarray(arr.reshape(-1, arr.shape[-1]))
    return np.split(flat, n, axis=0)


def kernel(**inputs):
    NT = (B * T) // NCORES
    CORES = list(range(NCORES))
    f32 = np.float32
    xfull = np.ascontiguousarray(inputs["L0_attn_x"].astype(f32, copy=False)).reshape(-1, D)

    H64 = hadamard(64)
    H2048c = (hadamard(2048) * (2.0 ** -5)).astype(BF)
    H128c = (hadamard(128) * (2.0 ** -6)).astype(BF)
    UINC = np.triu(np.ones((128, 128), f32))
    I128F = np.eye(128, dtype=f32)
    I128B = I128F.astype(BF)
    ONES1 = np.ones((1, 128), f32)
    ONES1B = ONES1.astype(BF)
    ONESCB = np.ones((128, 1), f32).astype(BF)
    ones2 = np.zeros((128, 2), f32)
    ones2[0:64, 0] = 1.0
    ones2[64:128, 1] = 1.0
    ones2 = ones2.astype(BF)

    def ct(a):
        return np.ascontiguousarray(a)

    nc_fold = build_fold()
    OS = D // NCORES
    wd0 = inputs["L0_ffn_W_down"].astype(f32, copy=False)
    wd1 = inputs["L1_ffn_W_down"].astype(f32, copy=False)
    im = [{"H64": H64.astype(BF), "Wdn0": ct(wd0[c * OS:(c + 1) * OS]),
           "Wdn1": ct(wd1[c * OS:(c + 1) * OS])} for c in range(NCORES)]
    res = run_bass_kernel_spmd(nc_fold, im, core_ids=CORES)
    d2 = {}
    for li in range(2):
        D2 = np.concatenate([r[f"D2_{li}"] for r in res.results], axis=0)
        d2[li] = ct(D2.T.reshape(64, 128, D))

    nc_ln0 = build_ln(NT, False)
    res = run_bass_kernel_spmd(nc_ln0, [{"x": ct(s)} for s in np.split(xfull, NCORES)], core_ids=CORES)
    xn0 = np.concatenate([r["xn"] for r in res.results], axis=0)

    nc_b = build_l0proj()
    Wqkv = inputs["L0_attn_W_qkv"].astype(f32, copy=False)
    Wb_ = inputs["L0_attn_W_b"].astype(f32, copy=False)
    Wa_ = inputs["L0_attn_W_a"].astype(f32, copy=False)
    convw = inputs["L0_attn_conv_w"].astype(f32, copy=False)[:, 0, :]
    dtb = inputs["L0_attn_dt_bias"].astype(f32, copy=False)
    alog = inputs["L0_attn_A_log"].astype(f32, copy=False)
    im = []
    for c in range(NCORES):
        b, hh = c // 2, c % 2
        rows = list(range(hh * 256, (hh + 1) * 256)) + \
               list(range(512 + hh * 256, 512 + (hh + 1) * 256)) + \
               list(range(1024 + hh * 512, 1024 + (hh + 1) * 512))
        im.append({"xnf": ct(xn0[b * T:(b + 1) * T].T),
                   "wqkvT": ct(Wqkv[rows].T),
                   "wbT": ct(Wb_[hh * 8:(hh + 1) * 8].T),
                   "waT": ct(Wa_[hh * 8:(hh + 1) * 8].T),
                   "convw": ct(convw[rows]),
                   "dtb": ct(dtb[hh * 8:(hh + 1) * 8, None]),
                   "alog": ct(alog[hh * 8:(hh + 1) * 8, None]),
                   "ones2": ones2})
    res = run_bass_kernel_spmd(nc_b, im, core_ids=CORES)
    bouts = res.results

    nc_c = build_scan()
    im = []
    for c in range(NCORES):
        r = bouts[c]
        qkvf = r["qkvf"]
        im.append({"qf": ct(qkvf[0:256].reshape(2, 128, T)),
                   "kf": ct(qkvf[256:512].reshape(2, 128, T)),
                   "ktok": ct(qkvf[256:512].T), "vtok": ct(qkvf[512:1024].T),
                   "gcol": ct(r["grow"].T), "betacol": ct(r["betarow"].T),
                   "lnbcol": ct(r["lnbrow"].T), "lnrkcol": ct(r["lnrkrow"].T),
                   "lnrqcol": ct(r["lnrqrow"].T),
                   "UINC": UINC, "I128F": I128F, "I128B": I128B, "ONES1": ONES1})
    res = run_bass_kernel_spmd(nc_c, im, core_ids=CORES)
    o_glob = np.zeros((B, T, 16, 64), dtype=BF)
    for c in range(NCORES):
        b, hh = c // 2, c % 2
        o_glob[b, :, hh * 8:(hh + 1) * 8] = res.results[c]["o"].reshape(T, 8, 64)
    o_flat = o_glob.reshape(-1, 1024)

    nc_d = build_epi()
    Wz = inputs["L0_attn_W_z"].astype(f32, copy=False)
    Wout = inputs["L0_attn_W_out"].astype(f32, copy=False)
    nw = inputs["L0_attn_norm_weight"].astype(f32, copy=False)
    nw128 = ct(np.tile(nw, 2)[:, None])
    im = [{"o_in": ct(o_flat[c * NT:(c + 1) * NT]),
           "xnf": ct(xn0[c * NT:(c + 1) * NT].T),
           "wzT": ct(Wz.T), "woutT": ct(Wout.T), "nw128": nw128} for c in range(NCORES)]
    res = run_bass_kernel_spmd(nc_d, im, core_ids=CORES)
    attn0 = np.concatenate([ct(r["attnf"].T) for r in res.results], axis=0)

    nc_ffn = build_ffn()

    def run_ffn(xn, li):
        p = "L0_ffn_" if li == 0 else "L1_ffn_"
        im = [{"xnf": ct(xn[c * NT:(c + 1) * NT].T),
               "H2048": H2048c, "H128": H128c,
               "suug": ct(inputs[p + "SU_ug"].astype(f32, copy=False).reshape(16, 128).T),
               "sudp": ct(inputs[p + "SU_dp"].astype(f32, copy=False).reshape(64, 128).T),
               "wsug": ct(inputs[p + "Wscale_ug"].astype(f32, copy=False).reshape(128, 128).T),
               "wsdp": ct(inputs[p + "Wscale_dp"].astype(f32, copy=False).reshape(16, 128).T),
               "wupT": ct(inputs[p + "W_up"].astype(f32, copy=False).T),
               "wgT": ct(inputs[p + "W_gate"].astype(f32, copy=False).T),
               "d2T": d2[li]} for c in range(NCORES)]
        res = run_bass_kernel_spmd(nc_ffn, im, core_ids=CORES)
        return np.concatenate([ct(r["ffnf"].T) for r in res.results], axis=0)

    ffn0 = run_ffn(xn0, 0)

    nc_lnr = build_ln(NT, True)
    im = [{"x": ct(xfull[c * NT:(c + 1) * NT]), "a": ct(attn0[c * NT:(c + 1) * NT]),
           "f": ct(ffn0[c * NT:(c + 1) * NT])} for c in range(NCORES)]
    res = run_bass_kernel_spmd(nc_lnr, im, core_ids=CORES)
    x1 = np.concatenate([r["x1"] for r in res.results], axis=0)
    xn1 = np.concatenate([r["xn"] for r in res.results], axis=0)

    nc_g = build_qkv1()
    wq = ct(inputs["L1_attn_Wq"].astype(f32, copy=False).T)
    wk = ct(inputs["L1_attn_Wk"].astype(f32, copy=False).T)
    wv = ct(inputs["L1_attn_Wv"].astype(f32, copy=False).T)
    bq = ct(inputs["L1_attn_bq"].astype(f32, copy=False).reshape(16, 128).T)
    bk = ct(inputs["L1_attn_bk"].astype(f32, copy=False).reshape(16, 128).T)
    bv = ct(inputs["L1_attn_bv"].astype(f32, copy=False).reshape(16, 128).T)
    im = [{"xnf": ct(xn1[c * NT:(c + 1) * NT].T), "wqT": wq, "wkT": wk, "wvT": wv,
           "bq": bq, "bk": bk, "bv": bv} for c in range(NCORES)]
    res = run_bass_kernel_spmd(nc_g, im, core_ids=CORES)
    Qf = np.concatenate([r["Qf"] for r in res.results], axis=1)
    Kf = np.concatenate([r["Kf"] for r in res.results], axis=1)
    Vf = np.concatenate([r["Vf"] for r in res.results], axis=1)

    nc_h = build_attn()
    im = []
    for c in range(NCORES):
        b, hh = c // 2, c % 2
        rr = slice(hh * 1024, (hh + 1) * 1024)
        cc = slice(b * T, (b + 1) * T)
        im.append({"Qf": ct(Qf[rr, cc]), "Kf": ct(Kf[rr, cc]),
                   "Vtok": ct(Vf[rr, cc].T), "ONES1B": ONES1B, "ONESCB": ONESCB})
    res = run_bass_kernel_spmd(nc_h, im, core_ids=CORES)
    ctxf = np.zeros((2048, B * T), dtype=BF)
    for c in range(NCORES):
        b, hh = c // 2, c % 2
        ctxf[hh * 1024:(hh + 1) * 1024, b * T:(b + 1) * T] = res.results[c]["ctxf"]

    nc_i = build_wo()
    wo = ct(inputs["L1_attn_Wo"].astype(f32, copy=False).T)
    bo = ct(inputs["L1_attn_bo"].astype(f32, copy=False).reshape(16, 128).T)
    im = [{"xnf": ct(ctxf[:, c * NT:(c + 1) * NT]), "woT": wo, "bo": bo} for c in range(NCORES)]
    res = run_bass_kernel_spmd(nc_i, im, core_ids=CORES)
    attn1 = np.concatenate([ct(r["attnf"].T) for r in res.results], axis=0)

    ffn1 = run_ffn(xn1, 1)

    im = [{"x": ct(x1[c * NT:(c + 1) * NT]), "a": ct(attn1[c * NT:(c + 1) * NT]),
           "f": ct(ffn1[c * NT:(c + 1) * NT])} for c in range(NCORES)]
    res = run_bass_kernel_spmd(nc_lnr, im, core_ids=CORES)
    x2 = np.concatenate([r["x1"] for r in res.results], axis=0)
    return x2.reshape(B, T, D)


if __name__ == "__main__":
    pass

def _dyn2048(e, extra=0):
    assert extra == 0
    if not hasattr(e, "_dynbase"):
        pid = e.partition_id()
        e._dynbase = e.compute_val((pid % 2) * 2048)
    return e._dynbase


def phase_ln(nc, kb, pre, x_ap, xn_out, NTtok, a_feat=None, f_feat=None, x1_out=None,
             dyn_x=False, a_ready=(), f_ready=(), x_ready=(), xn_keys=None):
    nt = NTtok // 128
    xt = kb.spool(128, D, F32)
    at = kb.spool(128, D, BF16) if a_feat is not None else None
    ft = kb.spool(128, D, BF16) if f_feat is not None else None
    sq = kb.spool(128, D, BF16)
    stats = kb.spool(128, 8, F32)
    xnb = kb.spool(128, D, BF16)
    epst, epsk = kb.constcol(nc, 1e-5)
    out_keys = []
    for i in range(nt):
        pp = 0
        xs = xt[:, 0:D]
        xn_s = xnb[:, 0:D]
        kx = pre + f"x{pp}"
        if dyn_x:
            kb.op("sync", lambda e, o=xs, i=i: e.dma_start(
                out=o, in_=x_ap[i * 128:i * 128 + 2176, :][bass.ds(_dyn2048(e), 128), :]),
                  reads=tuple(x_ready), writes=(kx,), dma=True)
        else:
            kb.op("sync", lambda e, o=xs, i=i: e.dma_start(out=o, in_=x_ap[i * 128:(i + 1) * 128, :]),
                  reads=tuple(x_ready), writes=(kx,), dma=True)
        if a_feat is not None:
            ka, kf = pre + f"a{pp}", pre + f"f{pp}"
            as_ = at[:, 0:D]
            fs = ft[:, 0:D]
            kb.op("sync", lambda e, o=as_, i=i: e.dma_start_transpose(out=o, in_=a_feat[:, i * 128:(i + 1) * 128]),
                  reads=tuple(a_ready), writes=(ka,), dma=True)
            kb.op("sync", lambda e, o=fs, i=i: e.dma_start_transpose(out=o, in_=f_feat[:, i * 128:(i + 1) * 128]),
                  reads=tuple(f_ready), writes=(kf,), dma=True)
            kb.op("dve", lambda e, o=xs, i1=as_: e.tensor_tensor(out=o, in0=o, in1=i1, op=ALU.add),
                  reads=(kx, ka), writes=(kx,))
            kb.op("dve", lambda e, o=xs, i1=fs: e.tensor_tensor(out=o, in0=o, in1=i1, op=ALU.add),
                  reads=(kx, kf), writes=(kx,))
            if x1_out is not None:
                kb.op("sync", lambda e, s=xs, i=i: e.dma_start(out=x1_out[i * 128:(i + 1) * 128, :], in_=s),
                      reads=(kx,), writes=(pre + f"x1w{i}",), dma=True)
                out_keys.append(pre + f"x1w{i}")
        if xn_out is None:
            continue
        kb.op("dve", lambda e, s=xs: e.tensor_reduce(out=stats[:, 0:1], in_=s, axis=AX.X, op=ALU.add),
              reads=(kx,), writes=(pre + "ss",))
        kb.op("act", lambda e, s=xs: e.activation(out=sq[:], in_=s, func=AF.Square, accum_out=stats[:, 1:2]),
              reads=(kx,), writes=(pre + "sq2",))
        kb.op("dve", lambda e: e.tensor_scalar(out=stats[:, 2:3], in0=stats[:, 0:1], scalar1=1.0 / D, scalar2=None, op0=ALU.mult),
              reads=(pre + "ss",), writes=(pre + "mu",))
        kb.op("dve", lambda e: e.scalar_tensor_tensor(out=stats[:, 3:4], in0=stats[:, 2:3], scalar=-1.0, in1=stats[:, 2:3],
                                                      op0=ALU.mult, op1=ALU.mult),
              reads=(pre + "mu",), writes=(pre + "var",))
        kb.op("dve", lambda e: e.scalar_tensor_tensor(out=stats[:, 3:4], in0=stats[:, 1:2], scalar=1.0 / D, in1=stats[:, 3:4],
                                                      op0=ALU.mult, op1=ALU.add),
              reads=(pre + "sq2", pre + "var"), writes=(pre + "var",))
        kb.op("act", lambda e, b=epst: e.activation(out=stats[:, 4:5], in_=stats[:, 3:4], func=AF.Ln, bias=b[:, 0:1]),
              reads=(pre + "var", epsk), writes=(pre + "rs",))
        kb.op("act", lambda e: e.activation(out=stats[:, 4:5], in_=stats[:, 4:5], func=AF.Exp, scale=-0.5),
              reads=(pre + "rs",), writes=(pre + "rs",))
        kb.op("dve", lambda e, s=xs, o=xn_s: e.tensor_scalar(out=o, in0=s, scalar1=stats[:, 2:3], scalar2=stats[:, 4:5],
                                                             op0=ALU.subtract, op1=ALU.mult),
              reads=(kx, pre + "mu", pre + "rs"), writes=(pre + f"xn{pp}",))
        wk = pre + f"xnw{i}"
        kb.op("sync", lambda e, s=xn_s, i=i: e.dma_start(out=xn_out[i * 128:(i + 1) * 128, :], in_=s),
              reads=(pre + f"xn{pp}",), writes=(wk,), dma=True)
        if xn_keys is not None:
            xn_keys.append(wk)
    return out_keys


def phase_extract_stripe(nc, kb, pre, src, dst, rows, cols, dt, eng, by_rows):
    stg = kb.spool(128, cols if not by_rows else src.shape[1], dt)
    keys = []
    for i in range(rows // 128 if by_rows else 16):
        if by_rows:
            kb.op(eng, lambda e, i=i: e.dma_start(out=stg[:, 0:src.shape[1]],
                                                  in_=src[i * 128:i * 128 + 2176, :][bass.ds(_dyn2048(e), 128), :]),
                  writes=(pre + "st",), dma=True)
            wk = pre + f"w{i}"
            kb.op(eng, lambda e, i=i: e.dma_start(out=dst[i * 128:(i + 1) * 128, :], in_=stg[:, 0:src.shape[1]]),
                  reads=(pre + "st",), writes=(wk,), dma=True)
        else:
            kb.op(eng, lambda e, i=i: e.dma_start(out=stg[:, 0:cols],
                                                  in_=src[i * 128:(i + 1) * 128, bass.ds(_dyn2048(e), cols)]),
                  writes=(pre + "st",), dma=True)
            wk = pre + f"w{i}"
            kb.op(eng, lambda e, i=i: e.dma_start(out=dst[i * 128:(i + 1) * 128, :], in_=stg[:, 0:cols]),
                  reads=(pre + "st",), writes=(wk,), dma=True)
        keys.append(wk)
    return keys


def phase_featize(nc, kb, pre, tok_scr, feat_scr, ready, Ttok=T):
    stg = kb.spool(128, Ttok, BF16)
    keys = []
    for kc in range(16):
        kb.op("sync", lambda e, kc=kc: e.dma_start_transpose(out=stg[:], in_=tok_scr[:, kc * 128:(kc + 1) * 128]),
              reads=tuple(ready), writes=(pre + "fzs",), dma=True)
        wk = pre + f"fz{kc}"
        kb.op("sync", lambda e, kc=kc: e.dma_start(out=feat_scr[kc * 128:(kc + 1) * 128, :], in_=stg[:]),
              reads=(pre + "fzs",), writes=(wk,), dma=True)
        keys.append(wk)
    return keys


def phase_proj16(nc, kb, A):
    HT = T // 2
    xn_sb, AR_B = A["AR_A"], A["AR_B"]
    PB = A["PB"]
    qp_sb = kb.spool(128, 2052, BF16)
    halo_sb = kb.spool(128, 16 * 3, BF16)
    conv_sb = kb.spool(128, 2048, F32)
    qkv_sb = kb.spool(128, 2048, BF16)
    sq_sb = conv_sb.bitcast(BF16)[:, 0:2048]
    wc_sb = kb.spool(128, 16 * 4, F32)
    o2_sb = kb.spool(128, 2, BF16)
    wba_sb = kb.spool(128, 16 * 32, BF16)
    ARCf = A["AR_C"].bitcast(F32)
    zrow_sb = ARCf[0:16, 0:2 * HT]
    ln_sb = ARCf[32:34, 0:HT]
    t1_sb = ARCf[0:16, 4096:4096 + HT]
    t3_sb = ARCf[32:48, 4096:4096 + HT]
    t4_sb = ARCf[32:48, 6144:6144 + HT]
    gout_sb = ARCf[0:16, 6144:6144 + HT]
    gout2_sb = ARCf[32:48, 2048:4096]
    alogc = kb.spool(16, 1, F32)
    dtbc = kb.spool(16, 1, F32)
    nexp_sb = kb.spool(16, 1, F32, base=32)
    one16, one16k = kb.constcol(nc, 1.0, parts=16)
    eps2, eps2k = kb.constcol(nc, 1e-6, parts=2)
    psA, psB = PB[0], PB[1]
    psS = A["PS_ALL"][0:2, 1024:3072]
    PSSK = ("ps2", "ps3", "ps4", "ps5")
    psZ = PB[6][0:16, :]

    kb.op("gp", lambda e: e.dma_start(out=AR_B[:].rearrange("p (k m) -> p k m", k=16),
                                      in_=A["wqkvT"].rearrange("(k p) m -> p k m", p=128)),
          writes=("wq",) + tuple(A["arB_old"]), dma=True)
    kb.op("gp", lambda e: e.dma_start(out=wba_sb[:].rearrange("p (k m) -> p k m", k=16)[:, :, 0:16],
                                      in_=A["wbT"].rearrange("(k p) m -> p k m", p=128)),
          writes=("wbab",), dma=True)
    kb.op("gp", lambda e: e.dma_start(out=wba_sb[:].rearrange("p (k m) -> p k m", k=16)[:, :, 16:32],
                                      in_=A["waT"].rearrange("(k p) m -> p k m", p=128)),
          writes=("wbaa",), dma=True)
    kb.op("sync", lambda e: e.dma_start(out=wc_sb[:].rearrange("p (c f) -> p c f", c=16),
                                        in_=A["convw"].rearrange("(c p) f -> p c f", p=128)),
          writes=("wc",), dma=True)
    kb.op("sync", lambda e: e.dma_start(out=o2_sb[:], in_=A["ones2"][:, :]), writes=("o2",), dma=True)
    kb.op("sync", lambda e: e.dma_start(out=alogc[:], in_=A["alog"][:, :]), writes=("alogv",), dma=True)
    kb.op("sync", lambda e: e.dma_start(out=dtbc[:], in_=A["dtb"][:, :]), writes=("dtbc",), dma=True)
    kb.op("act", lambda e: e.activation(out=nexp_sb[:], in_=alogc[:], func=AF.Exp),
          reads=("alogv",), writes=("nexp",))
    kb.op("dve", lambda e: e.tensor_scalar(out=nexp_sb[:], in0=nexp_sb[:], scalar1=-1.0, scalar2=None, op0=ALU.mult),
          reads=("nexp",), writes=("nexp",))
    kb.op("dve", lambda e: e.memset(halo_sb[:], 0.0), writes=("halo",))
    qkv_keys, row_keys = [], []
    for half in range(2):
        toff = half * HT
        for kc in range(16):
            kb.op("sync", lambda e, kc=kc, t=toff: e.dma_start(
                out=xn_sb[:, kc * HT:(kc + 1) * HT], in_=A["xn0f"][kc * 128:(kc + 1) * 128, t:t + HT]),
                  reads=tuple(A["xn0f_ready"]), writes=(f"xn{kc}",) + tuple(A["arA_old"]), dma=True)
        XNK = tuple(f"xn{kc}" for kc in range(16))
        A["XNK"] = XNK
        for m in range(16):
            qp_chunk = qp_sb[:, 0:2051]
            kqp = "qp0"
            for ts_ in range(HT // 512):
                ps = psA if (ts_ % 2 == 0) else psB
                psk = "ps0" if (ts_ % 2 == 0) else "ps1"
                pairs = [(AR_B[:, k * 2048 + m * 128:(k * 2048 + (m + 1) * 128)],
                          xn_sb[:, k * HT + ts_ * 512: k * HT + (ts_ + 1) * 512])
                         for k in range(16)]
                mm_group(kb, ps[:], pairs, reads=("wq",) + XNK, writes=(psk,))
                kb.op("act", lambda e, o=qp_chunk[:, 3 + ts_ * 512: 3 + (ts_ + 1) * 512], s=ps[:]:
                      e.activation(out=o, in_=s, func=AF.Copy),
                      reads=(psk,), writes=(kqp,))
            kb.op("dve", lambda e, m=m, qp=qp_chunk: e.tensor_copy(qp[:, 0:3], halo_sb[:, m * 3:(m + 1) * 3]),
                  reads=("halo", kqp), writes=(kqp,))
            for tau in range(4):
                if tau == 0:
                    kb.op("dve", lambda e, m=m, qp=qp_chunk: e.tensor_scalar(
                        out=conv_sb[:], in0=qp[:, 0:2048], scalar1=wc_sb[:, m * 4:m * 4 + 1], scalar2=None, op0=ALU.mult),
                          reads=(kqp, "wc"), writes=("conv",))
                else:
                    kb.op("dve", lambda e, m=m, tau=tau, qp=qp_chunk: e.scalar_tensor_tensor(
                        out=conv_sb[:], in0=qp[:, tau:tau + 2048], scalar=wc_sb[:, m * 4 + tau:m * 4 + tau + 1],
                        in1=conv_sb[:], op0=ALU.mult, op1=ALU.add),
                          reads=(kqp, "wc", "conv"), writes=("conv",))
            if half == 0:
                kb.op("dve", lambda e, m=m, qp=qp_chunk: e.tensor_copy(halo_sb[:, m * 3:(m + 1) * 3], qp[:, 2048:2051]),
                      reads=(kqp,), writes=("halo",))
            kb.op("act", lambda e: e.activation(out=qkv_sb[:], in_=conv_sb[:], func=AF.Silu),
                  reads=("conv",), writes=("qkvv",))
            wk = f"qkvw{m}_{half}"
            kb.op("sync", lambda e, m=m, t=toff: e.dma_start(out=A["qkvf_scr"][m * 128:(m + 1) * 128, t:t + 2048], in_=qkv_sb[:]),
                  reads=("qkvv",), writes=(wk,), dma=True)
            qkv_keys.append(wk)
            if m < 8:
                kb.op("act", lambda e: e.activation(out=sq_sb[:], in_=qkv_sb[:], func=AF.Square),
                      reads=("qkvv",), writes=("sqv", "conv"))
                for ts_ in range(4):
                    kb.op("pe", lambda e, ts_=ts_: e.matmul(psS[:, ts_ * 512:(ts_ + 1) * 512], o2_sb[:],
                                                            sq_sb[:, ts_ * 512:(ts_ + 1) * 512], start=True, stop=True),
                          reads=("sqv", "o2"), writes=PSSK)
                kb.op("act", lambda e, b=eps2: e.activation(out=ln_sb[:], in_=psS[0:2, 0:2048], func=AF.Ln, bias=b[:, 0:1]),
                      reads=PSSK + (eps2k,), writes=("lnv",))
                if m < 4:
                    kb.op("dve", lambda e: e.tensor_scalar(out=ln_sb[:], in0=ln_sb[:], scalar1=-0.5,
                                                           scalar2=float(-0.5 * np.log(64.0)), op0=ALU.mult, op1=ALU.add),
                          reads=("lnv",), writes=("lnv",))
                    r0 = 56 + m * 2
                else:
                    kb.op("dve", lambda e: e.tensor_scalar(out=ln_sb[:], in0=ln_sb[:], scalar1=-0.5, scalar2=None, op0=ALU.mult),
                          reads=("lnv",), writes=("lnv",))
                    r0 = 48 + (m - 4) * 2
                wk2 = f"rw{r0}_{half}"
                kb.op("sync", lambda e, r0=r0, t=toff: e.dma_start(out=A["rows_scr"][r0:r0 + 2, t:t + 2048], in_=ln_sb[:]),
                      reads=("lnv",), writes=(wk2,), dma=True)
                row_keys.append(wk2)
        for which in range(2):
            zrow = zrow_sb[:, which * HT:(which + 1) * HT]
            for ts_ in range(4):
                pairs = [(wba_sb[:, k * 32 + which * 16: k * 32 + which * 16 + 16],
                          xn_sb[:, k * HT + ts_ * 512: k * HT + (ts_ + 1) * 512])
                         for k in range(16)]
                mm_group(kb, psZ[:], pairs, reads=("wbab", "wbaa") + XNK, writes=("ps6",))
                kb.op("dve", lambda e, o=zrow[:, ts_ * 512:(ts_ + 1) * 512]: e.tensor_copy(o, psZ[:]),
                      reads=("ps6",), writes=(f"z{which}",))
        zb = zrow_sb[:, 0:HT]
        za = zrow_sb[:, HT:2 * HT]
        kb.op("act", lambda e: e.activation(out=gout_sb[:], in_=zb, func=AF.Sigmoid),
              reads=("z0",), writes=("betav",))
        wk = f"rwb_{half}"
        kb.op("sync", lambda e, t=toff: e.dma_start(out=A["rows_scr"][16:32, t:t + 2048], in_=gout_sb[:]),
              reads=("betav",), writes=(wk,), dma=True)
        row_keys.append(wk)
        kb.op("act", lambda e: e.activation(out=t1_sb[:], in_=zb, func=AF.Abs),
              reads=("z0",), writes=("pt1",))
        kb.op("act", lambda e: e.activation(out=t1_sb[:], in_=t1_sb[:], func=AF.Exp, scale=-1.0),
              reads=("pt1",), writes=("pt1",))
        kb.op("act", lambda e, b=one16: e.activation(out=t3_sb[:], in_=t1_sb[:], func=AF.Ln, bias=b[:, 0:1]),
              reads=("pt1", one16k), writes=("pt3",))
        kb.op("act", lambda e: e.activation(out=t4_sb[:], in_=zb, func=AF.Relu, scale=-1.0),
              reads=("z0",), writes=("pt4",))
        kb.op("dve", lambda e: e.scalar_tensor_tensor(out=gout2_sb[:], in0=t4_sb[:], scalar=-1.0, in1=t3_sb[:],
                                                      op0=ALU.mult, op1=ALU.subtract),
              reads=("pt3", "pt4"), writes=("lnbv",))
        wk = f"rwl_{half}"
        kb.op("sync", lambda e, t=toff: e.dma_start(out=A["rows_scr"][32:48, t:t + 2048], in_=gout2_sb[:]),
              reads=("lnbv",), writes=(wk,), dma=True)
        row_keys.append(wk)
        kb.op("act", lambda e: e.activation(out=t1_sb[:], in_=za, func=AF.Abs, bias=dtbc[:, 0:1]),
              reads=("z1", "dtbc"), writes=("pt1",))
        kb.op("act", lambda e: e.activation(out=t1_sb[:], in_=t1_sb[:], func=AF.Exp, scale=-1.0),
              reads=("pt1",), writes=("pt1",))
        kb.op("act", lambda e, b=one16: e.activation(out=t3_sb[:], in_=t1_sb[:], func=AF.Ln, bias=b[:, 0:1]),
              reads=("pt1", one16k), writes=("pt3",))
        kb.op("act", lambda e: e.activation(out=t4_sb[:], in_=za, func=AF.Relu, bias=dtbc[:, 0:1]),
              reads=("z1", "dtbc"), writes=("pt4",))
        kb.op("dve", lambda e: e.tensor_tensor(out=t4_sb[:], in0=t4_sb[:], in1=t3_sb[:], op=ALU.add),
              reads=("pt3", "pt4"), writes=("pt4",))
        kb.op("dve", lambda e: e.tensor_scalar(out=gout2_sb[:], in0=t4_sb[:], scalar1=nexp_sb[:, 0:1], scalar2=None, op0=ALU.mult),
              reads=("pt4", "nexp"), writes=("gv", "lnbv"))
        wk = f"rwg_{half}"
        kb.op("sync", lambda e, t=toff: e.dma_start(out=A["rows_scr"][0:16, t:t + 2048], in_=gout2_sb[:]),
              reads=("gv",), writes=(wk,), dma=True)
        row_keys.append(wk)
    A["qkv_keys"] = tuple(qkv_keys)
    A["row_keys"] = tuple(row_keys)
    A["arA_old"] = tuple(f"xn{kc}" for kc in range(16))
    A["arB_old"] = ("wq",)


def phase_scan16(nc, kb, A):
    C = 128
    PB = A["PB"]
    qkvf = A["qkvf_scr"]
    rows = A["rows_scr"]
    QR = A["qkv_keys"]
    RR = A["row_keys"]
    uinc_sb = kb.spool(128, 128, F32)
    i128f_sb = kb.spool(128, 128, F32)
    i128b_sb = kb.spool(128, 128, BF16)
    ones1_sb = kb.spool(1, 128, F32)
    for nm, dst, src in (("uincv", uinc_sb, A["UINC"]), ("i128fv", i128f_sb, A["I128F"]),
                         ("i128bv", i128b_sb, A["I128B"]), ("ones1v", ones1_sb, A["ONES1"])):
        kb.op("sync", lambda e, d=dst, s=src: e.dma_start(out=d[:], in_=s[:, :]), writes=(nm,), dma=True)
    S_f = kb.spool(64, 16 * 64, F32)
    S_b = kb.spool(128, 16 * 64, BF16)
    kb.op("dve", lambda e: e.memset(S_f[:], 0.0), writes=("S_f",))
    kb.op("dve", lambda e: e.memset(S_b[:], 0.0), writes=("S_b",))
    NB = 2
    kf_sb = kb.spool(128, NB * 4 * 128, BF16)
    qf_sb = kb.spool(128, NB * 4 * 128, BF16)
    kt_sb = kb.spool(128, NB * 512, BF16)
    vt_sb = kb.spool(128, NB * 1024, BF16)
    cols_sb = kb.spool(128, NB * 64, F32)
    cum_sb = kb.spool(128, 16, F32)
    biasA_sb = kb.spool(128, 16, F32)
    rowAc_sb = kb.spool(128, 16, F32)
    biasP_sb = kb.spool(128, 16, F32)
    rows_sb = kb.spool(1, 4 * 128, F32)
    ccol_sb = kb.spool(128, 4, F32)
    arg_sb = kb.spool(128, 128, F32)
    dmat_sb = kb.spool(128, 128, F32)
    araw_sb = kb.spool(128, 128, F32)
    kq_sb = kb.spool(128, 128, F32)
    B_sb = kb.spool(128, 128, BF16)
    Useq = [kb.spool(128, 128, BF16) for i in range(2)]
    Lseq = [kb.spool(128, 128, BF16) for i in range(2)]
    Pp_sb = kb.spool(128, 128, BF16)
    V_sb = [kb.spool(128, 64, BF16) for i in range(2)]
    R_sb = kb.spool(128, 64, BF16)
    tmp_sb = kb.spool(128, 64, F32)
    tmp2_sb = kb.spool(128, 64, BF16)
    kw_sb = kb.spool(128, 64, BF16)
    lamc_sb = kb.spool(64, 1, F32)
    o_sb = kb.spool(128, 2 * 1024, BF16)
    rload_sb = kb.spool(64, 128, F32)
    o_keys = []

    for c in range(T // C):
        t0 = c * C
        bb = c % NB
        kfs = kf_sb[:, bb * 512:(bb + 1) * 512]
        qfs = qf_sb[:, bb * 512:(bb + 1) * 512]
        kts = kt_sb[:, bb * 512:(bb + 1) * 512]
        vts = vt_sb[:, bb * 1024:(bb + 1) * 1024]
        cls = cols_sb[:, bb * 64:(bb + 1) * 64]
        osb = o_sb[:, bb * 1024:(bb + 1) * 1024]
        kL = {"kf": f"skf{bb}", "qf": f"sqf{bb}", "kt": f"skt{bb}", "vt": f"svt{bb}", "cl": f"scl{bb}"}
        for jj_ in range(4):
            kb.op("sync", lambda e, o=kfs[:, jj_ * 128:(jj_ + 1) * 128], s=qkvf[512 + jj_ * 128:512 + (jj_ + 1) * 128, t0:t0 + C]:
                  e.dma_start(out=o, in_=s), reads=QR, writes=(kL["kf"],), dma=True)
            kb.op("sync", lambda e, o=qfs[:, jj_ * 128:(jj_ + 1) * 128], s=qkvf[jj_ * 128:(jj_ + 1) * 128, t0:t0 + C]:
                  e.dma_start(out=o, in_=s), reads=QR, writes=(kL["qf"],), dma=True)
        kb.op("sync", lambda e, o=kts, t0=t0: e.dma_start_transpose(out=o, in_=qkvf[512:1024, t0:t0 + C]),
              reads=QR, writes=(kL["kt"],), dma=True)
        kb.op("sync", lambda e, o=vts, t0=t0: e.dma_start_transpose(out=o, in_=qkvf[1024:2048, t0:t0 + C]),
              reads=QR, writes=(kL["vt"],), dma=True)
        kb.op("sync", lambda e, t0=t0: e.dma_start(out=rload_sb[:], in_=rows[:, t0:t0 + C]),
              reads=RR, writes=("rload",), dma=True)
        kb.op("pe", lambda e: e.matmul(PB[2][0:128, 0:64], rload_sb[:], i128f_sb[0:64, 0:64], start=True, stop=True),
              reads=("rload", "i128fv"), writes=("ps2",))
        kb.op("dve", lambda e, o=cls: e.tensor_copy(o, PB[2][0:128, 0:64]),
              reads=("ps2",), writes=(kL["cl"],))
        CLK = (kL["cl"],)
        kb.op("pe", lambda e, r=cls[:, 0:16]: e.matmul(PB[2][0:128, 0:16], uinc_sb[:], r, start=True, stop=True),
              reads=("uincv",) + CLK, writes=("ps2",))
        kb.op("dve", lambda e: e.tensor_copy(cum_sb[:], PB[2][0:128, 0:16]), reads=("ps2",), writes=("cum",))
        kb.op("dve", lambda e, cl=cls: e.tensor_tensor(out=biasA_sb[:], in0=cum_sb[:], in1=cl[:, 32:48], op=ALU.add),
              reads=("cum",) + CLK, writes=("biasA",))
        kb.op("dve", lambda e, cl=cls: e.tensor_tensor(
            out=biasA_sb[:].rearrange("p (a b) -> p a b", b=2), in0=biasA_sb[:].rearrange("p (a b) -> p a b", b=2),
            in1=cl[:, 48:56].unsqueeze(2).broadcast_to((128, 8, 2)), op=ALU.add),
              reads=("biasA",) + CLK, writes=("biasA",))
        kb.op("dve", lambda e, cl=cls: e.scalar_tensor_tensor(
            out=rowAc_sb[:].rearrange("p (a b) -> p a b", b=2), in0=cum_sb[:].rearrange("p (a b) -> p a b", b=2),
            scalar=-1.0, in1=cl[:, 48:56].unsqueeze(2).broadcast_to((128, 8, 2)), op0=ALU.mult, op1=ALU.add),
              reads=("cum",) + CLK, writes=("rowAc",))
        kb.op("dve", lambda e, cl=cls: e.scalar_tensor_tensor(
            out=biasP_sb[:].rearrange("p (a b) -> p a b", b=2), in0=cum_sb[:].rearrange("p (a b) -> p a b", b=2),
            scalar=1.0, in1=cl[:, 56:64].unsqueeze(2).broadcast_to((128, 8, 2)), op0=ALU.mult, op1=ALU.add),
              reads=("cum",) + CLK, writes=("biasP",))
        for h in range(16):
            khl = h // 2
            jj = khl // 2
            pb = 64 * (khl % 2)
            kft = kfs[pb:pb + 64, jj * 128:(jj + 1) * 128]
            qft = qfs[pb:pb + 64, jj * 128:(jj + 1) * 128]
            if h % 2 == 0:
                kb.op("pe", lambda e, l=kft: e.matmul(PB[0][:, 0:128], l, l, start=True, stop=True),
                      reads=(kL["kf"],), writes=("ps0",))
                kb.op("pe", lambda e, l=kft, r=qft: e.matmul(PB[1][:, 0:128], l, r, start=True, stop=True),
                      reads=(kL["kf"], kL["qf"]), writes=("ps1",))
                kb.op("dve", lambda e: e.tensor_copy(araw_sb[:], PB[0][:, 0:128]), reads=("ps0",), writes=("araw",))
                kb.op("dve", lambda e: e.tensor_copy(kq_sb[:], PB[1][:, 0:128]), reads=("ps1",), writes=("kq",))
            for ri, (colt, colk) in enumerate(((rowAc_sb, "rowAc"), (biasA_sb, "biasA"),
                                               (biasP_sb, "biasP"), (cum_sb, "cum"))):
                kb.op("pe", lambda e, ct=colt, h=h: e.matmul(PB[2][0:1, 0:128], ct[:, h:h + 1], i128f_sb[:],
                                                             start=True, stop=True),
                      reads=(colk, "i128fv"), writes=("ps2",))
                kb.op("dve", lambda e, ri=ri: e.tensor_copy(rows_sb[:, ri * 128:(ri + 1) * 128], PB[2][0:1, 0:128]),
                      reads=("ps2",), writes=(f"rows{ri}",))
            ROWK = ("rows0", "rows1", "rows2", "rows3")
            kb.op("pe", lambda e: e.matmul(PB[2][0:128, 0:1], ones1_sb[:], rows_sb[:, 3 * 128 + 127: 3 * 128 + 128],
                                           start=True, stop=True),
                  reads=("rows3", "ones1v"), writes=("ps2",))
            kb.op("dve", lambda e: e.tensor_copy(ccol_sb[:, 0:1], PB[2][0:128, 0:1]), reads=("ps2",), writes=("cC",))
            kb.op("act", lambda e, h=h: e.activation(out=ccol_sb[:, 1:2], in_=biasA_sb[:, h:h + 1], func=AF.Exp),
                  reads=("biasA",), writes=("rho",))
            kb.op("act", lambda e, h=h: e.activation(out=ccol_sb[:, 2:3], in_=biasP_sb[:, h:h + 1], func=AF.Exp),
                  reads=("biasP",), writes=("rhoP",))
            kb.op("dve", lambda e, h=h: e.tensor_tensor(out=ccol_sb[:, 3:4], in0=rowAc_sb[:, h:h + 1],
                                                        in1=ccol_sb[:, 0:1], op=ALU.add),
                  reads=("rowAc", "cC"), writes=("wcol",))
            kb.op("act", lambda e: e.activation(out=ccol_sb[:, 3:4], in_=ccol_sb[:, 3:4], func=AF.Exp),
                  reads=("wcol",), writes=("wcol",))
            specs = (
                (0, biasA_sb, "biasA", 0, "B"),
                (1, rowAc_sb, "rowAc", 1, "U"),
                (2, rowAc_sb, "rowAc", 2, "Pp"),
            )
            for (rsl, bcol, colk, mk, what) in specs:
                kb.op("pe", lambda e, rsl=rsl: e.matmul(PB[2][0:128, 0:128], ones1_sb[:],
                                                        rows_sb[:, rsl * 128:(rsl + 1) * 128], start=True, stop=True),
                      reads=(ROWK[rsl], "ones1v"), writes=("ps2",))
                kb.op("dve", lambda e, bc=bcol, h=h: e.tensor_scalar(
                    out=arg_sb[:], in0=PB[2][0:128, 0:128], scalar1=bc[:, h:h + 1], scalar2=30.0,
                    op0=ALU.add, op1=ALU.min),
                      reads=("ps2", colk), writes=("argv",))
                if mk == 0:
                    pat, cm, op = [[-1, 128]], 1, ALU.is_gt
                elif mk == 1:
                    pat, cm, op = [[1, 128]], -1, ALU.is_gt
                else:
                    pat, cm, op = [[1, 128]], -1, ALU.is_ge
                def _mask(e, pat=pat, cm=cm, op=op):
                    if not hasattr(e, "_fillreg"):
                        e._fillreg = e.to_reg(-200.0)
                    return e.affine_select(out=dmat_sb[:], in_=arg_sb[:], pattern=pat, compare_op=op,
                                           fill=e._fillreg, base=0, channel_multiplier=cm)
                kb.op("gp", _mask, reads=("argv",), writes=("dmatv",))
                kb.op("act", lambda e: e.activation(out=dmat_sb[:], in_=dmat_sb[:], func=AF.Exp),
                      reads=("dmatv",), writes=("dmatv",))
                if what == "B":
                    kb.op("dve", lambda e: e.scalar_tensor_tensor(out=B_sb[:], in0=dmat_sb[:], scalar=-1.0,
                                                                  in1=araw_sb[:], op0=ALU.mult, op1=ALU.mult),
                          reads=("dmatv", "araw"), writes=("Bm",))
                elif what == "U":
                    kb.op("dve", lambda e: e.scalar_tensor_tensor(out=Useq[0][:], in0=dmat_sb[:], scalar=-1.0,
                                                                  in1=araw_sb[:], op0=ALU.mult, op1=ALU.mult),
                          reads=("dmatv", "araw"), writes=("U0",))
                else:
                    kb.op("dve", lambda e: e.scalar_tensor_tensor(out=Pp_sb[:], in0=dmat_sb[:], scalar=1.0,
                                                                  in1=kq_sb[:], op0=ALU.mult, op1=ALU.mult),
                          reads=("dmatv", "kq"), writes=("Ppv",))
            kb.op("pe", lambda e, l=kft, pb=pb, h=h: e.matmul(PB[3][:, 0:64], l, S_b[pb:pb + 64, h * 64:(h + 1) * 64],
                                                              start=True, stop=True),
                  reads=(kL["kf"], "S_b"), writes=("ps3",))
            kb.op("dve", lambda e: e.tensor_scalar(out=tmp_sb[:], in0=PB[3][:, 0:64], scalar1=ccol_sb[:, 1:2],
                                                   scalar2=None, op0=ALU.mult),
                  reads=("ps3", "rho"), writes=("tmpv",))
            kb.op("dve", lambda e, h=h, v=vts, cl=cls: e.scalar_tensor_tensor(
                out=R_sb[:], in0=v[:, h * 64:(h + 1) * 64], scalar=cl[:, 16 + h:17 + h], in1=tmp_sb[:],
                op0=ALU.mult, op1=ALU.subtract),
                  reads=(kL["vt"],) + CLK + ("tmpv",), writes=("V0v",))
            Vcur, Vk = R_sb, "V0v"
            Ucur, Uk = Useq[0], "U0"
            Lcur, Lk = B_sb, "Bm"
            for s in range(7):
                Vnext = V_sb[s % 2]
                mm_group(kb, PB[3][:, 0:64], [(Ucur[:], Vcur[:]), (i128b_sb[:], Vcur[:])],
                         reads=(Uk, Vk, "i128bv"), writes=("ps3",))
                kb.op("act", lambda e, o=Vnext: e.activation(out=o[:], in_=PB[3][:, 0:64], func=AF.Copy),
                      reads=("ps3",), writes=(f"V{s + 1}v",))
                if s < 6:
                    Unext = Useq[(s + 1) % 2]
                    kb.op("pe", lambda e, l=Lcur, r=Ucur: e.matmul(PB[4][:, 0:128], l[:], r[:], start=True, stop=True),
                          reads=(Lk, Uk), writes=("ps4",))
                    if s < 5:
                        Lnext = Lseq[(s + 1) % 2]
                        kb.op("pe", lambda e, l=Ucur, r=Lcur: e.matmul(PB[5][:, 0:128], l[:], r[:], start=True, stop=True),
                              reads=(Uk, Lk), writes=("ps5",))
                        kb.op("dve", lambda e, o=Lnext: e.tensor_copy(o[:], PB[5][:, 0:128]),
                              reads=("ps5",), writes=(f"L{s + 1}m",))
                        Lcur, Lk = Lnext, f"L{s + 1}m"
                    kb.op("act", lambda e, o=Unext: e.activation(out=o[:], in_=PB[4][:, 0:128], func=AF.Copy),
                          reads=("ps4",), writes=(f"U{s + 1}m",))
                    Ucur, Uk = Unext, f"U{s + 1}m"
                Vcur, Vk = Vnext, f"V{s + 1}v"
            kb.op("pe", lambda e, l=qft, pb=pb, h=h: e.matmul(PB[7][:, 0:64], l, S_b[pb:pb + 64, h * 64:(h + 1) * 64],
                                                              start=True, stop=True),
                  reads=(kL["qf"], "S_b"), writes=("ps7",))
            kb.op("dve", lambda e: e.tensor_scalar(out=tmp2_sb[:], in0=PB[7][:, 0:64], scalar1=ccol_sb[:, 2:3],
                                                   scalar2=None, op0=ALU.mult),
                  reads=("ps7", "rhoP"), writes=("tmp2v",))
            mm_group(kb, PB[6][:, 0:64], [(Pp_sb[:], Vcur[:]), (i128b_sb[:], tmp2_sb[:])],
                     reads=("Ppv", Vk, "tmp2v", "i128bv"), writes=("ps6",))
            kb.op("act", lambda e, h=h, o=osb: e.activation(out=o[:, h * 64:(h + 1) * 64], in_=PB[6][:, 0:64], func=AF.Copy),
                  reads=("ps6",), writes=(f"osl{h}_{bb}",))
            kb.op("dve", lambda e, h=h, kt=kts, khl=khl: e.tensor_scalar(out=kw_sb[:], in0=kt[:, khl * 64:(khl + 1) * 64],
                                                                         scalar1=ccol_sb[:, 3:4], scalar2=None, op0=ALU.mult),
                  reads=(kL["kt"], "wcol"), writes=("kwv",))
            kb.op("pe", lambda e: e.matmul(PB[7][0:64, 64:128], kw_sb[:], Vcur[:], start=True, stop=True),
                  reads=("kwv", Vk), writes=("ps7",))
            kb.op("act", lambda e: e.activation(out=lamc_sb[:], in_=ccol_sb[0:64, 0:1], func=AF.Exp),
                  reads=("cC",), writes=("lamC",))
            kb.op("dve", lambda e, h=h: e.scalar_tensor_tensor(
                out=S_f[:, h * 64:(h + 1) * 64], in0=S_f[:, h * 64:(h + 1) * 64], scalar=lamc_sb[:, 0:1],
                in1=PB[7][0:64, 64:128], op0=ALU.mult, op1=ALU.add),
                  reads=("S_f", "lamC", "ps7"), writes=("S_f",))
            kb.op("act", lambda e, h=h: e.activation(out=S_b[0:64, h * 64:(h + 1) * 64], in_=S_f[:, h * 64:(h + 1) * 64],
                                                     func=AF.Copy),
                  reads=("S_f",), writes=("S_b",))
            kb.op("act", lambda e, h=h: e.activation(out=S_b[64:128, h * 64:(h + 1) * 64], in_=S_f[:, h * 64:(h + 1) * 64],
                                                     func=AF.Copy),
                  reads=("S_f",), writes=("S_b",))
        wk = f"ow{c}"
        kb.op("sync", lambda e, t0=t0, o=osb: e.dma_start(out=A["o_scr"][t0:t0 + C, :], in_=o),
              reads=tuple(f"osl{h}_{bb}" for h in range(16)), writes=(wk,), dma=True)
        o_keys.append(wk)
    A["o_keys"] = tuple(o_keys)


def phase_epi16(nc, kb, A):
    PB = A["PB"]
    AR_A, AR_B, AR_C = A["AR_A"], A["AR_B"], A["AR_C"]
    zs_sb = kb.spool(128, 2 * 1024, BF16)
    ot_sb = kb.spool(128, 2 * 1024, BF16)
    ssq_sb = kb.spool(128, 16, F32)
    rstd_sb = kb.spool(128, 16, F32)
    ynt_sb = kb.spool(128, 2 * 1024, BF16)
    osq_sb = kb.spool(128, 1024, BF16)
    out_sb = kb.spool(128, 2 * 512, BF16)
    nw_sb = kb.spool(128, 1, F32)
    eps_c, eps_k = kb.constcol(nc, 1e-6)
    wz = AR_B[:, 0:16 * 1024]
    wo = AR_C[:, 0:8 * 2048]
    kb.op("gp", lambda e: e.dma_start(out=wz.rearrange("p (k m) -> p k m", k=16),
                                      in_=A["wzT"].rearrange("(k p) m -> p k m", p=128)),
          writes=("wzv",) + tuple(A["arB_old"]), dma=True)
    kb.op("gp", lambda e: e.dma_start(out=wo.rearrange("p (k m) -> p k m", k=8),
                                      in_=A["woutT"].rearrange("(k p) m -> p k m", p=128)),
          writes=("wov",) + tuple(A["arC_old"]), dma=True)
    kb.op("sync", lambda e: e.dma_start(out=nw_sb[:], in_=A["nw128"][:, :]), writes=("nwv",), dma=True)
    for ks in range(8):
        kb.op("dve", lambda e, ks=ks: e.tensor_scalar(out=wo[:, ks * 2048:(ks + 1) * 2048],
                                                      in0=wo[:, ks * 2048:(ks + 1) * 2048],
                                                      scalar1=nw_sb[:, 0:1], scalar2=None, op0=ALU.mult),
              reads=("wov", "nwv"), writes=("wov",))
    yn_keys = []
    for half in range(2):
        toff = half * 2048
        for kc in range(16):
            kb.op("sync", lambda e, kc=kc, t=toff: e.dma_start(
                out=AR_A[:, kc * 2048:(kc + 1) * 2048], in_=A["xn0f"][kc * 128:(kc + 1) * 128, t:t + 2048]),
                  reads=tuple(A["xn0f_ready"]), writes=(f"exn{kc}",) + tuple(A["arA_old"]), dma=True)
        A["arA_old"] = tuple(f"exn{kc}" for kc in range(16))
        EXNK = A["arA_old"]
        for i in range(16):
            gi = half * 16 + i
            pp = i % 2
            zt = zs_sb[:, pp * 1024:(pp + 1) * 1024]
            ot = ot_sb[:, pp * 1024:(pp + 1) * 1024]
            ynt = ynt_sb[:, pp * 1024:(pp + 1) * 1024]
            ko, kz, ky = f"eot{pp}", f"ezs{pp}", f"eyn{pp}"
            kb.op("sync", lambda e, gi=gi, o=ot: e.dma_start(out=o, in_=A["o_scr"][gi * 128:(gi + 1) * 128, :]),
                  reads=A["o_keys"], writes=(ko,), dma=True)
            for zs_ in range(2):
                ps = PB[0] if zs_ == 0 else PB[1]
                psk = "ps0" if zs_ == 0 else "ps1"
                pairs = [(AR_A[:, k * 2048 + i * 128: k * 2048 + (i + 1) * 128],
                          wz[:, k * 1024 + zs_ * 512: k * 1024 + (zs_ + 1) * 512]) for k in range(16)]
                mm_group(kb, ps[:], pairs, reads=("wzv",) + EXNK, writes=(psk,))
                kb.op("act", lambda e, o=zt[:, zs_ * 512:(zs_ + 1) * 512], s=ps[:]: e.activation(out=o, in_=s, func=AF.Silu),
                      reads=(psk,), writes=(kz,))
            kb.op("dve", lambda e, o=ot: e.scalar_tensor_tensor(out=osq_sb[:], in0=o, scalar=1.0, in1=o,
                                                                op0=ALU.mult, op1=ALU.mult),
                  reads=(ko,), writes=("eosq",))
            kb.op("dve", lambda e: e.tensor_reduce(out=ssq_sb[:], in_=osq_sb[:].rearrange("p (h d) -> p h d", d=64),
                                                   axis=AX.X, op=ALU.add),
                  reads=("eosq",), writes=("essq",))
            kb.op("act", lambda e, b=eps_c: e.activation(out=rstd_sb[:], in_=ssq_sb[:], func=AF.Ln, bias=b[:, 0:1], scale=1.0 / 64),
                  reads=("essq", eps_k), writes=("erstd",))
            kb.op("act", lambda e: e.activation(out=rstd_sb[:], in_=rstd_sb[:], func=AF.Exp, scale=-0.5),
                  reads=("erstd",), writes=("erstd",))
            kb.op("dve", lambda e, o=ot, y=ynt: e.tensor_tensor(
                out=y.rearrange("p (h d) -> p h d", d=64), in0=o.rearrange("p (h d) -> p h d", d=64),
                in1=rstd_sb[:].unsqueeze(2).broadcast_to((128, 16, 64)), op=ALU.mult),
                  reads=(ko, "erstd"), writes=(ky,))
            kb.op("dve", lambda e, y=ynt, z=zt: e.tensor_tensor(out=y, in0=y, in1=z, op=ALU.mult),
                  reads=(ky, kz), writes=(ky,))
            wk = f"eynw{gi}"
            kb.op("sync", lambda e, gi=gi, y=ynt: e.dma_start(out=A["yn_scr"][gi * 128:(gi + 1) * 128, :], in_=y),
                  reads=(ky,), writes=(wk,), dma=True)
            yn_keys.append(wk)
    for cc in range(8):
        kb.op("sync", lambda e, cc=cc: e.dma_start_transpose(out=AR_A[:, cc * 4096:(cc + 1) * 4096],
                                                             in_=A["yn_scr"][:, cc * 128:(cc + 1) * 128]),
              reads=tuple(yn_keys), writes=(f"eynf{cc}",) + tuple(A["arA_old"]), dma=True)
    A["arA_old"] = tuple(f"eynf{cc}" for cc in range(8))
    YNFK = A["arA_old"]
    a0_keys = []
    for m in range(16):
        for ts_ in range(8):
            ps = PB[0] if ts_ % 2 == 0 else PB[1]
            psk = "ps0" if ts_ % 2 == 0 else "ps1"
            pairs = [(wo[:, k * 2048 + m * 128:(k * 2048 + (m + 1) * 128)],
                      AR_A[:, k * 4096 + ts_ * 512: k * 4096 + (ts_ + 1) * 512]) for k in range(8)]
            mm_group(kb, ps[:], pairs, reads=("wov",) + YNFK, writes=(psk,))
            kb.op("act", lambda e, s=ps[:], o=out_sb[:, (ts_ % 2) * 512:(ts_ % 2 + 1) * 512]:
                  e.activation(out=o, in_=s, func=AF.Copy),
                  reads=(psk,), writes=(f"eob{ts_ % 2}",))
            wk = f"ea0w{m}_{ts_}"
            kb.op("sync", lambda e, m=m, ts_=ts_, s=out_sb[:, (ts_ % 2) * 512:(ts_ % 2 + 1) * 512]:
                  e.dma_start(out=A["attn0f"][m * 128:(m + 1) * 128, ts_ * 512:(ts_ + 1) * 512], in_=s),
                  reads=(f"eob{ts_ % 2}",), writes=(wk,), dma=True)
            a0_keys.append(wk)
    A["a0_keys"] = tuple(a0_keys)
    A["arC_old"] = ("wov",)


def phase_ffnP(nc, kb, pre, A, src_feat, src_ready, out_feat, out_col0, dyn, cfg, y_scr):
    NT = 2048
    PB = A["PB"]
    hu = A["AR_A"]
    hh = A["AR_B"]
    wsl = A["AR_C"]
    if A.get("xns_sb") is None:
        A["xns_sb"] = kb.spool(128, 16 * 512, BF16)
    xns_sb = A["xns_sb"]
    if A.get("ffn_smalls") is None:
        A["ffn_smalls"] = dict(
            h128_sb=kb.spool(128, 128, BF16),
            sc_sb=kb.spool(128, 224, F32),
            sg_sb=kb.spool(128, 512, F32),
            yt_sb=kb.spool(128, 2 * 512, BF16),
            out_sb=kb.spool(128, 2 * 512, BF16),
        )
    S = A["ffn_smalls"]
    h128_sb, sc_sb, sg_sb, yt_sb, out_sb = S["h128_sb"], S["sc_sb"], S["sg_sb"], S["yt_sb"], S["out_sb"]
    SQH = float(2.0 ** -0.5)
    ld = cfg["loaded"]
    if not ld.get(cfg["tag"]):
        kb.op("sync", lambda e: e.dma_start(out=hh[:].rearrange("p (k m) -> p k m", k=16),
                                            in_=cfg["h2048"].rearrange("(k p) m -> p k m", p=128)),
              writes=tuple(f"H{k}" for k in range(16)) + tuple(A["arB_old"]), dma=True)
        kb.op("sync", lambda e: e.dma_start(out=h128_sb[:], in_=cfg["h128"][:, :]), writes=("h128v",), dma=True)
        kb.op("sync", lambda e: e.dma_start(out=sc_sb[:, 0:16], in_=cfg["suug"][:, :]), writes=("suugv",), dma=True)
        kb.op("sync", lambda e: e.dma_start(out=sc_sb[:, 16:80], in_=cfg["sudp"][:, :]), writes=("sudpv",), dma=True)
        kb.op("sync", lambda e: e.dma_start(out=sc_sb[:, 80:144], in_=cfg["wsug"][:, 0:64]), writes=("wsuv",), dma=True)
        kb.op("sync", lambda e: e.dma_start(out=sc_sb[:, 144:208], in_=cfg["wsug"][:, 64:128]), writes=("wsgv",), dma=True)
        kb.op("sync", lambda e: e.dma_start(out=sc_sb[:, 208:224], in_=cfg["wsdp"][:, :]), writes=("wsdv",), dma=True)
        kb.op("dve", lambda e: e.tensor_tensor(out=sc_sb[:, 80:144], in0=sc_sb[:, 80:144], in1=sc_sb[:, 16:80], op=ALU.mult),
              reads=("wsuv", "sudpv"), writes=("wsuv",))
        kb.op("dve", lambda e: e.tensor_scalar(out=sc_sb[:, 80:144], in0=sc_sb[:, 80:144], scalar1=SQH, scalar2=None, op0=ALU.mult),
              reads=("wsuv",), writes=("wsuv",))
        kb.op("dve", lambda e: e.tensor_scalar(out=sc_sb[:, 144:208], in0=sc_sb[:, 144:208], scalar1=SQH, scalar2=None, op0=ALU.mult),
              reads=("wsgv",), writes=("wsgv",))
        kb.op("dve", lambda e: e.tensor_scalar(out=sc_sb[:, 208:224], in0=sc_sb[:, 208:224], scalar1=SQH, scalar2=None, op0=ALU.mult),
              reads=("wsdv",), writes=("wsdv",))
        ld[cfg["tag"]] = True
        A["arB_old"] = tuple(f"H{k}" for k in range(16))
    HCK = tuple(f"H{k}" for k in range(16))
    for ts_ in range(4):
        for k in range(16):
            kxn = f"xnsl{k}"
            if dyn:
                kb.op("sync", lambda e, k=k, ts_=ts_: e.dma_start(
                    out=xns_sb[:, k * 512:(k + 1) * 512],
                    in_=src_feat[k * 128:(k + 1) * 128, ts_ * 512:ts_ * 512 + 2560][:, bass.ds(_dyn2048(e), 512)]),
                      reads=tuple(src_ready), writes=(kxn,), dma=True)
            else:
                kb.op("sync", lambda e, k=k, ts_=ts_: e.dma_start(
                    out=xns_sb[:, k * 512:(k + 1) * 512],
                    in_=src_feat[k * 128:(k + 1) * 128, out_col0 + ts_ * 512:out_col0 + (ts_ + 1) * 512]),
                      reads=tuple(src_ready), writes=(kxn,), dma=True)
            kb.op("dve", lambda e, k=k: e.tensor_scalar(out=xns_sb[:, k * 512:(k + 1) * 512],
                                                        in0=xns_sb[:, k * 512:(k + 1) * 512],
                                                        scalar1=sc_sb[:, k:k + 1], scalar2=None, op0=ALU.mult),
                  reads=(kxn, "suugv"), writes=(kxn,))
        for m in range(16):
            ps = PB[0] if m % 2 == 0 else PB[1]
            psk = "ps0" if m % 2 == 0 else "ps1"
            pairs = [(hh[:, k * 2048 + m * 128:(k * 2048 + (m + 1) * 128)],
                      xns_sb[:, k * 512:(k + 1) * 512]) for k in range(16)]
            mm_group(kb, ps[:], pairs, reads=tuple(f"xnsl{k}" for k in range(16)) + HCK, writes=(psk,))
            kb.op("act", lambda e, m=m, ts_=ts_, s=ps[:]: e.activation(
                out=hu[:, m * NT + ts_ * 512: m * NT + (ts_ + 1) * 512], in_=s, func=AF.Copy),
                  reads=(psk,), writes=(pre + f"h{m}",) + (tuple(A["arA_old"]) if ts_ == 0 else ()))
    HK = tuple(pre + f"h{m}" for m in range(16))
    A["arA_old"] = HK
    for io in range(64):
        wpp = io % 2
        wu = wsl[:, wpp * 2 * 2048: wpp * 2 * 2048 + 2048]
        wg = wsl[:, wpp * 2 * 2048 + 2048: (wpp + 1) * 2 * 2048]
        kwu, kwg = pre + f"wu{wpp}", pre + f"wg{wpp}"
        wold = tuple(A["arC_old"]) if io < 2 else ()
        kb.op("gp", lambda e, io=io, o=wu: e.dma_start(
            out=o.rearrange("p (k m) -> p k m", k=16),
            in_=cfg["wupT"].rearrange("(k p) m -> p k m", p=128)[:, :, io * 128:(io + 1) * 128]),
              writes=(kwu,) + wold, dma=True)
        kb.op("gp", lambda e, io=io, o=wg: e.dma_start(
            out=o.rearrange("p (k m) -> p k m", k=16),
            in_=cfg["wgT"].rearrange("(k p) m -> p k m", p=128)[:, :, io * 128:(io + 1) * 128]),
              writes=(kwg,) + wold, dma=True)
        for ts_ in range(4):
            ypp = ts_ % 2
            yt = yt_sb[:, ypp * 512:(ypp + 1) * 512]
            kyt = f"fyt{ypp}"
            pairs_u = [(wu[:, k * 128:(k + 1) * 128], hu[:, k * NT + ts_ * 512: k * NT + (ts_ + 1) * 512])
                       for k in range(16)]
            pairs_g = [(wg[:, k * 128:(k + 1) * 128], hu[:, k * NT + ts_ * 512: k * NT + (ts_ + 1) * 512])
                       for k in range(16)]
            mm_group(kb, PB[2][:], pairs_u, reads=(kwu,) + HK, writes=("ps2",))
            mm_group(kb, PB[3][:], pairs_g, reads=(kwg,) + HK, writes=("ps3",))
            kb.op("act", lambda e, io=io: e.activation(out=sg_sb[:], in_=PB[3][:], func=AF.Silu,
                                                       scale=sc_sb[:, 144 + io:145 + io]),
                  reads=("ps3", "wsgv"), writes=("sgv",))
            kb.op("dve", lambda e, io=io, y=yt: e.scalar_tensor_tensor(
                out=y, in0=sg_sb[:], scalar=sc_sb[:, 80 + io:81 + io],
                in1=PB[2][:], op0=ALU.mult, op1=ALU.mult),
                  reads=("sgv", "wsuv", "ps2"), writes=(kyt,))
            kb.op("sync", lambda e, io=io, ts_=ts_, y=yt: e.dma_start(
                out=y_scr[io * 128:(io + 1) * 128, ts_ * 512:(ts_ + 1) * 512], in_=y),
                  reads=(kyt,), writes=(f"ysc{io}_{ts_}",), dma=True)
    out_keys = []
    for ts_ in range(4):
        for c in range(64):
            ku = pre + f"u{c}"
            khs = pre + f"h{c // 4}"
            kb.op("sync", lambda e, c=c, ts_=ts_: e.dma_start(
                out=xns_sb[:, (c % 16) * 512:(c % 16 + 1) * 512],
                in_=y_scr[c * 128:(c + 1) * 128, ts_ * 512:(ts_ + 1) * 512]),
                  reads=(f"ysc{c}_{ts_}",), writes=(f"yl{c % 16}", f"xnsl{c % 16}"), dma=True)
            ps = PB[0] if c % 2 == 0 else PB[1]
            psk = "ps0" if c % 2 == 0 else "ps1"
            kb.op("pe", lambda e, c=c, s=ps: e.matmul(s[:], h128_sb[:], xns_sb[:, (c % 16) * 512:(c % 16 + 1) * 512],
                                                      start=True, stop=True),
                  reads=(f"yl{c % 16}", "h128v"), writes=(psk,))
            if c % 2 == 0:
                kb.op("act", lambda e, c=c, s=ps: e.activation(out=hu[:, c * 512:(c + 1) * 512], in_=s[:], func=AF.Copy),
                      reads=(psk,), writes=(ku, khs))
            else:
                kb.op("dve", lambda e, c=c, s=ps: e.tensor_copy(hu[:, c * 512:(c + 1) * 512], s[:]),
                      reads=(psk,), writes=(ku, khs))
        UK = tuple(pre + f"u{c}" for c in range(64))
        for o in range(16):
            wpp = o % 2
            dsl = wsl[:, wpp * 8192:(wpp + 1) * 8192]
            kds = pre + f"ds{wpp}"
            extraw = (pre + "wu0", pre + "wg0", pre + "wu1", pre + "wg1") if (ts_ == 0 and o < 2) else ()
            kb.op("gp", lambda e, o=o, dd=dsl: e.dma_start(
                out=dd.rearrange("p (c m) -> p c m", c=64),
                in_=cfg["d2T"][:, :, o * 128:(o + 1) * 128].rearrange("c p m -> p c m")),
                  writes=(kds,) + extraw, dma=True)
            ps = PB[2] if o % 2 == 0 else PB[3]
            psk = "ps2" if o % 2 == 0 else "ps3"
            pairs = [(dsl[:, c * 128:(c + 1) * 128], hu[:, c * 512:(c + 1) * 512]) for c in range(64)]
            mm_group(kb, ps[:], pairs, reads=(kds,) + UK, writes=(psk,))
            op2 = out_sb[:, (o % 2) * 512:(o % 2 + 1) * 512]
            kb.op("dve", lambda e, o=o, s=ps, oo=op2: e.tensor_scalar(
                out=oo, in0=s[:], scalar1=sc_sb[:, 208 + o:209 + o], scalar2=None, op0=ALU.mult),
                  reads=(psk, "wsdv"), writes=(f"fob{o % 2}",))
            wk = pre + f"ffw{o}_{ts_}"
            kb.op("sync", lambda e, o=o, ts_=ts_, oo=op2: e.dma_start(
                out=out_feat[o * 128:(o + 1) * 128, out_col0 + ts_ * 512:out_col0 + (ts_ + 1) * 512] if not dyn
                else out_feat[o * 128:(o + 1) * 128, ts_ * 512:(ts_ + 1) * 512], in_=oo),
                  reads=(f"fob{o % 2}",), writes=(wk,), dma=True)
            out_keys.append(wk)
    A["arC_old"] = (pre + "ds0", pre + "ds1")
    return out_keys


MEGA_DEBUG = False


def emit_alpha(nc, kb):
    x_in = nc.dram_tensor("x", [T, D], F32, kind="ExternalInput").ap()
    A = {"PB": None}
    for nm, shp, dt in (("wqkvT", [D, D], F32), ("wbT", [D, 16], F32), ("waT", [D, 16], F32),
                        ("convw", [D, 4], F32), ("dtb", [16, 1], F32), ("alog", [16, 1], F32),
                        ("ones2", [128, 2], BF16), ("UINC", [128, 128], F32), ("I128F", [128, 128], F32),
                        ("I128B", [128, 128], BF16), ("ONES1", [1, 128], F32),
                        ("wzT", [D, 1024], F32), ("woutT", [1024, D], F32), ("nw128", [128, 1], F32)):
        A[nm] = nc.dram_tensor(nm, shp, dt, kind="ExternalInput").ap()
    ffn_cfg = {"tag": "l0", "loaded": {}}
    for nm, shp, dt in (("h2048", [D, D], BF16), ("h128", [128, 128], BF16), ("suug", [128, 16], F32),
                        ("sudp", [128, 64], F32), ("wsug", [128, 128], F32), ("wsdp", [128, 16], F32),
                        ("wupT", [D, 8192], F32), ("wgT", [D, 8192], F32), ("d2T", [64, 128, D], BF16)):
        ffn_cfg[nm] = nc.dram_tensor("ffn_" + nm, shp, dt, kind="ExternalInput").ap()
    KD = "ExternalOutput" if MEGA_DEBUG else "Internal"
    xn0_scr = nc.dram_tensor("xn0_scr", [T, D], BF16, kind=KD).ap()
    A["xn0f"] = nc.dram_tensor("xn0f_scr", [D, T], BF16, kind=KD).ap()
    A["qkvf_scr"] = nc.dram_tensor("qkvf_scr", [D, T], BF16, kind=KD).ap()
    A["rows_scr"] = nc.dram_tensor("rows_scr", [64, T], F32, kind=KD).ap()
    A["o_scr"] = nc.dram_tensor("o_scr", [T, 1024], BF16, kind=KD).ap()
    A["yn_scr"] = nc.dram_tensor("yn_scr", [T, 1024], BF16, kind=KD).ap()
    A["attn0f"] = nc.dram_tensor("attn0f", [D, T], BF16, kind=KD).ap()
    ffn0f = nc.dram_tensor("ffn0f", [D, T], BF16, kind=KD).ap()
    y_scr = nc.dram_tensor("y_scr", [8192, 2048], BF16, kind="Internal").ap()
    xn1_out = nc.dram_tensor("xn1", [T, D], BF16, kind="ExternalOutput").ap()
    x1_out = nc.dram_tensor("x1", [T, D], F32, kind="ExternalOutput").ap()

    PS_ALL = nc.alloc_psum_tensor("PS", [128, 4096], F32)
    A["PS_ALL"] = PS_ALL[:]
    A["PB"] = [PS_ALL[:, i * 512:(i + 1) * 512] for i in range(8)]
    A["AR_A"] = nc.alloc_sbuf_tensor("AR_A", [128, 16 * 2048], BF16)[:]
    A["AR_B"] = nc.alloc_sbuf_tensor("AR_B", [128, 16 * 2048], BF16)[:]
    A["AR_C"] = nc.alloc_sbuf_tensor("AR_C", [128, 16384], BF16)[:]
    AR_S = nc.alloc_sbuf_tensor("AR_S", [128, 15360], BF16)
    kb.set_spool(AR_S, 15360)
    A["arA_old"] = ()
    A["arB_old"] = ()
    A["arC_old"] = ()
    A["xns_sb"] = None
    A["ffn_smalls"] = None

    xnk = []
    phase_ln(nc, kb, "l0", x_in, xn0_scr, T, xn_keys=xnk)
    kb.phase_fence(); kb.spool_reset()
    phase_featize(nc, kb, "z0", xn0_scr, A["xn0f"], ())
    kb.phase_fence(); kb.spool_reset()
    A["xn0f_ready"] = ()
    phase_proj16(nc, kb, A)
    kb.phase_fence(); kb.spool_reset()
    A["qkv_keys"] = ()
    A["row_keys"] = ()
    phase_scan16(nc, kb, A)
    kb.phase_fence(); kb.spool_reset()
    A["o_keys"] = ()
    A["arA_old"] = ()
    phase_epi16(nc, kb, A)
    kb.phase_fence(); kb.spool_reset()
    A["a0_keys"] = ()
    A["arA_old"] = ()
    A["arB_old"] = ()
    A["arC_old"] = ()
    fk0 = phase_ffnP(nc, kb, "fa", A, A["xn0f"], A["xn0f_ready"], ffn0f, 0, False, ffn_cfg, y_scr)
    fk1 = phase_ffnP(nc, kb, "fb", A, A["xn0f"], A["xn0f_ready"], ffn0f, 2048, False, ffn_cfg, y_scr)
    kb.phase_fence(); kb.spool_reset()
    phase_ln(nc, kb, "r1", x_in, xn1_out, T, a_feat=A["attn0f"], f_feat=ffn0f, x1_out=x1_out,
             a_ready=(), f_ready=())


def build_alpha():
    return build_program(emit_alpha)


def phase_attn16(nc, kb, A, Qf, Kf, Vf, ctxf, q_ready, k_ready, v_ready):
    PB = A["PB"]
    ARA, ARB = A["AR_A"], A["AR_B"]
    qh_sb = ARA[:, 0:T]
    kh_sb = ARA[:, T:2 * T]
    q2_sb = ARA[0:65, 2 * T:3 * T]
    k2_sb = ARA[0:65, 3 * T:4 * T]
    vh_sb = ARA[:, 4 * T:4 * T + 4096]
    sqf_sb = ARA[:, 5 * T:6 * T]
    pt_sb = ARA[:, 6 * T:6 * T + 2048]
    ARBf = ARB.bitcast(F32)
    row_sb = ARBf[0:1, 0:T]
    mrow_sb = ARBf[0:1, T:2 * T]
    kr_sb = ARBf[0:1, 2 * T:3 * T]
    sc1_sb = ARBf[0:1, 3 * T:3 * T + 4]
    invl_sb = ARBf[0:1, 3 * T + 4:3 * T + 1028]
    invb_sb = ARB[:, 7 * T:7 * T + 1024]
    ctx_sb = ARB[:, 7 * T + 1024:7 * T + 2048]
    o1f_sb = ARBf[0:1, 3 * T + 1028:3 * T + 1156]
    ocb_sb = ARB[:, 7 * T + 2048:7 * T + 2049]
    kb.op("dve", lambda e: e.memset(o1f_sb[:], 1.0), writes=("o1f",))
    kb.op("dve", lambda e: e.memset(ocb_sb[:], 1.0), writes=("ocb",))
    psS = A["PS_ALL"][:, 0:1024]
    psCx = A["PS_ALL"][:, 1024:2048]
    psL = A["PS_ALL"][0:1, 2048:3072]
    psR = A["PS_ALL"][0:1, 3072:3584]
    PSSK = ("ps0", "ps1")
    PSCK = ("ps2", "ps3")
    PSLK = ("ps4", "ps5")
    ctx_keys = []
    for h in range(16):
        r0 = h * 128
        kb.op("sync", lambda e, r0=r0: e.dma_start(out=qh_sb[:], in_=Qf[r0:r0 + 128, :]),
              reads=q_ready, writes=("qh",), dma=True)
        kb.op("sync", lambda e, r0=r0: e.dma_start(out=kh_sb[:], in_=Kf[r0:r0 + 128, :]),
              reads=k_ready, writes=("kh",), dma=True)
        for tt in range(32):
            kb.op("sync", lambda e, r0=r0, tt=tt: e.dma_start_transpose(
                out=vh_sb[:, tt * 128:(tt + 1) * 128], in_=Vf[r0:r0 + 128, tt * 128:(tt + 1) * 128]),
                  reads=v_ready, writes=("vh",), dma=True)
        kb.op("dve", lambda e: e.tensor_copy(k2_sb[0:64, :], kh_sb[64:128, :]), reads=("kh",), writes=("k2",))
        kb.op("dve", lambda e: e.memset(k2_sb[64:65, :], 1.0), writes=("k2o",))
        kb.op("dve", lambda e: e.tensor_copy(q2_sb[0:64, :], qh_sb[64:128, :]), reads=("qh",), writes=("q2",))
        kb.op("act", lambda e: e.activation(out=sqf_sb[:], in_=qh_sb[:], func=AF.Square), reads=("qh",), writes=("sqf",))
        for ts_ in range(8):
            kb.op("pe", lambda e, ts_=ts_: e.matmul(psR[0:1, 0:512], ocb_sb[:], sqf_sb[:, ts_ * 512:(ts_ + 1) * 512],
                                                    start=True, stop=True),
                  reads=("sqf", "ocb"), writes=("ps6",))
            kb.op("dve", lambda e, ts_=ts_: e.tensor_copy(row_sb[:, ts_ * 512:(ts_ + 1) * 512], psR[0:1, 0:512]),
                  reads=("ps6",), writes=("qsqr",))
        kb.op("act", lambda e: e.activation(out=sqf_sb[:], in_=kh_sb[:], func=AF.Square), reads=("kh",), writes=("sqf",))
        for ts_ in range(8):
            kb.op("pe", lambda e, ts_=ts_: e.matmul(psR[0:1, 0:512], ocb_sb[:], sqf_sb[:, ts_ * 512:(ts_ + 1) * 512],
                                                    start=True, stop=True),
                  reads=("sqf", "ocb"), writes=("ps6",))
            kb.op("dve", lambda e, ts_=ts_: e.tensor_copy(kr_sb[:, ts_ * 512:(ts_ + 1) * 512], psR[0:1, 0:512]),
                  reads=("ps6",), writes=("ksqr",))
        kb.op("dve", lambda e: e.tensor_reduce(out=sc1_sb[:, 0:1], in_=kr_sb[:], axis=AX.X, op=ALU.max),
              reads=("ksqr",), writes=("mk2",))
        kb.op("act", lambda e: e.activation(out=sc1_sb[:, 1:2], in_=sc1_sb[:, 0:1], func=AF.Ln),
              reads=("mk2",), writes=("lnmk",))
        kb.op("dve", lambda e: e.tensor_scalar(out=sc1_sb[:, 1:2], in0=sc1_sb[:, 1:2], scalar1=0.5, scalar2=0.1,
                                               op0=ALU.mult, op1=ALU.add),
              reads=("lnmk",), writes=("lnmk",))
        kb.op("act", lambda e: e.activation(out=mrow_sb[:], in_=row_sb[:], func=AF.Ln), reads=("qsqr",), writes=("mrow",))
        kb.op("act", lambda e: e.activation(out=mrow_sb[:], in_=mrow_sb[:], func=AF.Exp, scale=0.5, bias=sc1_sb[:, 1:2]),
              reads=("mrow", "lnmk"), writes=("mrow",))
        kb.op("dve", lambda e: e.tensor_scalar(out=q2_sb[64:65, :], in0=mrow_sb[:], scalar1=-1.0, scalar2=None, op0=ALU.mult),
              reads=("mrow",), writes=("q2o",))
        for sp in range(4):
            q0 = sp * 1024
            nk = (sp + 1) * 8
            for kt in range(nk):
                ppp = kt % 2
                pt = pt_sb[:, ppp * 1024:(ppp + 1) * 1024]
                kpt = f"hpt{ppp}"
                for sub in range(2):
                    qs = q0 + sub * 512
                    pairs = [(kh_sb[0:64, kt * 128:(kt + 1) * 128], qh_sb[0:64, qs:qs + 512]),
                             (k2_sb[0:65, kt * 128:(kt + 1) * 128], q2_sb[0:65, qs:qs + 512])]
                    mm_group(kb, psS[:, sub * 512:(sub + 1) * 512], pairs,
                             reads=("kh", "qh", "k2", "k2o", "q2", "q2o"), writes=(PSSK[sub],))
                kb.op("act", lambda e, pt=pt: e.activation(out=pt, in_=psS[:], func=AF.Exp),
                      reads=PSSK, writes=(kpt,))
                if kt * 128 >= q0:
                    def _mask2(e, pt=pt, base=q0 - kt * 128):
                        if not hasattr(e, "_fillreg0"):
                            e._fillreg0 = e.to_reg(0.0)
                        return e.affine_select(out=pt, in_=pt, pattern=[[1, 1024]], compare_op=ALU.is_ge,
                                               fill=e._fillreg0, base=base, channel_multiplier=-1)
                    kb.op("gp", _mask2, reads=(kpt,), writes=(kpt,))
                for sub in range(2):
                    kb.op("pe", lambda e, kt=kt, sub=sub, pt=pt, nk=nk: e.matmul(
                        psCx[:, sub * 512:(sub + 1) * 512], vh_sb[:, kt * 128:(kt + 1) * 128],
                        pt[:, sub * 512:(sub + 1) * 512], start=(kt == 0), stop=(kt == nk - 1)),
                          reads=("vh", kpt), writes=(PSCK[sub],))
                    kb.op("pe", lambda e, kt=kt, sub=sub, pt=pt, nk=nk: e.matmul(
                        psL[0:1, sub * 512:(sub + 1) * 512], ocb_sb[:],
                        pt[:, sub * 512:(sub + 1) * 512], start=(kt == 0), stop=(kt == nk - 1)),
                          reads=("ocb", kpt), writes=(PSLK[sub],))
            kb.op("act", lambda e: e.activation(out=invl_sb[:], in_=psL[0:1, 0:1024], func=AF.Ln),
                  reads=PSLK, writes=("invl",))
            kb.op("act", lambda e: e.activation(out=invl_sb[:], in_=invl_sb[:], func=AF.Exp, scale=-1.0),
                  reads=("invl",), writes=("invl",))
            for sub in range(2):
                kb.op("pe", lambda e, sub=sub: e.matmul(psS[:, sub * 512:(sub + 1) * 512], o1f_sb[:],
                                                        invl_sb[0:1, sub * 512:(sub + 1) * 512], start=True, stop=True),
                      reads=("invl", "o1f"), writes=(PSSK[sub],))
            kb.op("dve", lambda e: e.tensor_copy(invb_sb[:], psS[:]), reads=PSSK, writes=("invb",))
            kb.op("dve", lambda e: e.scalar_tensor_tensor(out=ctx_sb[:], in0=invb_sb[:], scalar=1.0, in1=psCx[:],
                                                          op0=ALU.mult, op1=ALU.mult),
                  reads=("invb",) + PSCK, writes=("ctxv",))
            wk = f"ctxw{h}_{sp}"
            kb.op("sync", lambda e, r0=r0, q0=q0: e.dma_start(out=ctxf[r0:r0 + 128, q0:q0 + 1024], in_=ctx_sb[:]),
                  reads=("ctxv",), writes=(wk,), dma=True)
            ctx_keys.append(wk)
    return ctx_keys


def emit_beta(nc, kb):
    xn1_in = nc.dram_tensor("xn1", [T, D], BF16, kind="ExternalInput").ap()
    x1_in = nc.dram_tensor("x1", [T, D], F32, kind="ExternalInput").ap()
    wqT = nc.dram_tensor("wqT", [D, D], F32, kind="ExternalInput").ap()
    wkT = nc.dram_tensor("wkT", [D, D], F32, kind="ExternalInput").ap()
    wvT = nc.dram_tensor("wvT", [D, D], F32, kind="ExternalInput").ap()
    woT = nc.dram_tensor("woT", [D, D], F32, kind="ExternalInput").ap()
    bq = nc.dram_tensor("bq", [128, 16], F32, kind="ExternalInput").ap()
    bk = nc.dram_tensor("bk", [128, 16], F32, kind="ExternalInput").ap()
    bv = nc.dram_tensor("bv", [128, 16], F32, kind="ExternalInput").ap()
    bo = nc.dram_tensor("bo", [128, 16], F32, kind="ExternalInput").ap()
    ffn_cfg = {"tag": "l1", "loaded": {}}
    for nm, shp, dt in (("h2048", [D, D], BF16), ("h128", [128, 128], BF16), ("suug", [128, 16], F32),
                        ("sudp", [128, 64], F32), ("wsug", [128, 128], F32), ("wsdp", [128, 16], F32),
                        ("wupT", [D, 8192], F32), ("wgT", [D, 8192], F32), ("d2T", [64, 128, D], BF16)):
        ffn_cfg[nm] = nc.dram_tensor("ffn_" + nm, shp, dt, kind="ExternalInput").ap()
    KD = "ExternalOutput" if MEGA_DEBUG else "Internal"
    xn1f = nc.dram_tensor("xn1f_scr", [D, T], BF16, kind=KD).ap()
    Qf = nc.dram_tensor("Qf_scr", [D, T], BF16, kind=KD).ap()
    Kf = nc.dram_tensor("Kf_scr", [D, T], BF16, kind=KD).ap()
    Vf = nc.dram_tensor("Vf_scr", [D, T], BF16, kind=KD).ap()
    ctxf = nc.dram_tensor("ctxf_scr", [D, T], BF16, kind=KD).ap()
    attn1f = nc.dram_tensor("attn1f", [D, 2048], BF16, kind=KD).ap()
    ffn1f = nc.dram_tensor("ffn1f", [D, 2048], BF16, kind=KD).ap()
    y_scr = nc.dram_tensor("y_scr", [8192, 2048], BF16, kind="Internal").ap()
    out_rows = nc.dram_tensor("outp", [2048, D], F32, kind="ExternalOutput").ap()

    A = {}
    PS_ALL = nc.alloc_psum_tensor("PS", [128, 4096], F32)
    A["PS_ALL"] = PS_ALL[:]
    A["PB"] = [PS_ALL[:, i * 512:(i + 1) * 512] for i in range(8)]
    A["AR_A"] = nc.alloc_sbuf_tensor("AR_A", [128, 16 * 2048], BF16)[:]
    A["AR_B"] = nc.alloc_sbuf_tensor("AR_B", [128, 16 * 2048], BF16)[:]
    A["AR_C"] = nc.alloc_sbuf_tensor("AR_C", [128, 16384], BF16)[:]
    AR_S = nc.alloc_sbuf_tensor("AR_S", [128, 15360], BF16)
    kb.set_spool(AR_S, 15360)
    A["arA_old"] = ()
    A["arB_old"] = ()
    A["arC_old"] = ()
    A["xns_sb"] = None
    A["ffn_smalls"] = None
    PB = A["PB"]

    phase_featize(nc, kb, "b0", xn1_in, xn1f, ())
    kb.phase_fence(); kb.spool_reset()
    xn1f_ready = ()
    b_sb = nc.alloc_sbuf_tensor("b_bias", [128, 16], F32)
    out_sb = nc.alloc_sbuf_tensor("b_out", [128, 2 * 512], BF16)
    proj_keys = {}
    for (wap, bap, scale, dst, tag) in ((wqT, bq, float(128 ** -0.5), Qf, "q"),
                                        (wkT, bk, 1.0, Kf, "k"),
                                        (wvT, bv, 1.0, Vf, "v")):
        kb.op("gp", lambda e, w=wap: e.dma_start(out=A["AR_B"][:].rearrange("p (k m) -> p k m", k=16),
                                                 in_=w.rearrange("(k p) m -> p k m", p=128)),
              writes=("bw",) + tuple(A["arB_old"]), dma=True)
        A["arB_old"] = ("bw",)
        kb.op("sync", lambda e, b=bap: e.dma_start(out=b_sb[:], in_=b[:, :]), writes=("bb",), dma=True)
        if scale != 1.0:
            kb.op("dve", lambda e, sc=scale: e.tensor_scalar(out=b_sb[:], in0=b_sb[:], scalar1=sc, scalar2=None, op0=ALU.mult),
                  reads=("bb",), writes=("bb",))
        pk = []
        for half in range(2):
            toff = half * 2048
            for kc in range(16):
                kb.op("sync", lambda e, kc=kc, t=toff: e.dma_start(
                    out=A["AR_A"][:, kc * 2048:(kc + 1) * 2048], in_=xn1f[kc * 128:(kc + 1) * 128, t:t + 2048]),
                      reads=xn1f_ready, writes=(f"bxn{kc}",) + tuple(A["arA_old"]), dma=True)
            BXNK = tuple(f"bxn{kc}" for kc in range(16))
            A["arA_old"] = BXNK
            for m in range(16):
                for ts_ in range(4):
                    ps = PB[0] if ts_ % 2 == 0 else PB[1]
                    psk = "ps0" if ts_ % 2 == 0 else "ps1"
                    pairs = [(A["AR_B"][:, k * 2048 + m * 128:(k * 2048 + (m + 1) * 128)],
                              A["AR_A"][:, k * 2048 + ts_ * 512: k * 2048 + (ts_ + 1) * 512]) for k in range(16)]
                    mm_group(kb, ps[:], pairs, reads=("bw",) + BXNK, writes=(psk,))
                    ob = out_sb[:, (ts_ % 2) * 512:(ts_ % 2 + 1) * 512]
                    kb.op("dve", lambda e, s=ps[:], ob=ob, m=m, sc=scale: e.tensor_scalar(
                        out=ob, in0=s, scalar1=sc, scalar2=b_sb[:, m:m + 1], op0=ALU.mult, op1=ALU.add),
                          reads=(psk, "bb"), writes=(f"bob{ts_ % 2}",))
                    wk = f"{tag}w{m}_{half}_{ts_}"
                    kb.op("sync", lambda e, m=m, ts_=ts_, t=toff, ob=ob, dst=dst: e.dma_start(
                        out=dst[m * 128:(m + 1) * 128, t + ts_ * 512:t + (ts_ + 1) * 512], in_=ob),
                          reads=(f"bob{ts_ % 2}",), writes=(wk,), dma=True)
                    pk.append(wk)
        proj_keys[tag] = tuple(pk)
    kb.phase_fence(); kb.spool_reset()
    ctx_keys = phase_attn16(nc, kb, A, Qf, Kf, Vf, ctxf, (), (), ())
    kb.phase_fence(); kb.spool_reset()
    ctx_keys = ()
    A["arA_old"] = ()
    A["arB_old"] = ()
    ctxs = nc.dram_tensor("ctxs_scr", [D, 2048], BF16, kind=KD).ap()
    xn1s = nc.dram_tensor("xn1s_scr", [D, 2048], BF16, kind=KD).ap()
    x1s = nc.dram_tensor("x1s_scr", [2048, D], F32, kind=KD).ap()
    phase_extract_stripe(nc, kb, "exc", ctxf, ctxs, 0, 2048, BF16, "act", False)
    phase_extract_stripe(nc, kb, "exn", xn1f, xn1s, 0, 2048, BF16, "gp", False)
    phase_extract_stripe(nc, kb, "exx", x1_in, x1s, 2048, 0, F32, "sync", True)
    kb.phase_fence(); kb.spool_reset()
    for kc in range(16):
        kb.op("sync", lambda e, kc=kc: e.dma_start(
            out=A["AR_A"][:, kc * 2048:(kc + 1) * 2048],
            in_=ctxs[kc * 128:(kc + 1) * 128, :]),
              reads=(), writes=(f"cxn{kc}",), dma=True)
    CXNK = tuple(f"cxn{kc}" for kc in range(16))
    A["arA_old"] = CXNK
    kb.op("gp", lambda e: e.dma_start(out=A["AR_B"][:].rearrange("p (k m) -> p k m", k=16),
                                      in_=woT.rearrange("(k p) m -> p k m", p=128)),
          writes=("bwo",) + tuple(A["arB_old"]), dma=True)
    A["arB_old"] = ("bwo",)
    kb.op("sync", lambda e: e.dma_start(out=b_sb[:], in_=bo[:, :]), writes=("bb",), dma=True)
    a1_keys = []
    for m in range(16):
        for ts_ in range(4):
            ps = PB[0] if ts_ % 2 == 0 else PB[1]
            psk = "ps0" if ts_ % 2 == 0 else "ps1"
            pairs = [(A["AR_B"][:, k * 2048 + m * 128:(k * 2048 + (m + 1) * 128)],
                      A["AR_A"][:, k * 2048 + ts_ * 512: k * 2048 + (ts_ + 1) * 512]) for k in range(16)]
            mm_group(kb, ps[:], pairs, reads=("bwo",) + CXNK, writes=(psk,))
            ob = out_sb[:, (ts_ % 2) * 512:(ts_ % 2 + 1) * 512]
            kb.op("dve", lambda e, s=ps[:], ob=ob, m=m: e.tensor_scalar(
                out=ob, in0=s, scalar1=1.0, scalar2=b_sb[:, m:m + 1], op0=ALU.mult, op1=ALU.add),
                  reads=(psk, "bb"), writes=(f"bob{ts_ % 2}",))
            wk = f"a1w{m}_{ts_}"
            kb.op("sync", lambda e, m=m, ts_=ts_, ob=ob: e.dma_start(
                out=attn1f[m * 128:(m + 1) * 128, ts_ * 512:(ts_ + 1) * 512], in_=ob),
                  reads=(f"bob{ts_ % 2}",), writes=(wk,), dma=True)
            a1_keys.append(wk)
    kb.phase_fence(); kb.spool_reset()
    A["arB_old"] = ()
    A["arA_old"] = ()
    A["arC_old"] = ()
    fk = phase_ffnP(nc, kb, "f1", A, xn1s, (), ffn1f, 0, False, ffn_cfg, y_scr)
    kb.phase_fence(); kb.spool_reset()
    phase_ln(nc, kb, "j", x1s, None, 2048, a_feat=attn1f, f_feat=ffn1f, x1_out=out_rows,
             dyn_x=False, a_ready=(), f_ready=())


def build_beta():
    return build_program(emit_beta)



def _ffn_inputs(inputs, li, d2, H2048c, H128c):
    p = "L0_ffn_" if li == 0 else "L1_ffn_"
    f32 = np.float32
    ct = np.ascontiguousarray
    return {
        "ffn_h2048": H2048c, "ffn_h128": H128c,
        "ffn_suug": ct(inputs[p + "SU_ug"].astype(f32, copy=False).reshape(16, 128).T),
        "ffn_sudp": ct(inputs[p + "SU_dp"].astype(f32, copy=False).reshape(64, 128).T),
        "ffn_wsug": ct(inputs[p + "Wscale_ug"].astype(f32, copy=False).reshape(128, 128).T),
        "ffn_wsdp": ct(inputs[p + "Wscale_dp"].astype(f32, copy=False).reshape(16, 128).T),
        "ffn_wupT": ct(inputs[p + "W_up"].astype(f32, copy=False).T),
        "ffn_wgT": ct(inputs[p + "W_gate"].astype(f32, copy=False).T),
        "ffn_d2T": d2[li],
    }


def kernel(**inputs):
    CORES = list(range(NCORES))
    f32 = np.float32
    ct = np.ascontiguousarray
    xfull = ct(inputs["L0_attn_x"].astype(f32, copy=False))

    H64 = hadamard(64)
    H2048c = (hadamard(2048) * (2.0 ** -5)).astype(BF)
    H128c = (hadamard(128) * (2.0 ** -6)).astype(BF)
    UINC = np.triu(np.ones((128, 128), f32))
    I128F = np.eye(128, dtype=f32)
    I128B = I128F.astype(BF)
    ONES1 = np.ones((1, 128), f32)
    ones2 = np.zeros((128, 2), f32)
    ones2[0:64, 0] = 1.0
    ones2[64:128, 1] = 1.0
    ones2 = ones2.astype(BF)

    nc_fold = build_fold()
    OS = D // NCORES
    wd0 = inputs["L0_ffn_W_down"].astype(f32, copy=False)
    wd1 = inputs["L1_ffn_W_down"].astype(f32, copy=False)
    im = [{"H64": H64.astype(BF), "Wdn0": ct(wd0[c * OS:(c + 1) * OS]),
           "Wdn1": ct(wd1[c * OS:(c + 1) * OS])} for c in range(NCORES)]
    res = run_bass_kernel_spmd(nc_fold, im, core_ids=CORES)
    d2 = {}
    for li in range(2):
        D2 = np.concatenate([r[f"D2_{li}"] for r in res.results], axis=0)
        d2[li] = ct(D2.T.reshape(64, 128, D))

    nc_a = build_alpha()
    base_a = {
        "wqkvT": ct(inputs["L0_attn_W_qkv"].astype(f32, copy=False).T),
        "wbT": ct(inputs["L0_attn_W_b"].astype(f32, copy=False).T),
        "waT": ct(inputs["L0_attn_W_a"].astype(f32, copy=False).T),
        "convw": ct(inputs["L0_attn_conv_w"].astype(f32, copy=False)[:, 0, :]),
        "dtb": ct(inputs["L0_attn_dt_bias"].astype(f32, copy=False)[:, None]),
        "alog": ct(inputs["L0_attn_A_log"].astype(f32, copy=False)[:, None]),
        "ones2": ones2, "UINC": UINC, "I128F": I128F, "I128B": I128B, "ONES1": ONES1,
        "wzT": ct(inputs["L0_attn_W_z"].astype(f32, copy=False).T),
        "woutT": ct(inputs["L0_attn_W_out"].astype(f32, copy=False).T),
        "nw128": ct(np.tile(inputs["L0_attn_norm_weight"].astype(f32, copy=False), 2)[:, None]),
    }
    base_a.update(_ffn_inputs(inputs, 0, d2, H2048c, H128c))
    im = []
    for c in range(NCORES):
        b = c // 2
        m = dict(base_a)
        m["x"] = ct(xfull[b])
        im.append(m)
    res = run_bass_kernel_spmd(nc_a, im, core_ids=CORES)
    xn1 = [res.results[2 * b]["xn1"] for b in range(B)]
    x1 = [res.results[2 * b]["x1"] for b in range(B)]

    nc_b = build_beta()
    base_b = {
        "wqT": ct(inputs["L1_attn_Wq"].astype(f32, copy=False).T),
        "wkT": ct(inputs["L1_attn_Wk"].astype(f32, copy=False).T),
        "wvT": ct(inputs["L1_attn_Wv"].astype(f32, copy=False).T),
        "woT": ct(inputs["L1_attn_Wo"].astype(f32, copy=False).T),
        "bq": ct(inputs["L1_attn_bq"].astype(f32, copy=False).reshape(16, 128).T),
        "bk": ct(inputs["L1_attn_bk"].astype(f32, copy=False).reshape(16, 128).T),
        "bv": ct(inputs["L1_attn_bv"].astype(f32, copy=False).reshape(16, 128).T),
        "bo": ct(inputs["L1_attn_bo"].astype(f32, copy=False).reshape(16, 128).T),
    }
    base_b.update(_ffn_inputs(inputs, 1, d2, H2048c, H128c))
    im = []
    for c in range(NCORES):
        b = c // 2
        m = dict(base_b)
        m["xn1"] = xn1[b]
        m["x1"] = x1[b]
        im.append(m)
    res = run_bass_kernel_spmd(nc_b, im, core_ids=CORES)
    out = np.zeros((B, T, D), f32)
    for c in range(NCORES):
        b, s = c // 2, c % 2
        out[b, s * 2048:(s + 1) * 2048] = res.results[c]["outp"]
    return out



def phase_fold_full(nc, kb, A, h64_ap, wdn_ap, d2scr, pre):
    h64_sb = kb.spool(64, 64, BF16)
    slab = kb.spool(64, 4 * 512, BF16)
    oslab = kb.spool(64, 4 * 512, BF16)
    ps = A["PB"][0][0:64, :]
    kb.op("sync", lambda e: e.dma_start(out=h64_sb[:], in_=h64_ap[:, :]), writes=(pre + "h64",), dma=True)
    for og in range(D // 4):
        pp = og % 4
        s_in = slab[:, pp * 512:(pp + 1) * 512]
        s_out = oslab[:, pp * 512:(pp + 1) * 512]
        src = wdn_ap[og * 4:(og + 1) * 4, :].rearrange("o (c i) -> c o i", i=128)
        dst = d2scr[og * 4:(og + 1) * 4, :].rearrange("o (c i) -> c o i", i=128)
        kin, kout = pre + f"sl{pp}", pre + f"so{pp}"
        kb.op("gp", lambda e, o=s_in, s=src: e.dma_start(out=o.rearrange("p (a b) -> p a b", a=4), in_=s),
              writes=(kin,), dma=True)
        kb.op("pe", lambda e, o=ps, l=h64_sb, r=s_in: e.matmul(o, l[:], r, start=True, stop=True),
              reads=(kin, pre + "h64"), writes=("ps0",))
        kb.op("act", lambda e, o=s_out, s=ps: e.activation(out=o, in_=s, func=AF.Copy),
              reads=("ps0",), writes=(kout,))
        wk = pre + f"w{og}"
        kb.op("sync", lambda e, o=dst, s=s_out: e.dma_start(out=o, in_=s.rearrange("p (a b) -> p a b", a=4)),
              reads=(kout,), writes=(wk,), dma=True)


def phase_fold_T(nc, kb, d2scr, d2T3, pre):
    stg = kb.spool(128, D, BF16)
    d2flat = d2T3.rearrange("c p m -> (c p) m")
    for s in range(64):
        kt = pre + "tp"
        kb.op("sync", lambda e, s=s: e.dma_start_transpose(out=stg[:], in_=d2scr[:, s * 128:(s + 1) * 128]),
              writes=(kt,), dma=True)
        kb.op("sync", lambda e, s=s: e.dma_start(out=d2flat[s * 128:(s + 1) * 128, :], in_=stg[:]),
              reads=(kt,), writes=(pre + f"w{s}",), dma=True)


def emit_mega(nc, kb):
    x_in = nc.dram_tensor("x", [T, D], F32, kind="ExternalInput").ap()
    A = {}
    for nm, shp, dt in (("wqkvT", [D, D], F32), ("wbT", [D, 16], F32), ("waT", [D, 16], F32),
                        ("convw", [D, 4], F32), ("dtb", [16, 1], F32), ("alog", [16, 1], F32),
                        ("ones2", [128, 2], BF16), ("UINC", [128, 128], F32), ("I128F", [128, 128], F32),
                        ("I128B", [128, 128], BF16), ("ONES1", [1, 128], F32),
                        ("wzT", [D, 1024], F32), ("woutT", [1024, D], F32), ("nw128", [128, 1], F32)):
        A[nm] = nc.dram_tensor(nm, shp, dt, kind="ExternalInput").ap()
    wqT = nc.dram_tensor("wqT", [D, D], F32, kind="ExternalInput").ap()
    wkT = nc.dram_tensor("wkT", [D, D], F32, kind="ExternalInput").ap()
    wvT = nc.dram_tensor("wvT", [D, D], F32, kind="ExternalInput").ap()
    woT = nc.dram_tensor("woT", [D, D], F32, kind="ExternalInput").ap()
    bq = nc.dram_tensor("bq", [128, 16], F32, kind="ExternalInput").ap()
    bk = nc.dram_tensor("bk", [128, 16], F32, kind="ExternalInput").ap()
    bv = nc.dram_tensor("bv", [128, 16], F32, kind="ExternalInput").ap()
    bo = nc.dram_tensor("bo", [128, 16], F32, kind="ExternalInput").ap()
    h64_ap = nc.dram_tensor("H64", [64, 64], BF16, kind="ExternalInput").ap()
    cfgs = {}
    for li in range(2):
        p = f"f{li}_"
        cfg = {"tag": f"l{li}", "loaded": {}}
        for nm, shp, dt in (("h2048", [D, D], BF16), ("h128", [128, 128], BF16), ("suug", [128, 16], F32),
                            ("sudp", [128, 64], F32), ("wsug", [128, 128], F32), ("wsdp", [128, 16], F32),
                            ("wupT", [D, 8192], F32), ("wgT", [D, 8192], F32)):
            cfg[nm] = nc.dram_tensor(p + nm, shp, dt, kind="ExternalInput").ap()
        cfg["wdn"] = nc.dram_tensor(p + "wdn", [D, 8192], F32, kind="ExternalInput").ap()
        cfg["d2scr"] = nc.dram_tensor(p + "d2scr", [D, 8192], BF16, kind="Internal").ap()
        cfg["d2T"] = nc.dram_tensor(p + "d2T", [64, 128, D], BF16, kind="Internal").ap()
        cfgs[li] = cfg
    xn0_scr = nc.dram_tensor("xn0_scr", [T, D], BF16, kind="Internal").ap()
    A["xn0f"] = nc.dram_tensor("xn0f_scr", [D, T], BF16, kind="Internal").ap()
    A["qkvf_scr"] = nc.dram_tensor("qkvf_scr", [D, T], BF16, kind="Internal").ap()
    A["rows_scr"] = nc.dram_tensor("rows_scr", [64, T], F32, kind="Internal").ap()
    A["o_scr"] = nc.dram_tensor("o_scr", [T, 1024], BF16, kind="Internal").ap()
    A["yn_scr"] = nc.dram_tensor("yn_scr", [T, 1024], BF16, kind="Internal").ap()
    A["attn0f"] = nc.dram_tensor("attn0f", [D, T], BF16, kind="Internal").ap()
    ffn0f = nc.dram_tensor("ffn0f", [D, T], BF16, kind="Internal").ap()
    y_scr = nc.dram_tensor("y_scr", [8192, 2048], BF16, kind="Internal").ap()
    xn1_scr = nc.dram_tensor("xn1_scr", [T, D], BF16, kind="Internal").ap()
    x1_scr = nc.dram_tensor("x1_scr", [T, D], F32, kind="Internal").ap()
    xn1f = nc.dram_tensor("xn1f_scr", [D, T], BF16, kind="Internal").ap()
    Qf = nc.dram_tensor("Qf_scr", [D, T], BF16, kind="Internal").ap()
    Kf = nc.dram_tensor("Kf_scr", [D, T], BF16, kind="Internal").ap()
    Vf = nc.dram_tensor("Vf_scr", [D, T], BF16, kind="Internal").ap()
    ctxf = nc.dram_tensor("ctxf_scr", [D, T], BF16, kind="Internal").ap()
    attn1f = nc.dram_tensor("attn1f", [D, 2048], BF16, kind="Internal").ap()
    ffn1f = nc.dram_tensor("ffn1f", [D, 2048], BF16, kind="Internal").ap()
    out_rows = nc.dram_tensor("outp", [2048, D], F32, kind="ExternalOutput").ap()

    PS_ALL = nc.alloc_psum_tensor("PS", [128, 4096], F32)
    A["PS_ALL"] = PS_ALL[:]
    A["PB"] = [PS_ALL[:, i * 512:(i + 1) * 512] for i in range(8)]
    A["AR_A"] = nc.alloc_sbuf_tensor("AR_A", [128, 16 * 2048], BF16)[:]
    A["AR_B"] = nc.alloc_sbuf_tensor("AR_B", [128, 16 * 2048], BF16)[:]
    A["AR_C"] = nc.alloc_sbuf_tensor("AR_C", [128, 16384], BF16)[:]
    AR_S = nc.alloc_sbuf_tensor("AR_S", [128, 15360], BF16)
    kb.set_spool(AR_S, 15360)
    b_sb = nc.alloc_sbuf_tensor("b_bias", [128, 16], F32)
    out_sb = nc.alloc_sbuf_tensor("b_out", [128, 2 * 512], BF16)
    PB = A["PB"]
    A["arA_old"] = ()
    A["arB_old"] = ()
    A["arC_old"] = ()
    A["xns_sb"] = None
    A["ffn_smalls"] = None

    for li in range(2):
        phase_fold_full(nc, kb, A, h64_ap, cfgs[li]["wdn"], cfgs[li]["d2scr"], f"fo{li}")
        kb.phase_fence(); kb.spool_reset()
        phase_fold_T(nc, kb, cfgs[li]["d2scr"], cfgs[li]["d2T"], f"ft{li}")
        kb.phase_fence(); kb.spool_reset()

    phase_ln(nc, kb, "l0", x_in, xn0_scr, T)
    kb.phase_fence(); kb.spool_reset()
    phase_featize(nc, kb, "z0", xn0_scr, A["xn0f"], ())
    kb.phase_fence(); kb.spool_reset()
    A["xn0f_ready"] = ()
    phase_proj16(nc, kb, A)
    kb.phase_fence(); kb.spool_reset()
    A["qkv_keys"] = ()
    A["row_keys"] = ()
    phase_scan16(nc, kb, A)
    kb.phase_fence(); kb.spool_reset()
    A["o_keys"] = ()
    A["arA_old"] = ()
    phase_epi16(nc, kb, A)
    kb.phase_fence(); kb.spool_reset()
    A["a0_keys"] = ()
    A["arA_old"] = ()
    A["arB_old"] = ()
    A["arC_old"] = ()
    phase_ffnP(nc, kb, "fa", A, A["xn0f"], A["xn0f_ready"], ffn0f, 0, False, cfgs[0], y_scr)
    phase_ffnP(nc, kb, "fb", A, A["xn0f"], A["xn0f_ready"], ffn0f, 2048, False, cfgs[0], y_scr)
    kb.phase_fence(); kb.spool_reset()
    phase_ln(nc, kb, "r1", x_in, xn1_scr, T, a_feat=A["attn0f"], f_feat=ffn0f, x1_out=x1_scr,
             a_ready=(), f_ready=())
    kb.phase_fence(); kb.spool_reset()

    A["arA_old"] = ()
    A["arB_old"] = ()
    A["arC_old"] = ()
    A["xns_sb"] = None
    A["ffn_smalls"] = None
    phase_featize(nc, kb, "b0", xn1_scr, xn1f, ())
    kb.phase_fence(); kb.spool_reset()
    xn1f_ready = ()
    proj_keys = {}
    for (wap, bap, scale, dst, tag) in ((wqT, bq, float(128 ** -0.5), Qf, "q"),
                                        (wkT, bk, 1.0, Kf, "k"),
                                        (wvT, bv, 1.0, Vf, "v")):
        kb.op("gp", lambda e, w=wap: e.dma_start(out=A["AR_B"][:].rearrange("p (k m) -> p k m", k=16),
                                                 in_=w.rearrange("(k p) m -> p k m", p=128)),
              writes=("bw",) + tuple(A["arB_old"]), dma=True)
        A["arB_old"] = ("bw",)
        kb.op("sync", lambda e, b=bap: e.dma_start(out=b_sb[:], in_=b[:, :]), writes=("bb",), dma=True)
        if scale != 1.0:
            kb.op("dve", lambda e, sc=scale: e.tensor_scalar(out=b_sb[:], in0=b_sb[:], scalar1=sc, scalar2=None, op0=ALU.mult),
                  reads=("bb",), writes=("bb",))
        pk = []
        for half in range(2):
            toff = half * 2048
            for kc in range(16):
                kb.op("sync", lambda e, kc=kc, t=toff: e.dma_start(
                    out=A["AR_A"][:, kc * 2048:(kc + 1) * 2048], in_=xn1f[kc * 128:(kc + 1) * 128, t:t + 2048]),
                      reads=xn1f_ready, writes=(f"bxn{kc}",) + tuple(A["arA_old"]), dma=True)
            BXNK = tuple(f"bxn{kc}" for kc in range(16))
            A["arA_old"] = BXNK
            for m in range(16):
                for ts_ in range(4):
                    ps = PB[0] if ts_ % 2 == 0 else PB[1]
                    psk = "ps0" if ts_ % 2 == 0 else "ps1"
                    pairs = [(A["AR_B"][:, k * 2048 + m * 128:(k * 2048 + (m + 1) * 128)],
                              A["AR_A"][:, k * 2048 + ts_ * 512: k * 2048 + (ts_ + 1) * 512]) for k in range(16)]
                    mm_group(kb, ps[:], pairs, reads=("bw",) + BXNK, writes=(psk,))
                    ob = out_sb[:, (ts_ % 2) * 512:(ts_ % 2 + 1) * 512]
                    kb.op("dve", lambda e, s=ps[:], ob=ob, m=m, sc=scale: e.tensor_scalar(
                        out=ob, in0=s, scalar1=sc, scalar2=b_sb[:, m:m + 1], op0=ALU.mult, op1=ALU.add),
                          reads=(psk, "bb"), writes=(f"bob{ts_ % 2}",))
                    wk = f"{tag}w{m}_{half}_{ts_}"
                    kb.op("sync", lambda e, m=m, ts_=ts_, t=toff, ob=ob, dst=dst: e.dma_start(
                        out=dst[m * 128:(m + 1) * 128, t + ts_ * 512:t + (ts_ + 1) * 512], in_=ob),
                          reads=(f"bob{ts_ % 2}",), writes=(wk,), dma=True)
                    pk.append(wk)
        proj_keys[tag] = tuple(pk)
    kb.phase_fence(); kb.spool_reset()
    phase_attn16(nc, kb, A, Qf, Kf, Vf, ctxf, (), (), ())
    kb.phase_fence(); kb.spool_reset()
    A["arA_old"] = ()
    A["arB_old"] = ()
    ctxs = nc.dram_tensor("ctxs_scr", [D, 2048], BF16, kind="Internal").ap()
    xn1s = nc.dram_tensor("xn1s_scr", [D, 2048], BF16, kind="Internal").ap()
    x1s = nc.dram_tensor("x1s_scr", [2048, D], F32, kind="Internal").ap()
    phase_extract_stripe(nc, kb, "exc", ctxf, ctxs, 0, 2048, BF16, "act", False)
    phase_extract_stripe(nc, kb, "exn", xn1f, xn1s, 0, 2048, BF16, "gp", False)
    phase_extract_stripe(nc, kb, "exx", x1_scr, x1s, 2048, 0, F32, "sync", True)
    kb.phase_fence(); kb.spool_reset()
    for kc in range(16):
        kb.op("sync", lambda e, kc=kc: e.dma_start(
            out=A["AR_A"][:, kc * 2048:(kc + 1) * 2048],
            in_=ctxs[kc * 128:(kc + 1) * 128, :]),
              reads=(), writes=(f"cxn{kc}",), dma=True)
    CXNK = tuple(f"cxn{kc}" for kc in range(16))
    A["arA_old"] = CXNK
    kb.op("gp", lambda e: e.dma_start(out=A["AR_B"][:].rearrange("p (k m) -> p k m", k=16),
                                      in_=woT.rearrange("(k p) m -> p k m", p=128)),
          writes=("bwo",) + tuple(A["arB_old"]), dma=True)
    A["arB_old"] = ("bwo",)
    kb.op("sync", lambda e: e.dma_start(out=b_sb[:], in_=bo[:, :]), writes=("bb",), dma=True)
    for m in range(16):
        for ts_ in range(4):
            ps = PB[0] if ts_ % 2 == 0 else PB[1]
            psk = "ps0" if ts_ % 2 == 0 else "ps1"
            pairs = [(A["AR_B"][:, k * 2048 + m * 128:(k * 2048 + (m + 1) * 128)],
                      A["AR_A"][:, k * 2048 + ts_ * 512: k * 2048 + (ts_ + 1) * 512]) for k in range(16)]
            mm_group(kb, ps[:], pairs, reads=("bwo",) + CXNK, writes=(psk,))
            ob = out_sb[:, (ts_ % 2) * 512:(ts_ % 2 + 1) * 512]
            kb.op("dve", lambda e, s=ps[:], ob=ob, m=m: e.tensor_scalar(
                out=ob, in0=s, scalar1=1.0, scalar2=b_sb[:, m:m + 1], op0=ALU.mult, op1=ALU.add),
                  reads=(psk, "bb"), writes=(f"bob{ts_ % 2}",))
            wk = f"a1w{m}_{ts_}"
            kb.op("sync", lambda e, m=m, ts_=ts_, ob=ob: e.dma_start(
                out=attn1f[m * 128:(m + 1) * 128, ts_ * 512:(ts_ + 1) * 512], in_=ob),
                  reads=(f"bob{ts_ % 2}",), writes=(wk,), dma=True)
    kb.phase_fence(); kb.spool_reset()
    A["arB_old"] = ()
    A["arA_old"] = ()
    A["arC_old"] = ()
    phase_ffnP(nc, kb, "f1", A, xn1s, (), ffn1f, 0, False, cfgs[1], y_scr)
    kb.phase_fence(); kb.spool_reset()
    phase_ln(nc, kb, "j", x1s, None, 2048, a_feat=attn1f, f_feat=ffn1f, x1_out=out_rows,
             dyn_x=False, a_ready=(), f_ready=())


def build_mega():
    return build_program(emit_mega)


def kernel(**inputs):
    CORES = list(range(NCORES))
    f32 = np.float32
    ct = np.ascontiguousarray
    xfull = ct(inputs["L0_attn_x"].astype(f32, copy=False))

    H64 = hadamard(64)
    H2048c = (hadamard(2048) * (2.0 ** -5)).astype(BF)
    H128c = (hadamard(128) * (2.0 ** -6)).astype(BF)
    UINC = np.triu(np.ones((128, 128), f32))
    I128F = np.eye(128, dtype=f32)
    ones2 = np.zeros((128, 2), f32)
    ones2[0:64, 0] = 1.0
    ones2[64:128, 1] = 1.0

    base = {
        "wqkvT": ct(inputs["L0_attn_W_qkv"].astype(f32, copy=False).T),
        "wbT": ct(inputs["L0_attn_W_b"].astype(f32, copy=False).T),
        "waT": ct(inputs["L0_attn_W_a"].astype(f32, copy=False).T),
        "convw": ct(inputs["L0_attn_conv_w"].astype(f32, copy=False)[:, 0, :]),
        "dtb": ct(inputs["L0_attn_dt_bias"].astype(f32, copy=False)[:, None]),
        "alog": ct(inputs["L0_attn_A_log"].astype(f32, copy=False)[:, None]),
        "ones2": ones2.astype(BF), "UINC": UINC, "I128F": I128F, "I128B": I128F.astype(BF),
        "ONES1": np.ones((1, 128), f32),
        "wzT": ct(inputs["L0_attn_W_z"].astype(f32, copy=False).T),
        "woutT": ct(inputs["L0_attn_W_out"].astype(f32, copy=False).T),
        "nw128": ct(np.tile(inputs["L0_attn_norm_weight"].astype(f32, copy=False), 2)[:, None]),
        "wqT": ct(inputs["L1_attn_Wq"].astype(f32, copy=False).T),
        "wkT": ct(inputs["L1_attn_Wk"].astype(f32, copy=False).T),
        "wvT": ct(inputs["L1_attn_Wv"].astype(f32, copy=False).T),
        "woT": ct(inputs["L1_attn_Wo"].astype(f32, copy=False).T),
        "bq": ct(inputs["L1_attn_bq"].astype(f32, copy=False).reshape(16, 128).T),
        "bk": ct(inputs["L1_attn_bk"].astype(f32, copy=False).reshape(16, 128).T),
        "bv": ct(inputs["L1_attn_bv"].astype(f32, copy=False).reshape(16, 128).T),
        "bo": ct(inputs["L1_attn_bo"].astype(f32, copy=False).reshape(16, 128).T),
        "H64": H64.astype(BF),
    }
    for li in range(2):
        p = "L0_ffn_" if li == 0 else "L1_ffn_"
        q = f"f{li}_"
        base[q + "h2048"] = H2048c
        base[q + "h128"] = H128c
        base[q + "suug"] = ct(inputs[p + "SU_ug"].astype(f32, copy=False).reshape(16, 128).T)
        base[q + "sudp"] = ct(inputs[p + "SU_dp"].astype(f32, copy=False).reshape(64, 128).T)
        base[q + "wsug"] = ct(inputs[p + "Wscale_ug"].astype(f32, copy=False).reshape(128, 128).T)
        base[q + "wsdp"] = ct(inputs[p + "Wscale_dp"].astype(f32, copy=False).reshape(16, 128).T)
        base[q + "wupT"] = ct(inputs[p + "W_up"].astype(f32, copy=False).T)
        base[q + "wgT"] = ct(inputs[p + "W_gate"].astype(f32, copy=False).T)
        base[q + "wdn"] = ct(inputs[p + "W_down"].astype(f32, copy=False))

    nc_m = build_mega()
    im = []
    for c in range(NCORES):
        m = dict(base)
        m["x"] = ct(xfull[c // 2])
        im.append(m)
    res = run_bass_kernel_spmd(nc_m, im, core_ids=CORES)
    out = np.zeros((B, T, D), f32)
    for c in range(NCORES):
        b, s = c // 2, c % 2
        out[b, s * 2048:(s + 1) * 2048] = res.results[c]["outp"]
    return out


# version 05
# speedup vs baseline: 1.1860x; 1.1860x over previous
import numpy as np
import ml_dtypes
import concourse.bass as bass
import concourse.mybir as mybir
from concourse.bass_utils import run_bass_kernel_spmd

F32 = mybir.dt.float32
BF16 = mybir.dt.bfloat16
AF = mybir.ActivationFunctionType
ALU = mybir.AluOpType
AX = mybir.AxisListType
BF = ml_dtypes.bfloat16

B, T, D = 4, 4096, 2048
NCORES = 8


class _Stream:
    def __init__(self, kb, name, sems):
        self.kb = kb
        self.name = name
        self.sems = sems
        self.ops = []
        self.known = {}
        self.snapshots = []

class KB:
    ENGINES = ("pe", "dve", "act", "gp", "sync")

    DMA_ROT = 6

    def __init__(self, nc):
        self.nc = nc
        self.streams = {}
        for e in self.ENGINES:
            sems = {e + "_c": 0}
            if e in ("gp", "sync", "act", "dve"):
                for r in range(self.DMA_ROT):
                    sems[f"{e}_d{r}"] = 0
            self.streams[e] = _Stream(self, e, sems)
        self._dma_idx = {e: 0 for e in self.ENGINES}
        self.last_w = {}
        self.readers = {}
        self.nops = 0

    def op(self, eng, fn, reads=(), writes=(), dma=False, label=""):
        st = self.streams[eng]
        if dma:
            r = self._dma_idx[eng] % self.DMA_ROT
            self._dma_idx[eng] += 1
            sem_name = f"{eng}_d{r}"
            inc = 16
        else:
            sem_name = eng + "_c"
            inc = 1
        waits = {}

        def add_wait(sem, val):
            if val is None:
                return
            if st.known.get(sem, 0) >= val:
                return
            if waits.get(sem, 0) < val:
                waits[sem] = val

        if dma and st.sems[sem_name] > 0:
            add_wait(sem_name, st.sems[sem_name])
        for r in reads:
            lw = self.last_w.get(r)
            if lw:
                add_wait(*lw)
        for w in writes:
            lw = self.last_w.get(w)
            if lw:
                add_wait(*lw)
            for rd in self.readers.get(w, ()):
                add_wait(*rd)
        for sem, val in waits.items():
            st.known[sem] = max(st.known.get(sem, 0), val)
            psname = sem.rsplit("_", 1)[0]
            ps = self.streams[psname]
            idx = self._idx_for(ps, sem, val)
            if idx is not None and idx < len(ps.snapshots):
                for s2, v2 in ps.snapshots[idx].items():
                    st.known[s2] = max(st.known.get(s2, 0), v2)

        st.sems[sem_name] += inc
        val_after = st.sems[sem_name]
        st.ops.append((dict(waits), fn, sem_name, inc, label))
        snap = dict(st.known)
        snap[sem_name] = val_after
        st.snapshots.append(snap)
        mark = (sem_name, val_after)
        for w in writes:
            self.last_w[w] = mark
            self.readers[w] = []
        for r in reads:
            self.readers.setdefault(r, []).append(mark)
        self.nops += 1
        return mark

    def phase_fence(self):
        targets = {}
        for e2 in self.ENGINES:
            for sn, v in self.streams[e2].sems.items():
                if v > 0:
                    targets[sn] = v
        for e in self.ENGINES:
            st = self.streams[e]
            waits = {sn: v for sn, v in targets.items() if st.known.get(sn, 0) < v}
            sem_name = e + "_c"
            st.sems[sem_name] += 1
            st.ops.append((waits, lambda eng: None, sem_name, 1, "fence"))
            for sn, v in targets.items():
                st.known[sn] = max(st.known.get(sn, 0), v)
            snap = dict(st.known)
            snap[sem_name] = st.sems[sem_name]
            st.snapshots.append(snap)
            targets[sem_name] = st.sems[sem_name]
        for e in self.ENGINES:
            st = self.streams[e]
            waits = {sn: v for sn, v in targets.items() if st.known.get(sn, 0) < v}
            sem_name = e + "_c"
            st.sems[sem_name] += 1
            st.ops.append((waits, lambda eng: None, sem_name, 1, "fence2"))
            for sn, v in targets.items():
                st.known[sn] = max(st.known.get(sn, 0), v)
            snap = dict(st.known)
            snap[sem_name] = st.sems[sem_name]
            st.snapshots.append(snap)
        self.last_w.clear()
        self.readers.clear()

    def set_spool(self, handle, cols):
        self._spool_h = handle
        self._spool_cols = cols
        self._spool_off = 0

    def spool_reset(self):
        self._spool_off = 0

    def spool(self, parts, cols, dtype, base=0):
        bcols = cols * (2 if dtype == F32 else 1)
        bcols = (bcols + 1) & ~1
        o = self._spool_off
        assert o + bcols <= self._spool_cols, f"spool overflow {o}+{bcols}"
        self._spool_off = o + bcols
        if dtype == F32:
            v = self._spool_h[:].bitcast(F32)
            return v[base:base + parts, o // 2:o // 2 + cols]
        return self._spool_h[base:base + parts, o:o + cols]

    def spool_group(self, specs):
        maxb = max(c * (2 if dt == F32 else 1) for _, c, dt in specs)
        maxb = (maxb + 1) & ~1
        o = self._spool_off
        assert o + maxb <= self._spool_cols
        self._spool_off = o + maxb
        out = []
        for i, (p, c, dt) in enumerate(specs):
            base = (0, 32, 64, 96)[i]
            if dt == F32:
                v = self._spool_h[:].bitcast(F32)
                out.append(v[base:base + p, o // 2:o // 2 + c])
            else:
                out.append(self._spool_h[base:base + p, o:o + c])
        return out

    def constcol(self, nc, value, parts=128):
        cache = self.__dict__.setdefault("_constcols", {})
        key = (value, parts)
        if key not in cache:
            t = nc.alloc_sbuf_tensor(f"cc_{len(cache)}", [parts, 1], F32)
            kk = f"__cc{len(cache)}"
            self.op("dve", lambda e, o=t[:], v=value: e.memset(o, v), writes=(kk,))
            cache[key] = (t, kk)
        return cache[key]

    def _idx_for(self, stream, sem_name, val):
        mc = stream.__dict__.setdefault("_mark_map", {})
        cts = stream.__dict__.setdefault("_mark_counts", {})
        n_done = stream.__dict__.setdefault("_mark_n", 0)
        if n_done < len(stream.ops):
            for i in range(n_done, len(stream.ops)):
                sn, inc = stream.ops[i][2], stream.ops[i][3]
                cts[sn] = cts.get(sn, 0) + inc
                mc[(sn, cts[sn])] = i
            stream._mark_n = len(stream.ops)
        return mc.get((sem_name, val))

    def build_and_emit(self, block):
        semh = {}
        for e in self.ENGINES:
            for sn in self.streams[e].sems:
                semh[sn] = self.nc.alloc_semaphore(name=sn)
        self.semh = semh
        eng_attr = {"pe": "tensor", "dve": "vector", "act": "scalar",
                    "gp": "gpsimd", "sync": "sync"}
        for e in self.ENGINES:
            st = self.streams[e]
            if not st.ops:
                continue
            def make_body(st=st):
                def body(eng):
                    for waits, fn, sem_name, inc, label in st.ops:
                        for sem, val in waits.items():
                            eng.wait_ge(semh[sem], val)
                        ins = fn(eng)
                        if ins is not None:
                            ins.then_inc(semh[sem_name], inc)
                        else:
                            eng.nop().then_inc(semh[sem_name], inc)
                return body
            getattr(block, eng_attr[e])(make_body())

    def finish(self, eng="sync"):
        st = self.streams[eng]
        targets = []
        for e in self.ENGINES:
            s2 = self.streams[e]
            for sn, v in s2.sems.items():
                if "_d" in sn and v > 0:
                    targets.append((sn, v))
        def fn(engobj):
            return None
        waits = {sn: v for sn, v in targets}
        st.sems[eng + "_c"] += 1
        st.ops.append((waits, lambda e: None, eng + "_c", 1, "finish"))
        st.snapshots.append(dict(st.known))


def build_program(emit_fn, name="prog"):
    nc = bass.Bass("TRN2", target_bir_lowering=False, debug=False)
    kb = KB(nc)
    emit_fn(nc, kb)
    kb.finish()
    with nc.Block() as block:
        kb.build_and_emit(block)
    return nc



def hadamard(n, dtype=np.float32):
    H = np.array([[1.0]], dtype=np.float64)
    while H.shape[0] < n:
        H = np.block([[H, H], [H, -H]])
    return H.astype(dtype)

_CONST_CACHE = {}

def consts():
    if _CONST_CACHE:
        return _CONST_CACHE
    c = _CONST_CACHE
    c["H2048_bf"] = (hadamard(2048) * (2.0 ** -5)).astype(BF)
    c["H128_bf"] = (hadamard(128) * (2.0 ** -6)).astype(BF)
    c["H64_bf"] = hadamard(64).astype(BF)
    c["I128_bf"] = np.eye(128, dtype=np.float32).astype(BF)
    c["LONES_f32"] = np.tril(np.ones((128, 128), np.float32))
    c["ONES2_bf"] = None
    return c



def emit_ln(nc, kb, NT, with_residual):
    x = nc.dram_tensor("x", [NT, D], F32, kind="ExternalInput").ap()
    if with_residual:
        a_in = nc.dram_tensor("a", [NT, D], BF16, kind="ExternalInput").ap()
        f_in = nc.dram_tensor("f", [NT, D], BF16, kind="ExternalInput").ap()
        x1_out = nc.dram_tensor("x1", [NT, D], F32, kind="ExternalOutput").ap()
    xn_out = nc.dram_tensor("xn", [NT, D], BF16, kind="ExternalOutput").ap()

    nt = NT // 128
    xt = nc.alloc_sbuf_tensor("xt", [128, 2 * D], F32)
    if with_residual:
        at = nc.alloc_sbuf_tensor("at", [128, 2 * D], BF16)
        ft = nc.alloc_sbuf_tensor("ft", [128, 2 * D], BF16)
    sq = nc.alloc_sbuf_tensor("sq", [128, D], BF16)
    ssum = nc.alloc_sbuf_tensor("ssum", [128, 2], F32)
    ssq = nc.alloc_sbuf_tensor("ssq", [128, 2], F32)
    mu = nc.alloc_sbuf_tensor("mu", [128, 1], F32)
    var = nc.alloc_sbuf_tensor("var", [128, 1], F32)
    rstd = nc.alloc_sbuf_tensor("rstd", [128, 1], F32)
    xnb = nc.alloc_sbuf_tensor("xnb", [128, 2 * D], BF16)

    for i in range(nt):
        pp = i % 2
        xs = xt[:, pp * D:(pp + 1) * D]
        xn_s = xnb[:, pp * D:(pp + 1) * D]
        kx = f"x{pp}"
        kb.op("sync", lambda e, o=xs, s=x[i * 128:(i + 1) * 128, :]: e.dma_start(out=o, in_=s),
              reads=(), writes=(kx,), dma=True)
        if with_residual:
            ka, kf = f"a{pp}", f"f{pp}"
            as_ = at[:, 0:D]
            fs = ft[:, 0:D]
            kb.op("sync", lambda e, o=as_, s=a_in[i * 128:(i + 1) * 128, :]: e.dma_start(out=o, in_=s),
                  writes=(ka,), dma=True)
            kb.op("sync", lambda e, o=fs, s=f_in[i * 128:(i + 1) * 128, :]: e.dma_start(out=o, in_=s),
                  writes=(kf,), dma=True)
            kb.op("dve", lambda e, o=xs, i0=xs, i1=as_: e.tensor_tensor(out=o, in0=i0, in1=i1, op=ALU.add),
                  reads=(kx, ka), writes=(kx,))
            kb.op("dve", lambda e, o=xs, i0=xs, i1=fs: e.tensor_tensor(out=o, in0=i0, in1=i1, op=ALU.add),
                  reads=(kx, kf), writes=(kx,))
            kb.op("sync", lambda e, o=x1_out[i * 128:(i + 1) * 128, :], s=xs: e.dma_start(out=o, in_=s),
                  reads=(kx,), dma=True)
        kb.op("dve", lambda e, o=ssum[:, 0:1], s=xs: e.tensor_reduce(out=o, in_=s, axis=AX.X, op=ALU.add),
              reads=(kx,), writes=("ssum",))
        kb.op("act", lambda e, o=sq[:], s=xs, a=ssq[:, 0:1]: e.activation(out=o, in_=s, func=AF.Square, accum_out=a),
              reads=(kx,), writes=("ssq", "sq"))
        kb.op("dve", lambda e, o=mu[:]: e.tensor_scalar(out=o, in0=ssum[:, 0:1], scalar1=1.0 / D, scalar2=None, op0=ALU.mult),
              reads=("ssum",), writes=("mu",))
        kb.op("dve", lambda e, o=var[:]: e.scalar_tensor_tensor(out=o, in0=mu[:], scalar=-1.0, in1=mu[:], op0=ALU.mult, op1=ALU.mult),
              reads=("mu",), writes=("var",))
        kb.op("dve", lambda e, o=var[:]: e.scalar_tensor_tensor(out=o, in0=ssq[:, 0:1], scalar=1.0 / D, in1=var[:], op0=ALU.mult, op1=ALU.add),
              reads=("ssq", "var"), writes=("var",))
        epst, epsk = kb.constcol(nc, 1e-5)
        kb.op("act", lambda e, o=rstd[:], b=epst: e.activation(out=o, in_=var[:], func=AF.Ln, bias=b[:, 0:1], scale=1.0),
              reads=("var", epsk), writes=("rstd",))
        kb.op("act", lambda e, o=rstd[:]: e.activation(out=o, in_=rstd[:], func=AF.Exp, scale=-0.5),
              reads=("rstd",), writes=("rstd",))
        kb.op("dve", lambda e, o=xn_s, s=xs: e.tensor_scalar(out=o, in0=s, scalar1=mu[:, 0:1], scalar2=rstd[:, 0:1], op0=ALU.subtract, op1=ALU.mult),
              reads=(kx, "mu", "rstd"), writes=(f"xn{pp}",))
        kb.op("sync", lambda e, o=xn_out[i * 128:(i + 1) * 128, :], s=xn_s: e.dma_start(out=o, in_=s),
              reads=(f"xn{pp}",), dma=True)


def build_ln(NT, with_residual):
    return build_program(lambda nc, kb: emit_ln(nc, kb, NT, with_residual))



def mm_group(kb, psum_ap, pairs, reads, writes, fp32=False):
    def fn(e, pairs=tuple(pairs), o=psum_ap):
        ins = None
        n = len(pairs)
        for idx, (l, r) in enumerate(pairs):
            ins = e.matmul(o, l, r, start=(idx == 0), stop=(idx == n - 1))
        return ins
    return kb.op("pe", fn, reads=tuple(reads), writes=tuple(writes))



def emit_fold(nc, kb):
    OS = D // NCORES
    h64 = nc.dram_tensor("H64", [64, 64], BF16, kind="ExternalInput").ap()
    h64_sb = nc.alloc_sbuf_tensor("h64sb", [64, 64], BF16)
    kb.op("sync", lambda e: e.dma_start(out=h64_sb[:], in_=h64[:, :]), writes=("h64",), dma=True)
    slab = nc.alloc_sbuf_tensor("slab", [64, 4 * 512], BF16)
    oslab = nc.alloc_sbuf_tensor("oslab", [64, 4 * 512], BF16)
    ps = nc.alloc_psum_tensor("psf", [64, 512], F32)
    for li in range(2):
        wd = nc.dram_tensor(f"Wdn{li}", [OS, 8192], F32, kind="ExternalInput").ap()
        d2 = nc.dram_tensor(f"D2_{li}", [OS, 8192], BF16, kind="ExternalOutput").ap()
        for og in range(OS // 4):
            pp = og % 4
            s_in = slab[:, pp * 512:(pp + 1) * 512]
            s_out = oslab[:, pp * 512:(pp + 1) * 512]
            src = wd[og * 4:(og + 1) * 4, :].rearrange("o (c i) -> c o i", i=128)
            dst = d2[og * 4:(og + 1) * 4, :].rearrange("o (c i) -> c o i", i=128)
            kin, kout = f"sl{pp}", f"so{pp}"
            kb.op("gp", lambda e, o=s_in, s=src: e.dma_start(out=o.rearrange("p (a b) -> p a b", a=4), in_=s),
                  writes=(kin,), dma=True)
            kb.op("pe", lambda e, o=ps[:], l=h64_sb[:], r=s_in: e.matmul(o, l, r, start=True, stop=True),
                  reads=(kin, "h64"), writes=("psf",))
            kb.op("act", lambda e, o=s_out, s=ps[:]: e.activation(out=o, in_=s, func=AF.Copy),
                  reads=("psf",), writes=(kout,))
            kb.op("sync", lambda e, o=dst, s=s_out: e.dma_start(out=o, in_=s.rearrange("p (a b) -> p a b", a=4)),
                  reads=(kout,), dma=True)


def build_fold():
    return build_program(emit_fold)



def emit_l0proj(nc, kb):
    TT = T
    xnf = nc.dram_tensor("xnf", [D, TT], BF16, kind="ExternalInput").ap()
    wqkvT = nc.dram_tensor("wqkvT", [D, 1024], F32, kind="ExternalInput").ap()
    wbT = nc.dram_tensor("wbT", [D, 8], F32, kind="ExternalInput").ap()
    waT = nc.dram_tensor("waT", [D, 8], F32, kind="ExternalInput").ap()
    convw = nc.dram_tensor("convw", [1024, 4], F32, kind="ExternalInput").ap()
    dtb = nc.dram_tensor("dtb", [8, 1], F32, kind="ExternalInput").ap()
    alog = nc.dram_tensor("alog", [8, 1], F32, kind="ExternalInput").ap()
    ones2 = nc.dram_tensor("ones2", [128, 2], BF16, kind="ExternalInput").ap()

    qkvf = nc.dram_tensor("qkvf", [1024, TT], BF16, kind="ExternalOutput").ap()
    grow = nc.dram_tensor("grow", [8, TT], F32, kind="ExternalOutput").ap()
    betarow = nc.dram_tensor("betarow", [8, TT], F32, kind="ExternalOutput").ap()
    lnbrow = nc.dram_tensor("lnbrow", [8, TT], F32, kind="ExternalOutput").ap()
    lnrkrow = nc.dram_tensor("lnrkrow", [4, TT], F32, kind="ExternalOutput").ap()
    lnrqrow = nc.dram_tensor("lnrqrow", [4, TT], F32, kind="ExternalOutput").ap()

    HT = TT // 2
    xn_sb = nc.alloc_sbuf_tensor("xn_sb", [128, 16 * HT], BF16)
    wq_sb = nc.alloc_sbuf_tensor("wq_sb", [128, 16 * 1024], BF16)
    wba_sb = nc.alloc_sbuf_tensor("wba_sb", [128, 16 * 16], BF16)
    qp_sb = nc.alloc_sbuf_tensor("qp_sb", [128, 8 * (HT + 3)], BF16)
    conv_sb = nc.alloc_sbuf_tensor("conv_sb", [128, HT], F32)
    qkv_sb = nc.alloc_sbuf_tensor("qkv_sb", [128, HT], BF16)
    sq_sb = nc.alloc_sbuf_tensor("sq_sb", [128, HT], BF16)
    wc_sb = nc.alloc_sbuf_tensor("wc_sb", [128, 8 * 4], F32)
    o2_sb = nc.alloc_sbuf_tensor("o2_sb", [128, 2], BF16)
    zrow_sb = nc.alloc_sbuf_tensor("zrow_sb", [8, 2 * HT], F32)
    t1_sb = nc.alloc_sbuf_tensor("t1_sb", [8, HT], F32)
    t3_sb = nc.alloc_sbuf_tensor("t3_sb", [8, HT], F32)
    t4_sb = nc.alloc_sbuf_tensor("t4_sb", [8, HT], F32)
    gout_sb = nc.alloc_sbuf_tensor("gout_sb", [8, HT], F32)
    ln_sb = nc.alloc_sbuf_tensor("ln_sb", [2, HT], F32)
    alogc = nc.alloc_sbuf_tensor("alogc", [8, 1], F32)
    dtbc = nc.alloc_sbuf_tensor("dtbc", [8, 1], F32)
    nexp_sb = nc.alloc_sbuf_tensor("nexp_sb", [8, 1], F32)

    psA = nc.alloc_psum_tensor("psA", [128, 512], F32)
    psB = nc.alloc_psum_tensor("psB", [128, 512], F32)
    psS = nc.alloc_psum_tensor("psS", [2, 2048], F32)
    psZ = nc.alloc_psum_tensor("psZ", [8, 512], F32)

    one_c, one_k = kb.constcol(nc, 1.0, parts=8)
    eps_c, eps_k = kb.constcol(nc, 1e-6, parts=2)

    kb.op("gp", lambda e: e.dma_start(out=wq_sb[:].rearrange("p (k m) -> p k m", k=16),
                                      in_=wqkvT.rearrange("(k p) m -> p k m", p=128)),
          writes=("wq",), dma=True)
    kb.op("gp", lambda e: e.dma_start(out=wba_sb[:].rearrange("p (k m) -> p k m", k=16)[:, :, 0:8],
                                      in_=wbT.rearrange("(k p) m -> p k m", p=128)),
          writes=("wba_b",), dma=True)
    kb.op("gp", lambda e: e.dma_start(out=wba_sb[:].rearrange("p (k m) -> p k m", k=16)[:, :, 8:16],
                                      in_=waT.rearrange("(k p) m -> p k m", p=128)),
          writes=("wba_a",), dma=True)
    kb.op("sync", lambda e: e.dma_start(out=wc_sb[:].rearrange("p (c f) -> p c f", c=8),
                                        in_=convw.rearrange("(c p) f -> p c f", p=128)),
          writes=("wc",), dma=True)
    kb.op("sync", lambda e: e.dma_start(out=o2_sb[:], in_=ones2[:, :]), writes=("o2",), dma=True)
    kb.op("sync", lambda e: e.dma_start(out=alogc[:], in_=alog[:, :]), writes=("alog",), dma=True)
    kb.op("sync", lambda e: e.dma_start(out=dtbc[:], in_=dtb[:, :]), writes=("dtb",), dma=True)
    kb.op("act", lambda e: e.activation(out=nexp_sb[:], in_=alogc[:], func=AF.Exp),
          reads=("alog",), writes=("nexp",))
    kb.op("dve", lambda e: e.tensor_scalar(out=nexp_sb[:], in0=nexp_sb[:], scalar1=-1.0, scalar2=None, op0=ALU.mult),
          reads=("nexp",), writes=("nexp",))
    kb.op("dve", lambda e: e.memset(qp_sb[:].rearrange("p (c f) -> p c f", c=8)[:, :, 0:3], 0.0),
          writes=("halo",))

    for half in range(2):
        toff = half * HT
        kb.op("sync", lambda e, t=toff: e.dma_start(
            out=xn_sb[:].rearrange("p (k t) -> p k t", k=16),
            in_=xnf.rearrange("(k p) t -> p k t", p=128)[:, :, t:t + HT]),
              writes=("xn",), dma=True)
        for m in range(8):
            qp_chunk = qp_sb[:, m * (HT + 3):(m + 1) * (HT + 3)]
            for ts_ in range(HT // 512):
                ps = psA if (ts_ % 2 == 0) else psB
                psk = "psA" if (ts_ % 2 == 0) else "psB"
                pairs = [(wq_sb[:, k * 1024 + m * 128:(k * 1024 + (m + 1) * 128)],
                          xn_sb[:, k * HT + ts_ * 512: k * HT + (ts_ + 1) * 512])
                         for k in range(16)]
                mm_group(kb, ps[:], pairs, reads=("wq", "xn"), writes=(psk,))
                kb.op("act", lambda e, o=qp_chunk[:, 3 + ts_ * 512: 3 + (ts_ + 1) * 512], s=ps[:]:
                      e.activation(out=o, in_=s, func=AF.Copy),
                      reads=(psk,), writes=(f"qp{m}",))
            kb.op("dve", lambda e, m=m, qp=qp_chunk: e.tensor_scalar(
                out=conv_sb[:], in0=qp[:, 0:HT], scalar1=wc_sb[:, m * 4:m * 4 + 1], scalar2=None, op0=ALU.mult),
                  reads=(f"qp{m}", "wc"), writes=("conv",))
            for tau in range(1, 4):
                kb.op("dve", lambda e, m=m, tau=tau, qp=qp_chunk: e.scalar_tensor_tensor(
                    out=conv_sb[:], in0=qp[:, tau:tau + HT], scalar=wc_sb[:, m * 4 + tau:m * 4 + tau + 1],
                    in1=conv_sb[:], op0=ALU.mult, op1=ALU.add),
                      reads=(f"qp{m}", "wc", "conv"), writes=("conv",))
            kb.op("act", lambda e: e.activation(out=qkv_sb[:], in_=conv_sb[:], func=AF.Silu),
                  reads=("conv",), writes=("qkv",))
            kb.op("sync", lambda e, m=m, t=toff: e.dma_start(out=qkvf[m * 128:(m + 1) * 128, t:t + HT], in_=qkv_sb[:]),
                  reads=("qkv",), dma=True)
            if half == 0:
                kb.op("dve", lambda e, qp=qp_chunk: e.tensor_copy(qp[:, 0:3], qp[:, HT:HT + 3]),
                      reads=(f"qp{m}",), writes=(f"qp{m}",))
            if m < 4:
                kb.op("act", lambda e: e.activation(out=sq_sb[:], in_=qkv_sb[:], func=AF.Square),
                      reads=("qkv",), writes=("sq",))
                for ts_ in range(HT // 512):
                    kb.op("pe", lambda e, o=psS[:, ts_ * 512:(ts_ + 1) * 512], l=o2_sb[:], r=sq_sb[:, ts_ * 512:(ts_ + 1) * 512]:
                          e.matmul(o, l, r, start=True, stop=True),
                          reads=("sq", "o2"), writes=("psS",))
                kb.op("act", lambda e: e.activation(out=ln_sb[:], in_=psS[0:2, 0:HT], func=AF.Ln, bias=eps_c[:, 0:1]),
                      reads=("psS", eps_k), writes=("lnv",))
                if m < 2:
                    kb.op("dve", lambda e: e.tensor_scalar(out=ln_sb[:], in0=ln_sb[:], scalar1=-0.5,
                                                           scalar2=float(-0.5 * np.log(64.0)), op0=ALU.mult, op1=ALU.add),
                          reads=("lnv",), writes=("lnv",))
                    dst = lnrqrow
                else:
                    kb.op("dve", lambda e: e.tensor_scalar(out=ln_sb[:], in0=ln_sb[:], scalar1=-0.5, scalar2=None, op0=ALU.mult),
                          reads=("lnv",), writes=("lnv",))
                    dst = lnrkrow
                r0 = (m % 2) * 2
                kb.op("sync", lambda e, d=dst, r0=r0, t=toff: e.dma_start(out=d[r0:r0 + 2, t:t + HT], in_=ln_sb[:]),
                      reads=("lnv",), dma=True)
        for which in range(2):
            zrow = zrow_sb[:, which * HT:(which + 1) * HT]
            for ts_ in range(HT // 512):
                pairs = [(wba_sb[:, k * 16 + which * 8: k * 16 + which * 8 + 8],
                          xn_sb[:, k * HT + ts_ * 512: k * HT + (ts_ + 1) * 512])
                         for k in range(16)]
                mm_group(kb, psZ[:], pairs, reads=("wba_b", "wba_a", "xn"), writes=("psZ",))
                kb.op("dve", lambda e, o=zrow[:, ts_ * 512:(ts_ + 1) * 512], s=psZ[:]: e.tensor_copy(o, s),
                      reads=("psZ",), writes=(f"z{which}",))
        zb = zrow_sb[:, 0:HT]
        za = zrow_sb[:, HT:2 * HT]
        kb.op("act", lambda e: e.activation(out=gout_sb[:], in_=zb, func=AF.Sigmoid),
              reads=("z0",), writes=("beta",))
        kb.op("sync", lambda e, t=toff: e.dma_start(out=betarow[:, t:t + HT], in_=gout_sb[:]),
              reads=("beta",), dma=True)
        kb.op("act", lambda e: e.activation(out=t1_sb[:], in_=zb, func=AF.Abs),
              reads=("z0",), writes=("t1",))
        kb.op("act", lambda e: e.activation(out=t1_sb[:], in_=t1_sb[:], func=AF.Exp, scale=-1.0),
              reads=("t1",), writes=("t1",))
        kb.op("act", lambda e: e.activation(out=t3_sb[:], in_=t1_sb[:], func=AF.Ln, bias=one_c[:, 0:1]),
              reads=("t1", one_k), writes=("t3",))
        kb.op("act", lambda e: e.activation(out=t4_sb[:], in_=zb, func=AF.Relu, scale=-1.0),
              reads=("z0",), writes=("t4",))
        kb.op("dve", lambda e: e.scalar_tensor_tensor(out=gout_sb[:], in0=t4_sb[:], scalar=-1.0, in1=t3_sb[:],
                                                      op0=ALU.mult, op1=ALU.subtract),
              reads=("t3", "t4"), writes=("lnb",))
        kb.op("sync", lambda e, t=toff: e.dma_start(out=lnbrow[:, t:t + HT], in_=gout_sb[:]),
              reads=("lnb",), dma=True)
        kb.op("act", lambda e: e.activation(out=t1_sb[:], in_=za, func=AF.Abs, bias=dtbc[:, 0:1]),
              reads=("z1", "dtb"), writes=("t1",))
        kb.op("act", lambda e: e.activation(out=t1_sb[:], in_=t1_sb[:], func=AF.Exp, scale=-1.0),
              reads=("t1",), writes=("t1",))
        kb.op("act", lambda e: e.activation(out=t3_sb[:], in_=t1_sb[:], func=AF.Ln, bias=one_c[:, 0:1]),
              reads=("t1", one_k), writes=("t3",))
        kb.op("act", lambda e: e.activation(out=t4_sb[:], in_=za, func=AF.Relu, bias=dtbc[:, 0:1]),
              reads=("z1", "dtb"), writes=("t4",))
        kb.op("dve", lambda e: e.tensor_tensor(out=t4_sb[:], in0=t4_sb[:], in1=t3_sb[:], op=ALU.add),
              reads=("t3", "t4"), writes=("t4",))
        kb.op("dve", lambda e: e.tensor_scalar(out=gout_sb[:], in0=t4_sb[:], scalar1=nexp_sb[:, 0:1], scalar2=None, op0=ALU.mult),
              reads=("t4", "nexp"), writes=("gv",))
        kb.op("sync", lambda e, t=toff: e.dma_start(out=grow[:, t:t + HT], in_=gout_sb[:]),
              reads=("gv",), dma=True)


def build_l0proj():
    return build_program(emit_l0proj)



def emit_scan(nc, kb, nchunks=T // 128):
    C = 128
    qf = nc.dram_tensor("qf", [2, 128, T], BF16, kind="ExternalInput").ap()
    kf = nc.dram_tensor("kf", [2, 128, T], BF16, kind="ExternalInput").ap()
    ktok = nc.dram_tensor("ktok", [T, 256], BF16, kind="ExternalInput").ap()
    vtok = nc.dram_tensor("vtok", [T, 512], BF16, kind="ExternalInput").ap()
    gcol = nc.dram_tensor("gcol", [T, 8], F32, kind="ExternalInput").ap()
    betacol = nc.dram_tensor("betacol", [T, 8], F32, kind="ExternalInput").ap()
    lnbcol = nc.dram_tensor("lnbcol", [T, 8], F32, kind="ExternalInput").ap()
    lnrkcol = nc.dram_tensor("lnrkcol", [T, 4], F32, kind="ExternalInput").ap()
    lnrqcol = nc.dram_tensor("lnrqcol", [T, 4], F32, kind="ExternalInput").ap()
    uinc = nc.dram_tensor("UINC", [128, 128], F32, kind="ExternalInput").ap()
    i128f = nc.dram_tensor("I128F", [128, 128], F32, kind="ExternalInput").ap()
    i128b = nc.dram_tensor("I128B", [128, 128], BF16, kind="ExternalInput").ap()
    ones1 = nc.dram_tensor("ONES1", [1, 128], F32, kind="ExternalInput").ap()
    oout = nc.dram_tensor("o", [T, 512], BF16, kind="ExternalOutput").ap()

    uinc_sb = nc.alloc_sbuf_tensor("uinc_sb", [128, 128], F32)
    i128f_sb = nc.alloc_sbuf_tensor("i128f_sb", [128, 128], F32)
    i128b_sb = nc.alloc_sbuf_tensor("i128b_sb", [128, 128], BF16)
    ones1_sb = nc.alloc_sbuf_tensor("ones1_sb", [1, 128], F32)
    for nm, dst, src in (("uinc", uinc_sb, uinc), ("i128f", i128f_sb, i128f),
                         ("i128b", i128b_sb, i128b), ("ones1", ones1_sb, ones1)):
        kb.op("sync", lambda e, d=dst, s=src: e.dma_start(out=d[:], in_=s[:, :]), writes=(nm,), dma=True)
    CONSTS = ("uinc", "i128f", "i128b", "ones1")

    S_f = nc.alloc_sbuf_tensor("S_f", [64, 8 * 64], F32)
    S_b = nc.alloc_sbuf_tensor("S_b", [128, 8 * 64], BF16)
    kb.op("dve", lambda e: e.memset(S_f[:], 0.0), writes=("S_f",))
    kb.op("dve", lambda e: e.memset(S_b[:], 0.0), writes=("S_b",))

    NB = 2
    kf_sb = nc.alloc_sbuf_tensor("kf_sb", [128, NB * 2 * 128], BF16)
    qf_sb = nc.alloc_sbuf_tensor("qf_sb", [128, NB * 2 * 128], BF16)
    kt_sb = nc.alloc_sbuf_tensor("kt_sb", [128, NB * 256], BF16)
    vt_sb = nc.alloc_sbuf_tensor("vt_sb", [128, NB * 512], BF16)
    cols_sb = nc.alloc_sbuf_tensor("cols_sb", [128, NB * 32], F32)
    cum_sb = nc.alloc_sbuf_tensor("cum_sb", [128, 8], F32)
    biasA_sb = nc.alloc_sbuf_tensor("biasA_sb", [128, 8], F32)
    rowAc_sb = nc.alloc_sbuf_tensor("rowAc_sb", [128, 8], F32)
    biasP_sb = nc.alloc_sbuf_tensor("biasP_sb", [128, 8], F32)
    rows_sb = nc.alloc_sbuf_tensor("rows_sb", [1, 4 * 128], F32)
    ccol_sb = nc.alloc_sbuf_tensor("ccol_sb", [128, 4], F32)
    arg_sb = nc.alloc_sbuf_tensor("arg_sb", [128, 128], F32)
    dmat_sb = nc.alloc_sbuf_tensor("dmat_sb", [128, 128], F32)
    araw_sb = nc.alloc_sbuf_tensor("araw_sb", [128, 128], F32)
    kq_sb = nc.alloc_sbuf_tensor("kq_sb", [128, 128], F32)
    B_sb = nc.alloc_sbuf_tensor("B_sb", [128, 128], BF16)
    Useq = [nc.alloc_sbuf_tensor(f"U{i}", [128, 128], BF16) for i in range(2)]
    Lseq = [nc.alloc_sbuf_tensor(f"L{i}", [128, 128], BF16) for i in range(2)]
    Pp_sb = nc.alloc_sbuf_tensor("Pp_sb", [128, 128], BF16)
    V_sb = [nc.alloc_sbuf_tensor(f"V{i}", [128, 64], BF16) for i in range(2)]
    R_sb = nc.alloc_sbuf_tensor("R_sb", [128, 64], BF16)
    tmp_sb = nc.alloc_sbuf_tensor("tmp_sb", [128, 64], F32)
    tmp2_sb = nc.alloc_sbuf_tensor("tmp2_sb", [128, 64], BF16)
    kw_sb = nc.alloc_sbuf_tensor("kw_sb", [128, 64], BF16)
    lamc_sb = nc.alloc_sbuf_tensor("lamc_sb", [64, 1], F32)
    o_sb = nc.alloc_sbuf_tensor("o_sb", [128, 512], BF16)

    P = [nc.alloc_psum_tensor(f"P{i}", [128, 512], F32) for i in range(8)]

    for c in range(nchunks):
        t0 = c * C
        bb = c % NB
        kfs = kf_sb[:, bb * 256:(bb + 1) * 256]
        qfs = qf_sb[:, bb * 256:(bb + 1) * 256]
        kts = kt_sb[:, bb * 256:(bb + 1) * 256]
        vts = vt_sb[:, bb * 512:(bb + 1) * 512]
        cls = cols_sb[:, bb * 32:(bb + 1) * 32]
        kL = {"kf": f"kf{bb}", "qf": f"qf{bb}", "kt": f"kt{bb}", "vt": f"vt{bb}", "cl": f"cl{bb}"}
        for jj_ in range(2):
            kb.op("sync", lambda e, o=kfs[:, jj_ * 128:(jj_ + 1) * 128], s=kf[jj_, :, t0:t0 + C]: e.dma_start(out=o, in_=s),
                  writes=(kL["kf"],), dma=True)
            kb.op("sync", lambda e, o=qfs[:, jj_ * 128:(jj_ + 1) * 128], s=qf[jj_, :, t0:t0 + C]: e.dma_start(out=o, in_=s),
                  writes=(kL["qf"],), dma=True)
        kb.op("sync", lambda e, o=kts, s=ktok[t0:t0 + C, :]: e.dma_start(out=o, in_=s),
              writes=(kL["kt"],), dma=True)
        kb.op("sync", lambda e, o=vts, s=vtok[t0:t0 + C, :]: e.dma_start(out=o, in_=s),
              writes=(kL["vt"],), dma=True)
        kb.op("sync", lambda e, o=cls[:, 0:8], s=gcol[t0:t0 + C, :]: e.dma_start(out=o, in_=s),
              writes=(kL["cl"],), dma=True)
        kb.op("sync", lambda e, o=cls[:, 8:16], s=betacol[t0:t0 + C, :]: e.dma_start(out=o, in_=s),
              writes=(kL["cl"] + "b",), dma=True)
        kb.op("sync", lambda e, o=cls[:, 16:24], s=lnbcol[t0:t0 + C, :]: e.dma_start(out=o, in_=s),
              writes=(kL["cl"] + "c",), dma=True)
        kb.op("sync", lambda e, o=cls[:, 24:28], s=lnrkcol[t0:t0 + C, :]: e.dma_start(out=o, in_=s),
              writes=(kL["cl"] + "d",), dma=True)
        kb.op("sync", lambda e, o=cls[:, 28:32], s=lnrqcol[t0:t0 + C, :]: e.dma_start(out=o, in_=s),
              writes=(kL["cl"] + "e",), dma=True)
        CLK = (kL["cl"], kL["cl"] + "b", kL["cl"] + "c", kL["cl"] + "d", kL["cl"] + "e")

        kb.op("pe", lambda e, r=cls[:, 0:8]: e.matmul(P[2][0:128, 0:8], uinc_sb[:], r, start=True, stop=True),
              reads=("uinc",) + CLK, writes=("P2",))
        kb.op("dve", lambda e: e.tensor_copy(cum_sb[:], P[2][0:128, 0:8]), reads=("P2",), writes=("cum",))
        kb.op("dve", lambda e, cl=cls: e.tensor_tensor(out=biasA_sb[:], in0=cum_sb[:], in1=cl[:, 16:24], op=ALU.add),
              reads=("cum",) + CLK, writes=("biasA",))
        kb.op("dve", lambda e, cl=cls: e.tensor_tensor(
            out=biasA_sb[:].rearrange("p (a b) -> p a b", b=2), in0=biasA_sb[:].rearrange("p (a b) -> p a b", b=2),
            in1=cl[:, 24:28].unsqueeze(2).broadcast_to((128, 4, 2)), op=ALU.add),
              reads=("biasA",) + CLK, writes=("biasA",))
        kb.op("dve", lambda e, cl=cls: e.scalar_tensor_tensor(
            out=rowAc_sb[:].rearrange("p (a b) -> p a b", b=2), in0=cum_sb[:].rearrange("p (a b) -> p a b", b=2),
            scalar=-1.0, in1=cl[:, 24:28].unsqueeze(2).broadcast_to((128, 4, 2)), op0=ALU.mult, op1=ALU.add),
              reads=("cum",) + CLK, writes=("rowAc",))
        kb.op("dve", lambda e, cl=cls: e.scalar_tensor_tensor(
            out=biasP_sb[:].rearrange("p (a b) -> p a b", b=2), in0=cum_sb[:].rearrange("p (a b) -> p a b", b=2),
            scalar=1.0, in1=cl[:, 28:32].unsqueeze(2).broadcast_to((128, 4, 2)), op0=ALU.mult, op1=ALU.add),
              reads=("cum",) + CLK, writes=("biasP",))

        for h in range(8):
            khl = h // 2
            jj = khl // 2
            pb = 64 * (khl % 2)
            kft = kfs[pb:pb + 64, jj * 128:(jj + 1) * 128]
            qft = qfs[pb:pb + 64, jj * 128:(jj + 1) * 128]
            HK = (f"h{h}",)
            if h % 2 == 0:
                kb.op("pe", lambda e, l=kft: e.matmul(P[0][:, 0:128], l, l, start=True, stop=True),
                      reads=(kL["kf"],), writes=("P0",))
                kb.op("pe", lambda e, l=kft, r=qft: e.matmul(P[1][:, 0:128], l, r, start=True, stop=True),
                      reads=(kL["kf"], kL["qf"]), writes=("P1",))
                kb.op("dve", lambda e: e.tensor_copy(araw_sb[:], P[0][:, 0:128]), reads=("P0",), writes=("araw",))
                kb.op("dve", lambda e: e.tensor_copy(kq_sb[:], P[1][:, 0:128]), reads=("P1",), writes=("kq",))
            for ri, (colt, colk) in enumerate(((rowAc_sb, "rowAc"), (biasA_sb, "biasA"),
                                               (biasP_sb, "biasP"), (cum_sb, "cum"))):
                kb.op("pe", lambda e, ct=colt, h=h, ri=ri: e.matmul(
                    P[2][0:1, 128 + ri * 128: 128 + (ri + 1) * 128] if False else P[2][0:1, 0:128],
                    ct[:, h:h + 1], i128f_sb[:], start=True, stop=True),
                      reads=(colk, "i128f"), writes=("P2",))
                kb.op("dve", lambda e, ri=ri: e.tensor_copy(rows_sb[:, ri * 128:(ri + 1) * 128], P[2][0:1, 0:128]),
                      reads=("P2",), writes=(f"rows{ri}",))
            ROWK = ("rows0", "rows1", "rows2", "rows3")
            kb.op("pe", lambda e: e.matmul(P[2][0:128, 0:1], ones1_sb[:], rows_sb[:, 3 * 128 + 127: 3 * 128 + 128],
                                           start=True, stop=True),
                  reads=("rows3", "ones1"), writes=("P2",))
            kb.op("dve", lambda e: e.tensor_copy(ccol_sb[:, 0:1], P[2][0:128, 0:1]), reads=("P2",), writes=("cC",))
            kb.op("act", lambda e, h=h: e.activation(out=ccol_sb[:, 1:2], in_=biasA_sb[:, h:h + 1], func=AF.Exp),
                  reads=("biasA",), writes=("rho",))
            kb.op("act", lambda e, h=h: e.activation(out=ccol_sb[:, 2:3], in_=biasP_sb[:, h:h + 1], func=AF.Exp),
                  reads=("biasP",), writes=("rhoP",))
            kb.op("dve", lambda e, h=h: e.tensor_tensor(out=ccol_sb[:, 3:4], in0=rowAc_sb[:, h:h + 1],
                                                        in1=ccol_sb[:, 0:1], op=ALU.add),
                  reads=("rowAc", "cC"), writes=("wc",))
            kb.op("act", lambda e: e.activation(out=ccol_sb[:, 3:4], in_=ccol_sb[:, 3:4], func=AF.Exp),
                  reads=("wc",), writes=("wc",))
            specs = (
                (0, biasA_sb, "biasA", 0, "B"),
                (1, rowAc_sb, "rowAc", 1, "U"),
                (2, rowAc_sb, "rowAc", 2, "Pp"),
            )
            for (rsl, bcol, colk, mk, what) in specs:
                kb.op("pe", lambda e, rsl=rsl: e.matmul(P[2][0:128, 0:128], ones1_sb[:],
                                                        rows_sb[:, rsl * 128:(rsl + 1) * 128], start=True, stop=True),
                      reads=(ROWK[rsl], "ones1"), writes=("P2",))
                kb.op("dve", lambda e, bc=bcol, h=h: e.tensor_scalar(
                    out=arg_sb[:], in0=P[2][0:128, 0:128], scalar1=bc[:, h:h + 1], scalar2=30.0,
                    op0=ALU.add, op1=ALU.min),
                      reads=("P2", colk), writes=("arg",))
                if mk == 0:
                    pat, cm, op = [[-1, 128]], 1, ALU.is_gt
                elif mk == 1:
                    pat, cm, op = [[1, 128]], -1, ALU.is_gt
                else:
                    pat, cm, op = [[1, 128]], -1, ALU.is_ge
                def _mask(e, pat=pat, cm=cm, op=op):
                    if not hasattr(e, "_fillreg"):
                        e._fillreg = e.to_reg(-200.0)
                    return e.affine_select(out=dmat_sb[:], in_=arg_sb[:], pattern=pat, compare_op=op,
                                           fill=e._fillreg, base=0, channel_multiplier=cm)
                kb.op("gp", _mask, reads=("arg",), writes=("dmat",))
                kb.op("act", lambda e: e.activation(out=dmat_sb[:], in_=dmat_sb[:], func=AF.Exp),
                      reads=("dmat",), writes=("dmat",))
                if what == "B":
                    kb.op("dve", lambda e: e.scalar_tensor_tensor(out=B_sb[:], in0=dmat_sb[:], scalar=-1.0,
                                                                  in1=araw_sb[:], op0=ALU.mult, op1=ALU.mult),
                          reads=("dmat", "araw"), writes=("Bm",))
                elif what == "U":
                    kb.op("dve", lambda e: e.scalar_tensor_tensor(out=Useq[0][:], in0=dmat_sb[:], scalar=-1.0,
                                                                  in1=araw_sb[:], op0=ALU.mult, op1=ALU.mult),
                          reads=("dmat", "araw"), writes=("U0",))
                else:
                    kb.op("dve", lambda e: e.scalar_tensor_tensor(out=Pp_sb[:], in0=dmat_sb[:], scalar=1.0,
                                                                  in1=kq_sb[:], op0=ALU.mult, op1=ALU.mult),
                          reads=("dmat", "kq"), writes=("Pp",))
            kb.op("pe", lambda e, l=kft, pb=pb, h=h: e.matmul(P[3][:, 0:64], l, S_b[pb:pb + 64, h * 64:(h + 1) * 64], start=True, stop=True),
                  reads=(kL["kf"], "S_b"), writes=("P3",))
            kb.op("dve", lambda e: e.tensor_scalar(out=tmp_sb[:], in0=P[3][:, 0:64], scalar1=ccol_sb[:, 1:2],
                                                   scalar2=None, op0=ALU.mult),
                  reads=("P3", "rho"), writes=("tmp",))
            kb.op("dve", lambda e, h=h, v=vts, cl=cls: e.scalar_tensor_tensor(
                out=R_sb[:], in0=v[:, h * 64:(h + 1) * 64], scalar=cl[:, 8 + h:9 + h], in1=tmp_sb[:],
                op0=ALU.mult, op1=ALU.subtract),
                  reads=(kL["vt"],) + CLK + ("tmp",), writes=("V0v",))
            Vcur, Vk = R_sb, "V0v"
            Ucur, Uk = Useq[0], "U0"
            Lcur, Lk = B_sb, "Bm"
            for s in range(7):
                Vnext = V_sb[s % 2]
                mm_group(kb, P[3][:, 0:64], [(Ucur[:], Vcur[:]), (i128b_sb[:], Vcur[:])],
                         reads=(Uk, Vk, "i128b"), writes=("P3",))
                kb.op("act", lambda e, o=Vnext: e.activation(out=o[:], in_=P[3][:, 0:64], func=AF.Copy),
                      reads=("P3",), writes=(f"V{s + 1}v",))
                if s < 6:
                    Unext = Useq[(s + 1) % 2]
                    kb.op("pe", lambda e, l=Lcur, r=Ucur: e.matmul(P[4][:, 0:128], l[:], r[:], start=True, stop=True),
                          reads=(Lk, Uk), writes=("P4",))
                    if s < 5:
                        Lnext = Lseq[(s + 1) % 2]
                        kb.op("pe", lambda e, l=Ucur, r=Lcur: e.matmul(P[5][:, 0:128], l[:], r[:], start=True, stop=True),
                              reads=(Uk, Lk), writes=("P5",))
                        kb.op("dve", lambda e, o=Lnext: e.tensor_copy(o[:], P[5][:, 0:128]),
                              reads=("P5",), writes=(f"L{s + 1}m",))
                        Lcur, Lk = Lnext, f"L{s + 1}m"
                    kb.op("act", lambda e, o=Unext: e.activation(out=o[:], in_=P[4][:, 0:128], func=AF.Copy),
                          reads=("P4",), writes=(f"U{s + 1}m",))
                    Ucur, Uk = Unext, f"U{s + 1}m"
                Vcur, Vk = Vnext, f"V{s + 1}v"
            kb.op("pe", lambda e, l=qft, pb=pb, h=h: e.matmul(P[7][:, 0:64], l, S_b[pb:pb + 64, h * 64:(h + 1) * 64], start=True, stop=True),
                  reads=(kL["qf"], "S_b"), writes=("P7",))
            kb.op("dve", lambda e: e.tensor_scalar(out=tmp2_sb[:], in0=P[7][:, 0:64], scalar1=ccol_sb[:, 2:3],
                                                   scalar2=None, op0=ALU.mult),
                  reads=("P7", "rhoP"), writes=("tmp2",))
            mm_group(kb, P[6][:, 0:64], [(Pp_sb[:], Vcur[:]), (i128b_sb[:], tmp2_sb[:])],
                     reads=("Pp", Vk, "tmp2", "i128b"), writes=("P6",))
            kb.op("act", lambda e, h=h: e.activation(out=o_sb[:, h * 64:(h + 1) * 64], in_=P[6][:, 0:64], func=AF.Copy),
                  reads=("P6",), writes=(f"osl{h}",))
            kb.op("dve", lambda e, h=h, kt=kts, khl=khl: e.tensor_scalar(out=kw_sb[:], in0=kt[:, khl * 64:(khl + 1) * 64],
                                                                scalar1=ccol_sb[:, 3:4], scalar2=None, op0=ALU.mult),
                  reads=(kL["kt"], "wc"), writes=("kw",))
            kb.op("pe", lambda e: e.matmul(P[7][0:64, 64:128], kw_sb[:], Vcur[:], start=True, stop=True),
                  reads=("kw", Vk), writes=("P7",))
            kb.op("act", lambda e: e.activation(out=lamc_sb[:], in_=ccol_sb[0:64, 0:1], func=AF.Exp),
                  reads=("cC",), writes=("lamC",))
            kb.op("dve", lambda e, h=h: e.scalar_tensor_tensor(
                out=S_f[:, h * 64:(h + 1) * 64], in0=S_f[:, h * 64:(h + 1) * 64], scalar=lamc_sb[:, 0:1],
                in1=P[7][0:64, 64:128], op0=ALU.mult, op1=ALU.add),
                  reads=("S_f", "lamC", "P7"), writes=("S_f",))
            kb.op("act", lambda e, h=h: e.activation(out=S_b[0:64, h * 64:(h + 1) * 64], in_=S_f[:, h * 64:(h + 1) * 64],
                                                     func=AF.Copy),
                  reads=("S_f",), writes=("S_b",))
            kb.op("act", lambda e, h=h: e.activation(out=S_b[64:128, h * 64:(h + 1) * 64], in_=S_f[:, h * 64:(h + 1) * 64],
                                                     func=AF.Copy),
                  reads=("S_f",), writes=("S_b",))
        kb.op("sync", lambda e, t0=t0: e.dma_start(out=oout[t0:t0 + C, :], in_=o_sb[:]),
              reads=tuple(f"osl{h}" for h in range(8)), dma=True)


def build_scan():
    return build_program(emit_scan)



def emit_epi(nc, kb, NT=2048):
    o_in = nc.dram_tensor("o_in", [NT, 1024], BF16, kind="ExternalInput").ap()
    xnf = nc.dram_tensor("xnf", [D, NT], BF16, kind="ExternalInput").ap()
    wzT = nc.dram_tensor("wzT", [D, 1024], F32, kind="ExternalInput").ap()
    woutT = nc.dram_tensor("woutT", [1024, D], F32, kind="ExternalInput").ap()
    nw128 = nc.dram_tensor("nw128", [128, 1], F32, kind="ExternalInput").ap()
    attnf = nc.dram_tensor("attnf", [D, NT], BF16, kind="ExternalOutput").ap()
    yn_scr = nc.dram_tensor("yn_scr", [NT, 1024], BF16, kind="Internal").ap()

    xn_sb = nc.alloc_sbuf_tensor("xn_sb", [128, 16 * NT], BF16)
    wz_sb = nc.alloc_sbuf_tensor("wz_sb", [128, 16 * 1024], BF16)
    wo_sb = nc.alloc_sbuf_tensor("wo_sb", [128, 8 * 2048], BF16)
    ynf_sb = nc.alloc_sbuf_tensor("ynf_sb", [128, 8 * 2048], BF16)
    nw_sb = nc.alloc_sbuf_tensor("nw_sb", [128, 1], F32)
    zs_sb = nc.alloc_sbuf_tensor("zs_sb", [128, 2 * 1024], BF16)
    ot_sb = nc.alloc_sbuf_tensor("ot_sb", [128, 2 * 1024], BF16)
    ssq_sb = nc.alloc_sbuf_tensor("ssq_sb", [128, 16], F32)
    rstd_sb = nc.alloc_sbuf_tensor("rstd_sb", [128, 16], F32)
    ynt_sb = nc.alloc_sbuf_tensor("ynt_sb", [128, 2 * 1024], BF16)
    osq_sb = nc.alloc_sbuf_tensor("osq_sb", [128, 1024], BF16)
    out_sb = nc.alloc_sbuf_tensor("out_sb", [128, 512], BF16)
    psA = nc.alloc_psum_tensor("psA", [128, 512], F32)
    psB = nc.alloc_psum_tensor("psB", [128, 512], F32)
    eps_c, eps_k = kb.constcol(nc, 1e-6)

    kb.op("sync", lambda e: e.dma_start(out=xn_sb[:].rearrange("p (k t) -> p k t", k=16),
                                        in_=xnf.rearrange("(k p) t -> p k t", p=128)),
          writes=("xn",), dma=True)
    kb.op("gp", lambda e: e.dma_start(out=wz_sb[:].rearrange("p (k m) -> p k m", k=16),
                                      in_=wzT.rearrange("(k p) m -> p k m", p=128)),
          writes=("wz",), dma=True)
    kb.op("gp", lambda e: e.dma_start(out=wo_sb[:].rearrange("p (k m) -> p k m", k=8),
                                      in_=woutT.rearrange("(k p) m -> p k m", p=128)),
          writes=("wo",), dma=True)
    kb.op("sync", lambda e: e.dma_start(out=nw_sb[:], in_=nw128[:, :]), writes=("nw",), dma=True)
    for ks in range(8):
        kb.op("dve", lambda e, ks=ks: e.tensor_scalar(out=wo_sb[:, ks * 2048:(ks + 1) * 2048],
                                                      in0=wo_sb[:, ks * 2048:(ks + 1) * 2048],
                                                      scalar1=nw_sb[:, 0:1], scalar2=None, op0=ALU.mult),
              reads=("wo", "nw"), writes=("wo",))

    nt = NT // 128
    for i in range(nt):
        pp = i % 2
        zt = zs_sb[:, pp * 1024:(pp + 1) * 1024]
        ot = ot_sb[:, pp * 1024:(pp + 1) * 1024]
        ynt = ynt_sb[:, pp * 1024:(pp + 1) * 1024]
        ko, kz, ky = f"ot{pp}", f"zs{pp}", f"ynt{pp}"
        kb.op("sync", lambda e, i=i, o=ot: e.dma_start(out=o, in_=o_in[i * 128:(i + 1) * 128, :]),
              writes=(ko,), dma=True)
        for zs_ in range(2):
            ps = psA if zs_ == 0 else psB
            psk = "psA" if zs_ == 0 else "psB"
            pairs = [(xn_sb[:, k * NT + i * 128: k * NT + (i + 1) * 128],
                      wz_sb[:, k * 1024 + zs_ * 512: k * 1024 + (zs_ + 1) * 512]) for k in range(16)]
            mm_group(kb, ps[:], pairs, reads=("xn", "wz"), writes=(psk,))
            kb.op("act", lambda e, o=zt[:, zs_ * 512:(zs_ + 1) * 512], s=ps[:]: e.activation(out=o, in_=s, func=AF.Silu),
                  reads=(psk,), writes=(kz,))
        kb.op("dve", lambda e, o=ot: e.scalar_tensor_tensor(out=osq_sb[:], in0=o, scalar=1.0, in1=o,
                                                            op0=ALU.mult, op1=ALU.mult),
              reads=(ko,), writes=("osq",))
        kb.op("dve", lambda e: e.tensor_reduce(out=ssq_sb[:], in_=osq_sb[:].rearrange("p (h d) -> p h d", d=64),
                                               axis=AX.X, op=ALU.add),
              reads=("osq",), writes=("ssq",))
        kb.op("act", lambda e: e.activation(out=rstd_sb[:], in_=ssq_sb[:], func=AF.Ln, bias=eps_c[:, 0:1], scale=1.0 / 64),
              reads=("ssq", eps_k), writes=("rstd",))
        kb.op("act", lambda e: e.activation(out=rstd_sb[:], in_=rstd_sb[:], func=AF.Exp, scale=-0.5),
              reads=("rstd",), writes=("rstd",))
        kb.op("dve", lambda e, o=ot, y=ynt: e.tensor_tensor(
            out=y.rearrange("p (h d) -> p h d", d=64), in0=o.rearrange("p (h d) -> p h d", d=64),
            in1=rstd_sb[:].unsqueeze(2).broadcast_to((128, 16, 64)), op=ALU.mult),
              reads=(ko, "rstd"), writes=(ky,))
        kb.op("dve", lambda e, y=ynt, z=zt: e.tensor_tensor(out=y, in0=y, in1=z, op=ALU.mult),
              reads=(ky, kz), writes=(ky,))
        kb.op("sync", lambda e, i=i, y=ynt: e.dma_start(out=yn_scr[i * 128:(i + 1) * 128, :], in_=y),
              reads=(ky,), writes=(f"ysc{i}",), dma=True)
    for c in range(8):
        kb.op("sync", lambda e, c=c: e.dma_start_transpose(out=ynf_sb[:, c * 2048:(c + 1) * 2048],
                                                           in_=yn_scr[:, c * 128:(c + 1) * 128]),
              reads=tuple(f"ysc{i}" for i in range(16)), writes=(f"ynf{c}",), dma=True)
    for m in range(16):
        for ts_ in range(NT // 512):
            ps = psA if ts_ % 2 == 0 else psB
            psk = "psA" if ts_ % 2 == 0 else "psB"
            pairs = [(wo_sb[:, k * 2048 + m * 128:(k * 2048 + (m + 1) * 128)],
                      ynf_sb[:, k * 2048 + ts_ * 512: k * 2048 + (ts_ + 1) * 512]) for k in range(8)]
            mm_group(kb, ps[:], pairs, reads=("wo",) + tuple(f"ynf{c}" for c in range(8)), writes=(psk,))
            kb.op("act", lambda e, s=ps[:]: e.activation(out=out_sb[:], in_=s, func=AF.Copy),
                  reads=(psk,), writes=("outp",))
            kb.op("sync", lambda e, m=m, ts_=ts_: e.dma_start(out=attnf[m * 128:(m + 1) * 128, ts_ * 512:(ts_ + 1) * 512],
                                                              in_=out_sb[:]),
                  reads=("outp",), dma=True)


def build_epi():
    return build_program(emit_epi)



def emit_ffn(nc, kb, NT=2048):
    xnf = nc.dram_tensor("xnf", [D, NT], BF16, kind="ExternalInput").ap()
    h2048 = nc.dram_tensor("H2048", [D, D], BF16, kind="ExternalInput").ap()
    h128 = nc.dram_tensor("H128", [128, 128], BF16, kind="ExternalInput").ap()
    suug = nc.dram_tensor("suug", [128, 16], F32, kind="ExternalInput").ap()
    sudp = nc.dram_tensor("sudp", [128, 64], F32, kind="ExternalInput").ap()
    wsug = nc.dram_tensor("wsug", [128, 128], F32, kind="ExternalInput").ap()
    wsdp = nc.dram_tensor("wsdp", [128, 16], F32, kind="ExternalInput").ap()
    wupT = nc.dram_tensor("wupT", [D, 8192], F32, kind="ExternalInput").ap()
    wgT = nc.dram_tensor("wgT", [D, 8192], F32, kind="ExternalInput").ap()
    d2T = nc.dram_tensor("d2T", [64, 128, D], BF16, kind="ExternalInput").ap()
    ffnf = nc.dram_tensor("ffnf", [D, NT], BF16, kind="ExternalOutput").ap()
    y_scr = nc.dram_tensor("y_scr", [8192, NT], BF16, kind="Internal").ap()

    hu_sb = nc.alloc_sbuf_tensor("hu_sb", [128, 16 * NT], BF16)
    hh_sb = nc.alloc_sbuf_tensor("hh_sb", [128, 16 * 2048], BF16)
    xns_sb = nc.alloc_sbuf_tensor("xns_sb", [128, 16 * 512], BF16)
    w_sb = nc.alloc_sbuf_tensor("w_sb", [128, 2 * 8192], BF16)
    h128_sb = nc.alloc_sbuf_tensor("h128_sb", [128, 128], BF16)
    sc_sb = nc.alloc_sbuf_tensor("sc_sb", [128, 16 + 64 + 64 + 64 + 16 + 16], F32)
    sg_sb = nc.alloc_sbuf_tensor("sg_sb", [128, 512], F32)
    yt_sb = nc.alloc_sbuf_tensor("yt_sb", [128, 2 * 512], BF16)
    out_sb = nc.alloc_sbuf_tensor("out_sb", [128, 2 * 512], BF16)
    psA = nc.alloc_psum_tensor("psA", [128, 512], F32)
    psB = nc.alloc_psum_tensor("psB", [128, 512], F32)
    psC = nc.alloc_psum_tensor("psC", [128, 512], F32)
    psD = nc.alloc_psum_tensor("psD", [128, 512], F32)

    SQH = float(2.0 ** -0.5)
    kb.op("sync", lambda e: e.dma_start(out=hh_sb[:].rearrange("p (k m) -> p k m", k=16),
                                        in_=h2048.rearrange("(k p) m -> p k m", p=128)),
          writes=tuple(f"H{k}" for k in range(16)), dma=True)
    kb.op("sync", lambda e: e.dma_start(out=h128_sb[:], in_=h128[:, :]), writes=("h128",), dma=True)
    kb.op("sync", lambda e: e.dma_start(out=sc_sb[:, 0:16], in_=suug[:, :]), writes=("suug",), dma=True)
    kb.op("sync", lambda e: e.dma_start(out=sc_sb[:, 16:80], in_=sudp[:, :]), writes=("sudp",), dma=True)
    kb.op("sync", lambda e: e.dma_start(out=sc_sb[:, 80:144], in_=wsug[:, 0:64]), writes=("wsu",), dma=True)
    kb.op("sync", lambda e: e.dma_start(out=sc_sb[:, 144:208], in_=wsug[:, 64:128]), writes=("wsg",), dma=True)
    kb.op("sync", lambda e: e.dma_start(out=sc_sb[:, 208:224], in_=wsdp[:, :]), writes=("wsd",), dma=True)
    kb.op("dve", lambda e: e.tensor_tensor(out=sc_sb[:, 80:144], in0=sc_sb[:, 80:144], in1=sc_sb[:, 16:80], op=ALU.mult),
          reads=("wsu", "sudp"), writes=("wsu",))
    kb.op("dve", lambda e: e.tensor_scalar(out=sc_sb[:, 80:144], in0=sc_sb[:, 80:144], scalar1=SQH, scalar2=None, op0=ALU.mult),
          reads=("wsu",), writes=("wsu",))
    kb.op("dve", lambda e: e.tensor_scalar(out=sc_sb[:, 144:208], in0=sc_sb[:, 144:208], scalar1=SQH, scalar2=None, op0=ALU.mult),
          reads=("wsg",), writes=("wsg",))
    kb.op("dve", lambda e: e.tensor_scalar(out=sc_sb[:, 208:224], in0=sc_sb[:, 208:224], scalar1=SQH, scalar2=None, op0=ALU.mult),
          reads=("wsd",), writes=("wsd",))

    for ts_ in range(NT // 512):
        for k in range(16):
            kxn = f"xns{k}"
            kb.op("sync", lambda e, k=k, ts_=ts_: e.dma_start(
                out=xns_sb[:, k * 512:(k + 1) * 512],
                in_=xnf[k * 128:(k + 1) * 128, ts_ * 512:(ts_ + 1) * 512]),
                  writes=(kxn,), dma=True)
            kb.op("dve", lambda e, k=k: e.tensor_scalar(out=xns_sb[:, k * 512:(k + 1) * 512],
                                                        in0=xns_sb[:, k * 512:(k + 1) * 512],
                                                        scalar1=sc_sb[:, k:k + 1], scalar2=None, op0=ALU.mult),
                  reads=(kxn, "suug"), writes=(kxn,))
        for m in range(16):
            ps = psA if m % 2 == 0 else psB
            psk = "psA" if m % 2 == 0 else "psB"
            pairs = [(hh_sb[:, k * 2048 + m * 128:(k * 2048 + (m + 1) * 128)],
                      xns_sb[:, k * 512:(k + 1) * 512]) for k in range(16)]
            mm_group(kb, ps[:], pairs, reads=tuple(f"xns{k}" for k in range(16)) + tuple(f"H{k}" for k in range(16)),
                     writes=(psk,))
            kb.op("act", lambda e, m=m, ts_=ts_, s=ps[:]: e.activation(
                out=hu_sb[:, m * NT + ts_ * 512: m * NT + (ts_ + 1) * 512], in_=s, func=AF.Copy),
                  reads=(psk,), writes=(f"h{m}",))
    HK = tuple(f"h{m}" for m in range(16))
    for io in range(64):
        wpp = io % 2
        wu = w_sb[:, wpp * 2 * 2048: wpp * 2 * 2048 + 2048]
        wg = w_sb[:, wpp * 2 * 2048 + 2048: (wpp + 1) * 2 * 2048]
        kwu, kwg = f"wu{wpp}", f"wg{wpp}"
        kb.op("gp", lambda e, io=io, o=wu: e.dma_start(
            out=o.rearrange("p (k m) -> p k m", k=16),
            in_=wupT.rearrange("(k p) m -> p k m", p=128)[:, :, io * 128:(io + 1) * 128]),
              writes=(kwu,), dma=True)
        kb.op("gp", lambda e, io=io, o=wg: e.dma_start(
            out=o.rearrange("p (k m) -> p k m", k=16),
            in_=wgT.rearrange("(k p) m -> p k m", p=128)[:, :, io * 128:(io + 1) * 128]),
              writes=(kwg,), dma=True)
        for ts_ in range(NT // 512):
            ypp = ts_ % 2
            yt = yt_sb[:, ypp * 512:(ypp + 1) * 512]
            kyt = f"yt{ypp}"
            pairs_u = [(wu[:, k * 128:(k + 1) * 128], hu_sb[:, k * NT + ts_ * 512: k * NT + (ts_ + 1) * 512])
                       for k in range(16)]
            pairs_g = [(wg[:, k * 128:(k + 1) * 128], hu_sb[:, k * NT + ts_ * 512: k * NT + (ts_ + 1) * 512])
                       for k in range(16)]
            mm_group(kb, psC[:], pairs_u, reads=(kwu,) + HK, writes=("psC",))
            mm_group(kb, psD[:], pairs_g, reads=(kwg,) + HK, writes=("psD",))
            kb.op("act", lambda e, io=io: e.activation(out=sg_sb[:], in_=psD[:], func=AF.Silu,
                                                       scale=sc_sb[:, 144 + io:145 + io]),
                  reads=("psD", "wsg"), writes=("sg",))
            kb.op("dve", lambda e, io=io, y=yt: e.scalar_tensor_tensor(
                out=y, in0=sg_sb[:], scalar=sc_sb[:, 80 + io:81 + io],
                in1=psC[:], op0=ALU.mult, op1=ALU.mult),
                  reads=("sg", "wsu", "psC"), writes=(kyt,))
            kb.op("sync", lambda e, io=io, ts_=ts_, y=yt: e.dma_start(
                out=y_scr[io * 128:(io + 1) * 128, ts_ * 512:(ts_ + 1) * 512], in_=y),
                  reads=(kyt,), writes=(f"ysc{io}_{ts_}",), dma=True)
    for ts_ in range(NT // 512):
        for c in range(64):
            ku = f"u{c}"
            khs = f"h{c // 4}"
            kb.op("sync", lambda e, c=c, ts_=ts_: e.dma_start(
                out=xns_sb[:, (c % 16) * 512:(c % 16 + 1) * 512],
                in_=y_scr[c * 128:(c + 1) * 128, ts_ * 512:(ts_ + 1) * 512]),
                  reads=(f"ysc{c}_{ts_}",), writes=(f"yl{c % 16}", f"xns{c % 16}"), dma=True)
            ps = psA if c % 2 == 0 else psB
            psk = "psA" if c % 2 == 0 else "psB"
            kb.op("pe", lambda e, c=c, s=ps: e.matmul(s[:], h128_sb[:], xns_sb[:, (c % 16) * 512:(c % 16 + 1) * 512],
                                                      start=True, stop=True),
                  reads=(f"yl{c % 16}", "h128"), writes=(psk,))
            eng = "act" if c % 2 == 0 else "dve"
            if eng == "act":
                kb.op("act", lambda e, c=c, s=ps: e.activation(out=hu_sb[:, c * 512:(c + 1) * 512], in_=s[:], func=AF.Copy),
                      reads=(psk,), writes=(ku, khs))
            else:
                kb.op("dve", lambda e, c=c, s=ps: e.tensor_copy(hu_sb[:, c * 512:(c + 1) * 512], s[:]),
                      reads=(psk,), writes=(ku, khs))
        UK = tuple(f"u{c}" for c in range(64))
        for o in range(16):
            wpp = o % 2
            dsl = w_sb[:, wpp * 8192:(wpp + 1) * 8192]
            kds = f"ds{wpp}"
            extraw = ("wu0", "wg0", "wu1", "wg1") if wpp == 0 else ()
            kb.op("gp", lambda e, o=o, dd=dsl: e.dma_start(
                out=dd.rearrange("p (c m) -> p c m", c=64),
                in_=d2T[:, :, o * 128:(o + 1) * 128].rearrange("c p m -> p c m")),
                  writes=(kds,) + extraw, dma=True)
            ps = psC if o % 2 == 0 else psD
            psk = "psC" if o % 2 == 0 else "psD"
            pairs = [(dsl[:, c * 128:(c + 1) * 128], hu_sb[:, c * 512:(c + 1) * 512]) for c in range(64)]
            mm_group(kb, ps[:], pairs, reads=(kds,) + UK, writes=(psk,))
            op2 = out_sb[:, (o % 2) * 512:(o % 2 + 1) * 512]
            kb.op("dve", lambda e, o=o, s=ps, oo=op2: e.tensor_scalar(
                out=oo, in0=s[:], scalar1=sc_sb[:, 208 + o:209 + o], scalar2=None, op0=ALU.mult),
                  reads=(psk, "wsd"), writes=(f"ob{o % 2}",))
            kb.op("sync", lambda e, o=o, ts_=ts_, oo=op2: e.dma_start(
                out=ffnf[o * 128:(o + 1) * 128, ts_ * 512:(ts_ + 1) * 512], in_=oo),
                  reads=(f"ob{o % 2}",), dma=True)


def build_ffn():
    return build_program(emit_ffn)



def emit_proj(nc, kb, projs, NT=2048):
    xnf = nc.dram_tensor("xnf", [D, NT], BF16, kind="ExternalInput").ap()
    xn_sb = nc.alloc_sbuf_tensor("xn_sb", [128, 16 * NT], BF16)
    w_sb = nc.alloc_sbuf_tensor("w_sb", [128, 16 * 2048], BF16)
    b_sb = nc.alloc_sbuf_tensor("b_sb", [128, 16], F32)
    out_sb = nc.alloc_sbuf_tensor("out_sb", [128, 2 * 512], BF16)
    psA = nc.alloc_psum_tensor("psA", [128, 512], F32)
    psB = nc.alloc_psum_tensor("psB", [128, 512], F32)
    kb.op("sync", lambda e: e.dma_start(out=xn_sb[:].rearrange("p (k t) -> p k t", k=16),
                                        in_=xnf.rearrange("(k p) t -> p k t", p=128)),
          writes=("xn",), dma=True)
    zero_c, zero_k = kb.constcol(nc, 0.0)
    for (wname, bname, scale, outname) in projs:
        wT = nc.dram_tensor(wname, [D, D], F32, kind="ExternalInput").ap()
        ov = nc.dram_tensor(outname, [D, NT], BF16, kind="ExternalOutput").ap()
        if bname:
            bv = nc.dram_tensor(bname, [128, 16], F32, kind="ExternalInput").ap()
            kb.op("sync", lambda e, bv=bv: e.dma_start(out=b_sb[:], in_=bv[:, :]), writes=("b",), dma=True)
            if scale != 1.0:
                kb.op("dve", lambda e, sc=scale: e.tensor_scalar(out=b_sb[:], in0=b_sb[:], scalar1=sc,
                                                                 scalar2=None, op0=ALU.mult),
                      reads=("b",), writes=("b",))
        kb.op("gp", lambda e, wT=wT: e.dma_start(out=w_sb[:].rearrange("p (k m) -> p k m", k=16),
                                                 in_=wT.rearrange("(k p) m -> p k m", p=128)),
              writes=("w",), dma=True)
        for m in range(16):
            for ts_ in range(NT // 512):
                ps = psA if ts_ % 2 == 0 else psB
                psk = "psA" if ts_ % 2 == 0 else "psB"
                pairs = [(w_sb[:, k * 2048 + m * 128:(k * 2048 + (m + 1) * 128)],
                          xn_sb[:, k * NT + ts_ * 512: k * NT + (ts_ + 1) * 512]) for k in range(16)]
                mm_group(kb, ps[:], pairs, reads=("w", "xn"), writes=(psk,))
                ob = out_sb[:, (ts_ % 2) * 512:(ts_ % 2 + 1) * 512]
                bias_ap = b_sb[:, m:m + 1] if bname else zero_c[:, 0:1]
                kb.op("dve", lambda e, s=ps[:], ob=ob, ba=bias_ap, sc=scale: e.tensor_scalar(
                    out=ob, in0=s, scalar1=sc, scalar2=ba, op0=ALU.mult, op1=ALU.add),
                      reads=(psk, "b" if bname else zero_k), writes=(f"ob{ts_ % 2}",))
                kb.op("sync", lambda e, m=m, ts_=ts_, ob=ob, ov=ov: e.dma_start(
                    out=ov[m * 128:(m + 1) * 128, ts_ * 512:(ts_ + 1) * 512], in_=ob),
                      reads=(f"ob{ts_ % 2}",), dma=True)


def build_qkv1():
    return build_program(lambda nc, kb: emit_proj(nc, kb, [
        ("wqT", "bq", float(128 ** -0.5), "Qf"), ("wkT", "bk", 1.0, "Kf"), ("wvT", "bv", 1.0, "Vf")]))


def build_wo():
    return build_program(lambda nc, kb: emit_proj(nc, kb, [("woT", "bo", 1.0, "attnf")]))



def emit_attn(nc, kb):
    Qf = nc.dram_tensor("Qf", [1024, T], BF16, kind="ExternalInput").ap()
    Kf = nc.dram_tensor("Kf", [1024, T], BF16, kind="ExternalInput").ap()
    Vtok = nc.dram_tensor("Vtok", [T, 1024], BF16, kind="ExternalInput").ap()
    ones1b = nc.dram_tensor("ONES1B", [1, 128], BF16, kind="ExternalInput").ap()
    onescb = nc.dram_tensor("ONESCB", [128, 1], BF16, kind="ExternalInput").ap()
    ctxf = nc.dram_tensor("ctxf", [1024, T], BF16, kind="ExternalOutput").ap()

    qh_sb = nc.alloc_sbuf_tensor("qh_sb", [128, T], BF16)
    kh_sb = nc.alloc_sbuf_tensor("kh_sb", [128, T], BF16)
    q2_sb = nc.alloc_sbuf_tensor("q2_sb", [65, T], BF16)
    k2_sb = nc.alloc_sbuf_tensor("k2_sb", [65, T], BF16)
    vh_sb = nc.alloc_sbuf_tensor("vh_sb", [128, 32 * 128], BF16)
    sqf_sb = nc.alloc_sbuf_tensor("sqf_sb", [128, T], BF16)
    row_sb = nc.alloc_sbuf_tensor("row_sb", [1, T], F32)
    mrow_sb = nc.alloc_sbuf_tensor("mrow_sb", [1, T], F32)
    kr_sb = nc.alloc_sbuf_tensor("kr_sb", [1, T], F32)
    sc1_sb = nc.alloc_sbuf_tensor("sc1_sb", [1, 4], F32)
    pt_sb = nc.alloc_sbuf_tensor("pt_sb", [128, 2 * 1024], BF16)
    invl_sb = nc.alloc_sbuf_tensor("invl_sb", [1, 1024], F32)
    invb_sb = nc.alloc_sbuf_tensor("invb_sb", [128, 1024], BF16)
    ctx_sb = nc.alloc_sbuf_tensor("ctx_sb", [128, 1024], BF16)
    o1b_sb = nc.alloc_sbuf_tensor("o1b_sb", [1, 128], BF16)
    ocb_sb = nc.alloc_sbuf_tensor("ocb_sb", [128, 1], BF16)
    psS = nc.alloc_psum_tensor("psS", [128, 1024], F32)
    psCx = nc.alloc_psum_tensor("psCx", [128, 1024], F32)
    psL = nc.alloc_psum_tensor("psL", [1, 1024], F32)
    psR = nc.alloc_psum_tensor("psR", [1, 512], F32)
    psBc = psS

    kb.op("sync", lambda e: e.dma_start(out=o1b_sb[:], in_=ones1b[:, :]), writes=("o1b",), dma=True)
    o1f_sb = nc.alloc_sbuf_tensor("o1f_sb", [1, 128], F32)
    kb.op("dve", lambda e: e.memset(o1f_sb[:], 1.0), writes=("o1f",))
    kb.op("sync", lambda e: e.dma_start(out=ocb_sb[:], in_=onescb[:, :]), writes=("ocb",), dma=True)

    for h in range(8):
        r0 = h * 128
        kb.op("sync", lambda e, r0=r0: e.dma_start(out=qh_sb[:], in_=Qf[r0:r0 + 128, :]), writes=("qh",), dma=True)
        kb.op("sync", lambda e, r0=r0: e.dma_start(out=kh_sb[:], in_=Kf[r0:r0 + 128, :]), writes=("kh",), dma=True)
        kb.op("sync", lambda e, r0=r0: e.dma_start(
            out=vh_sb[:].rearrange("p (n d) -> p n d", n=32),
            in_=Vtok.rearrange("(n p) c -> p n c", p=128)[:, :, r0:r0 + 128]),
              writes=("vh",), dma=True)
        kb.op("dve", lambda e: e.tensor_copy(k2_sb[0:64, :], kh_sb[64:128, :]), reads=("kh",), writes=("k2",))
        kb.op("dve", lambda e: e.memset(k2_sb[64:65, :], 1.0), writes=("k2o",))
        kb.op("dve", lambda e: e.tensor_copy(q2_sb[0:64, :], qh_sb[64:128, :]), reads=("qh",), writes=("q2",))
        kb.op("act", lambda e: e.activation(out=sqf_sb[:], in_=qh_sb[:], func=AF.Square), reads=("qh",), writes=("sqf",))
        for ts_ in range(T // 512):
            kb.op("pe", lambda e, ts_=ts_: e.matmul(psR[0:1, 0:512], ocb_sb[:], sqf_sb[:, ts_ * 512:(ts_ + 1) * 512],
                                                    start=True, stop=True),
                  reads=("sqf", "ocb"), writes=("psR",))
            kb.op("dve", lambda e, ts_=ts_: e.tensor_copy(row_sb[:, ts_ * 512:(ts_ + 1) * 512], psR[0:1, 0:512]),
                  reads=("psR",), writes=("qsqr",))
        kb.op("act", lambda e: e.activation(out=sqf_sb[:], in_=kh_sb[:], func=AF.Square), reads=("kh",), writes=("sqf",))
        for ts_ in range(T // 512):
            kb.op("pe", lambda e, ts_=ts_: e.matmul(psR[0:1, 0:512], ocb_sb[:], sqf_sb[:, ts_ * 512:(ts_ + 1) * 512],
                                                    start=True, stop=True),
                  reads=("sqf", "ocb"), writes=("psR",))
            kb.op("dve", lambda e, ts_=ts_: e.tensor_copy(kr_sb[:, ts_ * 512:(ts_ + 1) * 512], psR[0:1, 0:512]),
                  reads=("psR",), writes=("ksqr",))
        kb.op("dve", lambda e: e.tensor_reduce(out=sc1_sb[:, 0:1], in_=kr_sb[:], axis=AX.X, op=ALU.max),
              reads=("ksqr",), writes=("mk2",))
        kb.op("act", lambda e: e.activation(out=sc1_sb[:, 1:2], in_=sc1_sb[:, 0:1], func=AF.Ln),
              reads=("mk2",), writes=("lnmk",))
        kb.op("dve", lambda e: e.tensor_scalar(out=sc1_sb[:, 1:2], in0=sc1_sb[:, 1:2], scalar1=0.5, scalar2=0.1,
                                               op0=ALU.mult, op1=ALU.add),
              reads=("lnmk",), writes=("lnmk",))
        kb.op("act", lambda e: e.activation(out=mrow_sb[:], in_=row_sb[:], func=AF.Ln), reads=("qsqr",), writes=("mrow",))
        kb.op("act", lambda e: e.activation(out=mrow_sb[:], in_=mrow_sb[:], func=AF.Exp, scale=0.5, bias=sc1_sb[:, 1:2]),
              reads=("mrow", "lnmk"), writes=("mrow",))
        kb.op("dve", lambda e: e.tensor_scalar(out=q2_sb[64:65, :], in0=mrow_sb[:], scalar1=-1.0, scalar2=None, op0=ALU.mult),
              reads=("mrow",), writes=("q2o",))
        for sp in range(4):
            q0 = sp * 1024
            nk = (sp + 1) * 8
            for kt in range(nk):
                ppp = kt % 2
                pt = pt_sb[:, ppp * 1024:(ppp + 1) * 1024]
                kpt = f"pt{ppp}"
                for sub in range(2):
                    qs = q0 + sub * 512
                    pairs = [(kh_sb[0:64, kt * 128:(kt + 1) * 128], qh_sb[0:64, qs:qs + 512]),
                             (k2_sb[0:65, kt * 128:(kt + 1) * 128], q2_sb[0:65, qs:qs + 512])]
                    mm_group(kb, psS[:, sub * 512:(sub + 1) * 512], pairs,
                             reads=("kh", "qh", "k2", "k2o", "q2", "q2o"), writes=(f"psS{sub}",))
                kb.op("act", lambda e, pt=pt: e.activation(out=pt, in_=psS[:], func=AF.Exp),
                      reads=("psS0", "psS1"), writes=(kpt,))
                if kt * 128 >= q0:
                    def _mask2(e, pt=pt, base=q0 - kt * 128):
                        if not hasattr(e, "_fillreg0"):
                            e._fillreg0 = e.to_reg(0.0)
                        return e.affine_select(out=pt, in_=pt, pattern=[[1, 1024]], compare_op=ALU.is_ge,
                                               fill=e._fillreg0, base=base, channel_multiplier=-1)
                    kb.op("gp", _mask2, reads=(kpt,), writes=(kpt,))
                for sub in range(2):
                    kb.op("pe", lambda e, kt=kt, sub=sub, pt=pt, nk=nk: e.matmul(
                        psCx[:, sub * 512:(sub + 1) * 512], vh_sb[:, kt * 128:(kt + 1) * 128],
                        pt[:, sub * 512:(sub + 1) * 512], start=(kt == 0), stop=(kt == nk - 1)),
                          reads=("vh", kpt), writes=(f"psCx{sub}",))
                    kb.op("pe", lambda e, kt=kt, sub=sub, pt=pt, nk=nk: e.matmul(
                        psL[0:1, sub * 512:(sub + 1) * 512], ocb_sb[:],
                        pt[:, sub * 512:(sub + 1) * 512], start=(kt == 0), stop=(kt == nk - 1)),
                          reads=("ocb", kpt), writes=(f"psL{sub}",))
            kb.op("act", lambda e: e.activation(out=invl_sb[:], in_=psL[0:1, 0:1024], func=AF.Ln),
                  reads=("psL0", "psL1"), writes=("invl",))
            kb.op("act", lambda e: e.activation(out=invl_sb[:], in_=invl_sb[:], func=AF.Exp, scale=-1.0),
                  reads=("invl",), writes=("invl",))
            for sub in range(2):
                kb.op("pe", lambda e, sub=sub: e.matmul(psBc[:, sub * 512:(sub + 1) * 512], o1f_sb[:],
                                                        invl_sb[0:1, sub * 512:(sub + 1) * 512], start=True, stop=True),
                      reads=("invl", "o1f"), writes=(f"psS{sub}",))
            kb.op("dve", lambda e: e.tensor_copy(invb_sb[:], psBc[:]), reads=("psS0", "psS1"), writes=("invb",))
            kb.op("dve", lambda e: e.scalar_tensor_tensor(out=ctx_sb[:], in0=invb_sb[:], scalar=1.0, in1=psCx[:],
                                                          op0=ALU.mult, op1=ALU.mult),
                  reads=("invb", "psCx0", "psCx1"), writes=("ctx",))
            kb.op("sync", lambda e, r0=r0, q0=q0: e.dma_start(out=ctxf[r0:r0 + 128, q0:q0 + 1024], in_=ctx_sb[:]),
                  reads=("ctx",), dma=True)


def build_attn():
    return build_program(emit_attn)



def _split_tokens(arr, n=NCORES):
    flat = np.ascontiguousarray(arr.reshape(-1, arr.shape[-1]))
    return np.split(flat, n, axis=0)


def kernel(**inputs):
    NT = (B * T) // NCORES
    CORES = list(range(NCORES))
    f32 = np.float32
    xfull = np.ascontiguousarray(inputs["L0_attn_x"].astype(f32, copy=False)).reshape(-1, D)

    H64 = hadamard(64)
    H2048c = (hadamard(2048) * (2.0 ** -5)).astype(BF)
    H128c = (hadamard(128) * (2.0 ** -6)).astype(BF)
    UINC = np.triu(np.ones((128, 128), f32))
    I128F = np.eye(128, dtype=f32)
    I128B = I128F.astype(BF)
    ONES1 = np.ones((1, 128), f32)
    ONES1B = ONES1.astype(BF)
    ONESCB = np.ones((128, 1), f32).astype(BF)
    ones2 = np.zeros((128, 2), f32)
    ones2[0:64, 0] = 1.0
    ones2[64:128, 1] = 1.0
    ones2 = ones2.astype(BF)

    def ct(a):
        return np.ascontiguousarray(a)

    nc_fold = build_fold()
    OS = D // NCORES
    wd0 = inputs["L0_ffn_W_down"].astype(f32, copy=False)
    wd1 = inputs["L1_ffn_W_down"].astype(f32, copy=False)
    im = [{"H64": H64.astype(BF), "Wdn0": ct(wd0[c * OS:(c + 1) * OS]),
           "Wdn1": ct(wd1[c * OS:(c + 1) * OS])} for c in range(NCORES)]
    res = run_bass_kernel_spmd(nc_fold, im, core_ids=CORES)
    d2 = {}
    for li in range(2):
        D2 = np.concatenate([r[f"D2_{li}"] for r in res.results], axis=0)
        d2[li] = ct(D2.T.reshape(64, 128, D))

    nc_ln0 = build_ln(NT, False)
    res = run_bass_kernel_spmd(nc_ln0, [{"x": ct(s)} for s in np.split(xfull, NCORES)], core_ids=CORES)
    xn0 = np.concatenate([r["xn"] for r in res.results], axis=0)

    nc_b = build_l0proj()
    Wqkv = inputs["L0_attn_W_qkv"].astype(f32, copy=False)
    Wb_ = inputs["L0_attn_W_b"].astype(f32, copy=False)
    Wa_ = inputs["L0_attn_W_a"].astype(f32, copy=False)
    convw = inputs["L0_attn_conv_w"].astype(f32, copy=False)[:, 0, :]
    dtb = inputs["L0_attn_dt_bias"].astype(f32, copy=False)
    alog = inputs["L0_attn_A_log"].astype(f32, copy=False)
    im = []
    for c in range(NCORES):
        b, hh = c // 2, c % 2
        rows = list(range(hh * 256, (hh + 1) * 256)) + \
               list(range(512 + hh * 256, 512 + (hh + 1) * 256)) + \
               list(range(1024 + hh * 512, 1024 + (hh + 1) * 512))
        im.append({"xnf": ct(xn0[b * T:(b + 1) * T].T),
                   "wqkvT": ct(Wqkv[rows].T),
                   "wbT": ct(Wb_[hh * 8:(hh + 1) * 8].T),
                   "waT": ct(Wa_[hh * 8:(hh + 1) * 8].T),
                   "convw": ct(convw[rows]),
                   "dtb": ct(dtb[hh * 8:(hh + 1) * 8, None]),
                   "alog": ct(alog[hh * 8:(hh + 1) * 8, None]),
                   "ones2": ones2})
    res = run_bass_kernel_spmd(nc_b, im, core_ids=CORES)
    bouts = res.results

    nc_c = build_scan()
    im = []
    for c in range(NCORES):
        r = bouts[c]
        qkvf = r["qkvf"]
        im.append({"qf": ct(qkvf[0:256].reshape(2, 128, T)),
                   "kf": ct(qkvf[256:512].reshape(2, 128, T)),
                   "ktok": ct(qkvf[256:512].T), "vtok": ct(qkvf[512:1024].T),
                   "gcol": ct(r["grow"].T), "betacol": ct(r["betarow"].T),
                   "lnbcol": ct(r["lnbrow"].T), "lnrkcol": ct(r["lnrkrow"].T),
                   "lnrqcol": ct(r["lnrqrow"].T),
                   "UINC": UINC, "I128F": I128F, "I128B": I128B, "ONES1": ONES1})
    res = run_bass_kernel_spmd(nc_c, im, core_ids=CORES)
    o_glob = np.zeros((B, T, 16, 64), dtype=BF)
    for c in range(NCORES):
        b, hh = c // 2, c % 2
        o_glob[b, :, hh * 8:(hh + 1) * 8] = res.results[c]["o"].reshape(T, 8, 64)
    o_flat = o_glob.reshape(-1, 1024)

    nc_d = build_epi()
    Wz = inputs["L0_attn_W_z"].astype(f32, copy=False)
    Wout = inputs["L0_attn_W_out"].astype(f32, copy=False)
    nw = inputs["L0_attn_norm_weight"].astype(f32, copy=False)
    nw128 = ct(np.tile(nw, 2)[:, None])
    im = [{"o_in": ct(o_flat[c * NT:(c + 1) * NT]),
           "xnf": ct(xn0[c * NT:(c + 1) * NT].T),
           "wzT": ct(Wz.T), "woutT": ct(Wout.T), "nw128": nw128} for c in range(NCORES)]
    res = run_bass_kernel_spmd(nc_d, im, core_ids=CORES)
    attn0 = np.concatenate([ct(r["attnf"].T) for r in res.results], axis=0)

    nc_ffn = build_ffn()

    def run_ffn(xn, li):
        p = "L0_ffn_" if li == 0 else "L1_ffn_"
        im = [{"xnf": ct(xn[c * NT:(c + 1) * NT].T),
               "H2048": H2048c, "H128": H128c,
               "suug": ct(inputs[p + "SU_ug"].astype(f32, copy=False).reshape(16, 128).T),
               "sudp": ct(inputs[p + "SU_dp"].astype(f32, copy=False).reshape(64, 128).T),
               "wsug": ct(inputs[p + "Wscale_ug"].astype(f32, copy=False).reshape(128, 128).T),
               "wsdp": ct(inputs[p + "Wscale_dp"].astype(f32, copy=False).reshape(16, 128).T),
               "wupT": ct(inputs[p + "W_up"].astype(f32, copy=False).T),
               "wgT": ct(inputs[p + "W_gate"].astype(f32, copy=False).T),
               "d2T": d2[li]} for c in range(NCORES)]
        res = run_bass_kernel_spmd(nc_ffn, im, core_ids=CORES)
        return np.concatenate([ct(r["ffnf"].T) for r in res.results], axis=0)

    ffn0 = run_ffn(xn0, 0)

    nc_lnr = build_ln(NT, True)
    im = [{"x": ct(xfull[c * NT:(c + 1) * NT]), "a": ct(attn0[c * NT:(c + 1) * NT]),
           "f": ct(ffn0[c * NT:(c + 1) * NT])} for c in range(NCORES)]
    res = run_bass_kernel_spmd(nc_lnr, im, core_ids=CORES)
    x1 = np.concatenate([r["x1"] for r in res.results], axis=0)
    xn1 = np.concatenate([r["xn"] for r in res.results], axis=0)

    nc_g = build_qkv1()
    wq = ct(inputs["L1_attn_Wq"].astype(f32, copy=False).T)
    wk = ct(inputs["L1_attn_Wk"].astype(f32, copy=False).T)
    wv = ct(inputs["L1_attn_Wv"].astype(f32, copy=False).T)
    bq = ct(inputs["L1_attn_bq"].astype(f32, copy=False).reshape(16, 128).T)
    bk = ct(inputs["L1_attn_bk"].astype(f32, copy=False).reshape(16, 128).T)
    bv = ct(inputs["L1_attn_bv"].astype(f32, copy=False).reshape(16, 128).T)
    im = [{"xnf": ct(xn1[c * NT:(c + 1) * NT].T), "wqT": wq, "wkT": wk, "wvT": wv,
           "bq": bq, "bk": bk, "bv": bv} for c in range(NCORES)]
    res = run_bass_kernel_spmd(nc_g, im, core_ids=CORES)
    Qf = np.concatenate([r["Qf"] for r in res.results], axis=1)
    Kf = np.concatenate([r["Kf"] for r in res.results], axis=1)
    Vf = np.concatenate([r["Vf"] for r in res.results], axis=1)

    nc_h = build_attn()
    im = []
    for c in range(NCORES):
        b, hh = c // 2, c % 2
        rr = slice(hh * 1024, (hh + 1) * 1024)
        cc = slice(b * T, (b + 1) * T)
        im.append({"Qf": ct(Qf[rr, cc]), "Kf": ct(Kf[rr, cc]),
                   "Vtok": ct(Vf[rr, cc].T), "ONES1B": ONES1B, "ONESCB": ONESCB})
    res = run_bass_kernel_spmd(nc_h, im, core_ids=CORES)
    ctxf = np.zeros((2048, B * T), dtype=BF)
    for c in range(NCORES):
        b, hh = c // 2, c % 2
        ctxf[hh * 1024:(hh + 1) * 1024, b * T:(b + 1) * T] = res.results[c]["ctxf"]

    nc_i = build_wo()
    wo = ct(inputs["L1_attn_Wo"].astype(f32, copy=False).T)
    bo = ct(inputs["L1_attn_bo"].astype(f32, copy=False).reshape(16, 128).T)
    im = [{"xnf": ct(ctxf[:, c * NT:(c + 1) * NT]), "woT": wo, "bo": bo} for c in range(NCORES)]
    res = run_bass_kernel_spmd(nc_i, im, core_ids=CORES)
    attn1 = np.concatenate([ct(r["attnf"].T) for r in res.results], axis=0)

    ffn1 = run_ffn(xn1, 1)

    im = [{"x": ct(x1[c * NT:(c + 1) * NT]), "a": ct(attn1[c * NT:(c + 1) * NT]),
           "f": ct(ffn1[c * NT:(c + 1) * NT])} for c in range(NCORES)]
    res = run_bass_kernel_spmd(nc_lnr, im, core_ids=CORES)
    x2 = np.concatenate([r["x1"] for r in res.results], axis=0)
    return x2.reshape(B, T, D)


if __name__ == "__main__":
    pass

def _dyn2048(e, extra=0):
    assert extra == 0
    if not hasattr(e, "_dynbase"):
        pid = e.partition_id()
        e._dynbase = e.compute_val((pid % 2) * 2048)
    return e._dynbase


def phase_ln(nc, kb, pre, x_ap, xn_out, NTtok, a_feat=None, f_feat=None, x1_out=None,
             dyn_x=False, a_ready=(), f_ready=(), x_ready=(), xn_keys=None):
    nt = NTtok // 128
    xt = kb.spool(128, D, F32)
    at = kb.spool(128, D, BF16) if a_feat is not None else None
    ft = kb.spool(128, D, BF16) if f_feat is not None else None
    sq = kb.spool(128, D, BF16)
    stats = kb.spool(128, 8, F32)
    xnb = kb.spool(128, D, BF16)
    epst, epsk = kb.constcol(nc, 1e-5)
    out_keys = []
    for i in range(nt):
        pp = 0
        xs = xt[:, 0:D]
        xn_s = xnb[:, 0:D]
        kx = pre + f"x{pp}"
        if dyn_x:
            kb.op("sync", lambda e, o=xs, i=i: e.dma_start(
                out=o, in_=x_ap[i * 128:i * 128 + 2176, :][bass.ds(_dyn2048(e), 128), :]),
                  reads=tuple(x_ready), writes=(kx,), dma=True)
        else:
            kb.op("sync", lambda e, o=xs, i=i: e.dma_start(out=o, in_=x_ap[i * 128:(i + 1) * 128, :]),
                  reads=tuple(x_ready), writes=(kx,), dma=True)
        if a_feat is not None:
            ka, kf = pre + f"a{pp}", pre + f"f{pp}"
            as_ = at[:, 0:D]
            fs = ft[:, 0:D]
            kb.op("sync", lambda e, o=as_, i=i: e.dma_start_transpose(out=o, in_=a_feat[:, i * 128:(i + 1) * 128]),
                  reads=tuple(a_ready), writes=(ka,), dma=True)
            kb.op("sync", lambda e, o=fs, i=i: e.dma_start_transpose(out=o, in_=f_feat[:, i * 128:(i + 1) * 128]),
                  reads=tuple(f_ready), writes=(kf,), dma=True)
            kb.op("dve", lambda e, o=xs, i1=as_: e.tensor_tensor(out=o, in0=o, in1=i1, op=ALU.add),
                  reads=(kx, ka), writes=(kx,))
            kb.op("dve", lambda e, o=xs, i1=fs: e.tensor_tensor(out=o, in0=o, in1=i1, op=ALU.add),
                  reads=(kx, kf), writes=(kx,))
            if x1_out is not None:
                kb.op("sync", lambda e, s=xs, i=i: e.dma_start(out=x1_out[i * 128:(i + 1) * 128, :], in_=s),
                      reads=(kx,), writes=(pre + f"x1w{i}",), dma=True)
                out_keys.append(pre + f"x1w{i}")
        if xn_out is None:
            continue
        kb.op("dve", lambda e, s=xs: e.tensor_reduce(out=stats[:, 0:1], in_=s, axis=AX.X, op=ALU.add),
              reads=(kx,), writes=(pre + "ss",))
        kb.op("act", lambda e, s=xs: e.activation(out=sq[:], in_=s, func=AF.Square, accum_out=stats[:, 1:2]),
              reads=(kx,), writes=(pre + "sq2",))
        kb.op("dve", lambda e: e.tensor_scalar(out=stats[:, 2:3], in0=stats[:, 0:1], scalar1=1.0 / D, scalar2=None, op0=ALU.mult),
              reads=(pre + "ss",), writes=(pre + "mu",))
        kb.op("dve", lambda e: e.scalar_tensor_tensor(out=stats[:, 3:4], in0=stats[:, 2:3], scalar=-1.0, in1=stats[:, 2:3],
                                                      op0=ALU.mult, op1=ALU.mult),
              reads=(pre + "mu",), writes=(pre + "var",))
        kb.op("dve", lambda e: e.scalar_tensor_tensor(out=stats[:, 3:4], in0=stats[:, 1:2], scalar=1.0 / D, in1=stats[:, 3:4],
                                                      op0=ALU.mult, op1=ALU.add),
              reads=(pre + "sq2", pre + "var"), writes=(pre + "var",))
        kb.op("act", lambda e, b=epst: e.activation(out=stats[:, 4:5], in_=stats[:, 3:4], func=AF.Ln, bias=b[:, 0:1]),
              reads=(pre + "var", epsk), writes=(pre + "rs",))
        kb.op("act", lambda e: e.activation(out=stats[:, 4:5], in_=stats[:, 4:5], func=AF.Exp, scale=-0.5),
              reads=(pre + "rs",), writes=(pre + "rs",))
        kb.op("dve", lambda e, s=xs, o=xn_s: e.tensor_scalar(out=o, in0=s, scalar1=stats[:, 2:3], scalar2=stats[:, 4:5],
                                                             op0=ALU.subtract, op1=ALU.mult),
              reads=(kx, pre + "mu", pre + "rs"), writes=(pre + f"xn{pp}",))
        wk = pre + f"xnw{i}"
        kb.op("sync", lambda e, s=xn_s, i=i: e.dma_start(out=xn_out[i * 128:(i + 1) * 128, :], in_=s),
              reads=(pre + f"xn{pp}",), writes=(wk,), dma=True)
        if xn_keys is not None:
            xn_keys.append(wk)
    return out_keys


def phase_extract_stripe(nc, kb, pre, src, dst, rows, cols, dt, eng, by_rows):
    stg = kb.spool(128, cols if not by_rows else src.shape[1], dt)
    keys = []
    for i in range(rows // 128 if by_rows else 16):
        if by_rows:
            kb.op(eng, lambda e, i=i: e.dma_start(out=stg[:, 0:src.shape[1]],
                                                  in_=src[i * 128:i * 128 + 2176, :][bass.ds(_dyn2048(e), 128), :]),
                  writes=(pre + "st",), dma=True)
            wk = pre + f"w{i}"
            kb.op(eng, lambda e, i=i: e.dma_start(out=dst[i * 128:(i + 1) * 128, :], in_=stg[:, 0:src.shape[1]]),
                  reads=(pre + "st",), writes=(wk,), dma=True)
        else:
            kb.op(eng, lambda e, i=i: e.dma_start(out=stg[:, 0:cols],
                                                  in_=src[i * 128:(i + 1) * 128, bass.ds(_dyn2048(e), cols)]),
                  writes=(pre + "st",), dma=True)
            wk = pre + f"w{i}"
            kb.op(eng, lambda e, i=i: e.dma_start(out=dst[i * 128:(i + 1) * 128, :], in_=stg[:, 0:cols]),
                  reads=(pre + "st",), writes=(wk,), dma=True)
        keys.append(wk)
    return keys


def phase_featize(nc, kb, pre, tok_scr, feat_scr, ready, Ttok=T):
    stg = kb.spool(128, Ttok, BF16)
    keys = []
    for kc in range(16):
        kb.op("sync", lambda e, kc=kc: e.dma_start_transpose(out=stg[:], in_=tok_scr[:, kc * 128:(kc + 1) * 128]),
              reads=tuple(ready), writes=(pre + "fzs",), dma=True)
        wk = pre + f"fz{kc}"
        kb.op("sync", lambda e, kc=kc: e.dma_start(out=feat_scr[kc * 128:(kc + 1) * 128, :], in_=stg[:]),
              reads=(pre + "fzs",), writes=(wk,), dma=True)
        keys.append(wk)
    return keys


def phase_proj16(nc, kb, A):
    HT = T // 2
    xn_sb, AR_B = A["AR_A"], A["AR_B"]
    PB = A["PB"]
    qp_sb = kb.spool(128, 2052, BF16)
    halo_sb = kb.spool(128, 16 * 3, BF16)
    conv_sb = kb.spool(128, 2048, F32)
    qkv_sb = kb.spool(128, 2048, BF16)
    sq_sb = conv_sb.bitcast(BF16)[:, 0:2048]
    wc_sb = kb.spool(128, 16 * 4, F32)
    o2_sb = kb.spool(128, 2, BF16)
    wba_sb = kb.spool(128, 16 * 32, BF16)
    ARCf = A["AR_C"].bitcast(F32)
    zrow_sb = ARCf[0:16, 0:2 * HT]
    ln_sb = ARCf[32:34, 0:HT]
    t1_sb = ARCf[0:16, 4096:4096 + HT]
    t3_sb = ARCf[32:48, 4096:4096 + HT]
    t4_sb = ARCf[32:48, 6144:6144 + HT]
    gout_sb = ARCf[0:16, 6144:6144 + HT]
    gout2_sb = ARCf[32:48, 2048:4096]
    alogc = kb.spool(16, 1, F32)
    dtbc = kb.spool(16, 1, F32)
    nexp_sb = kb.spool(16, 1, F32, base=32)
    one16, one16k = kb.constcol(nc, 1.0, parts=16)
    eps2, eps2k = kb.constcol(nc, 1e-6, parts=2)
    psA, psB = PB[0], PB[1]
    psS = A["PS_ALL"][0:2, 1024:3072]
    PSSK = ("ps2", "ps3", "ps4", "ps5")
    psZ = PB[6][0:16, :]

    kb.op("gp", lambda e: e.dma_start(out=AR_B[:].rearrange("p (k m) -> p k m", k=16),
                                      in_=A["wqkvT"].rearrange("(k p) m -> p k m", p=128)),
          writes=("wq",) + tuple(A["arB_old"]), dma=True)
    kb.op("gp", lambda e: e.dma_start(out=wba_sb[:].rearrange("p (k m) -> p k m", k=16)[:, :, 0:16],
                                      in_=A["wbT"].rearrange("(k p) m -> p k m", p=128)),
          writes=("wbab",), dma=True)
    kb.op("gp", lambda e: e.dma_start(out=wba_sb[:].rearrange("p (k m) -> p k m", k=16)[:, :, 16:32],
                                      in_=A["waT"].rearrange("(k p) m -> p k m", p=128)),
          writes=("wbaa",), dma=True)
    kb.op("sync", lambda e: e.dma_start(out=wc_sb[:].rearrange("p (c f) -> p c f", c=16),
                                        in_=A["convw"].rearrange("(c p) f -> p c f", p=128)),
          writes=("wc",), dma=True)
    kb.op("sync", lambda e: e.dma_start(out=o2_sb[:], in_=A["ones2"][:, :]), writes=("o2",), dma=True)
    kb.op("sync", lambda e: e.dma_start(out=alogc[:], in_=A["alog"][:, :]), writes=("alogv",), dma=True)
    kb.op("sync", lambda e: e.dma_start(out=dtbc[:], in_=A["dtb"][:, :]), writes=("dtbc",), dma=True)
    kb.op("act", lambda e: e.activation(out=nexp_sb[:], in_=alogc[:], func=AF.Exp),
          reads=("alogv",), writes=("nexp",))
    kb.op("dve", lambda e: e.tensor_scalar(out=nexp_sb[:], in0=nexp_sb[:], scalar1=-1.0, scalar2=None, op0=ALU.mult),
          reads=("nexp",), writes=("nexp",))
    kb.op("dve", lambda e: e.memset(halo_sb[:], 0.0), writes=("halo",))
    qkv_keys, row_keys = [], []
    for half in range(2):
        toff = half * HT
        for kc in range(16):
            kb.op("sync", lambda e, kc=kc, t=toff: e.dma_start(
                out=xn_sb[:, kc * HT:(kc + 1) * HT], in_=A["xn0f"][kc * 128:(kc + 1) * 128, t:t + HT]),
                  reads=tuple(A["xn0f_ready"]), writes=(f"xn{kc}",) + tuple(A["arA_old"]), dma=True)
        XNK = tuple(f"xn{kc}" for kc in range(16))
        A["XNK"] = XNK
        for m in range(16):
            qp_chunk = qp_sb[:, 0:2051]
            kqp = "qp0"
            for ts_ in range(HT // 512):
                ps = psA if (ts_ % 2 == 0) else psB
                psk = "ps0" if (ts_ % 2 == 0) else "ps1"
                pairs = [(AR_B[:, k * 2048 + m * 128:(k * 2048 + (m + 1) * 128)],
                          xn_sb[:, k * HT + ts_ * 512: k * HT + (ts_ + 1) * 512])
                         for k in range(16)]
                mm_group(kb, ps[:], pairs, reads=("wq",) + XNK, writes=(psk,))
                kb.op("act", lambda e, o=qp_chunk[:, 3 + ts_ * 512: 3 + (ts_ + 1) * 512], s=ps[:]:
                      e.activation(out=o, in_=s, func=AF.Copy),
                      reads=(psk,), writes=(kqp,))
            kb.op("dve", lambda e, m=m, qp=qp_chunk: e.tensor_copy(qp[:, 0:3], halo_sb[:, m * 3:(m + 1) * 3]),
                  reads=("halo", kqp), writes=(kqp,))
            for tau in range(4):
                if tau == 0:
                    kb.op("dve", lambda e, m=m, qp=qp_chunk: e.tensor_scalar(
                        out=conv_sb[:], in0=qp[:, 0:2048], scalar1=wc_sb[:, m * 4:m * 4 + 1], scalar2=None, op0=ALU.mult),
                          reads=(kqp, "wc"), writes=("conv",))
                else:
                    kb.op("dve", lambda e, m=m, tau=tau, qp=qp_chunk: e.scalar_tensor_tensor(
                        out=conv_sb[:], in0=qp[:, tau:tau + 2048], scalar=wc_sb[:, m * 4 + tau:m * 4 + tau + 1],
                        in1=conv_sb[:], op0=ALU.mult, op1=ALU.add),
                          reads=(kqp, "wc", "conv"), writes=("conv",))
            if half == 0:
                kb.op("dve", lambda e, m=m, qp=qp_chunk: e.tensor_copy(halo_sb[:, m * 3:(m + 1) * 3], qp[:, 2048:2051]),
                      reads=(kqp,), writes=("halo",))
            kb.op("act", lambda e: e.activation(out=qkv_sb[:], in_=conv_sb[:], func=AF.Silu),
                  reads=("conv",), writes=("qkvv",))
            wk = f"qkvw{m}_{half}"
            kb.op("sync", lambda e, m=m, t=toff: e.dma_start(out=A["qkvf_scr"][m * 128:(m + 1) * 128, t:t + 2048], in_=qkv_sb[:]),
                  reads=("qkvv",), writes=(wk,), dma=True)
            qkv_keys.append(wk)
            if m < 8:
                kb.op("act", lambda e: e.activation(out=sq_sb[:], in_=qkv_sb[:], func=AF.Square),
                      reads=("qkvv",), writes=("sqv", "conv"))
                for ts_ in range(4):
                    kb.op("pe", lambda e, ts_=ts_: e.matmul(psS[:, ts_ * 512:(ts_ + 1) * 512], o2_sb[:],
                                                            sq_sb[:, ts_ * 512:(ts_ + 1) * 512], start=True, stop=True),
                          reads=("sqv", "o2"), writes=PSSK)
                kb.op("act", lambda e, b=eps2: e.activation(out=ln_sb[:], in_=psS[0:2, 0:2048], func=AF.Ln, bias=b[:, 0:1]),
                      reads=PSSK + (eps2k,), writes=("lnv",))
                if m < 4:
                    kb.op("dve", lambda e: e.tensor_scalar(out=ln_sb[:], in0=ln_sb[:], scalar1=-0.5,
                                                           scalar2=float(-0.5 * np.log(64.0)), op0=ALU.mult, op1=ALU.add),
                          reads=("lnv",), writes=("lnv",))
                    r0 = 56 + m * 2
                else:
                    kb.op("dve", lambda e: e.tensor_scalar(out=ln_sb[:], in0=ln_sb[:], scalar1=-0.5, scalar2=None, op0=ALU.mult),
                          reads=("lnv",), writes=("lnv",))
                    r0 = 48 + (m - 4) * 2
                wk2 = f"rw{r0}_{half}"
                kb.op("sync", lambda e, r0=r0, t=toff: e.dma_start(out=A["rows_scr"][r0:r0 + 2, t:t + 2048], in_=ln_sb[:]),
                      reads=("lnv",), writes=(wk2,), dma=True)
                row_keys.append(wk2)
        for which in range(2):
            zrow = zrow_sb[:, which * HT:(which + 1) * HT]
            for ts_ in range(4):
                pairs = [(wba_sb[:, k * 32 + which * 16: k * 32 + which * 16 + 16],
                          xn_sb[:, k * HT + ts_ * 512: k * HT + (ts_ + 1) * 512])
                         for k in range(16)]
                mm_group(kb, psZ[:], pairs, reads=("wbab", "wbaa") + XNK, writes=("ps6",))
                kb.op("dve", lambda e, o=zrow[:, ts_ * 512:(ts_ + 1) * 512]: e.tensor_copy(o, psZ[:]),
                      reads=("ps6",), writes=(f"z{which}",))
        zb = zrow_sb[:, 0:HT]
        za = zrow_sb[:, HT:2 * HT]
        kb.op("act", lambda e: e.activation(out=gout_sb[:], in_=zb, func=AF.Sigmoid),
              reads=("z0",), writes=("betav",))
        wk = f"rwb_{half}"
        kb.op("sync", lambda e, t=toff: e.dma_start(out=A["rows_scr"][16:32, t:t + 2048], in_=gout_sb[:]),
              reads=("betav",), writes=(wk,), dma=True)
        row_keys.append(wk)
        kb.op("act", lambda e: e.activation(out=t1_sb[:], in_=zb, func=AF.Abs),
              reads=("z0",), writes=("pt1",))
        kb.op("act", lambda e: e.activation(out=t1_sb[:], in_=t1_sb[:], func=AF.Exp, scale=-1.0),
              reads=("pt1",), writes=("pt1",))
        kb.op("act", lambda e, b=one16: e.activation(out=t3_sb[:], in_=t1_sb[:], func=AF.Ln, bias=b[:, 0:1]),
              reads=("pt1", one16k), writes=("pt3",))
        kb.op("act", lambda e: e.activation(out=t4_sb[:], in_=zb, func=AF.Relu, scale=-1.0),
              reads=("z0",), writes=("pt4",))
        kb.op("dve", lambda e: e.scalar_tensor_tensor(out=gout2_sb[:], in0=t4_sb[:], scalar=-1.0, in1=t3_sb[:],
                                                      op0=ALU.mult, op1=ALU.subtract),
              reads=("pt3", "pt4"), writes=("lnbv",))
        wk = f"rwl_{half}"
        kb.op("sync", lambda e, t=toff: e.dma_start(out=A["rows_scr"][32:48, t:t + 2048], in_=gout2_sb[:]),
              reads=("lnbv",), writes=(wk,), dma=True)
        row_keys.append(wk)
        kb.op("act", lambda e: e.activation(out=t1_sb[:], in_=za, func=AF.Abs, bias=dtbc[:, 0:1]),
              reads=("z1", "dtbc"), writes=("pt1",))
        kb.op("act", lambda e: e.activation(out=t1_sb[:], in_=t1_sb[:], func=AF.Exp, scale=-1.0),
              reads=("pt1",), writes=("pt1",))
        kb.op("act", lambda e, b=one16: e.activation(out=t3_sb[:], in_=t1_sb[:], func=AF.Ln, bias=b[:, 0:1]),
              reads=("pt1", one16k), writes=("pt3",))
        kb.op("act", lambda e: e.activation(out=t4_sb[:], in_=za, func=AF.Relu, bias=dtbc[:, 0:1]),
              reads=("z1", "dtbc"), writes=("pt4",))
        kb.op("dve", lambda e: e.tensor_tensor(out=t4_sb[:], in0=t4_sb[:], in1=t3_sb[:], op=ALU.add),
              reads=("pt3", "pt4"), writes=("pt4",))
        kb.op("dve", lambda e: e.tensor_scalar(out=gout2_sb[:], in0=t4_sb[:], scalar1=nexp_sb[:, 0:1], scalar2=None, op0=ALU.mult),
              reads=("pt4", "nexp"), writes=("gv", "lnbv"))
        wk = f"rwg_{half}"
        kb.op("sync", lambda e, t=toff: e.dma_start(out=A["rows_scr"][0:16, t:t + 2048], in_=gout2_sb[:]),
              reads=("gv",), writes=(wk,), dma=True)
        row_keys.append(wk)
    A["qkv_keys"] = tuple(qkv_keys)
    A["row_keys"] = tuple(row_keys)
    A["arA_old"] = tuple(f"xn{kc}" for kc in range(16))
    A["arB_old"] = ("wq",)


def phase_scan16(nc, kb, A):
    C = 128
    PB = A["PB"]
    qkvf = A["qkvf_scr"]
    rows = A["rows_scr"]
    QR = A["qkv_keys"]
    RR = A["row_keys"]
    ARA = A["AR_A"]
    uinc_sb = kb.spool(128, 128, F32)
    i128f_sb = kb.spool(128, 128, F32)
    i128b_sb = kb.spool(128, 128, BF16)
    ones1_sb = kb.spool(1, 128, F32)
    for nm, dst, src in (("uincv", uinc_sb, A["UINC"]), ("i128fv", i128f_sb, A["I128F"]),
                         ("i128bv", i128b_sb, A["I128B"]), ("ones1v", ones1_sb, A["ONES1"])):
        kb.op("sync", lambda e, d=dst, s=src: e.dma_start(out=d[:], in_=s[:, :]), writes=(nm,), dma=True)
    S_f = kb.spool(64, 16 * 64, F32)
    S_b = kb.spool(128, 16 * 64, BF16)
    SFK = tuple(f"S_f{h}" for h in range(16))
    SBK = tuple(f"S_b{h}" for h in range(16))
    kb.op("dve", lambda e: e.memset(S_f[:], 0.0), writes=SFK)
    kb.op("dve", lambda e: e.memset(S_b[:], 0.0), writes=SBK)
    NB = 2
    kf_sb = ARA[:, 0:1024]
    qf_sb = ARA[:, 1024:2048]
    kt_sb = ARA[:, 2048:3072]
    vt_sb = ARA[:, 3072:5120]
    o_sb = ARA[:, 5120:7168]
    cols_sb = kb.spool(128, NB * 64, F32)
    cum_sb = kb.spool(128, 16, F32)
    biasA_sb = kb.spool(128, 16, F32)
    rowAc_sb = kb.spool(128, 16, F32)
    biasP_sb = kb.spool(128, 16, F32)
    rload_sb = kb.spool(64, 128, F32)
    LN_ = []
    for L in range(2):
        LN_.append(dict(
            rows=kb.spool(1, 4 * 128, F32),
            ccol=kb.spool(128, 4, F32),
            arg=kb.spool(128, 128, F32),
            dmat=kb.spool(128, 128, F32),
            araw=kb.spool(128, 128, F32),
            kq=kb.spool(128, 128, F32),
            B=kb.spool(128, 128, BF16),
            U=[kb.spool(128, 128, BF16) for _ in range(2)],
            Lq=[kb.spool(128, 128, BF16) for _ in range(2)],
            Pp=kb.spool(128, 128, BF16),
            V=[kb.spool(128, 64, BF16) for _ in range(2)],
            R=kb.spool(128, 64, BF16),
            tmp=kb.spool(128, 64, F32),
            tmp2=kb.spool(128, 64, BF16),
            kw=kb.spool(128, 64, BF16),
            lamc=kb.spool(64, 1, F32),
        ))
    o_keys = []

    class _Coll:
        def __init__(self):
            self.ops = []

        def op(self, eng, fn, reads=(), writes=(), dma=False):
            self.ops.append((eng, fn, tuple(reads), tuple(writes), dma))

    def head_body(co, h, L, bb, kL, kfs, qfs, kts, vts, cls, osb):
        Z = LN_[L]
        khl = h // 2
        jj = khl // 2
        pb = 64 * (khl % 2)
        kft = kfs[pb:pb + 64, jj * 128:(jj + 1) * 128]
        qft = qfs[pb:pb + 64, jj * 128:(jj + 1) * 128]
        bb0, bb1, bb2, bb3 = PB[4 * L], PB[4 * L + 1], PB[4 * L + 2], PB[4 * L + 3]
        ps0 = bb0[:, 0:128]
        ps1 = bb1[:, 0:128]
        ps2 = bb2[:, 0:128]
        ps3 = bb3[:, 0:64]
        ps4 = bb0[:, 0:128]
        ps5 = bb1[:, 0:128]
        ps6 = bb2[:, 0:64]
        ps7a = bb3[:, 64:128]
        ps7b = bb3[0:64, 64:128]
        B0, B1, B2, B3 = (f"b{i}_{L}" for i in range(4))
        K0, K1, K2, K3, K4, K5, K6, K7 = B0, B1, B2, B3, B0, B1, B2, B3
        if h % 2 == 0:
            co.op("pe", lambda e, l=kft, o=ps0: e.matmul(o, l, l, start=True, stop=True),
                  reads=(kL["kf"],), writes=(K0,))
            co.op("pe", lambda e, l=kft, r=qft, o=ps1: e.matmul(o, l, r, start=True, stop=True),
                  reads=(kL["kf"], kL["qf"]), writes=(K1,))
            co.op("dve", lambda e, o=Z["araw"], s=ps0: e.tensor_copy(o[:], s), reads=(K0,), writes=(f"araw{L}",))
            co.op("dve", lambda e, o=Z["kq"], s=ps1: e.tensor_copy(o[:], s), reads=(K1,), writes=(f"kq{L}",))
        for ri, (colt, colk) in enumerate(((rowAc_sb, "rowAc"), (biasA_sb, "biasA"),
                                           (biasP_sb, "biasP"), (cum_sb, "cum"))):
            co.op("pe", lambda e, ct=colt, h=h, o=ps2[0:1, :]: e.matmul(o, ct[:, h:h + 1], i128f_sb[:],
                                                                        start=True, stop=True),
                  reads=(colk, "i128fv"), writes=(K2,))
            co.op("dve", lambda e, ri=ri, o=Z["rows"], s=ps2[0:1, :]: e.tensor_copy(o[:, ri * 128:(ri + 1) * 128], s),
                  reads=(K2,), writes=(f"rows{ri}_{L}",))
        ROWK = tuple(f"rows{ri}_{L}" for ri in range(4))
        co.op("pe", lambda e, o=ps2[:, 0:1], r=Z["rows"]: e.matmul(o, ones1_sb[:], r[:, 3 * 128 + 127: 3 * 128 + 128],
                                                                   start=True, stop=True),
              reads=(ROWK[3], "ones1v"), writes=(K2,))
        co.op("dve", lambda e, o=Z["ccol"], s=ps2[:, 0:1]: e.tensor_copy(o[:, 0:1], s), reads=(K2,), writes=(f"cC{L}",))
        co.op("act", lambda e, h=h, o=Z["ccol"]: e.activation(out=o[:, 1:2], in_=biasA_sb[:, h:h + 1], func=AF.Exp),
              reads=("biasA",), writes=(f"rho{L}",))
        co.op("act", lambda e, h=h, o=Z["ccol"]: e.activation(out=o[:, 2:3], in_=biasP_sb[:, h:h + 1], func=AF.Exp),
              reads=("biasP",), writes=(f"rhoP{L}",))
        co.op("dve", lambda e, h=h, o=Z["ccol"]: e.tensor_tensor(out=o[:, 3:4], in0=rowAc_sb[:, h:h + 1],
                                                                 in1=o[:, 0:1], op=ALU.add),
              reads=("rowAc", f"cC{L}"), writes=(f"wcol{L}",))
        co.op("act", lambda e, o=Z["ccol"]: e.activation(out=o[:, 3:4], in_=o[:, 3:4], func=AF.Exp),
              reads=(f"wcol{L}",), writes=(f"wcol{L}",))
        specs = (
            (0, biasA_sb, "biasA", 0, "B"),
            (1, rowAc_sb, "rowAc", 1, "U"),
            (2, rowAc_sb, "rowAc", 2, "Pp"),
        )
        for (rsl, bcol, colk, mk, what) in specs:
            co.op("pe", lambda e, rsl=rsl, o=ps2, r=Z["rows"]: e.matmul(o, ones1_sb[:],
                                                                        r[:, rsl * 128:(rsl + 1) * 128],
                                                                        start=True, stop=True),
                  reads=(ROWK[rsl], "ones1v"), writes=(K2,))
            co.op("dve", lambda e, bc=bcol, h=h, o=Z["arg"], s=ps2: e.tensor_scalar(
                out=o[:], in0=s, scalar1=bc[:, h:h + 1], scalar2=30.0,
                op0=ALU.add, op1=ALU.min),
                  reads=(K2, colk), writes=(f"argv{L}",))
            if mk == 0:
                pat, cm, op = [[-1, 128]], 1, ALU.is_gt
            elif mk == 1:
                pat, cm, op = [[1, 128]], -1, ALU.is_gt
            else:
                pat, cm, op = [[1, 128]], -1, ALU.is_ge
            def _mask(e, pat=pat, cm=cm, op=op, i_=Z["arg"], o_=Z["dmat"]):
                if not hasattr(e, "_fillreg"):
                    e._fillreg = e.to_reg(-200.0)
                return e.affine_select(out=o_[:], in_=i_[:], pattern=pat, compare_op=op,
                                       fill=e._fillreg, base=0, channel_multiplier=cm)
            co.op("gp", _mask, reads=(f"argv{L}",), writes=(f"dmatv{L}",))
            co.op("act", lambda e, o=Z["dmat"]: e.activation(out=o[:], in_=o[:], func=AF.Exp),
                  reads=(f"dmatv{L}",), writes=(f"dmatv{L}",))
            if what == "B":
                co.op("dve", lambda e, o=Z["B"], d=Z["dmat"], a=Z["araw"]: e.scalar_tensor_tensor(
                    out=o[:], in0=d[:], scalar=-1.0, in1=a[:], op0=ALU.mult, op1=ALU.mult),
                      reads=(f"dmatv{L}", f"araw{L}"), writes=(f"Bm{L}",))
            elif what == "U":
                co.op("dve", lambda e, o=Z["U"][0], d=Z["dmat"], a=Z["araw"]: e.scalar_tensor_tensor(
                    out=o[:], in0=d[:], scalar=-1.0, in1=a[:], op0=ALU.mult, op1=ALU.mult),
                      reads=(f"dmatv{L}", f"araw{L}"), writes=(f"U0{L}",))
            else:
                co.op("dve", lambda e, o=Z["Pp"], d=Z["dmat"], a=Z["kq"]: e.scalar_tensor_tensor(
                    out=o[:], in0=d[:], scalar=1.0, in1=a[:], op0=ALU.mult, op1=ALU.mult),
                      reads=(f"dmatv{L}", f"kq{L}"), writes=(f"Ppv{L}",))
        co.op("pe", lambda e, l=kft, pb=pb, h=h, o=ps3: e.matmul(o, l, S_b[pb:pb + 64, h * 64:(h + 1) * 64],
                                                                 start=True, stop=True),
              reads=(kL["kf"], f"S_b{h}"), writes=(K3,))
        co.op("dve", lambda e, o=Z["tmp"], s=ps3, cc=Z["ccol"]: e.tensor_scalar(out=o[:], in0=s, scalar1=cc[:, 1:2],
                                                                                scalar2=None, op0=ALU.mult),
              reads=(K3, f"rho{L}"), writes=(f"tmpv{L}",))
        co.op("dve", lambda e, h=h, v=vts, cl=cls, o=Z["R"], t=Z["tmp"]: e.scalar_tensor_tensor(
            out=o[:], in0=v[:, h * 64:(h + 1) * 64], scalar=cl[:, 16 + h:17 + h], in1=t[:],
            op0=ALU.mult, op1=ALU.subtract),
              reads=(kL["vt"], kL["cl"], f"tmpv{L}"), writes=(f"V0v{L}",))
        Vcur, Vk = Z["R"], f"V0v{L}"
        Ucur, Uk = Z["U"][0], f"U0{L}"
        Lcur, Lk = Z["B"], f"Bm{L}"
        for s in range(7):
            Vnext = Z["V"][s % 2]
            mm_group(co, ps3, [(Ucur[:], Vcur[:]), (i128b_sb[:], Vcur[:])],
                     reads=(Uk, Vk, "i128bv"), writes=(K3,))
            co.op("act", lambda e, o=Vnext, s_=ps3: e.activation(out=o[:], in_=s_, func=AF.Copy),
                  reads=(K3,), writes=(f"V{s + 1}v{L}",))
            if s < 6:
                Unext = Z["U"][(s + 1) % 2]
                co.op("pe", lambda e, l=Lcur, r=Ucur, o=ps4: e.matmul(o, l[:], r[:], start=True, stop=True),
                      reads=(Lk, Uk), writes=(K4,))
                if s < 5:
                    Lnext = Z["Lq"][(s + 1) % 2]
                    co.op("pe", lambda e, l=Ucur, r=Lcur, o=ps5: e.matmul(o, l[:], r[:], start=True, stop=True),
                          reads=(Uk, Lk), writes=(K5,))
                    co.op("dve", lambda e, o=Lnext, s_=ps5: e.tensor_copy(o[:], s_),
                          reads=(K5,), writes=(f"L{s + 1}m{L}",))
                    Lcur, Lk = Lnext, f"L{s + 1}m{L}"
                co.op("act", lambda e, o=Unext, s_=ps4: e.activation(out=o[:], in_=s_, func=AF.Copy),
                      reads=(K4,), writes=(f"U{s + 1}m{L}",))
                Ucur, Uk = Unext, f"U{s + 1}m{L}"
            Vcur, Vk = Vnext, f"V{s + 1}v{L}"
        co.op("pe", lambda e, l=qft, pb=pb, h=h, o=ps7a: e.matmul(o, l, S_b[pb:pb + 64, h * 64:(h + 1) * 64],
                                                                  start=True, stop=True),
              reads=(kL["qf"], f"S_b{h}"), writes=(K7,))
        co.op("dve", lambda e, o=Z["tmp2"], s=ps7a, cc=Z["ccol"]: e.tensor_scalar(out=o[:], in0=s, scalar1=cc[:, 2:3],
                                                                                  scalar2=None, op0=ALU.mult),
              reads=(K7, f"rhoP{L}"), writes=(f"tmp2v{L}",))
        mm_group(co, ps6, [(Z["Pp"][:], Vcur[:]), (i128b_sb[:], Z["tmp2"][:])],
                 reads=(f"Ppv{L}", Vk, f"tmp2v{L}", "i128bv"), writes=(K6,))
        co.op("act", lambda e, h=h, o=osb, s=ps6: e.activation(out=o[:, h * 64:(h + 1) * 64], in_=s, func=AF.Copy),
              reads=(K6,), writes=(f"osl{h}_{bb}",))
        co.op("dve", lambda e, h=h, kt=kts, khl=khl, o=Z["kw"], cc=Z["ccol"]: e.tensor_scalar(
            out=o[:], in0=kt[:, khl * 64:(khl + 1) * 64], scalar1=cc[:, 3:4], scalar2=None, op0=ALU.mult),
              reads=(kL["kt"], f"wcol{L}"), writes=(f"kwv{L}",))
        co.op("pe", lambda e, o=ps7b, l=Z["kw"], r=Vcur: e.matmul(o, l[:], r[:], start=True, stop=True),
              reads=(f"kwv{L}", Vk), writes=(K7,))
        co.op("act", lambda e, o=Z["lamc"], cc=Z["ccol"]: e.activation(out=o[:], in_=cc[0:64, 0:1], func=AF.Exp),
              reads=(f"cC{L}",), writes=(f"lamC{L}",))
        co.op("dve", lambda e, h=h, s=ps7b, lc=Z["lamc"]: e.scalar_tensor_tensor(
            out=S_f[:, h * 64:(h + 1) * 64], in0=S_f[:, h * 64:(h + 1) * 64], scalar=lc[:, 0:1],
            in1=s, op0=ALU.mult, op1=ALU.add),
              reads=(f"S_f{h}", f"lamC{L}", K7), writes=(f"S_f{h}",))
        co.op("act", lambda e, h=h: e.activation(out=S_b[0:64, h * 64:(h + 1) * 64], in_=S_f[:, h * 64:(h + 1) * 64],
                                                 func=AF.Copy),
              reads=(f"S_f{h}",), writes=(f"S_b{h}",))
        co.op("act", lambda e, h=h: e.activation(out=S_b[64:128, h * 64:(h + 1) * 64], in_=S_f[:, h * 64:(h + 1) * 64],
                                                 func=AF.Copy),
              reads=(f"S_f{h}",), writes=(f"S_b{h}",))

    for c in range(A.get("scan_nchunks", T // C)):
        t0 = c * C
        bb = c % NB
        kfs = kf_sb[:, bb * 512:(bb + 1) * 512]
        qfs = qf_sb[:, bb * 512:(bb + 1) * 512]
        kts = kt_sb[:, bb * 512:(bb + 1) * 512]
        vts = vt_sb[:, bb * 1024:(bb + 1) * 1024]
        cls = cols_sb[:, bb * 64:(bb + 1) * 64]
        osb = o_sb[:, bb * 1024:(bb + 1) * 1024]
        kL = {"kf": f"skf{bb}", "qf": f"sqf{bb}", "kt": f"skt{bb}", "vt": f"svt{bb}", "cl": f"scl{bb}"}
        for jj_ in range(4):
            kb.op("sync", lambda e, o=kfs[:, jj_ * 128:(jj_ + 1) * 128], s=qkvf[512 + jj_ * 128:512 + (jj_ + 1) * 128, t0:t0 + C]:
                  e.dma_start(out=o, in_=s), reads=QR, writes=(kL["kf"],), dma=True)
            kb.op("sync", lambda e, o=qfs[:, jj_ * 128:(jj_ + 1) * 128], s=qkvf[jj_ * 128:(jj_ + 1) * 128, t0:t0 + C]:
                  e.dma_start(out=o, in_=s), reads=QR, writes=(kL["qf"],), dma=True)
        kb.op("sync", lambda e, o=kts, t0=t0: e.dma_start_transpose(out=o, in_=qkvf[512:1024, t0:t0 + C]),
              reads=QR, writes=(kL["kt"],), dma=True)
        kb.op("sync", lambda e, o=vts, t0=t0: e.dma_start_transpose(out=o, in_=qkvf[1024:2048, t0:t0 + C]),
              reads=QR, writes=(kL["vt"],), dma=True)
        kb.op("sync", lambda e, t0=t0: e.dma_start(out=rload_sb[:], in_=rows[:, t0:t0 + C]),
              reads=RR, writes=("rload",), dma=True)
        kb.op("pe", lambda e: e.matmul(PB[2][0:128, 256:320], rload_sb[:], i128f_sb[0:64, 0:64], start=True, stop=True),
              reads=("rload", "i128fv"), writes=("b2_0",))
        kb.op("dve", lambda e, o=cls: e.tensor_copy(o, PB[2][0:128, 256:320]),
              reads=("b2_0",), writes=(kL["cl"],))
        CLK = (kL["cl"],)
        kb.op("pe", lambda e, r=cls[:, 0:16]: e.matmul(PB[2][0:128, 256:272], uinc_sb[:], r, start=True, stop=True),
              reads=("uincv",) + CLK, writes=("b2_0",))
        kb.op("dve", lambda e: e.tensor_copy(cum_sb[:], PB[2][0:128, 256:272]), reads=("b2_0",), writes=("cum",))
        kb.op("dve", lambda e, cl=cls: e.tensor_tensor(out=biasA_sb[:], in0=cum_sb[:], in1=cl[:, 32:48], op=ALU.add),
              reads=("cum",) + CLK, writes=("biasA",))
        kb.op("dve", lambda e, cl=cls: e.tensor_tensor(
            out=biasA_sb[:].rearrange("p (a b) -> p a b", b=2), in0=biasA_sb[:].rearrange("p (a b) -> p a b", b=2),
            in1=cl[:, 48:56].unsqueeze(2).broadcast_to((128, 8, 2)), op=ALU.add),
              reads=("biasA",) + CLK, writes=("biasA",))
        kb.op("dve", lambda e, cl=cls: e.scalar_tensor_tensor(
            out=rowAc_sb[:].rearrange("p (a b) -> p a b", b=2), in0=cum_sb[:].rearrange("p (a b) -> p a b", b=2),
            scalar=-1.0, in1=cl[:, 48:56].unsqueeze(2).broadcast_to((128, 8, 2)), op0=ALU.mult, op1=ALU.add),
              reads=("cum",) + CLK, writes=("rowAc",))
        kb.op("dve", lambda e, cl=cls: e.scalar_tensor_tensor(
            out=biasP_sb[:].rearrange("p (a b) -> p a b", b=2), in0=cum_sb[:].rearrange("p (a b) -> p a b", b=2),
            scalar=1.0, in1=cl[:, 56:64].unsqueeze(2).broadcast_to((128, 8, 2)), op0=ALU.mult, op1=ALU.add),
              reads=("cum",) + CLK, writes=("biasP",))
        for hp in range(0, 8, 2):
            ca, cb = _Coll(), _Coll()
            for hh in (2 * hp, 2 * hp + 1):
                head_body(ca, hh, 0, bb, kL, kfs, qfs, kts, vts, cls, osb)
            for hh in (2 * hp + 2, 2 * hp + 3):
                head_body(cb, hh, SCAN_LANE2, bb, kL, kfs, qfs, kts, vts, cls, osb)
            na, nbo = len(ca.ops), len(cb.ops)
            if SCAN_ILV:
                for i in range(max(na, nbo)):
                    if i < na:
                        kb.op(*ca.ops[i][:2], reads=ca.ops[i][2], writes=ca.ops[i][3], dma=ca.ops[i][4])
                    if i < nbo:
                        kb.op(*cb.ops[i][:2], reads=cb.ops[i][2], writes=cb.ops[i][3], dma=cb.ops[i][4])
            else:
                for opx in ca.ops + cb.ops:
                    kb.op(*opx[:2], reads=opx[2], writes=opx[3], dma=opx[4])
        wk = f"ow{c}"
        kb.op("sync", lambda e, t0=t0, o=osb: e.dma_start(out=A["o_scr"][t0:t0 + C, :], in_=o),
              reads=tuple(f"osl{h}_{bb}" for h in range(16)), writes=(wk,), dma=True)
        o_keys.append(wk)
    A["o_keys"] = tuple(o_keys)


def phase_epi16(nc, kb, A):
    PB = A["PB"]
    AR_A, AR_B, AR_C = A["AR_A"], A["AR_B"], A["AR_C"]
    zs_sb = kb.spool(128, 2 * 1024, BF16)
    ot_sb = kb.spool(128, 2 * 1024, BF16)
    ssq_sb = kb.spool(128, 16, F32)
    rstd_sb = kb.spool(128, 16, F32)
    ynt_sb = kb.spool(128, 2 * 1024, BF16)
    osq_sb = kb.spool(128, 1024, BF16)
    out_sb = kb.spool(128, 2 * 512, BF16)
    nw_sb = kb.spool(128, 1, F32)
    eps_c, eps_k = kb.constcol(nc, 1e-6)
    wz = AR_B[:, 0:16 * 1024]
    wo = AR_C[:, 0:8 * 2048]
    kb.op("gp", lambda e: e.dma_start(out=wz.rearrange("p (k m) -> p k m", k=16),
                                      in_=A["wzT"].rearrange("(k p) m -> p k m", p=128)),
          writes=("wzv",) + tuple(A["arB_old"]), dma=True)
    kb.op("gp", lambda e: e.dma_start(out=wo.rearrange("p (k m) -> p k m", k=8),
                                      in_=A["woutT"].rearrange("(k p) m -> p k m", p=128)),
          writes=("wov",) + tuple(A["arC_old"]), dma=True)
    kb.op("sync", lambda e: e.dma_start(out=nw_sb[:], in_=A["nw128"][:, :]), writes=("nwv",), dma=True)
    for ks in range(8):
        kb.op("dve", lambda e, ks=ks: e.tensor_scalar(out=wo[:, ks * 2048:(ks + 1) * 2048],
                                                      in0=wo[:, ks * 2048:(ks + 1) * 2048],
                                                      scalar1=nw_sb[:, 0:1], scalar2=None, op0=ALU.mult),
              reads=("wov", "nwv"), writes=("wov",))
    yn_keys = []
    for half in range(2):
        toff = half * 2048
        for kc in range(16):
            kb.op("sync", lambda e, kc=kc, t=toff: e.dma_start(
                out=AR_A[:, kc * 2048:(kc + 1) * 2048], in_=A["xn0f"][kc * 128:(kc + 1) * 128, t:t + 2048]),
                  reads=tuple(A["xn0f_ready"]), writes=(f"exn{kc}",) + tuple(A["arA_old"]), dma=True)
        A["arA_old"] = tuple(f"exn{kc}" for kc in range(16))
        EXNK = A["arA_old"]
        for i in range(16):
            gi = half * 16 + i
            pp = i % 2
            zt = zs_sb[:, pp * 1024:(pp + 1) * 1024]
            ot = ot_sb[:, pp * 1024:(pp + 1) * 1024]
            ynt = ynt_sb[:, pp * 1024:(pp + 1) * 1024]
            ko, kz, ky = f"eot{pp}", f"ezs{pp}", f"eyn{pp}"
            kb.op("sync", lambda e, gi=gi, o=ot: e.dma_start(out=o, in_=A["o_scr"][gi * 128:(gi + 1) * 128, :]),
                  reads=A["o_keys"], writes=(ko,), dma=True)
            for zs_ in range(2):
                ps = PB[0] if zs_ == 0 else PB[1]
                psk = "ps0" if zs_ == 0 else "ps1"
                pairs = [(AR_A[:, k * 2048 + i * 128: k * 2048 + (i + 1) * 128],
                          wz[:, k * 1024 + zs_ * 512: k * 1024 + (zs_ + 1) * 512]) for k in range(16)]
                mm_group(kb, ps[:], pairs, reads=("wzv",) + EXNK, writes=(psk,))
                kb.op("act", lambda e, o=zt[:, zs_ * 512:(zs_ + 1) * 512], s=ps[:]: e.activation(out=o, in_=s, func=AF.Silu),
                      reads=(psk,), writes=(kz,))
            kb.op("dve", lambda e, o=ot: e.scalar_tensor_tensor(out=osq_sb[:], in0=o, scalar=1.0, in1=o,
                                                                op0=ALU.mult, op1=ALU.mult),
                  reads=(ko,), writes=("eosq",))
            kb.op("dve", lambda e: e.tensor_reduce(out=ssq_sb[:], in_=osq_sb[:].rearrange("p (h d) -> p h d", d=64),
                                                   axis=AX.X, op=ALU.add),
                  reads=("eosq",), writes=("essq",))
            kb.op("act", lambda e, b=eps_c: e.activation(out=rstd_sb[:], in_=ssq_sb[:], func=AF.Ln, bias=b[:, 0:1], scale=1.0 / 64),
                  reads=("essq", eps_k), writes=("erstd",))
            kb.op("act", lambda e: e.activation(out=rstd_sb[:], in_=rstd_sb[:], func=AF.Exp, scale=-0.5),
                  reads=("erstd",), writes=("erstd",))
            kb.op("dve", lambda e, o=ot, y=ynt: e.tensor_tensor(
                out=y.rearrange("p (h d) -> p h d", d=64), in0=o.rearrange("p (h d) -> p h d", d=64),
                in1=rstd_sb[:].unsqueeze(2).broadcast_to((128, 16, 64)), op=ALU.mult),
                  reads=(ko, "erstd"), writes=(ky,))
            kb.op("dve", lambda e, y=ynt, z=zt: e.tensor_tensor(out=y, in0=y, in1=z, op=ALU.mult),
                  reads=(ky, kz), writes=(ky,))
            wk = f"eynw{gi}"
            kb.op("sync", lambda e, gi=gi, y=ynt: e.dma_start(out=A["yn_scr"][gi * 128:(gi + 1) * 128, :], in_=y),
                  reads=(ky,), writes=(wk,), dma=True)
            yn_keys.append(wk)
    for cc in range(8):
        kb.op("sync", lambda e, cc=cc: e.dma_start_transpose(out=AR_A[:, cc * 4096:(cc + 1) * 4096],
                                                             in_=A["yn_scr"][:, cc * 128:(cc + 1) * 128]),
              reads=tuple(yn_keys), writes=(f"eynf{cc}",) + tuple(A["arA_old"]), dma=True)
    A["arA_old"] = tuple(f"eynf{cc}" for cc in range(8))
    YNFK = A["arA_old"]
    a0_keys = []
    for m in range(16):
        for ts_ in range(8):
            ps = PB[0] if ts_ % 2 == 0 else PB[1]
            psk = "ps0" if ts_ % 2 == 0 else "ps1"
            pairs = [(wo[:, k * 2048 + m * 128:(k * 2048 + (m + 1) * 128)],
                      AR_A[:, k * 4096 + ts_ * 512: k * 4096 + (ts_ + 1) * 512]) for k in range(8)]
            mm_group(kb, ps[:], pairs, reads=("wov",) + YNFK, writes=(psk,))
            kb.op("act", lambda e, s=ps[:], o=out_sb[:, (ts_ % 2) * 512:(ts_ % 2 + 1) * 512]:
                  e.activation(out=o, in_=s, func=AF.Copy),
                  reads=(psk,), writes=(f"eob{ts_ % 2}",))
            wk = f"ea0w{m}_{ts_}"
            kb.op("sync", lambda e, m=m, ts_=ts_, s=out_sb[:, (ts_ % 2) * 512:(ts_ % 2 + 1) * 512]:
                  e.dma_start(out=A["attn0f"][m * 128:(m + 1) * 128, ts_ * 512:(ts_ + 1) * 512], in_=s),
                  reads=(f"eob{ts_ % 2}",), writes=(wk,), dma=True)
            a0_keys.append(wk)
    A["a0_keys"] = tuple(a0_keys)
    A["arC_old"] = ("wov",)


def phase_ffnP(nc, kb, pre, A, src_feat, src_ready, out_feat, out_col0, dyn, cfg, y_scr):
    NT = 2048
    PB = A["PB"]
    hu = A["AR_A"]
    hh = A["AR_B"]
    wsl = A["AR_C"]
    if A.get("xns_sb") is None:
        A["xns_sb"] = kb.spool(128, 16 * 512, BF16)
    xns_sb = A["xns_sb"]
    if A.get("ffn_smalls") is None:
        A["ffn_smalls"] = dict(
            h128_sb=kb.spool(128, 128, BF16),
            sc_sb=kb.spool(128, 224, F32),
            sg_sb=kb.spool(128, 512, F32),
            yt_sb=kb.spool(128, 2 * 512, BF16),
            out_sb=kb.spool(128, 2 * 512, BF16),
        )
    S = A["ffn_smalls"]
    h128_sb, sc_sb, sg_sb, yt_sb, out_sb = S["h128_sb"], S["sc_sb"], S["sg_sb"], S["yt_sb"], S["out_sb"]
    SQH = float(2.0 ** -0.5)
    ld = cfg["loaded"]
    if not ld.get(cfg["tag"]):
        kb.op("sync", lambda e: e.dma_start(out=hh[:].rearrange("p (k m) -> p k m", k=16),
                                            in_=cfg["h2048"].rearrange("(k p) m -> p k m", p=128)),
              writes=tuple(f"H{k}" for k in range(16)) + tuple(A["arB_old"]), dma=True)
        kb.op("sync", lambda e: e.dma_start(out=h128_sb[:], in_=cfg["h128"][:, :]), writes=("h128v",), dma=True)
        kb.op("sync", lambda e: e.dma_start(out=sc_sb[:, 0:16], in_=cfg["suug"][:, :]), writes=("suugv",), dma=True)
        kb.op("sync", lambda e: e.dma_start(out=sc_sb[:, 16:80], in_=cfg["sudp"][:, :]), writes=("sudpv",), dma=True)
        kb.op("sync", lambda e: e.dma_start(out=sc_sb[:, 80:144], in_=cfg["wsug"][:, 0:64]), writes=("wsuv",), dma=True)
        kb.op("sync", lambda e: e.dma_start(out=sc_sb[:, 144:208], in_=cfg["wsug"][:, 64:128]), writes=("wsgv",), dma=True)
        kb.op("sync", lambda e: e.dma_start(out=sc_sb[:, 208:224], in_=cfg["wsdp"][:, :]), writes=("wsdv",), dma=True)
        kb.op("dve", lambda e: e.tensor_tensor(out=sc_sb[:, 80:144], in0=sc_sb[:, 80:144], in1=sc_sb[:, 16:80], op=ALU.mult),
              reads=("wsuv", "sudpv"), writes=("wsuv",))
        kb.op("dve", lambda e: e.tensor_scalar(out=sc_sb[:, 80:144], in0=sc_sb[:, 80:144], scalar1=SQH, scalar2=None, op0=ALU.mult),
              reads=("wsuv",), writes=("wsuv",))
        kb.op("dve", lambda e: e.tensor_scalar(out=sc_sb[:, 144:208], in0=sc_sb[:, 144:208], scalar1=SQH, scalar2=None, op0=ALU.mult),
              reads=("wsgv",), writes=("wsgv",))
        kb.op("dve", lambda e: e.tensor_scalar(out=sc_sb[:, 208:224], in0=sc_sb[:, 208:224], scalar1=SQH, scalar2=None, op0=ALU.mult),
              reads=("wsdv",), writes=("wsdv",))
        ld[cfg["tag"]] = True
        A["arB_old"] = tuple(f"H{k}" for k in range(16))
    HCK = tuple(f"H{k}" for k in range(16))
    for ts_ in range(4):
        for k in range(16):
            kxn = f"xnsl{k}"
            if dyn:
                kb.op("sync", lambda e, k=k, ts_=ts_: e.dma_start(
                    out=xns_sb[:, k * 512:(k + 1) * 512],
                    in_=src_feat[k * 128:(k + 1) * 128, ts_ * 512:ts_ * 512 + 2560][:, bass.ds(_dyn2048(e), 512)]),
                      reads=tuple(src_ready), writes=(kxn,), dma=True)
            else:
                kb.op("sync", lambda e, k=k, ts_=ts_: e.dma_start(
                    out=xns_sb[:, k * 512:(k + 1) * 512],
                    in_=src_feat[k * 128:(k + 1) * 128, out_col0 + ts_ * 512:out_col0 + (ts_ + 1) * 512]),
                      reads=tuple(src_ready), writes=(kxn,), dma=True)
            kb.op("dve", lambda e, k=k: e.tensor_scalar(out=xns_sb[:, k * 512:(k + 1) * 512],
                                                        in0=xns_sb[:, k * 512:(k + 1) * 512],
                                                        scalar1=sc_sb[:, k:k + 1], scalar2=None, op0=ALU.mult),
                  reads=(kxn, "suugv"), writes=(kxn,))
        for m in range(16):
            ps = PB[0] if m % 2 == 0 else PB[1]
            psk = "ps0" if m % 2 == 0 else "ps1"
            pairs = [(hh[:, k * 2048 + m * 128:(k * 2048 + (m + 1) * 128)],
                      xns_sb[:, k * 512:(k + 1) * 512]) for k in range(16)]
            mm_group(kb, ps[:], pairs, reads=tuple(f"xnsl{k}" for k in range(16)) + HCK, writes=(psk,))
            kb.op("act", lambda e, m=m, ts_=ts_, s=ps[:]: e.activation(
                out=hu[:, m * NT + ts_ * 512: m * NT + (ts_ + 1) * 512], in_=s, func=AF.Copy),
                  reads=(psk,), writes=(pre + f"h{m}",) + (tuple(A["arA_old"]) if ts_ == 0 else ()))
    HK = tuple(pre + f"h{m}" for m in range(16))
    A["arA_old"] = HK
    for io in range(64):
        wpp = io % 2
        wu = wsl[:, wpp * 2 * 2048: wpp * 2 * 2048 + 2048]
        wg = wsl[:, wpp * 2 * 2048 + 2048: (wpp + 1) * 2 * 2048]
        kwu, kwg = pre + f"wu{wpp}", pre + f"wg{wpp}"
        wold = tuple(A["arC_old"]) if io < 2 else ()
        kb.op("gp", lambda e, io=io, o=wu: e.dma_start(
            out=o.rearrange("p (k m) -> p k m", k=16),
            in_=cfg["wupT"].rearrange("(k p) m -> p k m", p=128)[:, :, io * 128:(io + 1) * 128]),
              writes=(kwu,) + wold, dma=True)
        kb.op("gp", lambda e, io=io, o=wg: e.dma_start(
            out=o.rearrange("p (k m) -> p k m", k=16),
            in_=cfg["wgT"].rearrange("(k p) m -> p k m", p=128)[:, :, io * 128:(io + 1) * 128]),
              writes=(kwg,) + wold, dma=True)
        for ts_ in range(4):
            ypp = ts_ % 2
            yt = yt_sb[:, ypp * 512:(ypp + 1) * 512]
            kyt = f"fyt{ypp}"
            pairs_u = [(wu[:, k * 128:(k + 1) * 128], hu[:, k * NT + ts_ * 512: k * NT + (ts_ + 1) * 512])
                       for k in range(16)]
            pairs_g = [(wg[:, k * 128:(k + 1) * 128], hu[:, k * NT + ts_ * 512: k * NT + (ts_ + 1) * 512])
                       for k in range(16)]
            mm_group(kb, PB[2][:], pairs_u, reads=(kwu,) + HK, writes=("ps2",))
            mm_group(kb, PB[3][:], pairs_g, reads=(kwg,) + HK, writes=("ps3",))
            kb.op("act", lambda e, io=io: e.activation(out=sg_sb[:], in_=PB[3][:], func=AF.Silu,
                                                       scale=sc_sb[:, 144 + io:145 + io]),
                  reads=("ps3", "wsgv"), writes=("sgv",))
            kb.op("dve", lambda e, io=io, y=yt: e.scalar_tensor_tensor(
                out=y, in0=sg_sb[:], scalar=sc_sb[:, 80 + io:81 + io],
                in1=PB[2][:], op0=ALU.mult, op1=ALU.mult),
                  reads=("sgv", "wsuv", "ps2"), writes=(kyt,))
            kb.op("sync", lambda e, io=io, ts_=ts_, y=yt: e.dma_start(
                out=y_scr[io * 128:(io + 1) * 128, ts_ * 512:(ts_ + 1) * 512], in_=y),
                  reads=(kyt,), writes=(f"ysc{io}_{ts_}",), dma=True)
    out_keys = []
    for ts_ in range(4):
        for c in range(64):
            ku = pre + f"u{c}"
            khs = pre + f"h{c // 4}"
            kb.op("sync", lambda e, c=c, ts_=ts_: e.dma_start(
                out=xns_sb[:, (c % 16) * 512:(c % 16 + 1) * 512],
                in_=y_scr[c * 128:(c + 1) * 128, ts_ * 512:(ts_ + 1) * 512]),
                  reads=(f"ysc{c}_{ts_}",), writes=(f"yl{c % 16}", f"xnsl{c % 16}"), dma=True)
            ps = PB[0] if c % 2 == 0 else PB[1]
            psk = "ps0" if c % 2 == 0 else "ps1"
            kb.op("pe", lambda e, c=c, s=ps: e.matmul(s[:], h128_sb[:], xns_sb[:, (c % 16) * 512:(c % 16 + 1) * 512],
                                                      start=True, stop=True),
                  reads=(f"yl{c % 16}", "h128v"), writes=(psk,))
            if c % 2 == 0:
                kb.op("act", lambda e, c=c, s=ps: e.activation(out=hu[:, c * 512:(c + 1) * 512], in_=s[:], func=AF.Copy),
                      reads=(psk,), writes=(ku, khs))
            else:
                kb.op("dve", lambda e, c=c, s=ps: e.tensor_copy(hu[:, c * 512:(c + 1) * 512], s[:]),
                      reads=(psk,), writes=(ku, khs))
        UK = tuple(pre + f"u{c}" for c in range(64))
        for o in range(16):
            wpp = o % 2
            dsl = wsl[:, wpp * 8192:(wpp + 1) * 8192]
            kds = pre + f"ds{wpp}"
            extraw = (pre + "wu0", pre + "wg0", pre + "wu1", pre + "wg1") if (ts_ == 0 and o < 2) else ()
            kb.op("gp", lambda e, o=o, dd=dsl: e.dma_start(
                out=dd.rearrange("p (c m) -> p c m", c=64),
                in_=cfg["d2T"][:, :, o * 128:(o + 1) * 128].rearrange("c p m -> p c m")),
                  writes=(kds,) + extraw, dma=True)
            ps = PB[2] if o % 2 == 0 else PB[3]
            psk = "ps2" if o % 2 == 0 else "ps3"
            pairs = [(dsl[:, c * 128:(c + 1) * 128], hu[:, c * 512:(c + 1) * 512]) for c in range(64)]
            mm_group(kb, ps[:], pairs, reads=(kds,) + UK, writes=(psk,))
            op2 = out_sb[:, (o % 2) * 512:(o % 2 + 1) * 512]
            kb.op("dve", lambda e, o=o, s=ps, oo=op2: e.tensor_scalar(
                out=oo, in0=s[:], scalar1=sc_sb[:, 208 + o:209 + o], scalar2=None, op0=ALU.mult),
                  reads=(psk, "wsdv"), writes=(f"fob{o % 2}",))
            wk = pre + f"ffw{o}_{ts_}"
            kb.op("sync", lambda e, o=o, ts_=ts_, oo=op2: e.dma_start(
                out=out_feat[o * 128:(o + 1) * 128, out_col0 + ts_ * 512:out_col0 + (ts_ + 1) * 512] if not dyn
                else out_feat[o * 128:(o + 1) * 128, ts_ * 512:(ts_ + 1) * 512], in_=oo),
                  reads=(f"fob{o % 2}",), writes=(wk,), dma=True)
            out_keys.append(wk)
    A["arC_old"] = (pre + "ds0", pre + "ds1")
    return out_keys


MEGA_DEBUG = False
SCAN_ILV = True
SCAN_LANE2 = 1


def emit_alpha(nc, kb):
    x_in = nc.dram_tensor("x", [T, D], F32, kind="ExternalInput").ap()
    A = {"PB": None}
    for nm, shp, dt in (("wqkvT", [D, D], F32), ("wbT", [D, 16], F32), ("waT", [D, 16], F32),
                        ("convw", [D, 4], F32), ("dtb", [16, 1], F32), ("alog", [16, 1], F32),
                        ("ones2", [128, 2], BF16), ("UINC", [128, 128], F32), ("I128F", [128, 128], F32),
                        ("I128B", [128, 128], BF16), ("ONES1", [1, 128], F32),
                        ("wzT", [D, 1024], F32), ("woutT", [1024, D], F32), ("nw128", [128, 1], F32)):
        A[nm] = nc.dram_tensor(nm, shp, dt, kind="ExternalInput").ap()
    ffn_cfg = {"tag": "l0", "loaded": {}}
    for nm, shp, dt in (("h2048", [D, D], BF16), ("h128", [128, 128], BF16), ("suug", [128, 16], F32),
                        ("sudp", [128, 64], F32), ("wsug", [128, 128], F32), ("wsdp", [128, 16], F32),
                        ("wupT", [D, 8192], F32), ("wgT", [D, 8192], F32), ("d2T", [64, 128, D], BF16)):
        ffn_cfg[nm] = nc.dram_tensor("ffn_" + nm, shp, dt, kind="ExternalInput").ap()
    KD = "ExternalOutput" if MEGA_DEBUG else "Internal"
    xn0_scr = nc.dram_tensor("xn0_scr", [T, D], BF16, kind=KD).ap()
    A["xn0f"] = nc.dram_tensor("xn0f_scr", [D, T], BF16, kind=KD).ap()
    A["qkvf_scr"] = nc.dram_tensor("qkvf_scr", [D, T], BF16, kind=KD).ap()
    A["rows_scr"] = nc.dram_tensor("rows_scr", [64, T], F32, kind=KD).ap()
    A["o_scr"] = nc.dram_tensor("o_scr", [T, 1024], BF16, kind=KD).ap()
    A["yn_scr"] = nc.dram_tensor("yn_scr", [T, 1024], BF16, kind=KD).ap()
    A["attn0f"] = nc.dram_tensor("attn0f", [D, T], BF16, kind=KD).ap()
    ffn0f = nc.dram_tensor("ffn0f", [D, T], BF16, kind=KD).ap()
    y_scr = nc.dram_tensor("y_scr", [8192, 2048], BF16, kind="Internal").ap()
    xn1_out = nc.dram_tensor("xn1", [T, D], BF16, kind="ExternalOutput").ap()
    x1_out = nc.dram_tensor("x1", [T, D], F32, kind="ExternalOutput").ap()

    PS_ALL = nc.alloc_psum_tensor("PS", [128, 4096], F32)
    A["PS_ALL"] = PS_ALL[:]
    A["PB"] = [PS_ALL[:, i * 512:(i + 1) * 512] for i in range(8)]
    A["AR_A"] = nc.alloc_sbuf_tensor("AR_A", [128, 16 * 2048], BF16)[:]
    A["AR_B"] = nc.alloc_sbuf_tensor("AR_B", [128, 16 * 2048], BF16)[:]
    A["AR_C"] = nc.alloc_sbuf_tensor("AR_C", [128, 16384], BF16)[:]
    AR_S = nc.alloc_sbuf_tensor("AR_S", [128, 15360], BF16)
    kb.set_spool(AR_S, 15360)
    A["arA_old"] = ()
    A["arB_old"] = ()
    A["arC_old"] = ()
    A["xns_sb"] = None
    A["ffn_smalls"] = None

    xnk = []
    phase_ln(nc, kb, "l0", x_in, xn0_scr, T, xn_keys=xnk)
    kb.phase_fence(); kb.spool_reset()
    phase_featize(nc, kb, "z0", xn0_scr, A["xn0f"], ())
    kb.phase_fence(); kb.spool_reset()
    A["xn0f_ready"] = ()
    phase_proj16(nc, kb, A)
    kb.phase_fence(); kb.spool_reset()
    A["qkv_keys"] = ()
    A["row_keys"] = ()
    phase_scan16(nc, kb, A)
    kb.phase_fence(); kb.spool_reset()
    A["o_keys"] = ()
    A["arA_old"] = ()
    phase_epi16(nc, kb, A)
    kb.phase_fence(); kb.spool_reset()
    A["a0_keys"] = ()
    A["arA_old"] = ()
    A["arB_old"] = ()
    A["arC_old"] = ()
    fk0 = phase_ffnP(nc, kb, "fa", A, A["xn0f"], A["xn0f_ready"], ffn0f, 0, False, ffn_cfg, y_scr)
    fk1 = phase_ffnP(nc, kb, "fb", A, A["xn0f"], A["xn0f_ready"], ffn0f, 2048, False, ffn_cfg, y_scr)
    kb.phase_fence(); kb.spool_reset()
    phase_ln(nc, kb, "r1", x_in, xn1_out, T, a_feat=A["attn0f"], f_feat=ffn0f, x1_out=x1_out,
             a_ready=(), f_ready=())


def build_alpha():
    return build_program(emit_alpha)


def phase_attn16(nc, kb, A, Qf, Kf, Vf, ctxf, q_ready, k_ready, v_ready):
    PB = A["PB"]
    ARA, ARB = A["AR_A"], A["AR_B"]
    qh_sb = ARA[:, 0:T]
    kh_sb = ARA[:, T:2 * T]
    q2_sb = ARA[0:65, 2 * T:3 * T]
    k2_sb = ARA[0:65, 3 * T:4 * T]
    vh_sb = ARA[:, 4 * T:4 * T + 4096]
    sqf_sb = ARA[:, 5 * T:6 * T]
    pt_sb = ARA[:, 6 * T:6 * T + 2048]
    ARBf = ARB.bitcast(F32)
    row_sb = ARBf[0:1, 0:T]
    mrow_sb = ARBf[0:1, T:2 * T]
    kr_sb = ARBf[0:1, 2 * T:3 * T]
    sc1_sb = ARBf[0:1, 3 * T:3 * T + 4]
    invl_sb = ARBf[0:1, 3 * T + 4:3 * T + 1028]
    invb_sb = ARB[:, 7 * T:7 * T + 1024]
    ctx_sb = ARB[:, 7 * T + 1024:7 * T + 2048]
    o1f_sb = ARBf[0:1, 3 * T + 1028:3 * T + 1156]
    ocb_sb = ARB[:, 7 * T + 2048:7 * T + 2049]
    kb.op("dve", lambda e: e.memset(o1f_sb[:], 1.0), writes=("o1f",))
    kb.op("dve", lambda e: e.memset(ocb_sb[:], 1.0), writes=("ocb",))
    psS = A["PS_ALL"][:, 0:1024]
    psCx = A["PS_ALL"][:, 1024:2048]
    psL = A["PS_ALL"][0:1, 2048:3072]
    psR = A["PS_ALL"][0:1, 3072:3584]
    PSSK = ("ps0", "ps1")
    PSCK = ("ps2", "ps3")
    PSLK = ("ps4", "ps5")
    ctx_keys = []
    for h in range(16):
        r0 = h * 128
        kb.op("sync", lambda e, r0=r0: e.dma_start(out=qh_sb[:], in_=Qf[r0:r0 + 128, :]),
              reads=q_ready, writes=("qh",), dma=True)
        kb.op("sync", lambda e, r0=r0: e.dma_start(out=kh_sb[:], in_=Kf[r0:r0 + 128, :]),
              reads=k_ready, writes=("kh",), dma=True)
        for tt in range(32):
            kb.op("sync", lambda e, r0=r0, tt=tt: e.dma_start_transpose(
                out=vh_sb[:, tt * 128:(tt + 1) * 128], in_=Vf[r0:r0 + 128, tt * 128:(tt + 1) * 128]),
                  reads=v_ready, writes=("vh",), dma=True)
        kb.op("dve", lambda e: e.tensor_copy(k2_sb[0:64, :], kh_sb[64:128, :]), reads=("kh",), writes=("k2",))
        kb.op("dve", lambda e: e.memset(k2_sb[64:65, :], 1.0), writes=("k2o",))
        kb.op("dve", lambda e: e.tensor_copy(q2_sb[0:64, :], qh_sb[64:128, :]), reads=("qh",), writes=("q2",))
        kb.op("act", lambda e: e.activation(out=sqf_sb[:], in_=qh_sb[:], func=AF.Square), reads=("qh",), writes=("sqf",))
        for ts_ in range(8):
            kb.op("pe", lambda e, ts_=ts_: e.matmul(psR[0:1, 0:512], ocb_sb[:], sqf_sb[:, ts_ * 512:(ts_ + 1) * 512],
                                                    start=True, stop=True),
                  reads=("sqf", "ocb"), writes=("ps6",))
            kb.op("dve", lambda e, ts_=ts_: e.tensor_copy(row_sb[:, ts_ * 512:(ts_ + 1) * 512], psR[0:1, 0:512]),
                  reads=("ps6",), writes=("qsqr",))
        kb.op("act", lambda e: e.activation(out=sqf_sb[:], in_=kh_sb[:], func=AF.Square), reads=("kh",), writes=("sqf",))
        for ts_ in range(8):
            kb.op("pe", lambda e, ts_=ts_: e.matmul(psR[0:1, 0:512], ocb_sb[:], sqf_sb[:, ts_ * 512:(ts_ + 1) * 512],
                                                    start=True, stop=True),
                  reads=("sqf", "ocb"), writes=("ps6",))
            kb.op("dve", lambda e, ts_=ts_: e.tensor_copy(kr_sb[:, ts_ * 512:(ts_ + 1) * 512], psR[0:1, 0:512]),
                  reads=("ps6",), writes=("ksqr",))
        kb.op("dve", lambda e: e.tensor_reduce(out=sc1_sb[:, 0:1], in_=kr_sb[:], axis=AX.X, op=ALU.max),
              reads=("ksqr",), writes=("mk2",))
        kb.op("act", lambda e: e.activation(out=sc1_sb[:, 1:2], in_=sc1_sb[:, 0:1], func=AF.Ln),
              reads=("mk2",), writes=("lnmk",))
        kb.op("dve", lambda e: e.tensor_scalar(out=sc1_sb[:, 1:2], in0=sc1_sb[:, 1:2], scalar1=0.5, scalar2=0.1,
                                               op0=ALU.mult, op1=ALU.add),
              reads=("lnmk",), writes=("lnmk",))
        kb.op("act", lambda e: e.activation(out=mrow_sb[:], in_=row_sb[:], func=AF.Ln), reads=("qsqr",), writes=("mrow",))
        kb.op("act", lambda e: e.activation(out=mrow_sb[:], in_=mrow_sb[:], func=AF.Exp, scale=0.5, bias=sc1_sb[:, 1:2]),
              reads=("mrow", "lnmk"), writes=("mrow",))
        kb.op("dve", lambda e: e.tensor_scalar(out=q2_sb[64:65, :], in0=mrow_sb[:], scalar1=-1.0, scalar2=None, op0=ALU.mult),
              reads=("mrow",), writes=("q2o",))
        for sp in range(4):
            q0 = sp * 1024
            nk = (sp + 1) * 8
            pend = None
            for kt in range(nk):
                ppp = kt % 2
                pt = pt_sb[:, ppp * 1024:(ppp + 1) * 1024]
                kpt = f"hpt{ppp}"
                for sub in range(2):
                    qs = q0 + sub * 512
                    pairs = [(kh_sb[0:64, kt * 128:(kt + 1) * 128], qh_sb[0:64, qs:qs + 512]),
                             (k2_sb[0:65, kt * 128:(kt + 1) * 128], q2_sb[0:65, qs:qs + 512])]
                    mm_group(kb, psS[:, sub * 512:(sub + 1) * 512], pairs,
                             reads=("kh", "qh", "k2", "k2o", "q2", "q2o"), writes=(PSSK[sub],))
                kb.op("act", lambda e, pt=pt: e.activation(out=pt, in_=psS[:], func=AF.Exp),
                      reads=PSSK, writes=(kpt,))
                if kt * 128 >= q0:
                    def _mask2(e, pt=pt, base=q0 - kt * 128):
                        if not hasattr(e, "_fillreg0"):
                            e._fillreg0 = e.to_reg(0.0)
                        return e.affine_select(out=pt, in_=pt, pattern=[[1, 1024]], compare_op=ALU.is_ge,
                                               fill=e._fillreg0, base=base, channel_multiplier=-1)
                    kb.op("gp", _mask2, reads=(kpt,), writes=(kpt,))
                if pend is not None:
                    for fn_, rd_, wr_ in pend:
                        kb.op("pe", fn_, reads=rd_, writes=wr_)
                pend = []
                for sub in range(2):
                    pend.append((lambda e, kt=kt, sub=sub, pt=pt, nk=nk: e.matmul(
                        psCx[:, sub * 512:(sub + 1) * 512], vh_sb[:, kt * 128:(kt + 1) * 128],
                        pt[:, sub * 512:(sub + 1) * 512], start=(kt == 0), stop=(kt == nk - 1)),
                        ("vh", kpt), (PSCK[sub],)))
                    pend.append((lambda e, kt=kt, sub=sub, pt=pt, nk=nk: e.matmul(
                        psL[0:1, sub * 512:(sub + 1) * 512], ocb_sb[:],
                        pt[:, sub * 512:(sub + 1) * 512], start=(kt == 0), stop=(kt == nk - 1)),
                        ("ocb", kpt), (PSLK[sub],)))
            for fn_, rd_, wr_ in pend:
                kb.op("pe", fn_, reads=rd_, writes=wr_)
            kb.op("act", lambda e: e.activation(out=invl_sb[:], in_=psL[0:1, 0:1024], func=AF.Ln),
                  reads=PSLK, writes=("invl",))
            kb.op("act", lambda e: e.activation(out=invl_sb[:], in_=invl_sb[:], func=AF.Exp, scale=-1.0),
                  reads=("invl",), writes=("invl",))
            for sub in range(2):
                kb.op("pe", lambda e, sub=sub: e.matmul(psS[:, sub * 512:(sub + 1) * 512], o1f_sb[:],
                                                        invl_sb[0:1, sub * 512:(sub + 1) * 512], start=True, stop=True),
                      reads=("invl", "o1f"), writes=(PSSK[sub],))
            kb.op("dve", lambda e: e.tensor_copy(invb_sb[:], psS[:]), reads=PSSK, writes=("invb",))
            kb.op("dve", lambda e: e.scalar_tensor_tensor(out=ctx_sb[:], in0=invb_sb[:], scalar=1.0, in1=psCx[:],
                                                          op0=ALU.mult, op1=ALU.mult),
                  reads=("invb",) + PSCK, writes=("ctxv",))
            wk = f"ctxw{h}_{sp}"
            kb.op("sync", lambda e, r0=r0, q0=q0: e.dma_start(out=ctxf[r0:r0 + 128, q0:q0 + 1024], in_=ctx_sb[:]),
                  reads=("ctxv",), writes=(wk,), dma=True)
            ctx_keys.append(wk)
    return ctx_keys


def emit_beta(nc, kb):
    xn1_in = nc.dram_tensor("xn1", [T, D], BF16, kind="ExternalInput").ap()
    x1_in = nc.dram_tensor("x1", [T, D], F32, kind="ExternalInput").ap()
    wqT = nc.dram_tensor("wqT", [D, D], F32, kind="ExternalInput").ap()
    wkT = nc.dram_tensor("wkT", [D, D], F32, kind="ExternalInput").ap()
    wvT = nc.dram_tensor("wvT", [D, D], F32, kind="ExternalInput").ap()
    woT = nc.dram_tensor("woT", [D, D], F32, kind="ExternalInput").ap()
    bq = nc.dram_tensor("bq", [128, 16], F32, kind="ExternalInput").ap()
    bk = nc.dram_tensor("bk", [128, 16], F32, kind="ExternalInput").ap()
    bv = nc.dram_tensor("bv", [128, 16], F32, kind="ExternalInput").ap()
    bo = nc.dram_tensor("bo", [128, 16], F32, kind="ExternalInput").ap()
    ffn_cfg = {"tag": "l1", "loaded": {}}
    for nm, shp, dt in (("h2048", [D, D], BF16), ("h128", [128, 128], BF16), ("suug", [128, 16], F32),
                        ("sudp", [128, 64], F32), ("wsug", [128, 128], F32), ("wsdp", [128, 16], F32),
                        ("wupT", [D, 8192], F32), ("wgT", [D, 8192], F32), ("d2T", [64, 128, D], BF16)):
        ffn_cfg[nm] = nc.dram_tensor("ffn_" + nm, shp, dt, kind="ExternalInput").ap()
    KD = "ExternalOutput" if MEGA_DEBUG else "Internal"
    xn1f = nc.dram_tensor("xn1f_scr", [D, T], BF16, kind=KD).ap()
    Qf = nc.dram_tensor("Qf_scr", [D, T], BF16, kind=KD).ap()
    Kf = nc.dram_tensor("Kf_scr", [D, T], BF16, kind=KD).ap()
    Vf = nc.dram_tensor("Vf_scr", [D, T], BF16, kind=KD).ap()
    ctxf = nc.dram_tensor("ctxf_scr", [D, T], BF16, kind=KD).ap()
    attn1f = nc.dram_tensor("attn1f", [D, 2048], BF16, kind=KD).ap()
    ffn1f = nc.dram_tensor("ffn1f", [D, 2048], BF16, kind=KD).ap()
    y_scr = nc.dram_tensor("y_scr", [8192, 2048], BF16, kind="Internal").ap()
    out_rows = nc.dram_tensor("outp", [2048, D], F32, kind="ExternalOutput").ap()

    A = {}
    PS_ALL = nc.alloc_psum_tensor("PS", [128, 4096], F32)
    A["PS_ALL"] = PS_ALL[:]
    A["PB"] = [PS_ALL[:, i * 512:(i + 1) * 512] for i in range(8)]
    A["AR_A"] = nc.alloc_sbuf_tensor("AR_A", [128, 16 * 2048], BF16)[:]
    A["AR_B"] = nc.alloc_sbuf_tensor("AR_B", [128, 16 * 2048], BF16)[:]
    A["AR_C"] = nc.alloc_sbuf_tensor("AR_C", [128, 16384], BF16)[:]
    AR_S = nc.alloc_sbuf_tensor("AR_S", [128, 15360], BF16)
    kb.set_spool(AR_S, 15360)
    A["arA_old"] = ()
    A["arB_old"] = ()
    A["arC_old"] = ()
    A["xns_sb"] = None
    A["ffn_smalls"] = None
    PB = A["PB"]

    phase_featize(nc, kb, "b0", xn1_in, xn1f, ())
    kb.phase_fence(); kb.spool_reset()
    xn1f_ready = ()
    b_sb = nc.alloc_sbuf_tensor("b_bias", [128, 16], F32)
    out_sb = nc.alloc_sbuf_tensor("b_out", [128, 2 * 512], BF16)
    proj_keys = {}
    for (wap, bap, scale, dst, tag) in ((wqT, bq, float(128 ** -0.5), Qf, "q"),
                                        (wkT, bk, 1.0, Kf, "k"),
                                        (wvT, bv, 1.0, Vf, "v")):
        kb.op("gp", lambda e, w=wap: e.dma_start(out=A["AR_B"][:].rearrange("p (k m) -> p k m", k=16),
                                                 in_=w.rearrange("(k p) m -> p k m", p=128)),
              writes=("bw",) + tuple(A["arB_old"]), dma=True)
        A["arB_old"] = ("bw",)
        kb.op("sync", lambda e, b=bap: e.dma_start(out=b_sb[:], in_=b[:, :]), writes=("bb",), dma=True)
        if scale != 1.0:
            kb.op("dve", lambda e, sc=scale: e.tensor_scalar(out=b_sb[:], in0=b_sb[:], scalar1=sc, scalar2=None, op0=ALU.mult),
                  reads=("bb",), writes=("bb",))
        pk = []
        for half in range(2):
            toff = half * 2048
            for kc in range(16):
                kb.op("sync", lambda e, kc=kc, t=toff: e.dma_start(
                    out=A["AR_A"][:, kc * 2048:(kc + 1) * 2048], in_=xn1f[kc * 128:(kc + 1) * 128, t:t + 2048]),
                      reads=xn1f_ready, writes=(f"bxn{kc}",) + tuple(A["arA_old"]), dma=True)
            BXNK = tuple(f"bxn{kc}" for kc in range(16))
            A["arA_old"] = BXNK
            for m in range(16):
                for ts_ in range(4):
                    ps = PB[0] if ts_ % 2 == 0 else PB[1]
                    psk = "ps0" if ts_ % 2 == 0 else "ps1"
                    pairs = [(A["AR_B"][:, k * 2048 + m * 128:(k * 2048 + (m + 1) * 128)],
                              A["AR_A"][:, k * 2048 + ts_ * 512: k * 2048 + (ts_ + 1) * 512]) for k in range(16)]
                    mm_group(kb, ps[:], pairs, reads=("bw",) + BXNK, writes=(psk,))
                    ob = out_sb[:, (ts_ % 2) * 512:(ts_ % 2 + 1) * 512]
                    kb.op("dve", lambda e, s=ps[:], ob=ob, m=m, sc=scale: e.tensor_scalar(
                        out=ob, in0=s, scalar1=sc, scalar2=b_sb[:, m:m + 1], op0=ALU.mult, op1=ALU.add),
                          reads=(psk, "bb"), writes=(f"bob{ts_ % 2}",))
                    wk = f"{tag}w{m}_{half}_{ts_}"
                    kb.op("sync", lambda e, m=m, ts_=ts_, t=toff, ob=ob, dst=dst: e.dma_start(
                        out=dst[m * 128:(m + 1) * 128, t + ts_ * 512:t + (ts_ + 1) * 512], in_=ob),
                          reads=(f"bob{ts_ % 2}",), writes=(wk,), dma=True)
                    pk.append(wk)
        proj_keys[tag] = tuple(pk)
    kb.phase_fence(); kb.spool_reset()
    ctx_keys = phase_attn16(nc, kb, A, Qf, Kf, Vf, ctxf, (), (), ())
    kb.phase_fence(); kb.spool_reset()
    ctx_keys = ()
    A["arA_old"] = ()
    A["arB_old"] = ()
    ctxs = nc.dram_tensor("ctxs_scr", [D, 2048], BF16, kind=KD).ap()
    xn1s = nc.dram_tensor("xn1s_scr", [D, 2048], BF16, kind=KD).ap()
    x1s = nc.dram_tensor("x1s_scr", [2048, D], F32, kind=KD).ap()
    phase_extract_stripe(nc, kb, "exc", ctxf, ctxs, 0, 2048, BF16, "act", False)
    phase_extract_stripe(nc, kb, "exn", xn1f, xn1s, 0, 2048, BF16, "gp", False)
    phase_extract_stripe(nc, kb, "exx", x1_in, x1s, 2048, 0, F32, "sync", True)
    kb.phase_fence(); kb.spool_reset()
    for kc in range(16):
        kb.op("sync", lambda e, kc=kc: e.dma_start(
            out=A["AR_A"][:, kc * 2048:(kc + 1) * 2048],
            in_=ctxs[kc * 128:(kc + 1) * 128, :]),
              reads=(), writes=(f"cxn{kc}",), dma=True)
    CXNK = tuple(f"cxn{kc}" for kc in range(16))
    A["arA_old"] = CXNK
    kb.op("gp", lambda e: e.dma_start(out=A["AR_B"][:].rearrange("p (k m) -> p k m", k=16),
                                      in_=woT.rearrange("(k p) m -> p k m", p=128)),
          writes=("bwo",) + tuple(A["arB_old"]), dma=True)
    A["arB_old"] = ("bwo",)
    kb.op("sync", lambda e: e.dma_start(out=b_sb[:], in_=bo[:, :]), writes=("bb",), dma=True)
    a1_keys = []
    for m in range(16):
        for ts_ in range(4):
            ps = PB[0] if ts_ % 2 == 0 else PB[1]
            psk = "ps0" if ts_ % 2 == 0 else "ps1"
            pairs = [(A["AR_B"][:, k * 2048 + m * 128:(k * 2048 + (m + 1) * 128)],
                      A["AR_A"][:, k * 2048 + ts_ * 512: k * 2048 + (ts_ + 1) * 512]) for k in range(16)]
            mm_group(kb, ps[:], pairs, reads=("bwo",) + CXNK, writes=(psk,))
            ob = out_sb[:, (ts_ % 2) * 512:(ts_ % 2 + 1) * 512]
            kb.op("dve", lambda e, s=ps[:], ob=ob, m=m: e.tensor_scalar(
                out=ob, in0=s, scalar1=1.0, scalar2=b_sb[:, m:m + 1], op0=ALU.mult, op1=ALU.add),
                  reads=(psk, "bb"), writes=(f"bob{ts_ % 2}",))
            wk = f"a1w{m}_{ts_}"
            kb.op("sync", lambda e, m=m, ts_=ts_, ob=ob: e.dma_start(
                out=attn1f[m * 128:(m + 1) * 128, ts_ * 512:(ts_ + 1) * 512], in_=ob),
                  reads=(f"bob{ts_ % 2}",), writes=(wk,), dma=True)
            a1_keys.append(wk)
    kb.phase_fence(); kb.spool_reset()
    A["arB_old"] = ()
    A["arA_old"] = ()
    A["arC_old"] = ()
    fk = phase_ffnP(nc, kb, "f1", A, xn1s, (), ffn1f, 0, False, ffn_cfg, y_scr)
    kb.phase_fence(); kb.spool_reset()
    phase_ln(nc, kb, "j", x1s, None, 2048, a_feat=attn1f, f_feat=ffn1f, x1_out=out_rows,
             dyn_x=False, a_ready=(), f_ready=())


def build_beta():
    return build_program(emit_beta)



def _ffn_inputs(inputs, li, d2, H2048c, H128c):
    p = "L0_ffn_" if li == 0 else "L1_ffn_"
    f32 = np.float32
    ct = np.ascontiguousarray
    return {
        "ffn_h2048": H2048c, "ffn_h128": H128c,
        "ffn_suug": ct(inputs[p + "SU_ug"].astype(f32, copy=False).reshape(16, 128).T),
        "ffn_sudp": ct(inputs[p + "SU_dp"].astype(f32, copy=False).reshape(64, 128).T),
        "ffn_wsug": ct(inputs[p + "Wscale_ug"].astype(f32, copy=False).reshape(128, 128).T),
        "ffn_wsdp": ct(inputs[p + "Wscale_dp"].astype(f32, copy=False).reshape(16, 128).T),
        "ffn_wupT": ct(inputs[p + "W_up"].astype(f32, copy=False).T),
        "ffn_wgT": ct(inputs[p + "W_gate"].astype(f32, copy=False).T),
        "ffn_d2T": d2[li],
    }


def kernel(**inputs):
    CORES = list(range(NCORES))
    f32 = np.float32
    ct = np.ascontiguousarray
    xfull = ct(inputs["L0_attn_x"].astype(f32, copy=False))

    H64 = hadamard(64)
    H2048c = (hadamard(2048) * (2.0 ** -5)).astype(BF)
    H128c = (hadamard(128) * (2.0 ** -6)).astype(BF)
    UINC = np.triu(np.ones((128, 128), f32))
    I128F = np.eye(128, dtype=f32)
    I128B = I128F.astype(BF)
    ONES1 = np.ones((1, 128), f32)
    ones2 = np.zeros((128, 2), f32)
    ones2[0:64, 0] = 1.0
    ones2[64:128, 1] = 1.0
    ones2 = ones2.astype(BF)

    nc_fold = build_fold()
    OS = D // NCORES
    wd0 = inputs["L0_ffn_W_down"].astype(f32, copy=False)
    wd1 = inputs["L1_ffn_W_down"].astype(f32, copy=False)
    im = [{"H64": H64.astype(BF), "Wdn0": ct(wd0[c * OS:(c + 1) * OS]),
           "Wdn1": ct(wd1[c * OS:(c + 1) * OS])} for c in range(NCORES)]
    res = run_bass_kernel_spmd(nc_fold, im, core_ids=CORES)
    d2 = {}
    for li in range(2):
        D2 = np.concatenate([r[f"D2_{li}"] for r in res.results], axis=0)
        d2[li] = ct(D2.T.reshape(64, 128, D))

    nc_a = build_alpha()
    base_a = {
        "wqkvT": ct(inputs["L0_attn_W_qkv"].astype(f32, copy=False).T),
        "wbT": ct(inputs["L0_attn_W_b"].astype(f32, copy=False).T),
        "waT": ct(inputs["L0_attn_W_a"].astype(f32, copy=False).T),
        "convw": ct(inputs["L0_attn_conv_w"].astype(f32, copy=False)[:, 0, :]),
        "dtb": ct(inputs["L0_attn_dt_bias"].astype(f32, copy=False)[:, None]),
        "alog": ct(inputs["L0_attn_A_log"].astype(f32, copy=False)[:, None]),
        "ones2": ones2, "UINC": UINC, "I128F": I128F, "I128B": I128B, "ONES1": ONES1,
        "wzT": ct(inputs["L0_attn_W_z"].astype(f32, copy=False).T),
        "woutT": ct(inputs["L0_attn_W_out"].astype(f32, copy=False).T),
        "nw128": ct(np.tile(inputs["L0_attn_norm_weight"].astype(f32, copy=False), 2)[:, None]),
    }
    base_a.update(_ffn_inputs(inputs, 0, d2, H2048c, H128c))
    im = []
    for c in range(NCORES):
        b = c // 2
        m = dict(base_a)
        m["x"] = ct(xfull[b])
        im.append(m)
    res = run_bass_kernel_spmd(nc_a, im, core_ids=CORES)
    xn1 = [res.results[2 * b]["xn1"] for b in range(B)]
    x1 = [res.results[2 * b]["x1"] for b in range(B)]

    nc_b = build_beta()
    base_b = {
        "wqT": ct(inputs["L1_attn_Wq"].astype(f32, copy=False).T),
        "wkT": ct(inputs["L1_attn_Wk"].astype(f32, copy=False).T),
        "wvT": ct(inputs["L1_attn_Wv"].astype(f32, copy=False).T),
        "woT": ct(inputs["L1_attn_Wo"].astype(f32, copy=False).T),
        "bq": ct(inputs["L1_attn_bq"].astype(f32, copy=False).reshape(16, 128).T),
        "bk": ct(inputs["L1_attn_bk"].astype(f32, copy=False).reshape(16, 128).T),
        "bv": ct(inputs["L1_attn_bv"].astype(f32, copy=False).reshape(16, 128).T),
        "bo": ct(inputs["L1_attn_bo"].astype(f32, copy=False).reshape(16, 128).T),
    }
    base_b.update(_ffn_inputs(inputs, 1, d2, H2048c, H128c))
    im = []
    for c in range(NCORES):
        b = c // 2
        m = dict(base_b)
        m["xn1"] = xn1[b]
        m["x1"] = x1[b]
        im.append(m)
    res = run_bass_kernel_spmd(nc_b, im, core_ids=CORES)
    out = np.zeros((B, T, D), f32)
    for c in range(NCORES):
        b, s = c // 2, c % 2
        out[b, s * 2048:(s + 1) * 2048] = res.results[c]["outp"]
    return out



def phase_fold_full(nc, kb, A, h64_ap, wdn_ap, d2scr, pre):
    ARC = A["AR_C"]
    h64_sb = ARC[0:64, 8192:8256]
    slab = ARC[0:64, 0:2048]
    oslab = ARC[0:64, 2048:4096]
    ps = A["PB"][0][0:64, :]
    kb.op("act", lambda e: e.dma_start(out=h64_sb[:], in_=h64_ap[:, :]), writes=("fh64",), dma=True)
    for og in range(D // 4):
        pp = og % 4
        s_in = slab[:, pp * 512:(pp + 1) * 512]
        s_out = oslab[:, pp * 512:(pp + 1) * 512]
        src = wdn_ap[og * 4:(og + 1) * 4, :].rearrange("o (c i) -> c o i", i=128)
        dst = d2scr[og * 4:(og + 1) * 4, :].rearrange("o (c i) -> c o i", i=128)
        kin, kout = f"fsl{pp}", f"fso{pp}"
        kb.op("gp", lambda e, o=s_in, s=src: e.dma_start(out=o.rearrange("p (a b) -> p a b", a=4), in_=s),
              writes=(kin,), dma=True)
        kb.op("pe", lambda e, o=ps, l=h64_sb, r=s_in: e.matmul(o, l[:], r, start=True, stop=True),
              reads=(kin, "fh64"), writes=("ps0",))
        kb.op("act", lambda e, o=s_out, s=ps: e.activation(out=o, in_=s, func=AF.Copy),
              reads=("ps0",), writes=(kout,))
        wk = pre + f"w{og}"
        kb.op("act", lambda e, o=dst, s=s_out: e.dma_start(out=o, in_=s.rearrange("p (a b) -> p a b", a=4)),
              reads=(kout,), writes=(wk,), dma=True)


def phase_fold_T(nc, kb, A, d2scr, d2T3, pre):
    ARC = A["AR_C"]
    stgs = (ARC[:, 4096:6144], ARC[:, 6144:8192])
    d2flat = d2T3.rearrange("c p m -> (c p) m")
    for s in range(64):
        stg = stgs[s % 2]
        kt = f"ftp{s % 2}"
        kb.op("act", lambda e, s=s, o=stg: e.dma_start_transpose(out=o[:], in_=d2scr[:, s * 128:(s + 1) * 128]),
              writes=(kt,), dma=True)
        kb.op("act", lambda e, s=s, i_=stg: e.dma_start(out=d2flat[s * 128:(s + 1) * 128, :], in_=i_[:]),
              reads=(kt,), writes=(pre + f"w{s}",), dma=True)


def emit_mega(nc, kb):
    x_in = nc.dram_tensor("x", [T, D], F32, kind="ExternalInput").ap()
    A = {}
    for nm, shp, dt in (("wqkvT", [D, D], F32), ("wbT", [D, 16], F32), ("waT", [D, 16], F32),
                        ("convw", [D, 4], F32), ("dtb", [16, 1], F32), ("alog", [16, 1], F32),
                        ("ones2", [128, 2], BF16), ("UINC", [128, 128], F32), ("I128F", [128, 128], F32),
                        ("I128B", [128, 128], BF16), ("ONES1", [1, 128], F32),
                        ("wzT", [D, 1024], F32), ("woutT", [1024, D], F32), ("nw128", [128, 1], F32)):
        A[nm] = nc.dram_tensor(nm, shp, dt, kind="ExternalInput").ap()
    wqT = nc.dram_tensor("wqT", [D, D], F32, kind="ExternalInput").ap()
    wkT = nc.dram_tensor("wkT", [D, D], F32, kind="ExternalInput").ap()
    wvT = nc.dram_tensor("wvT", [D, D], F32, kind="ExternalInput").ap()
    woT = nc.dram_tensor("woT", [D, D], F32, kind="ExternalInput").ap()
    bq = nc.dram_tensor("bq", [128, 16], F32, kind="ExternalInput").ap()
    bk = nc.dram_tensor("bk", [128, 16], F32, kind="ExternalInput").ap()
    bv = nc.dram_tensor("bv", [128, 16], F32, kind="ExternalInput").ap()
    bo = nc.dram_tensor("bo", [128, 16], F32, kind="ExternalInput").ap()
    h64_ap = nc.dram_tensor("H64", [64, 64], BF16, kind="ExternalInput").ap()
    cfgs = {}
    for li in range(2):
        p = f"f{li}_"
        cfg = {"tag": f"l{li}", "loaded": {}}
        for nm, shp, dt in (("h2048", [D, D], BF16), ("h128", [128, 128], BF16), ("suug", [128, 16], F32),
                            ("sudp", [128, 64], F32), ("wsug", [128, 128], F32), ("wsdp", [128, 16], F32),
                            ("wupT", [D, 8192], F32), ("wgT", [D, 8192], F32)):
            cfg[nm] = nc.dram_tensor(p + nm, shp, dt, kind="ExternalInput").ap()
        cfg["wdn"] = nc.dram_tensor(p + "wdn", [D, 8192], F32, kind="ExternalInput").ap()
        cfg["d2scr"] = nc.dram_tensor(p + "d2scr", [D, 8192], BF16, kind="Internal").ap()
        cfg["d2T"] = nc.dram_tensor(p + "d2T", [64, 128, D], BF16, kind="Internal").ap()
        cfgs[li] = cfg
    xn0_scr = nc.dram_tensor("xn0_scr", [T, D], BF16, kind="Internal").ap()
    A["xn0f"] = nc.dram_tensor("xn0f_scr", [D, T], BF16, kind="Internal").ap()
    A["qkvf_scr"] = nc.dram_tensor("qkvf_scr", [D, T], BF16, kind="Internal").ap()
    A["rows_scr"] = nc.dram_tensor("rows_scr", [64, T], F32, kind="Internal").ap()
    A["o_scr"] = nc.dram_tensor("o_scr", [T, 1024], BF16, kind="Internal").ap()
    A["yn_scr"] = nc.dram_tensor("yn_scr", [T, 1024], BF16, kind="Internal").ap()
    A["attn0f"] = nc.dram_tensor("attn0f", [D, T], BF16, kind="Internal").ap()
    ffn0f = nc.dram_tensor("ffn0f", [D, T], BF16, kind="Internal").ap()
    y_scr = nc.dram_tensor("y_scr", [8192, 2048], BF16, kind="Internal").ap()
    xn1_scr = nc.dram_tensor("xn1_scr", [T, D], BF16, kind="Internal").ap()
    x1_scr = nc.dram_tensor("x1_scr", [T, D], F32, kind="Internal").ap()
    xn1f = nc.dram_tensor("xn1f_scr", [D, T], BF16, kind="Internal").ap()
    Qf = nc.dram_tensor("Qf_scr", [D, T], BF16, kind="Internal").ap()
    Kf = nc.dram_tensor("Kf_scr", [D, T], BF16, kind="Internal").ap()
    Vf = nc.dram_tensor("Vf_scr", [D, T], BF16, kind="Internal").ap()
    ctxf = nc.dram_tensor("ctxf_scr", [D, T], BF16, kind="Internal").ap()
    attn1f = nc.dram_tensor("attn1f", [D, 2048], BF16, kind="Internal").ap()
    ffn1f = nc.dram_tensor("ffn1f", [D, 2048], BF16, kind="Internal").ap()
    out_rows = nc.dram_tensor("outp", [2048, D], F32, kind="ExternalOutput").ap()

    PS_ALL = nc.alloc_psum_tensor("PS", [128, 4096], F32)
    A["PS_ALL"] = PS_ALL[:]
    A["PB"] = [PS_ALL[:, i * 512:(i + 1) * 512] for i in range(8)]
    A["AR_A"] = nc.alloc_sbuf_tensor("AR_A", [128, 16 * 2048], BF16)[:]
    A["AR_B"] = nc.alloc_sbuf_tensor("AR_B", [128, 16 * 2048], BF16)[:]
    A["AR_C"] = nc.alloc_sbuf_tensor("AR_C", [128, 16384], BF16)[:]
    AR_S = nc.alloc_sbuf_tensor("AR_S", [128, 15360], BF16)
    kb.set_spool(AR_S, 15360)
    b_sb = nc.alloc_sbuf_tensor("b_bias", [128, 16], F32)
    out_sb = nc.alloc_sbuf_tensor("b_out", [128, 2 * 512], BF16)
    PB = A["PB"]
    A["arA_old"] = ()
    A["arB_old"] = ()
    A["arC_old"] = ()
    A["xns_sb"] = None
    A["ffn_smalls"] = None

    for li in range(2):
        phase_fold_full(nc, kb, A, h64_ap, cfgs[li]["wdn"], cfgs[li]["d2scr"], f"fo{li}")
        phase_fold_T(nc, kb, A, cfgs[li]["d2scr"], cfgs[li]["d2T"], f"ft{li}")

    phase_ln(nc, kb, "l0", x_in, xn0_scr, T)
    kb.phase_fence(); kb.spool_reset()
    phase_featize(nc, kb, "z0", xn0_scr, A["xn0f"], ())
    kb.phase_fence(); kb.spool_reset()
    A["xn0f_ready"] = ()
    phase_proj16(nc, kb, A)
    kb.phase_fence(); kb.spool_reset()
    A["qkv_keys"] = ()
    A["row_keys"] = ()
    phase_scan16(nc, kb, A)
    kb.phase_fence(); kb.spool_reset()
    A["o_keys"] = ()
    A["arA_old"] = ()
    phase_epi16(nc, kb, A)
    kb.phase_fence(); kb.spool_reset()
    A["a0_keys"] = ()
    A["arA_old"] = ()
    A["arB_old"] = ()
    A["arC_old"] = ()
    phase_ffnP(nc, kb, "fa", A, A["xn0f"], A["xn0f_ready"], ffn0f, 0, False, cfgs[0], y_scr)
    phase_ffnP(nc, kb, "fb", A, A["xn0f"], A["xn0f_ready"], ffn0f, 2048, False, cfgs[0], y_scr)
    kb.phase_fence(); kb.spool_reset()
    phase_ln(nc, kb, "r1", x_in, xn1_scr, T, a_feat=A["attn0f"], f_feat=ffn0f, x1_out=x1_scr,
             a_ready=(), f_ready=())
    kb.phase_fence(); kb.spool_reset()

    A["arA_old"] = ()
    A["arB_old"] = ()
    A["arC_old"] = ()
    A["xns_sb"] = None
    A["ffn_smalls"] = None
    phase_featize(nc, kb, "b0", xn1_scr, xn1f, ())
    kb.phase_fence(); kb.spool_reset()
    xn1f_ready = ()
    proj_keys = {}
    for (wap, bap, scale, dst, tag) in ((wqT, bq, float(128 ** -0.5), Qf, "q"),
                                        (wkT, bk, 1.0, Kf, "k"),
                                        (wvT, bv, 1.0, Vf, "v")):
        kb.op("gp", lambda e, w=wap: e.dma_start(out=A["AR_B"][:].rearrange("p (k m) -> p k m", k=16),
                                                 in_=w.rearrange("(k p) m -> p k m", p=128)),
              writes=("bw",) + tuple(A["arB_old"]), dma=True)
        A["arB_old"] = ("bw",)
        kb.op("sync", lambda e, b=bap: e.dma_start(out=b_sb[:], in_=b[:, :]), writes=("bb",), dma=True)
        if scale != 1.0:
            kb.op("dve", lambda e, sc=scale: e.tensor_scalar(out=b_sb[:], in0=b_sb[:], scalar1=sc, scalar2=None, op0=ALU.mult),
                  reads=("bb",), writes=("bb",))
        pk = []
        for half in range(2):
            toff = half * 2048
            for kc in range(16):
                kb.op("sync", lambda e, kc=kc, t=toff: e.dma_start(
                    out=A["AR_A"][:, kc * 2048:(kc + 1) * 2048], in_=xn1f[kc * 128:(kc + 1) * 128, t:t + 2048]),
                      reads=xn1f_ready, writes=(f"bxn{kc}",) + tuple(A["arA_old"]), dma=True)
            BXNK = tuple(f"bxn{kc}" for kc in range(16))
            A["arA_old"] = BXNK
            for m in range(16):
                for ts_ in range(4):
                    ps = PB[0] if ts_ % 2 == 0 else PB[1]
                    psk = "ps0" if ts_ % 2 == 0 else "ps1"
                    pairs = [(A["AR_B"][:, k * 2048 + m * 128:(k * 2048 + (m + 1) * 128)],
                              A["AR_A"][:, k * 2048 + ts_ * 512: k * 2048 + (ts_ + 1) * 512]) for k in range(16)]
                    mm_group(kb, ps[:], pairs, reads=("bw",) + BXNK, writes=(psk,))
                    ob = out_sb[:, (ts_ % 2) * 512:(ts_ % 2 + 1) * 512]
                    kb.op("dve", lambda e, s=ps[:], ob=ob, m=m, sc=scale: e.tensor_scalar(
                        out=ob, in0=s, scalar1=sc, scalar2=b_sb[:, m:m + 1], op0=ALU.mult, op1=ALU.add),
                          reads=(psk, "bb"), writes=(f"bob{ts_ % 2}",))
                    wk = f"{tag}w{m}_{half}_{ts_}"
                    kb.op("sync", lambda e, m=m, ts_=ts_, t=toff, ob=ob, dst=dst: e.dma_start(
                        out=dst[m * 128:(m + 1) * 128, t + ts_ * 512:t + (ts_ + 1) * 512], in_=ob),
                          reads=(f"bob{ts_ % 2}",), writes=(wk,), dma=True)
                    pk.append(wk)
        proj_keys[tag] = tuple(pk)
    kb.phase_fence(); kb.spool_reset()
    phase_attn16(nc, kb, A, Qf, Kf, Vf, ctxf, (), (), ())
    kb.phase_fence(); kb.spool_reset()
    A["arA_old"] = ()
    A["arB_old"] = ()
    ctxs = nc.dram_tensor("ctxs_scr", [D, 2048], BF16, kind="Internal").ap()
    xn1s = nc.dram_tensor("xn1s_scr", [D, 2048], BF16, kind="Internal").ap()
    x1s = nc.dram_tensor("x1s_scr", [2048, D], F32, kind="Internal").ap()
    phase_extract_stripe(nc, kb, "exc", ctxf, ctxs, 0, 2048, BF16, "act", False)
    phase_extract_stripe(nc, kb, "exn", xn1f, xn1s, 0, 2048, BF16, "gp", False)
    phase_extract_stripe(nc, kb, "exx", x1_scr, x1s, 2048, 0, F32, "sync", True)
    kb.phase_fence(); kb.spool_reset()
    for kc in range(16):
        kb.op("sync", lambda e, kc=kc: e.dma_start(
            out=A["AR_A"][:, kc * 2048:(kc + 1) * 2048],
            in_=ctxs[kc * 128:(kc + 1) * 128, :]),
              reads=(), writes=(f"cxn{kc}",), dma=True)
    CXNK = tuple(f"cxn{kc}" for kc in range(16))
    A["arA_old"] = CXNK
    kb.op("gp", lambda e: e.dma_start(out=A["AR_B"][:].rearrange("p (k m) -> p k m", k=16),
                                      in_=woT.rearrange("(k p) m -> p k m", p=128)),
          writes=("bwo",) + tuple(A["arB_old"]), dma=True)
    A["arB_old"] = ("bwo",)
    kb.op("sync", lambda e: e.dma_start(out=b_sb[:], in_=bo[:, :]), writes=("bb",), dma=True)
    for m in range(16):
        for ts_ in range(4):
            ps = PB[0] if ts_ % 2 == 0 else PB[1]
            psk = "ps0" if ts_ % 2 == 0 else "ps1"
            pairs = [(A["AR_B"][:, k * 2048 + m * 128:(k * 2048 + (m + 1) * 128)],
                      A["AR_A"][:, k * 2048 + ts_ * 512: k * 2048 + (ts_ + 1) * 512]) for k in range(16)]
            mm_group(kb, ps[:], pairs, reads=("bwo",) + CXNK, writes=(psk,))
            ob = out_sb[:, (ts_ % 2) * 512:(ts_ % 2 + 1) * 512]
            kb.op("dve", lambda e, s=ps[:], ob=ob, m=m: e.tensor_scalar(
                out=ob, in0=s, scalar1=1.0, scalar2=b_sb[:, m:m + 1], op0=ALU.mult, op1=ALU.add),
                  reads=(psk, "bb"), writes=(f"bob{ts_ % 2}",))
            wk = f"a1w{m}_{ts_}"
            kb.op("sync", lambda e, m=m, ts_=ts_, ob=ob: e.dma_start(
                out=attn1f[m * 128:(m + 1) * 128, ts_ * 512:(ts_ + 1) * 512], in_=ob),
                  reads=(f"bob{ts_ % 2}",), writes=(wk,), dma=True)
    kb.phase_fence(); kb.spool_reset()
    A["arB_old"] = ()
    A["arA_old"] = ()
    A["arC_old"] = ()
    phase_ffnP(nc, kb, "f1", A, xn1s, (), ffn1f, 0, False, cfgs[1], y_scr)
    kb.phase_fence(); kb.spool_reset()
    phase_ln(nc, kb, "j", x1s, None, 2048, a_feat=attn1f, f_feat=ffn1f, x1_out=out_rows,
             dyn_x=False, a_ready=(), f_ready=())


def build_mega():
    return build_program(emit_mega)


def kernel(**inputs):
    CORES = list(range(NCORES))
    f32 = np.float32
    ct = np.ascontiguousarray
    xfull = ct(inputs["L0_attn_x"].astype(f32, copy=False))

    H64 = hadamard(64)
    H2048c = (hadamard(2048) * (2.0 ** -5)).astype(BF)
    H128c = (hadamard(128) * (2.0 ** -6)).astype(BF)
    UINC = np.triu(np.ones((128, 128), f32))
    I128F = np.eye(128, dtype=f32)
    ones2 = np.zeros((128, 2), f32)
    ones2[0:64, 0] = 1.0
    ones2[64:128, 1] = 1.0

    base = {
        "wqkvT": ct(inputs["L0_attn_W_qkv"].astype(f32, copy=False).T),
        "wbT": ct(inputs["L0_attn_W_b"].astype(f32, copy=False).T),
        "waT": ct(inputs["L0_attn_W_a"].astype(f32, copy=False).T),
        "convw": ct(inputs["L0_attn_conv_w"].astype(f32, copy=False)[:, 0, :]),
        "dtb": ct(inputs["L0_attn_dt_bias"].astype(f32, copy=False)[:, None]),
        "alog": ct(inputs["L0_attn_A_log"].astype(f32, copy=False)[:, None]),
        "ones2": ones2.astype(BF), "UINC": UINC, "I128F": I128F, "I128B": I128F.astype(BF),
        "ONES1": np.ones((1, 128), f32),
        "wzT": ct(inputs["L0_attn_W_z"].astype(f32, copy=False).T),
        "woutT": ct(inputs["L0_attn_W_out"].astype(f32, copy=False).T),
        "nw128": ct(np.tile(inputs["L0_attn_norm_weight"].astype(f32, copy=False), 2)[:, None]),
        "wqT": ct(inputs["L1_attn_Wq"].astype(f32, copy=False).T),
        "wkT": ct(inputs["L1_attn_Wk"].astype(f32, copy=False).T),
        "wvT": ct(inputs["L1_attn_Wv"].astype(f32, copy=False).T),
        "woT": ct(inputs["L1_attn_Wo"].astype(f32, copy=False).T),
        "bq": ct(inputs["L1_attn_bq"].astype(f32, copy=False).reshape(16, 128).T),
        "bk": ct(inputs["L1_attn_bk"].astype(f32, copy=False).reshape(16, 128).T),
        "bv": ct(inputs["L1_attn_bv"].astype(f32, copy=False).reshape(16, 128).T),
        "bo": ct(inputs["L1_attn_bo"].astype(f32, copy=False).reshape(16, 128).T),
        "H64": H64.astype(BF),
    }
    for li in range(2):
        p = "L0_ffn_" if li == 0 else "L1_ffn_"
        q = f"f{li}_"
        base[q + "h2048"] = H2048c
        base[q + "h128"] = H128c
        base[q + "suug"] = ct(inputs[p + "SU_ug"].astype(f32, copy=False).reshape(16, 128).T)
        base[q + "sudp"] = ct(inputs[p + "SU_dp"].astype(f32, copy=False).reshape(64, 128).T)
        base[q + "wsug"] = ct(inputs[p + "Wscale_ug"].astype(f32, copy=False).reshape(128, 128).T)
        base[q + "wsdp"] = ct(inputs[p + "Wscale_dp"].astype(f32, copy=False).reshape(16, 128).T)
        base[q + "wupT"] = ct(inputs[p + "W_up"].astype(f32, copy=False).T)
        base[q + "wgT"] = ct(inputs[p + "W_gate"].astype(f32, copy=False).T)
        base[q + "wdn"] = ct(inputs[p + "W_down"].astype(f32, copy=False))

    nc_m = build_mega()
    im = []
    for c in range(NCORES):
        m = dict(base)
        m["x"] = ct(xfull[c // 2])
        im.append(m)
    res = run_bass_kernel_spmd(nc_m, im, core_ids=CORES)
    out = np.zeros((B, T, D), f32)
    for c in range(NCORES):
        b, s = c // 2, c % 2
        out[b, s * 2048:(s + 1) * 2048] = res.results[c]["outp"]
    return out
